# version 01
# baseline (speedup 1.0000x reference)
from contextlib import ExitStack

import numpy as np
import concourse.bass as bass
import concourse.mybir as mybir
from concourse.bass_utils import run_bass_kernel_spmd

F32 = mybir.dt.float32
U32 = mybir.dt.uint32
AF = mybir.ActivationFunctionType
ALU = mybir.AluOpType
AX = mybir.AxisListType

T = 2048
D = 1024
NTL = 16
H, HD = 16, 64
FF = 4096
E = 8
CAP0 = 384
NSLOT0 = E * CAP0
NTILE1 = 40
NSLOT1 = NTILE1 * 128

ENGS = ("sync", "tensor", "vector", "scalar", "gpsimd")


class Op:
    __slots__ = ("eng", "fn", "deps", "dma", "ch", "sem_val", "tag", "seq")

    def __init__(self, eng, fn, deps, dma, ch, tag):
        self.eng, self.fn, self.deps, self.dma, self.ch, self.tag = (
            eng, fn, deps, dma, ch, tag)
        self.sem_val = None


class Buf:
    __slots__ = ("w", "readers", "name")

    def __init__(self, name=""):
        self.w = None
        self.readers = []
        self.name = name


class B:

    def __init__(self, nc):
        self.nc = nc
        self.q = {e: [] for e in ENGS}
        self.used = set()
        self.oneshot = set()
        self.seq = 0

    serial = False

    def do(self, eng, fn, reads=(), writes=(), deps=(), dma=None, tag=""):
        if dma == "c_dbg" and getattr(self, "_last_dbg", None) is not None:
            deps = list(deps) + [self._last_dbg]
        if self.serial and getattr(self, "_prev", None) is not None:
            deps = list(deps) + [self._prev]
        dd = []
        for bb in reads:
            if bb.w is not None:
                dd.append(bb.w)
        for bb in writes:
            if bb.w is not None:
                dd.append(bb.w)
            dd.extend(bb.readers)
        dd.extend(d for d in deps if d is not None)
        ch = dma if dma else eng
        op = Op(eng, fn, dd, dma is not None, ch, tag)
        op.seq = self.seq
        self.seq += 1
        self.q[eng].append(op)
        for bb in reads:
            bb.readers.append(op)
        for bb in writes:
            bb.w = op
            bb.readers = []
        if dma == "c_dbg":
            self._last_dbg = op
        self._prev = op
        return op

    def finalize(self, finals):
        for op in finals:
            self.used.add(op)
        for e in ENGS:
            for op in self.q[e]:
                kept = []
                for d in op.deps:
                    if d.eng == op.eng and not d.dma and op.eng == "tensor":
                        continue
                    if (op.dma and d.dma and d.ch == op.ch
                            and d.ch in self.oneshot):
                        continue
                    kept.append(d)
                    self.used.add(d)
                op.deps = kept
        counts = {}
        for e in ENGS:
            for op in self.q[e]:
                if op.dma or op in self.used:
                    counts[op.ch] = counts.get(op.ch, 0) + (16 if op.dma else 1)
                    op.sem_val = counts[op.ch]
        last_dma = {}
        first_use = {}
        for e in ENGS:
            for op in self.q[e]:
                if op.dma and op.ch in self.oneshot:
                    last_dma[op.ch] = max(last_dma.get(op.ch, -1), op.seq)
                for d in op.deps:
                    if d.ch in self.oneshot and d.dma:
                        first_use.setdefault(d.ch, op.seq)
                        first_use[d.ch] = min(first_use[d.ch], op.seq)
        for ch, lu in last_dma.items():
            fu = first_use.get(ch)
            assert fu is None or fu > lu, (
                f"oneshot channel {ch}: consumer at seq {fu} before last DMA "
                f"at seq {lu} -> deadlock risk")
        self.final_counts = counts
        return counts

    def emit(self, eng_name, eng, sems):
        watermark = {}
        for op in self.q[eng_name]:
            need = {}
            for d in op.deps:
                if d.sem_val is None:
                    raise RuntimeError(f"dep of {op.tag} has no sem: {d.tag}")
                v = (self.final_counts[d.ch]
                     if d.ch in self.oneshot else d.sem_val)
                if v > need.get(d.ch, 0):
                    need[d.ch] = v
            for se, v in sorted(need.items()):
                if watermark.get(se, 0) < v:
                    eng.wait_ge(sems[se], v)
                    watermark[se] = v
            inst = op.fn(eng)
            if op.sem_val is not None:
                inst.then_inc(sems[op.ch], 16 if op.dma else 1)



def build_nc(stage_lo=0, stage_hi=3, d_sub="all", serial=False):
    nc = bass.Bass("TRN2", target_bir_lowering=False, debug=False)
    st = ExitStack()

    def din(name, shape):
        return nc.dram_tensor(name, shape, F32, kind="ExternalInput").ap()

    xin = din("x", [T, D])
    W_attn = din("W_attn", [D, 3 * D])
    b_attn = din("b_attn", [1, 3 * D])
    W_proj = din("W_proj", [D, D])
    b_proj = din("b_proj", [1, D])
    rms_w = din("rms_w", [1, D])
    gate0_w = din("gate0_w", [E, D])
    W1_0 = din("W1_0", [E * D, FF])
    b1_0 = din("b1_0", [E * 32, 128])
    W2_0 = din("W2_0", [E * FF, D])
    b2_0 = din("b2_0", [E, D])
    Wq1 = din("Wq1", [HD, HD])
    Wk1 = din("Wk1", [HD, HD])
    Wv1 = din("Wv1", [HD, HD])
    Wq2 = din("Wq2", [D, D])
    Wk2 = din("Wk2", [D, D])
    Wv2 = din("Wv2", [D, D])
    Wo2 = din("Wo2", [D, D])
    bq2 = din("bq2", [1, D])
    bk2 = din("bk2", [1, D])
    bv2 = din("bv2", [1, D])
    bo2 = din("bo2", [1, D])
    ln1_g = din("ln1_g", [1, D])
    ln1_b = din("ln1_b", [1, D])
    ln2_g = din("ln2_g", [1, D])
    ln2_b = din("ln2_b", [1, D])
    gate1_w = din("gate1_w", [D, E])
    gate1_b = din("gate1_b", [1, E])
    W1_1 = din("W1_1", [E * D, FF])
    b1_1 = din("b1_1", [E * 32, 128])
    W2_1 = din("W2_1", [E * FF, D])
    b2_1 = din("b2_1", [E, D])
    out_hbm = nc.dram_tensor("out", [T, D], F32, kind="ExternalOutput").ap()

    def scratch(name, shape, **kw):
        return nc.dram_tensor(name, shape, F32, kind="Internal", **kw).ap()

    xg0_hbm = scratch("xg0", [NSLOT0 + 128, D])
    yg0_hbm = scratch("yg0", [NSLOT0 + 128, D])
    qT_hbm = scratch("qTh", [D, T])
    kT_hbm = scratch("kTh", [D, T])
    v_hbm = scratch("vh", [T, D])
    xg1_hbm = scratch("xg1", [NSLOT1 + 128, D])
    x3_hbm = scratch("x3h", [T, D])
    yg1_hbm = scratch("yg1", [NSLOT1 + 128, D])
    dbg_hbm = scratch("dbg", [2048, D])
    if stage_lo <= 2 <= stage_hi:
        cc_in = scratch("ccin", [T * H])
        cc_out = nc.dram_tensor("ccout", [8 * T * H], F32, kind="Internal",
                                addr_space="Shared").ap()

    ic = {}

    def const(name, arr):
        ic[name] = nc.inline_tensor(np.ascontiguousarray(arr, np.float32),
                                    name="c_" + name).ap()

    const("ident", np.eye(128))
    const("triu", np.triu(np.ones((128, 128))))
    const("triuS", np.triu(np.ones((128, 128)), 1))
    const("ones_col", np.ones((128, 1)))
    const("ones_1x128", np.ones((1, 128)))
    const("iota_col", np.arange(128)[:, None])
    const("iota8", np.tile(np.arange(8), (128, 1)))
    const("iota40", np.arange(40)[None, :])
    const("thr17", (np.arange(17) * 128.0)[None, :])

    ARENA = 33800
    xh = st.enter_context(nc.sbuf_tensor("Xs", [128, NTL, D], F32))
    ar = st.enter_context(nc.sbuf_tensor("AR", [128, ARENA], F32))
    cs = st.enter_context(nc.sbuf_tensor("CS", [128, 640], F32))
    idxs = st.enter_context(nc.sbuf_tensor("IDX", [128, 80], U32))
    psum = [st.enter_context(nc.psum_tensor(f"PS{i}", [128, 512], F32))
            for i in range(8)]

    b = B(nc)
    b.serial = serial
    b.oneshot = {"c_pre", "c_pre2", "c_pre3", "c_pre3b", "c_pre4", "c_pre5",
                 "c_pre6", "c_pre7", "c_st", "c_zero0", "c_zero1"}
    cbuf = Buf("consts")

    C = {}
    coff = 0
    for name, width in (("ident", 128), ("triu", 128), ("triuS", 128),
                        ("ones_col", 1), ("ones_1x128", 128),
                        ("iota_col", 1), ("iota8", 8), ("iota40", 40),
                        ("thr17", 17)):
        src = ic[name]
        rows = src.shape[0]
        C[name] = cs[0:rows, coff:coff + width]
        b.do("sync", lambda e, d=C[name], s=src:
             e.dma_start(out=d, in_=s), writes=[cbuf], dma="c_pre",
             tag=f"ldc_{name}")
        coff += width

    Xb = [Buf(f"X{i}") for i in range(NTL)]
    Xt = [xh[:, i, :] for i in range(NTL)]

    x_r = xin.rearrange("(a p) d -> p a d", p=128)
    for i in range(0, NTL, 4):
        b.do("sync", lambda e, i=i: e.dma_start(
            out=xh[:, i:i + 4, :], in_=x_r[:, i:i + 4, :]),
            writes=[Xb[j] for j in range(i, i + 4)], dma="c_pre", tag=f"ldX{i}")

    class Arena:
        def __init__(self):
            self.off = 0
            self.marks = []

        def push(self):
            self.marks.append(self.off)

        def pop(self):
            self.off = self.marks.pop()

        def fence(self):
            return [b.q[e][-1] for e in ENGS if b.q[e]]

        def take(self, rows, *shape):
            n = 1
            for s in shape:
                n *= s
            assert self.off + n <= ARENA, f"arena overflow {self.off}+{n}"
            ap = ar[0:rows, self.off:self.off + n]
            self.off += n
            if len(shape) > 1:
                ap = ap.rearrange("p (a c) -> p a c", a=shape[0], c=shape[1])
            return ap

    A = Arena()
    pb = [Buf(f"ps{i}") for i in range(8)]

    def mm(ps_ap, lhsT, rhs, start, stop, reads, bank, tag=""):
        wl = [pb[bank]] if start else []
        dl = [] if start else [pb[bank].w]
        op = b.do("tensor",
                  lambda e, o=ps_ap, l=lhsT, r=rhs, s1=start, s2=stop:
                  e.matmul(o, l, r, start=s1, stop=s2),
                  reads=list(reads), writes=wl, deps=dl, tag=tag)
        if not start:
            pb[bank].w = op
        return op

    def tr(ps_ap, src_ap, reads, bank, tag="tr"):
        kk_ = src_ap.shape[0]
        return b.do("tensor",
                    lambda e, o=ps_ap, s=src_ap, k=kk_:
                    e.transpose(o, s, C["ident"][0:k, 0:k]),
                    reads=list(reads) + [cbuf], writes=[pb[bank]], tag=tag)

    def ln_apply(res_ap, res_bufs, out_ap, out_bufs, tmp, tmpb, stc, stcb,
                 eps=1e-5, tag="ln"):
        ssum, ssq = stc[:, 0:1], stc[:, 1:2]
        mean, var = stc[:, 2:3], stc[:, 3:4]
        rstd, negmr = stc[:, 4:5], stc[:, 5:6]
        b.do("vector", lambda e: e.reduce_sum(ssum, res_ap, axis=AX.X),
             reads=list(res_bufs), writes=[stcb], tag=tag + "s")
        b.do("vector", lambda e: e.scalar_tensor_tensor(
            tmp, res_ap, 1.0, res_ap, ALU.mult, ALU.mult, accum_out=ssq),
            reads=list(res_bufs), writes=[tmpb, stcb], tag=tag + "q")
        b.do("vector", lambda e: e.tensor_scalar(
            mean, ssum, 1.0 / D, None, ALU.mult),
            reads=[stcb], writes=[stcb], tag=tag + "m")
        b.do("vector", lambda e: e.scalar_tensor_tensor(
            var, mean, -1.0, mean, ALU.mult, ALU.mult),
            reads=[stcb], writes=[stcb], tag=tag + "mm")
        b.do("vector", lambda e: e.scalar_tensor_tensor(
            var, ssq, 1.0 / D, var, ALU.mult, ALU.add),
            reads=[stcb], writes=[stcb], tag=tag + "v")
        b.do("vector", lambda e: e.tensor_scalar(
            var, var, eps, None, ALU.add),
            reads=[stcb], writes=[stcb], tag=tag + "e")
        b.do("vector", lambda e: e.reciprocal(rstd, var),
             reads=[stcb], writes=[stcb], tag=tag + "r")
        b.do("scalar", lambda e: e.activation(rstd, rstd, AF.Sqrt),
             reads=[stcb], writes=[stcb], tag=tag + "sq")
        b.do("vector", lambda e: e.tensor_scalar(
            negmr, mean, rstd, -1.0, ALU.mult, ALU.mult),
            reads=[stcb], writes=[stcb], tag=tag + "n")
        return b.do("vector", lambda e: e.tensor_scalar(
            out_ap, res_ap, rstd, negmr, ALU.mult, ALU.add),
            reads=list(res_bufs) + [stcb], writes=list(out_bufs), tag=tag + "a")

    def feat_map(ap_, bufs, scr, scrb, tag="fm"):
        b.do("vector", lambda e: e.tensor_scalar(
            scr, ap_, 0.0, None, ALU.min),
            reads=list(bufs), writes=[scrb], tag=tag + "1")
        b.do("scalar", lambda e: e.activation(scr, scr, AF.Exp),
             reads=[scrb], writes=[scrb], tag=tag + "2")
        b.do("vector", lambda e: e.scalar_tensor_tensor(
            ap_, ap_, 0.0, scr, ALU.max, ALU.add),
            reads=list(bufs) + [scrb], writes=list(bufs), tag=tag + "3")

    def stage_a():
        A.push()
        fenceA = A.fence()

        def Buf_(name):
            bb = Buf(name)
            bb.readers = list(fenceA)
            return bb
        SL, NCH = 256, 2
        NSL = T // SL
        xT = A.take(128, 8, SL)
        xTb = Buf_("xT")
        qT = A.take(128, 8, SL)
        qTb = Buf_("qT")
        kT = A.take(128, 8, SL)
        kTb = Buf_("kT")
        kk = A.take(128, NCH, D)
        kkb = Buf_("kk")
        vv = A.take(128, NCH, D)
        vvb = Buf_("vv")
        yy = A.take(128, NCH, D)
        yyb = Buf_("yy")
        yT = A.take(128, 8, SL)
        yTb = Buf_("yT")
        amt = A.take(128, 16, 128)
        amtb = Buf_("amt")
        fmscr = A.take(128, 8, SL)
        fmscrb = Buf_("fmscr")
        wqk = [(A.take(128, 512), Buf(f"wqk{i}")) for i in range(4)]
        wkv = [(A.take(128, 512), Buf(f"wkv{i}")) for i in range(2)]
        wpr = [(A.take(128, 512), Buf(f"wpr{i}")) for i in range(2)]
        kvsb = A.take(128, 520)
        kvsbb = Buf_("kvsb")
        rden = A.take(128, 16)
        rdenb = Buf_("rden")
        battc = A.take(128, 24)
        battcb = Buf_("battc")
        battr = A.take(128, 3072)
        battrb = Buf_("battr")
        bprjr = A.take(128, D)
        bprjrb = Buf_("bprjr")
        tmp = A.take(128, D)
        tmpb = Buf_("tmpA")
        res = A.take(128, D)
        resb = Buf_("resA")
        stc = A.take(128, 8)
        stcb = Buf_("stcA")

        b.do("sync", lambda e: e.dma_start(
            out=battc[:, :], in_=b_attn.rearrange("o (a p) -> p (o a)", p=128),
            allow_slow_non_contiguous=True),
            writes=[battcb], dma="c_pre", tag="ld_battc")
        b.do("sync", lambda e: e.dma_start(
            out=battr[0:1, :], in_=b_attn[0:1, :]),
            writes=[battrb], dma="c_pre", tag="ld_battr")
        b.do("sync", lambda e: e.dma_start(
            out=bprjr[0:1, :], in_=b_proj[0:1, :]),
            writes=[bprjrb], dma="c_pre", tag="ld_bprj")

        b.do("vector", lambda e: e.memset(kvsb[:, :], 0.0), writes=[kvsbb])

        KVE, KVO, KS = 5, 6, 7

        for s in range(NSL):
            t0 = s * SL
            tiles = [t0 // 128, t0 // 128 + 1]
            for c in range(8):
                for u in range(NCH):
                    bank = (c * NCH + u) % 2
                    i = tiles[u]
                    tr(psum[bank][:, 0:128], Xt[i][:, c * 128:(c + 1) * 128],
                       [Xb[i]], bank, tag=f"trx{s}")
                    b.do("vector", lambda e, c=c, u=u, bank=bank: e.tensor_copy(
                        xT[:, c, u * 128:(u + 1) * 128], psum[bank][:, 0:128]),
                        reads=[pb[bank]], writes=[xTb], tag="evx")
            for mg in range(4):
                for c in range(8):
                    slot = c % 4
                    wap, wb_ = wqk[slot]
                    b.do("sync", lambda e, c=c, mg=mg, w=wap: e.dma_start(
                        out=w[:, :],
                        in_=W_attn[c * 128:(c + 1) * 128,
                                   mg * 512:(mg + 1) * 512]),
                        writes=[wb_], dma=f"c_wqk{slot}", tag="ldwqk")
                    for j in range(4):
                        mm(psum[j][:, 0:SL], wap[:, j * 128:(j + 1) * 128],
                           xT[:, c, :], start=(c == 0), stop=(c == 7),
                           reads=[wb_, xTb], bank=j, tag=f"qkT{mg}{j}")
                for j in range(4):
                    m = mg * 4 + j
                    dst = qT if m < 8 else kT
                    dstb = qTb if m < 8 else kTb
                    ch = m % 8
                    b.do("vector", lambda e, d=dst, ch=ch, j=j, m=m:
                         e.tensor_scalar(d[:, ch, :], psum[j][:, 0:SL],
                                         battc[:, m:m + 1], None, ALU.add),
                         reads=[pb[j], battcb], writes=[dstb], tag="evqkT")
            for g in range(4):
                col0 = 1024 + 512 * g
                dst, dstb = (kk, kkb) if g < 2 else (vv, vvb)
                go = g % 2
                for c in range(8):
                    slot = c % 2
                    wap, wb_ = wkv[slot]
                    b.do("sync", lambda e, c=c, col0=col0, w=wap: e.dma_start(
                        out=w[:, :],
                        in_=W_attn[c * 128:(c + 1) * 128, col0:col0 + 512]),
                        writes=[wb_], dma=f"c_wkv{slot}", tag="ldwkv")
                    for u in range(NCH):
                        mm(psum[u][:, 0:512],
                           xT[:, c, u * 128:(u + 1) * 128], wap[:, :],
                           start=(c == 0), stop=False,
                           reads=[wb_, xTb], bank=u, tag=f"kv{g}")
                for u in range(NCH):
                    mm(psum[u][:, 0:512], C["ones_1x128"],
                       battr[0:1, col0:col0 + 512], start=False, stop=True,
                       reads=[cbuf, battrb], bank=u, tag="kvb")
                    b.do("scalar", lambda e, d=dst, u=u, go=go: e.activation(
                        d[:, u, go * 512:(go + 1) * 512], psum[u][:, 0:512],
                        AF.Copy),
                        reads=[pb[u]], writes=[dstb], tag="evkv")
            feat_map(qT[:, :, :], [qTb], fmscr[:, :, :], fmscrb, tag="fmq")
            feat_map(kT[:, :, :], [kTb], fmscr[:, :, :], fmscrb, tag="fmk")
            feat_map(kk.rearrange("p a c -> p (a c)"), [kkb],
                     fmscr.rearrange("p a c -> p (a c)"), fmscrb, tag="fmkk")
            for u in range(NCH):
                ci = s * NCH + u
                for p in range(8):
                    kpair = kk[:, u, p * 128:(p + 1) * 128]
                    vpair = vv[:, u, p * 128:(p + 1) * 128]
                    kvbank = KVE if p < 4 else KVO
                    cc0 = (p % 4) * 128
                    mm(psum[kvbank][:, cc0:cc0 + 128], kpair, vpair,
                       start=(p % 4 == 0), stop=(p % 4 == 3),
                       reads=[kkb, vvb], bank=kvbank, tag="kvup")
                    mm(psum[KS][:, p:p + 1], kpair, C["ones_col"],
                       start=(p == 0), stop=(p == 7),
                       reads=[kkb, cbuf], bank=KS, tag="ksup")
                for h in range(H):
                    bank = h % 2
                    hp = (h % 2) * 64
                    hc = h // 2
                    lk = kT[hp:hp + 64, hc, u * 128:(u + 1) * 128]
                    lq = qT[hp:hp + 64, hc, u * 128:(u + 1) * 128]
                    mm(psum[bank][:, 0:128], lk, lq, start=True, stop=True,
                       reads=[kTb, qTb], bank=bank, tag="AT")
                    b.do("vector", lambda e, h=h, bank=bank: e.tensor_tensor(
                        amt[:, h, :], psum[bank][:, 0:128], C["triu"],
                        ALU.mult),
                        reads=[pb[bank], cbuf], writes=[amtb], tag="mask")
                for h in range(H):
                    hp = (h % 2) * 64
                    hc = h // 2
                    nbank = 2 + (h % 2)
                    ncol = (h // 2) * 64
                    lq = qT[hp:hp + 64, hc, u * 128:(u + 1) * 128]
                    mm(psum[nbank][:, ncol:ncol + 64], amt[:, h, :],
                       vv[:, u, h * 64:(h + 1) * 64], start=(h < 2),
                       stop=False, reads=[amtb, vvb], bank=nbank, tag="numi")
                    mm(psum[nbank][:, ncol:ncol + 64], lq,
                       kvsb[hp:hp + 64, (h // 2) * 64:(h // 2) * 64 + 64],
                       start=False, stop=(h >= H - 2),
                       reads=[qTb, kvsbb], bank=nbank, tag="nume")
                    mm(psum[4][:, h:h + 1], amt[:, h, :], C["ones_col"],
                       start=(h == 0), stop=False,
                       reads=[amtb, cbuf], bank=4, tag="dena")
                    mm(psum[4][:, h:h + 1], lq,
                       kvsb[hp:hp + 64, 512 + h // 2:513 + h // 2],
                       start=False, stop=(h == H - 1),
                       reads=[qTb, kvsbb], bank=4, tag="denb")
                b.do("vector", lambda e: e.tensor_scalar(
                    rden[:, 0:16], psum[4][:, 0:16], 1e-6, None, ALU.add),
                    reads=[pb[4]], writes=[rdenb], tag="deps")
                b.do("vector", lambda e: e.reciprocal(rden[:, 0:16],
                                                      rden[:, 0:16]),
                     reads=[rdenb], writes=[rdenb], tag="drcp")
                for h in range(H):
                    nbank = 2 + (h % 2)
                    ncol = (h // 2) * 64
                    b.do("vector", lambda e, h=h, u=u, nbank=nbank, ncol=ncol:
                         e.tensor_scalar(yy[:, u, h * 64:(h + 1) * 64],
                                         psum[nbank][:, ncol:ncol + 64],
                                         rden[:, h:h + 1], None, ALU.mult),
                         reads=[pb[nbank], rdenb], writes=[yyb], tag="yev")
                for bank, p0 in ((KVE, 0), (KVO, 4)):
                    dvE = psum[bank][0:64, 0:512].rearrange(
                        "p (a c) -> p a c", a=4, c=128)[:, :, 0:64]
                    dvO = psum[bank][64:128, 0:512].rearrange(
                        "p (a c) -> p a c", a=4, c=128)[:, :, 64:128]
                    tgtE = kvsb[0:64, p0 * 64:(p0 + 4) * 64].rearrange(
                        "p (a c) -> p a c", a=4, c=64)
                    tgtO = kvsb[64:128, p0 * 64:(p0 + 4) * 64].rearrange(
                        "p (a c) -> p a c", a=4, c=64)
                    b.do("vector", lambda e, t=tgtE, dd=dvE: e.tensor_tensor(
                        t, t, dd, ALU.add),
                        reads=[pb[bank]], writes=[kvsbb], tag="addKVe")
                    b.do("vector", lambda e, t=tgtO, dd=dvO: e.tensor_tensor(
                        t, t, dd, ALU.add),
                        reads=[pb[bank]], writes=[kvsbb], tag="addKVo")
                b.do("vector", lambda e: e.tensor_tensor(
                    kvsb[:, 512:520], kvsb[:, 512:520], psum[KS][:, 0:8],
                    ALU.add),
                    reads=[pb[KS]], writes=[kvsbb], tag="addKS")
            for c in range(8):
                for u in range(NCH):
                    bank = (c * NCH + u) % 2
                    tr(psum[bank][:, 0:128], yy[:, u, c * 128:(c + 1) * 128],
                       [yyb], bank, tag="try")
                    b.do("vector", lambda e, c=c, u=u, bank=bank: e.tensor_copy(
                        yT[:, c, u * 128:(u + 1) * 128], psum[bank][:, 0:128]),
                        reads=[pb[bank]], writes=[yTb], tag="evy")
            for g in range(2):
                for c in range(8):
                    slot = c % 2
                    wap, wb_ = wpr[slot]
                    b.do("sync", lambda e, c=c, g=g, w=wap: e.dma_start(
                        out=w[:, :],
                        in_=W_proj[c * 128:(c + 1) * 128,
                                   g * 512:(g + 1) * 512]),
                        writes=[wb_], dma=f"c_wpr{slot}", tag="ldwpr")
                    for u in range(NCH):
                        mm(psum[g * 2 + u][:, 0:512],
                           yT[:, c, u * 128:(u + 1) * 128], wap[:, :],
                           start=(c == 0), stop=False,
                           reads=[wb_, yTb], bank=g * 2 + u, tag="proj")
                for u in range(NCH):
                    mm(psum[g * 2 + u][:, 0:512], C["ones_1x128"],
                       bprjr[0:1, g * 512:(g + 1) * 512], start=False,
                       stop=True, reads=[cbuf, bprjrb], bank=g * 2 + u,
                       tag="projb")
            for u in range(NCH):
                i = tiles[u]
                for g in range(2):
                    b.do("vector", lambda e, i=i, g=g, u=u: e.tensor_tensor(
                        res[:, g * 512:(g + 1) * 512],
                        psum[g * 2 + u][:, 0:512],
                        Xt[i][:, g * 512:(g + 1) * 512], ALU.add),
                        reads=[pb[g * 2 + u], Xb[i]], writes=[resb],
                        tag="resadd")
                ln_apply(res[:, :], [resb], Xt[i], [Xb[i]], tmp[:, :], tmpb,
                         stc, stcb, eps=1e-5, tag=f"lnA{i}")
        A.pop()

    def stage_b():
        A.push()
        fenceB = A.fence()

        def BufB(name):
            bb = Buf(name)
            bb.readers = list(fenceB)
            return bb
        xn_sl = A.take(128, 4, D)
        xn_slb = BufB("xn_sl")
        xnT = A.take(128, 8, 512)
        xnTb = BufB("xnT")
        rms_rep = A.take(128, D)
        rms_repb = BufB("rms_rep")
        g0T = A.take(128, 8, E)
        g0Tb = BufB("g0T")
        gate = A.take(128, 2, E)
        gateb = BufB("gate")
        gcol = A.take(128, 8)
        gcolb = BufB("gcol")
        score = A.take(128, NTL)
        scoreb = BufB("score")
        slotf = A.take(128, NTL)
        slotfb = BufB("slotf")
        postile = A.take(128, E)
        postileb = BufB("postile")
        carry = A.take(128, E)
        carryb = BufB("carry")
        zt = A.take(128, D)
        ztb = BufB("zt")
        xg_sl = A.take(128, 3, D)
        xg_slb = BufB("xg_sl")
        xgT = A.take(128, 8, CAP0)
        xgTb = BufB("xgT")
        y_sl = A.take(128, 3, D)
        y_slb = BufB("y_sl")
        w1s = [(A.take(128, 256), BufB(f"w1s{i}")) for i in range(2)]
        w2s = [(A.take(128, D), BufB(f"w2s{i}")) for i in range(2)]
        h_m = A.take(128, 2, CAP0)
        h_mb = BufB("h_m")
        b1c = A.take(128, 32)
        b1cb = BufB("b1c")
        b2r = A.take(128, D)
        b2rb = BufB("b2r")
        y1 = A.take(128, D)
        y1b = BufB("y1")
        tmp = A.take(128, D)
        tmpb = BufB("tmpB")
        res = A.take(128, D)
        resb = BufB("resB")
        stc = A.take(128, 8)
        stcb = BufB("stcB")

        idxb = BufB("idx")

        g0r = gate0_w.rearrange("e (c p) -> p c e", p=128)
        for c in range(8):
            b.do("sync", lambda e, c=c: e.dma_start(
                out=g0T[:, c, :], in_=g0r[:, c, :],
                allow_slow_non_contiguous=True),
                writes=[g0Tb], dma="c_pre2", tag="ld_g0T")
        rmsrow = A.take(128, D)
        rmsrowb = BufB("rmsrow")
        b.do("sync", lambda e: e.dma_start(out=rmsrow[0:1, :], in_=rms_w[0:1, :]),
             writes=[rmsrowb], dma="c_pre2", tag="ld_rms")
        for g in range(2):
            mm(psum[6][:, 0:512], C["ones_1x128"], rmsrow[0:1, g * 512:(g + 1) * 512],
               start=True, stop=True, reads=[cbuf, rmsrowb], bank=6, tag="rmsbc")
            b.do("vector", lambda e, g=g: e.tensor_copy(
                rms_rep[:, g * 512:(g + 1) * 512], psum[6][:, 0:512]),
                reads=[pb[6]], writes=[rms_repb], tag="rmsbc_ev")
        b.do("vector", lambda e: e.memset(zt[:, :], 0.0), writes=[ztb])
        NZR = (NSLOT0 + 128) // 128
        for i in range(NZR):
            b.do("gpsimd", lambda e, i=i: e.dma_start(
                out=xg0_hbm[i * 128:(i + 1) * 128, :], in_=zt[:, :]),
                reads=[ztb], dma="c_zero0", tag="zero_xg0")
        b.do("vector", lambda e: e.memset(carry[0:1, :], 0.0), writes=[carryb])

        for s in range(4):
            for u in range(4):
                i = s * 4 + u
                b.do("vector", lambda e, i=i: e.scalar_tensor_tensor(
                    tmp, Xt[i], 1.0, Xt[i], ALU.mult, ALU.mult,
                    accum_out=stc[:, 0:1]),
                    reads=[Xb[i]], writes=[tmpb, stcb], tag="rms_sq")
                b.do("vector", lambda e: e.tensor_scalar(
                    stc[:, 1:2], stc[:, 0:1], 1.0 / D, 1e-6, ALU.mult, ALU.add),
                    reads=[stcb], writes=[stcb], tag="rms_ms")
                b.do("vector", lambda e: e.reciprocal(stc[:, 1:2], stc[:, 1:2]),
                     reads=[stcb], writes=[stcb], tag="rms_rc")
                b.do("scalar", lambda e: e.activation(
                    stc[:, 1:2], stc[:, 1:2], AF.Sqrt),
                    reads=[stcb], writes=[stcb], tag="rms_sq2")
                b.do("scalar", lambda e, i=i, u=u: e.activation(
                    xn_sl[:, u, :], Xt[i], AF.Copy, scale=stc[:, 1:2]),
                    reads=[Xb[i], stcb], writes=[xn_slb], tag="rms_ap")
                b.do("vector", lambda e, u=u: e.tensor_tensor(
                    xn_sl[:, u, :], xn_sl[:, u, :], rms_rep[:, :], ALU.mult),
                    reads=[xn_slb, rms_repb], writes=[xn_slb], tag="rms_w")
            for c in range(8):
                for u in range(4):
                    bank = (c * 4 + u) % 2
                    tr(psum[bank][:, 0:128], xn_sl[:, u, c * 128:(c + 1) * 128],
                       [xn_slb], bank, tag="trxn")
                    b.do("vector", lambda e, c=c, u=u, bank=bank: e.tensor_copy(
                        xnT[:, c, u * 128:(u + 1) * 128], psum[bank][:, 0:128]),
                        reads=[pb[bank]], writes=[xnTb], tag="evxn")
            for u in range(4):
                i = s * 4 + u
                for c in range(8):
                    mm(psum[2][:, 0:E], xnT[:, c, u * 128:(u + 1) * 128],
                       g0T[:, c, :], start=(c == 0), stop=(c == 7),
                       reads=[xnTb, g0Tb], bank=2, tag="gate")
                b.do("vector", lambda e: e.reduce_max(
                    gcol[:, 0:1], psum[2][:, 0:E], axis=AX.X),
                    reads=[pb[2]], writes=[gcolb], tag="g_mx")
                b.do("vector", lambda e: e.tensor_scalar(
                    gcol[:, 1:2], gcol[:, 0:1], -1.0, None, ALU.mult),
                    reads=[gcolb], writes=[gcolb], tag="g_nmx")
                b.do("scalar", lambda e: e.activation(
                    gate[:, 0, :], psum[2][:, 0:E], AF.Exp,
                    bias=gcol[:, 1:2], accum_out=gcol[:, 2:3]),
                    reads=[pb[2], gcolb], writes=[gateb, gcolb], tag="g_exp")
                b.do("vector", lambda e: e.reciprocal(
                    gcol[:, 3:4], gcol[:, 2:3]),
                    reads=[gcolb], writes=[gcolb], tag="g_rs")
                b.do("vector", lambda e: e.tensor_scalar(
                    gate[:, 0, :], gate[:, 0, :], gcol[:, 3:4], None, ALU.mult),
                    reads=[gateb, gcolb], writes=[gateb], tag="g_pr")
                for ee in range(E):
                    b.do("vector", lambda e, ee=ee: e.tensor_scalar(
                        tmp[:, 0:E], gate[:, 0, :], gate[:, 0, ee:ee + 1],
                        None, ALU.is_gt),
                        reads=[gateb], writes=[tmpb], tag="r_gt")
                    b.do("vector", lambda e: e.reduce_sum(
                        gcol[:, 4:5], tmp[:, 0:E], axis=AX.X),
                        reads=[tmpb], writes=[gcolb], tag="r_gts")
                    if ee > 0:
                        b.do("vector", lambda e, ee=ee: e.tensor_scalar(
                            tmp[:, 0:ee], gate[:, 0, 0:ee],
                            gate[:, 0, ee:ee + 1], None, ALU.is_equal),
                            reads=[gateb], writes=[tmpb], tag="r_eq")
                        b.do("vector", lambda e, ee=ee: e.reduce_sum(
                            gcol[:, 5:6], tmp[:, 0:ee], axis=AX.X),
                            reads=[tmpb], writes=[gcolb], tag="r_eqs")
                        b.do("vector", lambda e: e.tensor_tensor(
                            gcol[:, 4:5], gcol[:, 4:5], gcol[:, 5:6], ALU.add),
                            reads=[gcolb], writes=[gcolb], tag="r_add")
                    b.do("vector", lambda e, ee=ee: e.tensor_scalar(
                        gate[:, 1, ee:ee + 1], gcol[:, 4:5], 0.0, None,
                        ALU.is_equal),
                        reads=[gcolb], writes=[gateb], tag="r_sel")
                b.do("vector", lambda e: e.tensor_tensor(
                    tmp[:, 0:E], gate[:, 1, :], C["iota8"], ALU.mult),
                    reads=[gateb, cbuf], writes=[tmpb], tag="q_em")
                b.do("vector", lambda e: e.reduce_sum(
                    gcol[:, 6:7], tmp[:, 0:E], axis=AX.X),
                    reads=[tmpb], writes=[gcolb], tag="q_e")
                b.do("vector", lambda e: e.tensor_tensor(
                    tmp[:, 0:E], gate[:, 1, :], gate[:, 0, :], ALU.mult),
                    reads=[gateb], writes=[tmpb], tag="q_sm")
                b.do("vector", lambda e, i=i: e.reduce_sum(
                    score[:, i:i + 1], tmp[:, 0:E], axis=AX.X),
                    reads=[tmpb], writes=[scoreb], tag="q_s")
                mm(psum[3][:, 0:E], C["triuS"], gate[:, 1, :],
                   start=True, stop=False, reads=[cbuf, gateb], bank=3,
                   tag="scanp")
                mm(psum[3][:, 0:E], C["ones_1x128"], carry[0:1, :],
                   start=False, stop=True, reads=[cbuf, carryb], bank=3,
                   tag="scanc")
                b.do("vector", lambda e: e.tensor_copy(
                    postile[:, :], psum[3][:, 0:E]),
                    reads=[pb[3]], writes=[postileb], tag="scanev")
                mm(psum[4][0:1, 0:E], C["ones_col"], gate[:, 1, :],
                   start=True, stop=True, reads=[cbuf, gateb], bank=4,
                   tag="colsum")
                b.do("vector", lambda e: e.tensor_tensor(
                    carry[0:1, :], carry[0:1, :], psum[4][0:1, 0:E], ALU.add),
                    reads=[pb[4]], writes=[carryb], tag="carryup")
                b.do("vector", lambda e: e.tensor_tensor(
                    tmp[:, 0:E], postile[:, :], gate[:, 1, :], ALU.mult),
                    reads=[postileb, gateb], writes=[tmpb], tag="q_pm")
                b.do("vector", lambda e: e.reduce_sum(
                    gcol[:, 7:8], tmp[:, 0:E], axis=AX.X),
                    reads=[tmpb], writes=[gcolb], tag="q_p")
                b.do("vector", lambda e, i=i: e.tensor_scalar(
                    slotf[:, i:i + 1], gcol[:, 6:7], float(CAP0),
                    None, ALU.mult),
                    reads=[gcolb], writes=[slotfb], tag="q_sl1")
                b.do("vector", lambda e, i=i: e.tensor_tensor(
                    slotf[:, i:i + 1], slotf[:, i:i + 1], gcol[:, 7:8],
                    ALU.add),
                    reads=[slotfb, gcolb], writes=[slotfb], tag="q_sl2")
                b.do("vector", lambda e, i=i: e.tensor_copy(
                    idxs[:, i:i + 1], slotf[:, i:i + 1]),
                    reads=[slotfb], writes=[idxb], tag="q_cast")
                b.do("gpsimd", lambda e, i=i, u=u: e.indirect_dma_start(
                    out=xg0_hbm[0:NSLOT0, :],
                    out_offset=bass.IndirectOffsetOnAxis(
                        ap=idxs[:, i:i + 1], axis=0),
                    in_=xn_sl[:, u, :], in_offset=None),
                    reads=[idxb, xn_slb], deps=[b.q["gpsimd"][-1]]
                    if b.q["gpsimd"] else [],
                    dma="c_scat0", tag="scat0")

        for ee in range(E):
            b.do("sync", lambda e, ee=ee: e.dma_start(
                out=xg_sl[:, :, :],
                in_=xg0_hbm[ee * CAP0:(ee + 1) * CAP0, :].rearrange(
                    "(a p) d -> p a d", p=128)),
                reads=[], writes=[xg_slb],
                deps=[o for o in b.q["gpsimd"] if o.tag == "scat0"],
                dma="c_xg0", tag="ldxg")
            b.do("sync", lambda e, ee=ee: e.dma_start(
                out=b1c[:, :],
                in_=b1_0[ee * 32:(ee + 1) * 32, :].rearrange("a p -> p a"),
                allow_slow_non_contiguous=True),
                writes=[b1cb], dma="c_b1", tag="ldb1")
            b.do("sync", lambda e, ee=ee: e.dma_start(
                out=b2r[0:1, :], in_=b2_0[ee:ee + 1, :]),
                writes=[b2rb], dma="c_b2", tag="ldb2")
            for c in range(8):
                for uu in range(3):
                    bank = (c * 3 + uu) % 2
                    tr(psum[bank][:, 0:128], xg_sl[:, uu, c * 128:(c + 1) * 128],
                       [xg_slb], bank, tag="trxg")
                    b.do("vector", lambda e, c=c, uu=uu, bank=bank:
                         e.tensor_copy(xgT[:, c, uu * 128:(uu + 1) * 128],
                                       psum[bank][:, 0:128]),
                         reads=[pb[bank]], writes=[xgTb], tag="evxg")
            for fg in range(16):
                for c in range(8):
                    slot = c % 2
                    wap, wb_ = w1s[slot]
                    b.do("sync", lambda e, ee=ee, fg=fg, c=c, w=wap:
                         e.dma_start(out=w[:, :],
                                     in_=W1_0[ee * 1024 + c * 128:
                                              ee * 1024 + (c + 1) * 128,
                                              fg * 256:(fg + 1) * 256]),
                         writes=[wb_], dma=f"c_w1{slot}", tag="ldw1")
                    for j in range(2):
                        mm(psum[j][:, 0:CAP0], wap[:, j * 128:(j + 1) * 128],
                           xgT[:, c, :], start=(c == 0), stop=(c == 7),
                           reads=[wb_, xgTb], bank=j, tag="hmm")
                for j in range(2):
                    fc = fg * 2 + j
                    b.do("scalar", lambda e, j=j, fc=fc: e.activation(
                        h_m[:, j, :], psum[j][:, 0:CAP0], AF.Silu,
                        bias=b1c[:, fc:fc + 1]),
                        reads=[pb[j], b1cb], writes=[h_mb], tag="silu")
                    wap2, wb2_ = w2s[j]
                    b.do("sync", lambda e, ee=ee, fc=fc, w=wap2: e.dma_start(
                        out=w[:, :],
                        in_=W2_0[ee * 4096 + fc * 128:
                                 ee * 4096 + (fc + 1) * 128, :]),
                        writes=[wb2_], dma=f"c_w2{j}", tag="ldw2")
                    for uu in range(3):
                        for g in range(2):
                            mm(psum[2 + uu * 2 + g][:, 0:512],
                               h_m[:, j, uu * 128:(uu + 1) * 128],
                               wap2[:, g * 512:(g + 1) * 512],
                               start=(fc == 0), stop=False,
                               reads=[h_mb, wb2_], bank=2 + uu * 2 + g,
                               tag="yemm")
            for uu in range(3):
                for g in range(2):
                    mm(psum[2 + uu * 2 + g][:, 0:512], C["ones_1x128"],
                       b2r[0:1, g * 512:(g + 1) * 512], start=False, stop=True,
                       reads=[cbuf, b2rb], bank=2 + uu * 2 + g, tag="yeb")
                    b.do("vector", lambda e, uu=uu, g=g: e.tensor_copy(
                        y_sl[:, uu, g * 512:(g + 1) * 512],
                        psum[2 + uu * 2 + g][:, 0:512]),
                        reads=[pb[2 + uu * 2 + g]], writes=[y_slb], tag="evye")
            b.do("gpsimd", lambda e, ee=ee: e.dma_start(
                out=yg0_hbm[ee * CAP0:(ee + 1) * CAP0, :].rearrange(
                    "(a p) d -> p a d", p=128), in_=y_sl[:, :, :]),
                reads=[y_slb], dma="c_yg0", tag="styg")

        yg_dep = [o for o in b.q["gpsimd"] if o.tag == "styg"]
        for i in range(NTL):
            b.do("gpsimd", lambda e, i=i: e.indirect_dma_start(
                out=y1[:, :], out_offset=None,
                in_=yg0_hbm[0:NSLOT0, :],
                in_offset=bass.IndirectOffsetOnAxis(ap=idxs[:, i:i + 1],
                                                    axis=0)),
                reads=[idxb], writes=[y1b], deps=yg_dep,
                dma="c_ga0", tag="ga0")
            b.do("vector", lambda e, i=i: e.scalar_tensor_tensor(
                res, y1, score[:, i:i + 1], Xt[i], ALU.mult, ALU.add),
                reads=[y1b, scoreb, Xb[i]], writes=[resb], tag="comb")
            ln_apply(res, [resb], Xt[i], [Xb[i]], tmp, tmpb, stc, stcb,
                     eps=1e-5, tag=f"lnB{i}")
        A.pop()

    def stage_c():
        A.push()
        fenceC = A.fence()

        def BufC(name):
            bb = Buf(name)
            bb.readers = list(fenceC)
            return bb
        xT = A.take(128, 8, 512)
        xTb = BufC("xTc")
        wq3 = A.take(128, 3, HD)
        wq3b = BufC("wq3")
        wraw = A.take(128, 3, HD)
        wrawbs = [BufC(f"wraw{j}") for j in range(3)]
        qkv = A.take(128, 3, HD)
        qkvb = BufC("qkv")
        ds_sb = A.take(128, NTL, H)
        ds_sbb = BufC("ds_sb")
        dsall = A.take(128, 8, NTL * H)
        dsallb = BufC("dsall")
        mx = A.take(128, NTL * H)
        mxb = BufC("mx")
        sexp = A.take(128, NTL * H)
        sexpb = BufC("sexp")
        t2 = A.take(128, NTL * H)
        t2b = BufC("t2")
        vsr = A.take(128, D)
        vsrb = BufC("vsr")
        vrep = A.take(128, D)
        vrepb = BufC("vrep")
        att = A.take(128, D)
        attb = BufC("att")
        tmp = A.take(128, D)
        tmpb = BufC("tmpC")
        res = A.take(128, D)
        resb = BufC("resC")
        stc = A.take(128, 8)
        stcb = BufC("stcC")

        for j, w in enumerate((Wq1, Wk1, Wv1)):
            b.do("sync", lambda e, j=j, w=w: e.dma_start(
                out=wraw[0:64, j, :], in_=w[:, :]),
                writes=[wrawbs[j]], dma="c_pre3", tag="ldw1a")
        for j in range(3):
            tr(psum[5][0:64, 0:64], wraw[0:64, j, :], [wrawbs[j]], 5,
               tag="trw1a")
            b.do("vector", lambda e, j=j: e.tensor_copy(
                wq3[0:64, j, :], psum[5][0:64, 0:64]),
                reads=[pb[5]], writes=[wq3b], tag="evw1a")
        for j in range(3):
            b.do("gpsimd", lambda e, j=j: e.dma_start(
                out=wq3[64:128, j, :], in_=wq3[0:64, j, :]),
                reads=[wq3b], dma="c_pre3b", tag="mirw1a")

        VS0, VS1 = 6, 7
        first_vs = [True, True]
        for s in range(4):
            for c in range(8):
                for u in range(4):
                    bank = (c * 4 + u) % 2
                    i = s * 4 + u
                    tr(psum[bank][:, 0:128], Xt[i][:, c * 128:(c + 1) * 128],
                       [Xb[i]], bank, tag="trxc")
                    b.do("vector", lambda e, c=c, u=u, bank=bank: e.tensor_copy(
                        xT[:, c, u * 128:(u + 1) * 128], psum[bank][:, 0:128]),
                        reads=[pb[bank]], writes=[xTb], tag="evxc")
            for u in range(4):
                i = s * 4 + u
                for h in range(H):
                    hp = (h % 2) * 64
                    hc = h // 2
                    lx = xT[hp:hp + 64, hc, u * 128:(u + 1) * 128]
                    for j in range(3):
                        mm(psum[2 + j][:, 0:HD], lx, wq3[hp:hp + 64, j, :],
                           start=True, stop=True, reads=[xTb, wq3b],
                           bank=2 + j, tag="qkvC")
                        b.do("vector", lambda e, j=j: e.tensor_copy(
                            qkv[:, j, :], psum[2 + j][:, 0:HD]),
                            reads=[pb[2 + j]], writes=[qkvb], tag="evqkvC")
                    b.do("vector", lambda e: e.tensor_tensor(
                        tmp[:, 0:HD], qkv[:, 0, :], qkv[:, 1, :], ALU.mult),
                        reads=[qkvb], writes=[tmpb], tag="dsm")
                    b.do("vector", lambda e, i=i, h=h: e.reduce_sum(
                        ds_sb[:, i, h:h + 1], tmp[:, 0:HD], axis=AX.X),
                        reads=[tmpb], writes=[ds_sbb], tag="dsr")
                    vsb = VS0 + (h % 2)
                    fidx = h % 2
                    mm(psum[vsb][0:1, (h // 2) * 64:(h // 2) * 64 + 64],
                       C["ones_col"], qkv[:, 2, :],
                       start=(first_vs[fidx]), stop=(i == NTL - 1 and h >= H - 2),
                       reads=[cbuf, qkvb], bank=vsb, tag="vsum")
                    first_vs[fidx] = False
        b.do("vector", lambda e: e.tensor_scalar(
            ds_sb[:, :, :], ds_sb[:, :, :], 1.0 / 32.0, None, ALU.mult),
            reads=[ds_sbb], writes=[ds_sbb], tag="dssc")
        cc_in_r = cc_in.rearrange("(p x) -> p x", p=128)
        b.do("gpsimd", lambda e: e.dma_start(
            out=cc_in_r,
            in_=ds_sb.rearrange("p a h -> p (a h)")),
            reads=[ds_sbb], dma="c_cc1", tag="stds")
        coll = b.do("gpsimd", lambda e: e.collective_compute(
            "AllGather", ALU.bypass, replica_groups=[list(range(8))],
            ins=[cc_in[:]], outs=[cc_out[:]]),
            deps=[b.q["gpsimd"][-1]], tag="allgather")
        b.do("sync", lambda e: e.dma_start(
            out=dsall[:, :, :],
            in_=cc_out.rearrange("(r p x) -> p r x", r=8, p=128)),
            writes=[dsallb], deps=[coll], dma="c_cc2", tag="lddsall")
        b.do("vector", lambda e: e.tensor_copy(mx[:, :], dsall[:, 0, :]),
             reads=[dsallb], writes=[mxb], tag="smx0")
        for r in range(1, 8):
            b.do("vector", lambda e, r=r: e.tensor_tensor(
                mx[:, :], mx[:, :], dsall[:, r, :], ALU.max),
                reads=[mxb, dsallb], writes=[mxb], tag="smx")
        for r in range(8):
            b.do("vector", lambda e, r=r: e.tensor_sub(
                t2[:, :], dsall[:, r, :], mx[:, :]),
                reads=[dsallb, mxb], writes=[t2b], tag="sse1")
            b.do("scalar", lambda e: e.activation(t2[:, :], t2[:, :], AF.Exp),
                 reads=[t2b], writes=[t2b], tag="sse2")
            if r == 0:
                b.do("vector", lambda e: e.tensor_copy(sexp[:, :], t2[:, :]),
                     reads=[t2b], writes=[sexpb], tag="sse3")
            else:
                b.do("vector", lambda e: e.tensor_add(
                    sexp[:, :], sexp[:, :], t2[:, :]),
                    reads=[sexpb, t2b], writes=[sexpb], tag="sse3")
        b.do("vector", lambda e: e.reciprocal(sexp[:, :], sexp[:, :]),
             reads=[sexpb], writes=[sexpb], tag="srcp")
        b.do("vector", lambda e: e.tensor_sub(
            t2[:, :], ds_sb.rearrange("p a h -> p (a h)"), mx[:, :]),
            reads=[ds_sbb, mxb], writes=[t2b], tag="spr1")
        b.do("scalar", lambda e: e.activation(t2[:, :], t2[:, :], AF.Exp),
             reads=[t2b], writes=[t2b], tag="spr2")
        b.do("vector", lambda e: e.tensor_tensor(
            t2[:, :], t2[:, :], sexp[:, :], ALU.mult),
            reads=[t2b, sexpb], writes=[t2b], tag="spr3")
        for j in range(2):
            b.do("vector", lambda e, j=j: e.tensor_copy(
                vsr[0:1, j * 512:(j + 1) * 512].rearrange(
                    "p (a c) -> p a c", a=4, c=128),
                psum[VS0 + j][0:1, 0:512].rearrange(
                    "p (a c) -> p a c", a=4, c=128)),
                reads=[pb[VS0 + j]], writes=[vsrb], tag="evvs")
        for h in range(H):
            src_off = (h % 2) * 512 + (h // 2) * 64
            mm(psum[0][:, h * 64:(h + 1) * 64] if h < 8 else
               psum[1][:, (h - 8) * 64:(h - 7) * 64],
               C["ones_1x128"], vsr[0:1, src_off:src_off + 64],
               start=(h % 8 == 0), stop=(h % 8 == 7),
               reads=[cbuf, vsrb], bank=(0 if h < 8 else 1), tag="vrepmm")
        for j in range(2):
            b.do("vector", lambda e, j=j: e.tensor_copy(
                vrep[:, j * 512:(j + 1) * 512], psum[j][:, 0:512]),
                reads=[pb[j]], writes=[vrepb], tag="evvrep")
        t2v = t2.rearrange("p (a h) -> p a h", a=NTL, h=H)
        for i in range(NTL):
            for h in range(H):
                b.do("vector", lambda e, i=i, h=h: e.tensor_scalar(
                    att[:, h * 64:(h + 1) * 64], vrep[:, h * 64:(h + 1) * 64],
                    t2v[:, i, h:h + 1], None, ALU.mult),
                    reads=[vrepb, t2b], writes=[attb], tag="attm")
            b.do("vector", lambda e, i=i: e.tensor_add(
                res, att, Xt[i]), reads=[attb, Xb[i]], writes=[resb],
                tag="resC")
            ln_apply(res, [resb], Xt[i], [Xb[i]], tmp, tmpb, stc, stcb,
                     eps=1e-5, tag=f"lnC{i}")
        A.pop()

    def stage_d():
        x3r = x3_hbm.rearrange("(a p) d -> p a d", p=128)
        for i in range(0, NTL, 4):
            b.do("gpsimd", lambda e, i=i: e.dma_start(
                out=x3r[:, i:i + 4, :], in_=xh[:, i:i + 4, :]),
                reads=[Xb[j] for j in range(i, i + 4)], dma="c_x3st",
                tag="stx3")
        x3_dep = [o for o in b.q["gpsimd"] if o.tag == "stx3"]
        A.push()
        fenceD = A.fence()

        def BufD(name):
            bb = Buf(name)
            bb.readers = list(fenceD)
            return bb
        bcols = A.take(128, 16)
        bcolsb = BufD("bcols")
        brows = A.take(128, 2, D)
        browsb = BufD("brows")
        ew = A.take(128, 6, NTL)
        ewb = BufD("ew")
        rowsD = A.take(128, 8, 40)
        rowsDb = BufD("rowsD")
        erep = A.take(128, 4, 40)
        erepb = BufD("erep")
        tb8 = A.take(128, E)
        tb8b = BufD("tb8")
        idxb1 = Buf("idxb1")

        A.push()
        fence0 = A.fence()

        def nb(name):
            bb = Buf(name)
            bb.readers = list(fence0)
            return bb

        xT = A.take(128, 8, 512)
        xTb = nb("xTd")
        wsl = [(A.take(128, 512), nb(f"wsld{i}")) for i in range(2)]
        stg = [(A.take(128, 512), nb(f"stgd{i}")) for i in range(2)]

        b.do("sync", lambda e: e.dma_start(
            out=bcols[:, 0:8], in_=bq2.rearrange("o (a p) -> p (o a)", p=128),
            allow_slow_non_contiguous=True),
            writes=[bcolsb], dma="c_pre4", tag="ldbq2")
        b.do("sync", lambda e: e.dma_start(
            out=bcols[:, 8:16], in_=bk2.rearrange("o (a p) -> p (o a)", p=128),
            allow_slow_non_contiguous=True),
            writes=[bcolsb], dma="c_pre4", tag="ldbk2")
        b.do("sync", lambda e: e.dma_start(
            out=brows[0:1, 0, :], in_=bv2[0:1, :]),
            writes=[browsb], dma="c_pre4", tag="ldbv2")
        b.do("sync", lambda e: e.dma_start(
            out=brows[0:1, 1, :], in_=bo2[0:1, :]),
            writes=[browsb], dma="c_pre4", tag="ldbo2")

        for s in range(4):
            for c in range(8):
                for u in range(4):
                    bank = (c * 4 + u) % 2
                    i = s * 4 + u
                    tr(psum[bank][:, 0:128], Xt[i][:, c * 128:(c + 1) * 128],
                       [Xb[i]], bank, tag="trxd")
                    b.do("vector", lambda e, c=c, u=u, bank=bank: e.tensor_copy(
                        xT[:, c, u * 128:(u + 1) * 128], psum[bank][:, 0:128]),
                        reads=[pb[bank]], writes=[xTb], tag="evxd")
            for wi, W in enumerate((Wq2, Wk2)):
                dsth = qT_hbm if wi == 0 else kT_hbm
                for mg in range(2):
                    for c in range(8):
                        slot = c % 2
                        wap, wb_ = wsl[slot]
                        b.do("sync", lambda e, W=W, c=c, mg=mg, w=wap:
                             e.dma_start(out=w[:, :],
                                         in_=W[c * 128:(c + 1) * 128,
                                              mg * 512:(mg + 1) * 512]),
                             writes=[wb_], dma=f"c_wd{slot}", tag="ldwqk2")
                        for j in range(4):
                            mm(psum[2 + j][:, 0:512],
                               wap[:, j * 128:(j + 1) * 128], xT[:, c, :],
                               start=(c == 0), stop=(c == 7),
                               reads=[wb_, xTb], bank=2 + j, tag="qkT2")
                    for j in range(4):
                        m = mg * 4 + j
                        slot2 = j % 2
                        sap, sb_ = stg[slot2]
                        if wi == 0:
                            b.do("vector", lambda e, j=j, m=m, sap=sap:
                                 e.tensor_scalar(sap[:, :], psum[2 + j][:, 0:512],
                                                 bcols[:, m:m + 1], 0.125,
                                                 ALU.add, ALU.mult),
                                 reads=[pb[2 + j], bcolsb], writes=[sb_],
                                 tag="evq2")
                        else:
                            b.do("vector", lambda e, j=j, m=m, sap=sap:
                                 e.tensor_scalar(sap[:, :], psum[2 + j][:, 0:512],
                                                 bcols[:, 8 + m:9 + m], None,
                                                 ALU.add),
                                 reads=[pb[2 + j], bcolsb], writes=[sb_],
                                 tag="evk2")
                        b.do("gpsimd", lambda e, dsth=dsth, m=m, s=s, sap=sap:
                             e.dma_start(out=dsth[m * 128:(m + 1) * 128,
                                                  s * 512:(s + 1) * 512],
                                         in_=sap[:, :]),
                             reads=[sb_], dma=f"c_qkst{slot2}", tag="stqkT")
            for g in range(2):
                for c in range(8):
                    slot = c % 2
                    wap, wb_ = wsl[slot]
                    b.do("sync", lambda e, c=c, g=g, w=wap: e.dma_start(
                        out=w[:, :], in_=Wv2[c * 128:(c + 1) * 128,
                                             g * 512:(g + 1) * 512]),
                        writes=[wb_], dma=f"c_wd{slot}", tag="ldwv2")
                    for u in range(4):
                        mm(psum[2 + u][:, 0:512],
                           xT[:, c, u * 128:(u + 1) * 128], wap[:, :],
                           start=(c == 0), stop=False,
                           reads=[wb_, xTb], bank=2 + u, tag="vmm2")
                for u in range(4):
                    mm(psum[2 + u][:, 0:512], C["ones_1x128"],
                       brows[0:1, 0, g * 512:(g + 1) * 512], start=False,
                       stop=True, reads=[cbuf, browsb], bank=2 + u, tag="vb2")
                    slot2 = u % 2
                    sap, sb_ = stg[slot2]
                    b.do("vector", lambda e, u=u, sap=sap: e.tensor_copy(
                        sap[:, :], psum[2 + u][:, 0:512]),
                        reads=[pb[2 + u]], writes=[sb_], tag="evv2")
                    b.do("gpsimd", lambda e, s=s, u=u, g=g, sap=sap:
                         e.dma_start(out=v_hbm[s * 512 + u * 128:
                                               s * 512 + (u + 1) * 128,
                                               g * 512:(g + 1) * 512],
                                     in_=sap[:, :]),
                         reads=[sb_], dma=f"c_qkst{slot2}", tag="stv2")
        A.pop()

        A.push()
        fence1 = A.fence()

        def nb1(name):
            bb = Buf(name)
            bb.readers = list(fence1)
            return bb

        qTs = A.take(128, 8, 256)
        qTsb = nb1("qTs")
        khs = [(A.take(128, 512), nb1(f"khs{i}")) for i in range(2)]
        SPT = A.take(128, 4096)
        SPTb = nb1("SPT")
        S_v = SPT.rearrange("p (a c) -> p a c", a=2, c=2048)
        PT_v = SPT.rearrange("p (a c) -> p a c", a=16, c=256)
        vcs = [(A.take(128, 64), nb1(f"vcs{i}")) for i in range(2)]
        O_sb = A.take(128, 2, D)
        O_sbb = nb1("O_sb")
        OT = A.take(128, 8, 256)
        OTb = nb1("OT")
        wos = [(A.take(128, 512), nb1(f"wos{i}")) for i in range(2)]
        mcols = A.take(128, 8)
        mcolsb = nb1("mcols")
        mrow = A.take(128, 256)
        mrowb = nb1("mrow")
        mB = A.take(128, 256)
        mBb = nb1("mB")
        lnr = A.take(128, 2, D)
        lnrb = nb1("lnr")
        lnrow = A.take(128, 2, D)
        lnrowb = nb1("lnrow")
        zz = A.take(128, D)
        zzb = nb1("zz")
        tt = A.take(128, D)
        ttb = nb1("tt")
        tmp = A.take(128, D)
        tmpb = nb1("tmpD1")
        stc = A.take(128, 8)
        stcb = nb1("stcD1")

        for j, wsrc in enumerate((ln1_g, ln1_b)):
            b.do("sync", lambda e, j=j, w=wsrc: e.dma_start(
                out=lnrow[0:1, j, :], in_=w[0:1, :]),
                writes=[], deps=fence1, dma="c_pre5", tag="ldln1")
        ld_ln1 = [o for o in b.q["sync"] if o.tag == "ldln1"]
        for j in range(2):
            for g in range(2):
                mm(psum[6][:, 0:512], C["ones_1x128"],
                   lnrow[0:1, j, g * 512:(g + 1) * 512], start=True, stop=True,
                   reads=[cbuf], bank=6, tag="ln1bc")
                b.q["tensor"][-1].deps.extend(ld_ln1)
                b.do("vector", lambda e, j=j, g=g: e.tensor_copy(
                    lnr[:, j, g * 512:(g + 1) * 512], psum[6][:, 0:512]),
                    reads=[pb[6]], writes=[lnrb], tag="ln1bc_ev")

        qk_dep = [o for o in b.q["gpsimd"] if o.tag in ("stqkT", "stv2")]
        qTr = qT_hbm.rearrange("(c p) t -> p c t", p=128)
        for s in range(8):
            b.do("sync", lambda e, s=s: e.dma_start(
                out=qTs[:, :, :], in_=qTr[:, :, s * 256:(s + 1) * 256]),
                writes=[qTsb], deps=qk_dep, dma="c_qts", tag="ldqTs")
            for h in range(H):
                hp = (h % 2) * 64
                hc = h // 2
                if d_sub == "dpt" and s == 0 and h == 0:
                    qv2 = qTs.rearrange("p a c -> p (a c)").rearrange(
                        "p (a c) -> p a c", a=2, c=1024)
                    for a_ in range(2):
                        b.do("gpsimd", lambda e, a_=a_: e.dma_start(
                            out=dbg_hbm[1280 + a_ * 128:1280 + (a_ + 1) * 128, :],
                            in_=qv2[:, a_, :]),
                            reads=[qTsb], dma="c_dbg", tag="dbgq")
                for tks in range(4):
                    kap, kb_ = khs[tks % 2]
                    b.do("sync", lambda e, h=h, tks=tks, k=kap, hp=hp:
                         e.dma_start(
                             out=k[hp:hp + 64, :],
                             in_=kT_hbm[h * 64:h * 64 + 64,
                                        tks * 512:(tks + 1) * 512]),
                         writes=[kb_], deps=qk_dep, dma=f"c_kh{tks % 2}",
                         tag="ldkh")
                    if d_sub == "dpt" and s == 0 and h == 0:
                        b.do("gpsimd", lambda e, tks=tks, k=kap, hp=hp:
                             e.dma_start(
                                 out=dbg_hbm[1536 + tks * 64:
                                             1536 + tks * 64 + 64, 0:512],
                                 in_=k[hp:hp + 64, :]),
                             reads=[kb_], dma="c_dbg", tag="dbgk")
                    for tqs in range(2):
                        mm(psum[tqs][:, 0:512],
                           qTs[hp:hp + 64, hc, tqs * 128:(tqs + 1) * 128],
                           kap[hp:hp + 64, :], start=True, stop=True,
                           reads=[qTsb, kb_], bank=tqs, tag="Smm")
                        b.do("vector", lambda e, tqs=tqs, tks=tks:
                             e.tensor_copy(S_v[:, tqs, tks * 512:(tks + 1) * 512],
                                           psum[tqs][:, 0:512]),
                             reads=[pb[tqs]], writes=[SPTb], tag="Sev")
                        if d_sub == "dpt" and s == 0 and h == 0 and tks == 0 \
                                and tqs == 0:
                            b.do("gpsimd", lambda e: e.dma_start(
                                out=dbg_hbm[1792:1920, 0:512],
                                in_=S_v[:, 0, 0:512]),
                                reads=[SPTb], dma="c_dbg", tag="dbgs1")
                            b.do("gpsimd", lambda e, k=kap: e.dma_start(
                                out=dbg_hbm[1920:1984, 0:512],
                                in_=k[0:64, :]),
                                reads=[kb_], dma="c_dbg", tag="dbgk2")
                            b.do("gpsimd", lambda e: e.dma_start(
                                out=dbg_hbm[1984:2048, 0:128],
                                in_=qTs[0:64, 0, 0:128]),
                                reads=[qTsb], dma="c_dbg", tag="dbgq2")
                if d_sub == "dpt" and s == 0 and h == 0:
                    SV4 = SPT.rearrange("p (a c) -> p a c", a=4, c=1024)
                    for a_ in range(4):
                        b.do("gpsimd", lambda e, a_=a_: e.dma_start(
                            out=dbg_hbm[640 + a_ * 128:640 + (a_ + 1) * 128, :],
                            in_=SV4[:, a_, :]),
                            reads=[SPTb], dma="c_dbg", tag="dbgsv")
                for tqs in range(2):
                    b.do("vector", lambda e, tqs=tqs: e.reduce_max(
                        mcols[:, tqs:tqs + 1], S_v[:, tqs, :], axis=AX.X),
                        reads=[SPTb], writes=[mcolsb], tag="Smax")
                    b.do("vector", lambda e, tqs=tqs: e.tensor_scalar(
                        mcols[:, 2 + tqs:3 + tqs], mcols[:, tqs:tqs + 1],
                        -1.0, None, ALU.mult),
                        reads=[mcolsb], writes=[mcolsb], tag="Snm")
                    b.do("scalar", lambda e, tqs=tqs: e.activation(
                        S_v[:, tqs, :], S_v[:, tqs, :], AF.Exp,
                        bias=mcols[:, 2 + tqs:3 + tqs],
                        accum_out=mcols[:, 4 + tqs:5 + tqs]),
                        reads=[SPTb, mcolsb], writes=[SPTb, mcolsb],
                        tag="Sexp")
                    b.do("vector", lambda e, tqs=tqs: e.reciprocal(
                        mcols[:, 6 + tqs:7 + tqs], mcols[:, 4 + tqs:5 + tqs]),
                        reads=[mcolsb], writes=[mcolsb], tag="Srd")
                    tr(psum[5][0:1, 0:128], mcols[:, tqs:tqs + 1], [mcolsb],
                       5, tag="trm")
                    b.do("vector", lambda e, tqs=tqs: e.tensor_copy(
                        mrow[0:1, tqs * 128:(tqs + 1) * 128],
                        psum[5][0:1, 0:128]),
                        reads=[pb[5]], writes=[mrowb], tag="evm")
                mm(psum[5][:, 0:256], C["ones_1x128"], mrow[0:1, 0:256],
                   start=True, stop=True, reads=[cbuf, mrowb], bank=5,
                   tag="mbc")
                b.do("vector", lambda e: e.tensor_copy(mB[:, :],
                                                       psum[5][:, 0:256]),
                     reads=[pb[5]], writes=[mBb], tag="evmb")
                if d_sub == "dpt" and s == 0 and h == 0:
                    b.do("gpsimd", lambda e: e.dma_start(
                        out=dbg_hbm[1152:1280, 0:256], in_=mB[:, :]),
                        reads=[mBb], dma="c_dbg", tag="dbgmb")
                for tks in range(4):
                    kap, kb_ = khs[tks % 2]
                    b.do("sync", lambda e, h=h, tks=tks, k=kap, hp=hp:
                         e.dma_start(
                             out=k[hp:hp + 64, :],
                             in_=kT_hbm[h * 64:h * 64 + 64,
                                        tks * 512:(tks + 1) * 512]),
                         writes=[kb_], deps=qk_dep, dma=f"c_kh{tks % 2}",
                         tag="ldkh2")
                    for tkc in range(4):
                        bank = 2 + tkc % 2
                        mm(psum[bank][:, 0:256],
                           kap[hp:hp + 64, tkc * 128:(tkc + 1) * 128],
                           qTs[hp:hp + 64, hc, :], start=True, stop=True,
                           reads=[kb_, qTsb], bank=bank, tag="STmm")
                        ti = tks * 4 + tkc
                        b.do("vector", lambda e, ti=ti, bank=bank:
                             e.tensor_sub(PT_v[:, ti, :], psum[bank][:, 0:256],
                                          mB[:, :]),
                             reads=[pb[bank], mBb], writes=[SPTb], tag="PTs")
                        b.do("scalar", lambda e, ti=ti: e.activation(
                            PT_v[:, ti, :], PT_v[:, ti, :], AF.Exp),
                            reads=[SPTb], writes=[SPTb], tag="PTe")
                for tkc in range(16):
                    vap, vb_ = vcs[tkc % 2]
                    b.do("sync", lambda e, h=h, tkc=tkc, v=vap: e.dma_start(
                        out=v[:, :], in_=v_hbm[tkc * 128:(tkc + 1) * 128,
                                               h * 64:(h + 1) * 64]),
                        writes=[vb_], deps=qk_dep, dma=f"c_vc{tkc % 2}",
                        tag="ldvc")
                    for tqs in range(2):
                        mm(psum[4][:, tqs * 64:(tqs + 1) * 64],
                           PT_v[:, tkc, tqs * 128:(tqs + 1) * 128], vap[:, :],
                           start=(tkc == 0 and tqs == 0),
                           stop=(tkc == 15 and tqs == 1),
                           reads=[SPTb, vb_], bank=4, tag="AVmm")
                for tqs in range(2):
                    b.do("vector", lambda e, tqs=tqs, h=h: e.tensor_scalar(
                        O_sb[:, tqs, h * 64:(h + 1) * 64],
                        psum[4][:, tqs * 64:(tqs + 1) * 64],
                        mcols[:, 6 + tqs:7 + tqs], None, ALU.mult),
                        reads=[pb[4], mcolsb], writes=[O_sbb], tag="Oev")
                if d_sub == "dpt" and s == 0 and h == 0:
                    PT4 = SPT.rearrange("p (a c) -> p a c", a=4, c=1024)
                    for a_ in range(4):
                        b.do("gpsimd", lambda e, a_=a_: e.dma_start(
                            out=dbg_hbm[a_ * 128:(a_ + 1) * 128, :],
                            in_=PT4[:, a_, :]),
                            reads=[SPTb], dma="c_dbg", tag="dbgpt")
                    b.do("gpsimd", lambda e: e.dma_start(
                        out=dbg_hbm[512:640, 0:8], in_=mcols[:, 0:8]),
                        reads=[mcolsb], dma="c_dbg", tag="dbgpt")
            for c in range(8):
                for tqs in range(2):
                    bank = (c * 2 + tqs) % 2
                    tr(psum[bank][:, 0:128],
                       O_sb[:, tqs, c * 128:(c + 1) * 128], [O_sbb], bank,
                       tag="trO")
                    b.do("vector", lambda e, c=c, tqs=tqs, bank=bank:
                         e.tensor_copy(OT[:, c, tqs * 128:(tqs + 1) * 128],
                                       psum[bank][:, 0:128]),
                         reads=[pb[bank]], writes=[OTb], tag="evO")
            for g in range(2):
                for c in range(8):
                    slot = c % 2
                    wap, wb_ = wos[slot]
                    b.do("sync", lambda e, c=c, g=g, w=wap: e.dma_start(
                        out=w[:, :], in_=Wo2[c * 128:(c + 1) * 128,
                                             g * 512:(g + 1) * 512]),
                        writes=[wb_], dma=f"c_wo{slot}", tag="ldwo2")
                    for tqs in range(2):
                        mm(psum[g * 2 + tqs][:, 0:512],
                           OT[:, c, tqs * 128:(tqs + 1) * 128], wap[:, :],
                           start=(c == 0), stop=False,
                           reads=[wb_, OTb], bank=g * 2 + tqs, tag="prj2")
                for tqs in range(2):
                    mm(psum[g * 2 + tqs][:, 0:512], C["ones_1x128"],
                       brows[0:1, 1, g * 512:(g + 1) * 512], start=False,
                       stop=True, reads=[cbuf, browsb], bank=g * 2 + tqs,
                       tag="prj2b")
            for tqs in range(2):
                i = s * 2 + tqs
                for g in range(2):
                    b.do("vector", lambda e, g=g, tqs=tqs: e.tensor_copy(
                        zz[:, g * 512:(g + 1) * 512],
                        psum[g * 2 + tqs][:, 0:512]),
                        reads=[pb[g * 2 + tqs]], writes=[zzb], tag="evz")
                if d_sub == "datt":
                    b.do("vector", lambda e, i=i: e.tensor_copy(Xt[i], zz),
                         reads=[zzb], writes=[Xb[i]], tag="dbgatt")
                    continue
                ln_apply(zz, [zzb], tt, [ttb], tmp, tmpb, stc, stcb,
                         eps=1e-5, tag=f"lnD1_{i}")
                b.do("vector", lambda e: e.tensor_tensor(
                    tt, tt, lnr[:, 0, :], ALU.mult),
                    reads=[ttb, lnrb], writes=[ttb], tag="ln1g")
                b.do("vector", lambda e: e.tensor_tensor(
                    tt, tt, lnr[:, 1, :], ALU.add),
                    reads=[ttb, lnrb], writes=[ttb], tag="ln1b")
                b.do("vector", lambda e, i=i: e.tensor_tensor(
                    Xt[i], Xt[i], tt, ALU.add),
                    reads=[Xb[i], ttb], writes=[Xb[i]], tag="res2")
        A.pop()

        if d_sub in ("d1", "datt"):
            A.pop()
            return
        A.push()
        fence2 = A.fence()

        def nb2(name):
            bb = Buf(name)
            bb.readers = list(fence2)
            return bb

        xT2 = A.take(128, 8, 512)
        xT2b = nb2("xT2")
        g1s = A.take(128, 8, E)
        g1sb = nb2("g1s")
        g1br = A.take(128, E)
        g1brb = nb2("g1br")
        gate = A.take(128, 3, E)
        gateb = nb2("gateD")
        gcol = A.take(128, 8)
        gcolb = nb2("gcolD")
        postile = A.take(128, E)
        postileb = nb2("postileD")
        carry = A.take(128, E)
        carryb = nb2("carryD")
        zt2 = A.take(128, D)
        zt2b = nb2("zt2")
        tmp2 = A.take(128, D)
        tmp2b = nb2("tmp2")

        for c in range(8):
            b.do("sync", lambda e, c=c: e.dma_start(
                out=g1s[:, c, :], in_=gate1_w[c * 128:(c + 1) * 128, :]),
                writes=[g1sb], dma="c_pre6", tag="ldg1")
        b.do("sync", lambda e: e.dma_start(out=g1br[0:1, :], in_=gate1_b[0:1, :]),
             writes=[g1brb], dma="c_pre6", tag="ldg1b")
        b.do("vector", lambda e: e.memset(carry[0:1, :], 0.0), writes=[carryb])
        b.do("vector", lambda e: e.memset(zt2[:, :], 0.0), writes=[zt2b])
        NZR1 = (NSLOT1 + 128) // 128
        for i in range(NZR1):
            b.do("gpsimd", lambda e, i=i: e.dma_start(
                out=xg1_hbm[i * 128:(i + 1) * 128, :], in_=zt2[:, :]),
                reads=[zt2b], dma="c_zero1", tag="zero_xg1")

        for s in range(4):
            for c in range(8):
                for u in range(4):
                    bank = (c * 4 + u) % 2
                    i = s * 4 + u
                    tr(psum[bank][:, 0:128], Xt[i][:, c * 128:(c + 1) * 128],
                       [Xb[i]], bank, tag="trx2")
                    b.do("vector", lambda e, c=c, u=u, bank=bank: e.tensor_copy(
                        xT2[:, c, u * 128:(u + 1) * 128], psum[bank][:, 0:128]),
                        reads=[pb[bank]], writes=[xT2b], tag="evx2")
            for u in range(4):
                i = s * 4 + u
                for c in range(8):
                    mm(psum[2][:, 0:E], xT2[:, c, u * 128:(u + 1) * 128],
                       g1s[:, c, :], start=(c == 0), stop=False,
                       reads=[xT2b, g1sb], bank=2, tag="gate1")
                mm(psum[2][:, 0:E], C["ones_1x128"], g1br[0:1, :],
                   start=False, stop=True, reads=[cbuf, g1brb], bank=2,
                   tag="gate1b")
                b.do("vector", lambda e: e.reduce_max(
                    gcol[:, 0:1], psum[2][:, 0:E], axis=AX.X),
                    reads=[pb[2]], writes=[gcolb], tag="g1mx")
                b.do("vector", lambda e: e.tensor_scalar(
                    gcol[:, 1:2], gcol[:, 0:1], -1.0, None, ALU.mult),
                    reads=[gcolb], writes=[gcolb], tag="g1nm")
                b.do("scalar", lambda e: e.activation(
                    gate[:, 0, :], psum[2][:, 0:E], AF.Exp,
                    bias=gcol[:, 1:2], accum_out=gcol[:, 2:3]),
                    reads=[pb[2], gcolb], writes=[gateb, gcolb], tag="g1ex")
                b.do("vector", lambda e: e.reciprocal(
                    gcol[:, 3:4], gcol[:, 2:3]),
                    reads=[gcolb], writes=[gcolb], tag="g1rc")
                b.do("vector", lambda e: e.tensor_scalar(
                    gate[:, 0, :], gate[:, 0, :], gcol[:, 3:4], None, ALU.mult),
                    reads=[gateb, gcolb], writes=[gateb], tag="g1pr")
                for ee in range(E):
                    b.do("vector", lambda e, ee=ee: e.tensor_scalar(
                        tmp2[:, 0:E], gate[:, 0, :], gate[:, 0, ee:ee + 1],
                        None, ALU.is_gt),
                        reads=[gateb], writes=[tmp2b], tag="r1gt")
                    b.do("vector", lambda e: e.reduce_sum(
                        gcol[:, 4:5], tmp2[:, 0:E], axis=AX.X),
                        reads=[tmp2b], writes=[gcolb], tag="r1gs")
                    if ee > 0:
                        b.do("vector", lambda e, ee=ee: e.tensor_scalar(
                            tmp2[:, 0:ee], gate[:, 0, 0:ee],
                            gate[:, 0, ee:ee + 1], None, ALU.is_equal),
                            reads=[gateb], writes=[tmp2b], tag="r1eq")
                        b.do("vector", lambda e, ee=ee: e.reduce_sum(
                            gcol[:, 5:6], tmp2[:, 0:ee], axis=AX.X),
                            reads=[tmp2b], writes=[gcolb], tag="r1es")
                        b.do("vector", lambda e: e.tensor_tensor(
                            gcol[:, 4:5], gcol[:, 4:5], gcol[:, 5:6], ALU.add),
                            reads=[gcolb], writes=[gcolb], tag="r1ad")
                    b.do("vector", lambda e, ee=ee: e.tensor_scalar(
                        gate[:, 1, ee:ee + 1], gcol[:, 4:5], 0.0, None,
                        ALU.is_equal),
                        reads=[gcolb], writes=[gateb], tag="r1s0")
                    b.do("vector", lambda e, ee=ee: e.tensor_scalar(
                        gate[:, 2, ee:ee + 1], gcol[:, 4:5], 1.0, None,
                        ALU.is_equal),
                        reads=[gcolb], writes=[gateb], tag="r1s1")
                for r_, (selr, dst) in enumerate(
                        (((1), 0), ((2), 1))):
                    b.do("vector", lambda e, selr=selr: e.tensor_tensor(
                        tmp2[:, 0:E], gate[:, selr, :], C["iota8"], ALU.mult),
                        reads=[gateb, cbuf], writes=[tmp2b], tag="q1em")
                    b.do("vector", lambda e, i=i, dst=dst: e.reduce_sum(
                        ew[:, dst, i:i + 1], tmp2[:, 0:E], axis=AX.X),
                        reads=[tmp2b], writes=[ewb], tag="q1e")
                    b.do("vector", lambda e, selr=selr: e.tensor_tensor(
                        tmp2[:, 0:E], gate[:, selr, :], gate[:, 0, :],
                        ALU.mult),
                        reads=[gateb], writes=[tmp2b], tag="q1wm")
                    b.do("vector", lambda e, i=i, dst=dst: e.reduce_sum(
                        ew[:, 2 + dst, i:i + 1], tmp2[:, 0:E], axis=AX.X),
                        reads=[tmp2b], writes=[ewb], tag="q1w")
                b.do("vector", lambda e: e.tensor_tensor(
                    tmp2[:, 0:E], gate[:, 1, :], gate[:, 2, :], ALU.add),
                    reads=[gateb], writes=[tmp2b], tag="clm")
                mm(psum[3][:, 0:E], C["triuS"], tmp2[:, 0:E],
                   start=True, stop=False, reads=[cbuf, tmp2b], bank=3,
                   tag="scan1p")
                mm(psum[3][:, 0:E], C["ones_1x128"], carry[0:1, :],
                   start=False, stop=True, reads=[cbuf, carryb], bank=3,
                   tag="scan1c")
                b.do("vector", lambda e: e.tensor_copy(
                    postile[:, :], psum[3][:, 0:E]),
                    reads=[pb[3]], writes=[postileb], tag="scan1e")
                mm(psum[4][0:1, 0:E], C["ones_col"], tmp2[:, 0:E],
                   start=True, stop=True, reads=[cbuf, tmp2b], bank=4,
                   tag="cols1")
                b.do("vector", lambda e: e.tensor_tensor(
                    carry[0:1, :], carry[0:1, :], psum[4][0:1, 0:E], ALU.add),
                    reads=[pb[4]], writes=[carryb], tag="cup1")
                for r_, selr in enumerate((1, 2)):
                    b.do("vector", lambda e, selr=selr: e.tensor_tensor(
                        tmp2[:, 0:E], postile[:, :], gate[:, selr, :],
                        ALU.mult),
                        reads=[postileb, gateb], writes=[tmp2b], tag="q1pm")
                    b.do("vector", lambda e, i=i, r_=r_: e.reduce_sum(
                        ew[:, 4 + r_, i:i + 1], tmp2[:, 0:E], axis=AX.X),
                        reads=[tmp2b], writes=[ewb], tag="q1p")

        ntscr = rowsD.rearrange("p a c -> p (a c)")[0:1, 184:320].rearrange(
            "p (a c) -> p a c", a=E, c=17)
        b.do("vector", lambda e: e.tensor_tensor(
            ntscr,
            carry[0:1, :].unsqueeze(2).broadcast_to([1, E, 17]),
            C["thr17"].unsqueeze(1).broadcast_to([1, E, 17]), ALU.is_gt),
            reads=[carryb, cbuf], writes=[rowsDb], tag="ntc")
        b.do("vector", lambda e: e.reduce_sum(
            rowsD[0:1, 1, 0:E], ntscr, axis=AX.X),
            reads=[rowsDb], writes=[rowsDb], tag="ntr")
        b.do("vector", lambda e: e.memset(rowsD[0:1, 2, 0:1], 0.0),
             writes=[rowsDb])
        for ee in range(1, E):
            b.do("vector", lambda e, ee=ee: e.tensor_tensor(
                rowsD[0:1, 2, ee:ee + 1], rowsD[0:1, 2, ee - 1:ee],
                rowsD[0:1, 1, ee - 1:ee], ALU.add),
                reads=[rowsDb], writes=[rowsDb], tag="cum")
        b.do("vector", lambda e: e.tensor_tensor(
            rowsD[0:1, 3, 0:E], rowsD[0:1, 2, 0:E], rowsD[0:1, 1, 0:E],
            ALU.add),
            reads=[rowsDb], writes=[rowsDb], tag="incl")
        b.do("vector", lambda e: e.memset(rowsD[0:1, 4, :], 0.0),
             writes=[rowsDb])
        for ee in range(E):
            b.do("vector", lambda e, ee=ee: e.tensor_scalar(
                rowsD[0:1, 5, :], C["iota40"], rowsD[0:1, 3, ee:ee + 1],
                None, ALU.is_ge),
                reads=[rowsDb, cbuf], writes=[rowsDb], tag="eof1")
            b.do("vector", lambda e: e.tensor_tensor(
                rowsD[0:1, 4, :], rowsD[0:1, 4, :], rowsD[0:1, 5, :], ALU.add),
                reads=[rowsDb], writes=[rowsDb], tag="eof2")
        b.do("vector", lambda e: e.tensor_scalar(
            rowsD[0:1, 4, :], rowsD[0:1, 4, :], 7.0, None, ALU.min),
            reads=[rowsDb], writes=[rowsDb], tag="eof3")
        for r_, mul in ((5, 1024.0), (6, 4096.0), (7, 32.0)):
            b.do("vector", lambda e, r_=r_, mul=mul: e.tensor_scalar(
                rowsD[0:1, r_, :], rowsD[0:1, 4, :], mul, None, ALU.mult),
                reads=[rowsDb], writes=[rowsDb], tag="eder")
        for r_, dst in ((5, 0), (6, 1), (7, 2), (4, 3)):
            mm(psum[5][:, 0:40], C["ones_1x128"], rowsD[0:1, r_, :],
               start=True, stop=True, reads=[cbuf, rowsDb], bank=5,
               tag="erepmm")
            b.do("vector", lambda e, dst=dst: e.tensor_copy(
                erep[:, dst, :], psum[5][:, 0:40]),
                reads=[pb[5]], writes=[erepb], tag="erepev")
        b.do("vector", lambda e: e.tensor_scalar(
            rowsD[0:1, 2, 0:E], rowsD[0:1, 2, 0:E], 128.0, None, ALU.mult),
            reads=[rowsDb], writes=[rowsDb], tag="tb1")
        mm(psum[5][:, 0:E], C["ones_1x128"], rowsD[0:1, 2, 0:E],
           start=True, stop=True, reads=[cbuf, rowsDb], bank=5, tag="tbmm")
        b.do("vector", lambda e: e.tensor_copy(tb8[:, :], psum[5][:, 0:E]),
             reads=[pb[5]], writes=[tb8b], tag="tbev")
        zero_dep1 = [o for o in b.q["gpsimd"] if o.tag == "zero_xg1"]
        for i in range(NTL):
            for r_ in range(2):
                b.do("vector", lambda e, i=i, r_=r_: e.tensor_scalar(
                    gate[:, 0, :], C["iota8"], ew[:, r_, i:i + 1], None,
                    ALU.is_equal),
                    reads=[ewb, cbuf], writes=[gateb], tag="sl_eq")
                b.do("vector", lambda e: e.tensor_tensor(
                    gate[:, 1, :], gate[:, 0, :], tb8[:, :], ALU.mult),
                    reads=[gateb, tb8b], writes=[gateb], tag="sl_tb")
                b.do("vector", lambda e: e.reduce_sum(
                    gcol[:, 0:1], gate[:, 1, :], axis=AX.X),
                    reads=[gateb], writes=[gcolb], tag="sl_rs")
                b.do("vector", lambda e, i=i, r_=r_: e.tensor_tensor(
                    gcol[:, 1:2], gcol[:, 0:1], ew[:, 4 + r_, i:i + 1],
                    ALU.add),
                    reads=[gcolb, ewb], writes=[gcolb], tag="sl_add")
                b.do("vector", lambda e, i=i, r_=r_: e.tensor_copy(
                    idxs[:, 16 + r_ * 16 + i:17 + r_ * 16 + i], gcol[:, 1:2]),
                    reads=[gcolb], writes=[idxb1], tag="sl_cast")
        A.pop()

        sc_prev = [None]
        for i in range(NTL):
            for r_ in range(2):
                op = b.do("gpsimd", lambda e, i=i, r_=r_: e.indirect_dma_start(
                    out=xg1_hbm[0:NSLOT1, :],
                    out_offset=bass.IndirectOffsetOnAxis(
                        ap=idxs[:, 16 + r_ * 16 + i:17 + r_ * 16 + i], axis=0),
                    in_=Xt[i], in_offset=None),
                    reads=[idxb1, Xb[i]],
                    deps=([sc_prev[0]] if sc_prev[0] is not None else
                          zero_dep1),
                    dma="c_scat1", tag="scat1")
                sc_prev[0] = op

        A.push()
        fence3 = A.fence()

        def nb3(name):
            bb = Buf(name)
            bb.readers = list(fence3)
            return bb

        xgt = A.take(128, D)
        xgtb = nb3("xgt")
        xgT2 = A.take(128, 8, 128)
        xgT2b = nb3("xgT2")
        w1f = [(A.take(128, 4096), nb3(f"w1f{i}")) for i in range(2)]
        w2f = [(A.take(128, D), nb3(f"w2f{i}")) for i in range(2)]
        h_m2 = A.take(128, 32, 128)
        h_m2b = nb3("h_m2")
        b1f = A.take(128, 128)
        b1fb = nb3("b1f")
        b1T = A.take(128, 32)
        b1Tb = nb3("b1T")
        b2f = A.take(128, D)
        b2fb = nb3("b2f")
        ygt = A.take(128, D)
        ygtb = nb3("ygt")

        idxw = [Buf(f"idxw{i}") for i in range(4)]
        scat1_dep = [o for o in b.q["gpsimd"] if o.tag == "scat1"]
        W1v = W1_1
        W2v = W2_1
        b1v = b1_1

        for j in range(NTILE1):
            b.do("sync", lambda e, j=j: e.dma_start(
                out=xgt[:, :], in_=xg1_hbm[j * 128:(j + 1) * 128, :]),
                writes=[xgtb], deps=scat1_dep, dma="c_xg1", tag="ldxg1")
            for c in range(8):
                bank = c % 2
                tr(psum[bank][:, 0:128], xgt[:, c * 128:(c + 1) * 128],
                   [xgtb], bank, tag="trxg1")
                b.do("vector", lambda e, c=c, bank=bank: e.tensor_copy(
                    xgT2[:, c, :], psum[bank][:, 0:128]),
                    reads=[pb[bank]], writes=[xgT2b], tag="evxg1")
            b.do("vector", lambda e, j=j: e.tensor_scalar(
                idxs[0:32, 64:65].bitcast(F32), C["iota_col"][0:32, :],
                erep[0:32, 2, j:j + 1], None, ALU.add),
                reads=[erepb, cbuf], writes=[idxw[2]], tag="i_b1f")
            b.do("vector", lambda e: e.tensor_copy(
                idxs[0:32, 65:66], idxs[0:32, 64:65].bitcast(F32)),
                reads=[idxw[2]], writes=[idxw[2]], tag="i_b1c")
            b.do("gpsimd", lambda e: e.indirect_dma_start(
                out=b1f[0:32, 0:128], out_offset=None,
                in_=b1v[:, :],
                in_offset=bass.IndirectOffsetOnAxis(ap=idxs[0:32, 65:66],
                                                    axis=0)),
                reads=[idxw[2]], writes=[b1fb], dma="c_b1f", tag="f_b1")
            tr(psum[2][:, 0:32], b1f[0:32, 0:128], [b1fb], 2, tag="trb1")
            b.do("vector", lambda e: e.tensor_copy(b1T[:, :], psum[2][:, 0:32]),
                 reads=[pb[2]], writes=[b1Tb], tag="evb1")
            b.do("vector", lambda e, j=j: e.tensor_copy(
                idxs[0:2, 66:67].bitcast(F32), erep[0:2, 3, j:j + 1]),
                reads=[erepb], writes=[idxw[3]], tag="i_b2f")
            b.do("vector", lambda e: e.tensor_copy(
                idxs[0:2, 67:68], idxs[0:2, 66:67].bitcast(F32)),
                reads=[idxw[3]], writes=[idxw[3]], tag="i_b2c")
            b.do("gpsimd", lambda e: e.indirect_dma_start(
                out=b2f[0:2, :], out_offset=None,
                in_=b2_1[:, :],
                in_offset=bass.IndirectOffsetOnAxis(ap=idxs[0:2, 67:68],
                                                    axis=0)),
                reads=[idxw[3]], writes=[b2fb], dma="c_b2f", tag="f_b2")
            for c in range(8):
                slot = c % 2
                wap, wb_ = w1f[slot]
                b.do("vector", lambda e, j=j, c=c, slot=slot: e.tensor_scalar(
                    idxs[:, 68 + slot:69 + slot].bitcast(F32), C["iota_col"],
                    erep[:, 0, j:j + 1], float(c * 128), ALU.add, ALU.add),
                    reads=[erepb, cbuf], writes=[idxw[slot]], tag="i_w1")
                b.do("vector", lambda e, slot=slot: e.tensor_copy(
                    idxs[:, 70 + slot:71 + slot],
                    idxs[:, 68 + slot:69 + slot].bitcast(F32)),
                    reads=[idxw[slot]], writes=[idxw[slot]], tag="i_w1c")
                b.do("gpsimd", lambda e, slot=slot, w=wap: e.indirect_dma_start(
                    out=w[:, :], out_offset=None,
                    in_=W1v[:, :],
                    in_offset=bass.IndirectOffsetOnAxis(
                        ap=idxs[:, 70 + slot:71 + slot], axis=0)),
                    reads=[idxw[slot]], writes=[wb_], dma=f"c_w1f{slot}",
                    tag="f_w1")
                for fc in range(32):
                    mm(psum[fc // 4][:, (fc % 4) * 128:(fc % 4) * 128 + 128],
                       wap[:, fc * 128:(fc + 1) * 128], xgT2[:, c, :],
                       start=(c == 0 and fc % 4 == 0),
                       stop=(c == 7 and fc % 4 == 3),
                       reads=[wb_, xgT2b], bank=fc // 4, tag="hmm1")
            for fc in range(32):
                b.do("scalar", lambda e, fc=fc: e.activation(
                    h_m2[:, fc, :],
                    psum[fc // 4][:, (fc % 4) * 128:(fc % 4) * 128 + 128],
                    AF.Gelu, bias=b1T[:, fc:fc + 1]),
                    reads=[pb[fc // 4], b1Tb], writes=[h_m2b], tag="gelu1")
            for fc in range(32):
                slot = fc % 2
                wap2, wb2_ = w2f[slot]
                b.do("vector", lambda e, j=j, fc=fc, slot=slot: e.tensor_scalar(
                    idxs[:, 72 + slot:73 + slot].bitcast(F32), C["iota_col"],
                    erep[:, 1, j:j + 1], float(fc * 128), ALU.add, ALU.add),
                    reads=[erepb, cbuf], writes=[idxw[slot]], tag="i_w2")
                b.do("vector", lambda e, slot=slot: e.tensor_copy(
                    idxs[:, 74 + slot:75 + slot],
                    idxs[:, 72 + slot:73 + slot].bitcast(F32)),
                    reads=[idxw[slot]], writes=[idxw[slot]], tag="i_w2c")
                b.do("gpsimd", lambda e, slot=slot, w=wap2:
                     e.indirect_dma_start(
                         out=w[:, :], out_offset=None, in_=W2v[:, :],
                         in_offset=bass.IndirectOffsetOnAxis(
                             ap=idxs[:, 74 + slot:75 + slot], axis=0)),
                     reads=[idxw[slot]], writes=[wb2_], dma=f"c_w2f{slot}",
                     tag="f_w2")
                for g in range(2):
                    mm(psum[6 + g][:, 0:512], h_m2[:, fc, :],
                       wap2[:, g * 512:(g + 1) * 512],
                       start=(fc == 0), stop=False,
                       reads=[h_m2b, wb2_], bank=6 + g, tag="yemm1")
            for g in range(2):
                mm(psum[6 + g][:, 0:512], C["ones_1x128"],
                   b2f[0:1, g * 512:(g + 1) * 512], start=False, stop=True,
                   reads=[cbuf, b2fb], bank=6 + g, tag="yeb1")
                b.do("vector", lambda e, g=g: e.tensor_copy(
                    ygt[:, g * 512:(g + 1) * 512], psum[6 + g][:, 0:512]),
                    reads=[pb[6 + g]], writes=[ygtb], tag="evye1")
            b.do("gpsimd", lambda e, j=j: e.dma_start(
                out=yg1_hbm[j * 128:(j + 1) * 128, :], in_=ygt[:, :]),
                reads=[ygtb], dma="c_yg1", tag="styg1")
        A.pop()

        A.push()
        fence4 = A.fence()

        def nb4(name):
            bb = Buf(name)
            bb.readers = list(fence4)
            return bb

        x3t = A.take(128, D)
        x3tb = nb4("x3t")
        y1 = A.take(128, D)
        y1b = nb4("y1D")
        y2 = A.take(128, D)
        y2b = nb4("y2D")
        vv2 = A.take(128, D)
        vv2b = nb4("vvD")
        tt2 = A.take(128, D)
        tt2b = nb4("ttD")
        lnr2 = A.take(128, 2, D)
        lnr2b = nb4("lnr2")
        lnrow2 = A.take(128, 2, D)
        lnrow2b = nb4("lnrow2")
        res = A.take(128, D)
        resb = nb4("resD")
        tmp3 = A.take(128, D)
        tmp3b = nb4("tmp3")
        stc3 = A.take(128, 8)
        stc3b = nb4("stc3")

        for j, wsrc in enumerate((ln2_g, ln2_b)):
            b.do("sync", lambda e, j=j, w=wsrc: e.dma_start(
                out=lnrow2[0:1, j, :], in_=w[0:1, :]),
                writes=[], deps=fence4, dma="c_pre7", tag="ldln2")
        ld_ln2 = [o for o in b.q["sync"] if o.tag == "ldln2"]
        for j in range(2):
            for g in range(2):
                mm(psum[6][:, 0:512], C["ones_1x128"],
                   lnrow2[0:1, j, g * 512:(g + 1) * 512], start=True,
                   stop=True, reads=[cbuf], bank=6, tag="ln2bc")
                b.q["tensor"][-1].deps.extend(ld_ln2)
                b.do("vector", lambda e, j=j, g=g: e.tensor_copy(
                    lnr2[:, j, g * 512:(g + 1) * 512], psum[6][:, 0:512]),
                    reads=[pb[6]], writes=[lnr2b], tag="ln2bc_ev")

        yg1_dep = [o for o in b.q["gpsimd"] if o.tag == "styg1"]
        for i in range(NTL):
            b.do("sync", lambda e, i=i: e.dma_start(
                out=x3t[:, :], in_=x3r[:, i, :]),
                writes=[x3tb], deps=x3_dep, dma="c_x3ld", tag="ldx3")
            b.do("gpsimd", lambda e, i=i: e.indirect_dma_start(
                out=y1[:, :], out_offset=None, in_=yg1_hbm[0:NSLOT1, :],
                in_offset=bass.IndirectOffsetOnAxis(
                    ap=idxs[:, 16 + i:17 + i], axis=0)),
                reads=[idxb1], writes=[y1b], deps=yg1_dep, dma="c_ga1",
                tag="ga1")
            b.do("gpsimd", lambda e, i=i: e.indirect_dma_start(
                out=y2[:, :], out_offset=None, in_=yg1_hbm[0:NSLOT1, :],
                in_offset=bass.IndirectOffsetOnAxis(
                    ap=idxs[:, 32 + i:33 + i], axis=0)),
                reads=[idxb1], writes=[y2b], deps=yg1_dep, dma="c_ga2",
                tag="ga2")
            b.do("vector", lambda e, i=i: e.tensor_scalar(
                vv2, y1, ew[:, 2, i:i + 1], None, ALU.mult),
                reads=[y1b, ewb], writes=[vv2b], tag="cmb1")
            b.do("vector", lambda e, i=i: e.scalar_tensor_tensor(
                vv2, y2, ew[:, 3, i:i + 1], vv2, ALU.mult, ALU.add),
                reads=[y2b, ewb, vv2b], writes=[vv2b], tag="cmb2")
            ln_apply(vv2, [vv2b], tt2, [tt2b], tmp3, tmp3b, stc3, stc3b,
                     eps=1e-5, tag=f"lnD2_{i}")
            b.do("vector", lambda e: e.tensor_tensor(
                tt2, tt2, lnr2[:, 0, :], ALU.mult),
                reads=[tt2b, lnr2b], writes=[tt2b], tag="ln2g")
            b.do("vector", lambda e: e.tensor_tensor(
                tt2, tt2, lnr2[:, 1, :], ALU.add),
                reads=[tt2b, lnr2b], writes=[tt2b], tag="ln2b")
            b.do("vector", lambda e, i=i: e.tensor_tensor(
                res, tt2, Xt[i], ALU.add),
                reads=[tt2b, Xb[i]], writes=[resb], tag="resD2")
            b.do("vector", lambda e: e.tensor_tensor(
                res, res, x3t, ALU.add),
                reads=[resb, x3tb], writes=[resb], tag="resD2b")
            ln_apply(res, [resb], Xt[i], [Xb[i]], tmp3, tmp3b, stc3, stc3b,
                     eps=1e-5, tag=f"lnD3_{i}")
        A.pop()
        A.pop()

    if stage_lo <= 0 <= stage_hi:
        stage_a()
    if stage_lo <= 1 <= stage_hi:
        stage_b()
    if stage_lo <= 2 <= stage_hi:
        stage_c()
    if stage_lo <= 3 <= stage_hi:
        stage_d()

    finals = []
    for i in range(0, NTL, 4):
        o = b.do("gpsimd", lambda e, i=i: e.dma_start(
            out=out_hbm.rearrange("(a p) d -> p a d", p=128)[:, i:i + 4, :],
            in_=xh[:, i:i + 4, :]),
            reads=[Xb[j] for j in range(i, i + 4)], dma="c_st", tag=f"stX{i}")
        finals.append(o)
    counts = b.finalize(finals)
    sems = {ch: st.enter_context(nc.semaphore(f"s_{ch}"))
            for ch in sorted(set(counts) | set(ENGS))}

    with nc.Block() as block:
        @block.sync
        def _(eng):
            b.emit("sync", eng, sems)

        @block.tensor
        def _(eng):
            b.emit("tensor", eng, sems)

        @block.vector
        def _(eng):
            b.emit("vector", eng, sems)

        @block.scalar
        def _(eng):
            b.emit("scalar", eng, sems)

        @block.gpsimd
        def _(eng):
            b.emit("gpsimd", eng, sems)
            eng.wait_ge(sems["c_st"], counts["c_st"])

    st.close()
    return nc


def shard_inputs(inputs, core):
    d = inputs
    r1 = lambda a: np.ascontiguousarray(a).reshape(1, -1)
    return {
        "x": np.ascontiguousarray(d["L0_attn_x"][core]),
        "W_attn": d["L0_attn_W_attn"],
        "b_attn": r1(d["L0_attn_b_attn"]),
        "W_proj": d["L0_attn_W_proj"],
        "b_proj": r1(d["L0_attn_b_proj"]),
        "rms_w": r1(d["L0_ffn_rms_w"]),
        "gate0_w": d["L0_ffn_gate_w"],
        "W1_0": np.ascontiguousarray(d["L0_ffn_W1"]).reshape(E * D, FF),
        "b1_0": np.ascontiguousarray(d["L0_ffn_b1"]).reshape(E * 32, 128),
        "W2_0": np.ascontiguousarray(d["L0_ffn_W2"]).reshape(E * FF, D),
        "b2_0": np.ascontiguousarray(d["L0_ffn_b2"]),
        "Wq1": d["L1_attn_Wq"], "Wk1": d["L1_attn_Wk"], "Wv1": d["L1_attn_Wv"],
        "Wq2": d["L1_ffn_Wq"], "Wk2": d["L1_ffn_Wk"], "Wv2": d["L1_ffn_Wv"],
        "Wo2": d["L1_ffn_Wo"],
        "bq2": r1(d["L1_ffn_bq"]), "bk2": r1(d["L1_ffn_bk"]),
        "bv2": r1(d["L1_ffn_bv"]), "bo2": r1(d["L1_ffn_bo"]),
        "ln1_g": r1(d["L1_ffn_ln1_g"]), "ln1_b": r1(d["L1_ffn_ln1_b"]),
        "ln2_g": r1(d["L1_ffn_ln2_g"]), "ln2_b": r1(d["L1_ffn_ln2_b"]),
        "gate1_w": d["L1_ffn_gate_w"], "gate1_b": r1(d["L1_ffn_gate_b"]),
        "W1_1": np.ascontiguousarray(d["L1_ffn_W1"]).reshape(E * D, FF),
        "b1_1": np.ascontiguousarray(d["L1_ffn_b1"]).reshape(E * 32, 128),
        "W2_1": np.ascontiguousarray(d["L1_ffn_W2"]).reshape(E * FF, D),
        "b2_1": np.ascontiguousarray(d["L1_ffn_b2"]),
    }


def kernel(**inputs):
    nc = build_nc()
    in_maps = [shard_inputs(inputs, c) for c in range(8)]
    res = run_bass_kernel_spmd(nc, in_maps, core_ids=list(range(8)))
    return np.stack([res.results[c]["out"] for c in range(8)])


# version 02
# speedup vs baseline: 1.3299x; 1.3299x over previous
from contextlib import ExitStack

import numpy as np
import concourse.bass as bass
import concourse.mybir as mybir
from concourse.bass_utils import run_bass_kernel_spmd

F32 = mybir.dt.float32
U32 = mybir.dt.uint32
AF = mybir.ActivationFunctionType
ALU = mybir.AluOpType
AX = mybir.AxisListType

T = 2048
D = 1024
NTL = 16
H, HD = 16, 64
FF = 4096
E = 8
CAP0 = 384
NSLOT0 = E * CAP0
NTILE1 = 40
NSLOT1 = NTILE1 * 128

ENGS = ("sync", "tensor", "vector", "scalar", "gpsimd")


class Op:
    __slots__ = ("eng", "fn", "deps", "dma", "ch", "sem_val", "tag", "seq")

    def __init__(self, eng, fn, deps, dma, ch, tag):
        self.eng, self.fn, self.deps, self.dma, self.ch, self.tag = (
            eng, fn, deps, dma, ch, tag)
        self.sem_val = None


class Buf:
    __slots__ = ("w", "readers", "name")

    def __init__(self, name=""):
        self.w = None
        self.readers = []
        self.name = name


class B:

    def __init__(self, nc):
        self.nc = nc
        self.q = {e: [] for e in ENGS}
        self.used = set()
        self.oneshot = set()
        self.seq = 0

    serial = False

    def do(self, eng, fn, reads=(), writes=(), deps=(), dma=None, tag=""):
        if dma == "c_dbg" and getattr(self, "_last_dbg", None) is not None:
            deps = list(deps) + [self._last_dbg]
        if self.serial and getattr(self, "_prev", None) is not None:
            deps = list(deps) + [self._prev]
        dd = []
        for bb in reads:
            if bb.w is not None:
                dd.append(bb.w)
        for bb in writes:
            if bb.w is not None:
                dd.append(bb.w)
            dd.extend(bb.readers)
        dd.extend(d for d in deps if d is not None)
        ch = dma if dma else eng
        op = Op(eng, fn, dd, dma is not None, ch, tag)
        op.seq = self.seq
        self.seq += 1
        self.q[eng].append(op)
        for bb in reads:
            bb.readers.append(op)
        for bb in writes:
            bb.w = op
            bb.readers = []
        if dma == "c_dbg":
            self._last_dbg = op
        self._prev = op
        return op

    def finalize(self, finals):
        for op in finals:
            self.used.add(op)
        for e in ENGS:
            for op in self.q[e]:
                kept = []
                for d in op.deps:
                    if d.eng == op.eng and not d.dma and op.eng == "tensor":
                        continue
                    if (op.dma and d.dma and d.ch == op.ch
                            and d.ch in self.oneshot):
                        continue
                    kept.append(d)
                    self.used.add(d)
                op.deps = kept
        counts = {}
        for e in ENGS:
            for op in self.q[e]:
                if op.dma or op in self.used:
                    counts[op.ch] = counts.get(op.ch, 0) + (16 if op.dma else 1)
                    op.sem_val = counts[op.ch]
        last_dma = {}
        first_use = {}
        for e in ENGS:
            for op in self.q[e]:
                if op.dma and op.ch in self.oneshot:
                    last_dma[op.ch] = max(last_dma.get(op.ch, -1), op.seq)
                for d in op.deps:
                    if d.ch in self.oneshot and d.dma:
                        first_use.setdefault(d.ch, op.seq)
                        first_use[d.ch] = min(first_use[d.ch], op.seq)
        for ch, lu in last_dma.items():
            fu = first_use.get(ch)
            assert fu is None or fu > lu, (
                f"oneshot channel {ch}: consumer at seq {fu} before last DMA "
                f"at seq {lu} -> deadlock risk")
        self.final_counts = counts
        return counts

    def emit(self, eng_name, eng, sems):
        watermark = {}
        for op in self.q[eng_name]:
            need = {}
            for d in op.deps:
                if d.sem_val is None:
                    raise RuntimeError(f"dep of {op.tag} has no sem: {d.tag}")
                v = (self.final_counts[d.ch]
                     if d.ch in self.oneshot else d.sem_val)
                if v > need.get(d.ch, 0):
                    need[d.ch] = v
            for se, v in sorted(need.items()):
                if watermark.get(se, 0) < v:
                    eng.wait_ge(sems[se], v)
                    watermark[se] = v
            inst = op.fn(eng)
            if op.sem_val is not None:
                inst.then_inc(sems[op.ch], 16 if op.dma else 1)



def build_nc(stage_lo=0, stage_hi=3, d_sub="all", serial=False):
    nc = bass.Bass("TRN2", target_bir_lowering=False, debug=False)
    st = ExitStack()

    def din(name, shape):
        return nc.dram_tensor(name, shape, F32, kind="ExternalInput").ap()

    xin = din("x", [T, D])
    W_attn = din("W_attn", [D, 3 * D])
    b_attn = din("b_attn", [1, 3 * D])
    W_proj = din("W_proj", [D, D])
    b_proj = din("b_proj", [1, D])
    rms_w = din("rms_w", [1, D])
    gate0_w = din("gate0_w", [E, D])
    W1_0 = din("W1_0", [E * D, FF])
    b1_0 = din("b1_0", [E * 32, 128])
    W2_0 = din("W2_0", [E * FF, D])
    b2_0 = din("b2_0", [E, D])
    Wq1 = din("Wq1", [HD, HD])
    Wk1 = din("Wk1", [HD, HD])
    Wv1 = din("Wv1", [HD, HD])
    Wq2 = din("Wq2", [D, D])
    Wk2 = din("Wk2", [D, D])
    Wv2 = din("Wv2", [D, D])
    Wo2 = din("Wo2", [D, D])
    bq2 = din("bq2", [1, D])
    bk2 = din("bk2", [1, D])
    bv2 = din("bv2", [1, D])
    bo2 = din("bo2", [1, D])
    ln1_g = din("ln1_g", [1, D])
    ln1_b = din("ln1_b", [1, D])
    ln2_g = din("ln2_g", [1, D])
    ln2_b = din("ln2_b", [1, D])
    gate1_w = din("gate1_w", [D, E])
    gate1_b = din("gate1_b", [1, E])
    W1_1 = din("W1_1", [E * D, FF])
    b1_1 = din("b1_1", [E * 32, 128])
    W2_1 = din("W2_1", [E * FF, D])
    b2_1 = din("b2_1", [E, D])
    out_hbm = nc.dram_tensor("out", [T, D], F32, kind="ExternalOutput").ap()

    def scratch(name, shape, **kw):
        return nc.dram_tensor(name, shape, F32, kind="Internal", **kw).ap()

    xg0_hbm = scratch("xg0", [NSLOT0 + 128, D])
    yg0_hbm = scratch("yg0", [NSLOT0 + 128, D])
    qT_hbm = scratch("qTh", [D, T])
    kT_hbm = scratch("kTh", [D, T])
    v_hbm = scratch("vh", [T, D])
    xg1_hbm = scratch("xg1", [NSLOT1 + 128, D])
    x3_hbm = scratch("x3h", [T, D])
    yg1_hbm = scratch("yg1", [NSLOT1 + 128, D])
    dbg_hbm = scratch("dbg", [2048, D])
    if stage_lo <= 2 <= stage_hi:
        cc_in = scratch("ccin", [T * H])
        cc_out = nc.dram_tensor("ccout", [8 * T * H], F32, kind="Internal",
                                addr_space="Shared").ap()

    ic = {}

    def const(name, arr):
        ic[name] = nc.inline_tensor(np.ascontiguousarray(arr, np.float32),
                                    name="c_" + name).ap()

    const("ident", np.eye(128))
    const("triu", np.triu(np.ones((128, 128))))
    const("triuS", np.triu(np.ones((128, 128)), 1))
    const("ones_col", np.ones((128, 1)))
    const("ones_1x128", np.ones((1, 128)))
    const("iota_col", np.arange(128)[:, None])
    const("iota8", np.tile(np.arange(8), (128, 1)))
    const("iota40", np.arange(40)[None, :])
    const("thr17", (np.arange(17) * 128.0)[None, :])

    ARENA = 36000
    xh = st.enter_context(nc.sbuf_tensor("Xs", [128, NTL, D], F32))
    ar = st.enter_context(nc.sbuf_tensor("AR", [128, ARENA], F32))
    cs = st.enter_context(nc.sbuf_tensor("CS", [128, 580], F32))
    idxs = st.enter_context(nc.sbuf_tensor("IDX", [128, 208], U32))
    psum = [st.enter_context(nc.psum_tensor(f"PS{i}", [128, 512], F32))
            for i in range(8)]

    b = B(nc)
    b.serial = serial
    b.oneshot = {"c_pre", "c_pre2", "c_pre3", "c_pre3b", "c_pre4", "c_pre5",
                 "c_pre6", "c_pre7", "c_st", "c_zero0", "c_zero1"}
    cbuf = Buf("consts")

    C = {}
    coff = 0
    for name, width in (("ident", 128), ("triu", 128), ("triuS", 128),
                        ("ones_col", 1), ("ones_1x128", 128),
                        ("iota_col", 1), ("iota8", 8), ("iota40", 40),
                        ("thr17", 17)):
        src = ic[name]
        rows = src.shape[0]
        C[name] = cs[0:rows, coff:coff + width]
        b.do("sync", lambda e, d=C[name], s=src:
             e.dma_start(out=d, in_=s), writes=[cbuf], dma="c_pre",
             tag=f"ldc_{name}")
        coff += width

    Xb = [Buf(f"X{i}") for i in range(NTL)]
    Xt = [xh[:, i, :] for i in range(NTL)]

    x_r = xin.rearrange("(a p) d -> p a d", p=128)
    for i in range(0, NTL, 4):
        b.do("sync", lambda e, i=i: e.dma_start(
            out=xh[:, i:i + 4, :], in_=x_r[:, i:i + 4, :]),
            writes=[Xb[j] for j in range(i, i + 4)], dma="c_pre", tag=f"ldX{i}")

    class Arena:
        def __init__(self):
            self.off = 0
            self.marks = []

        def push(self):
            self.marks.append(self.off)

        def pop(self):
            self.off = self.marks.pop()

        def fence(self):
            return [b.q[e][-1] for e in ENGS if b.q[e]]

        def take(self, rows, *shape):
            n = 1
            for s in shape:
                n *= s
            assert self.off + n <= ARENA, f"arena overflow {self.off}+{n}"
            ap = ar[0:rows, self.off:self.off + n]
            self.off += n
            if len(shape) > 1:
                ap = ap.rearrange("p (a c) -> p a c", a=shape[0], c=shape[1])
            return ap

    A = Arena()
    pb = [Buf(f"ps{i}") for i in range(8)]

    def mm(ps_ap, lhsT, rhs, start, stop, reads, bank, tag="", tp=None):
        wl = [pb[bank]] if start else []
        dl = [] if start else [pb[bank].w]
        op = b.do("tensor",
                  lambda e, o=ps_ap, l=lhsT, r=rhs, s1=start, s2=stop, t=tp:
                  e.matmul(o, l, r, start=s1, stop=s2, tile_position=t),
                  reads=list(reads), writes=wl, deps=dl, tag=tag)
        if not start:
            pb[bank].w = op
        return op

    def tr(ps_ap, src_ap, reads, bank, tag="tr"):
        kk_ = src_ap.shape[0]
        return b.do("tensor",
                    lambda e, o=ps_ap, s=src_ap, k=kk_:
                    e.transpose(o, s, C["ident"][0:k, 0:k]),
                    reads=list(reads) + [cbuf], writes=[pb[bank]], tag=tag)

    def ln_apply(res_ap, res_bufs, out_ap, out_bufs, tmp, tmpb, stc, stcb,
                 eps=1e-5, tag="ln"):
        ssum, ssq = stc[:, 0:1], stc[:, 1:2]
        mean, var = stc[:, 2:3], stc[:, 3:4]
        rstd, negmr = stc[:, 4:5], stc[:, 5:6]
        b.do("vector", lambda e: e.reduce_sum(ssum, res_ap, axis=AX.X),
             reads=list(res_bufs), writes=[stcb], tag=tag + "s")
        b.do("vector", lambda e: e.scalar_tensor_tensor(
            tmp, res_ap, 1.0, res_ap, ALU.mult, ALU.mult, accum_out=ssq),
            reads=list(res_bufs), writes=[tmpb, stcb], tag=tag + "q")
        b.do("vector", lambda e: e.tensor_scalar(
            mean, ssum, 1.0 / D, None, ALU.mult),
            reads=[stcb], writes=[stcb], tag=tag + "m")
        b.do("vector", lambda e: e.scalar_tensor_tensor(
            var, mean, -1.0, mean, ALU.mult, ALU.mult),
            reads=[stcb], writes=[stcb], tag=tag + "mm")
        b.do("vector", lambda e: e.scalar_tensor_tensor(
            var, ssq, 1.0 / D, var, ALU.mult, ALU.add),
            reads=[stcb], writes=[stcb], tag=tag + "v")
        b.do("vector", lambda e: e.tensor_scalar(
            var, var, eps, None, ALU.add),
            reads=[stcb], writes=[stcb], tag=tag + "e")
        b.do("vector", lambda e: e.reciprocal(rstd, var),
             reads=[stcb], writes=[stcb], tag=tag + "r")
        b.do("scalar", lambda e: e.activation(rstd, rstd, AF.Sqrt),
             reads=[stcb], writes=[stcb], tag=tag + "sq")
        b.do("vector", lambda e: e.tensor_scalar(
            negmr, mean, rstd, -1.0, ALU.mult, ALU.mult),
            reads=[stcb], writes=[stcb], tag=tag + "n")
        return b.do("vector", lambda e: e.tensor_scalar(
            out_ap, res_ap, rstd, negmr, ALU.mult, ALU.add),
            reads=list(res_bufs) + [stcb], writes=list(out_bufs), tag=tag + "a")

    def feat_map(ap_, bufs, scr, scrb, tag="fm"):
        b.do("vector", lambda e: e.tensor_scalar(
            scr, ap_, 0.0, None, ALU.min),
            reads=list(bufs), writes=[scrb], tag=tag + "1")
        b.do("scalar", lambda e: e.activation(scr, scr, AF.Exp),
             reads=[scrb], writes=[scrb], tag=tag + "2")
        b.do("vector", lambda e: e.scalar_tensor_tensor(
            ap_, ap_, 0.0, scr, ALU.max, ALU.add),
            reads=list(bufs) + [scrb], writes=list(bufs), tag=tag + "3")

    def stage_a():
        A.push()
        fenceA = A.fence()

        def Buf_(name):
            bb = Buf(name)
            bb.readers = list(fenceA)
            return bb
        SL, NCH = 256, 2
        NSL = T // SL
        xT = A.take(128, 8, SL)
        xTb = Buf_("xT")
        qT = A.take(128, 8, SL)
        qTb = Buf_("qT")
        kT = A.take(128, 8, SL)
        kTb = Buf_("kT")
        kk = A.take(128, NCH, D)
        kkb = Buf_("kk")
        vv = A.take(128, NCH, D)
        vvb = Buf_("vv")
        yy = A.take(128, NCH, D)
        yyb = Buf_("yy")
        yT = A.take(128, 8, SL)
        yTb = Buf_("yT")
        amt = A.take(128, 16, 128)
        amtb = Buf_("amt")
        fmscr = A.take(128, 8, SL)
        fmscrb = Buf_("fmscr")
        wqk = [(A.take(128, 512), Buf(f"wqk{i}")) for i in range(4)]
        wkv = [(A.take(128, 512), Buf(f"wkv{i}")) for i in range(2)]
        wpr = [(A.take(128, 512), Buf(f"wpr{i}")) for i in range(2)]
        kvsb = A.take(128, 520)
        kvsbb = Buf_("kvsb")
        rden = A.take(128, 16)
        rdenb = Buf_("rden")
        battc = A.take(128, 24)
        battcb = Buf_("battc")
        battr = A.take(128, 3072)
        battrb = Buf_("battr")
        bprjr = A.take(128, D)
        bprjrb = Buf_("bprjr")
        tmp = A.take(128, D)
        tmpb = Buf_("tmpA")
        res = A.take(128, D)
        resb = Buf_("resA")
        stc = A.take(128, 8)
        stcb = Buf_("stcA")

        b.do("sync", lambda e: e.dma_start(
            out=battc[:, :], in_=b_attn.rearrange("o (a p) -> p (o a)", p=128),
            allow_slow_non_contiguous=True),
            writes=[battcb], dma="c_pre", tag="ld_battc")
        b.do("sync", lambda e: e.dma_start(
            out=battr[0:1, :], in_=b_attn[0:1, :]),
            writes=[battrb], dma="c_pre", tag="ld_battr")
        b.do("sync", lambda e: e.dma_start(
            out=bprjr[0:1, :], in_=b_proj[0:1, :]),
            writes=[bprjrb], dma="c_pre", tag="ld_bprj")

        b.do("vector", lambda e: e.memset(kvsb[:, :], 0.0), writes=[kvsbb])

        KVE, KVO, KS = 5, 6, 7

        for s in range(NSL):
            t0 = s * SL
            tiles = [t0 // 128, t0 // 128 + 1]
            for c in range(8):
                for u in range(NCH):
                    bank = (c * NCH + u) % 2
                    i = tiles[u]
                    tr(psum[bank][:, 0:128], Xt[i][:, c * 128:(c + 1) * 128],
                       [Xb[i]], bank, tag=f"trx{s}")
                    b.do("vector", lambda e, c=c, u=u, bank=bank: e.tensor_copy(
                        xT[:, c, u * 128:(u + 1) * 128], psum[bank][:, 0:128]),
                        reads=[pb[bank]], writes=[xTb], tag="evx")
            for mg in range(4):
                for c in range(8):
                    slot = c % 4
                    wap, wb_ = wqk[slot]
                    b.do("sync", lambda e, c=c, mg=mg, w=wap: e.dma_start(
                        out=w[:, :],
                        in_=W_attn[c * 128:(c + 1) * 128,
                                   mg * 512:(mg + 1) * 512]),
                        writes=[wb_], dma=f"c_wqk{slot}", tag="ldwqk")
                    for j in range(4):
                        mm(psum[j][:, 0:SL], wap[:, j * 128:(j + 1) * 128],
                           xT[:, c, :], start=(c == 0), stop=(c == 7),
                           reads=[wb_, xTb], bank=j, tag=f"qkT{mg}{j}")
                for j in range(4):
                    m = mg * 4 + j
                    dst = qT if m < 8 else kT
                    dstb = qTb if m < 8 else kTb
                    ch = m % 8
                    b.do("vector", lambda e, d=dst, ch=ch, j=j, m=m:
                         e.tensor_scalar(d[:, ch, :], psum[j][:, 0:SL],
                                         battc[:, m:m + 1], None, ALU.add),
                         reads=[pb[j], battcb], writes=[dstb], tag="evqkT")
            for g in range(4):
                col0 = 1024 + 512 * g
                dst, dstb = (kk, kkb) if g < 2 else (vv, vvb)
                go = g % 2
                for c in range(8):
                    slot = c % 2
                    wap, wb_ = wkv[slot]
                    b.do("sync", lambda e, c=c, col0=col0, w=wap: e.dma_start(
                        out=w[:, :],
                        in_=W_attn[c * 128:(c + 1) * 128, col0:col0 + 512]),
                        writes=[wb_], dma=f"c_wkv{slot}", tag="ldwkv")
                    for u in range(NCH):
                        mm(psum[u][:, 0:512],
                           xT[:, c, u * 128:(u + 1) * 128], wap[:, :],
                           start=(c == 0), stop=False,
                           reads=[wb_, xTb], bank=u, tag=f"kv{g}")
                for u in range(NCH):
                    mm(psum[u][:, 0:512], C["ones_1x128"],
                       battr[0:1, col0:col0 + 512], start=False, stop=True,
                       reads=[cbuf, battrb], bank=u, tag="kvb")
                    b.do("scalar", lambda e, d=dst, u=u, go=go: e.activation(
                        d[:, u, go * 512:(go + 1) * 512], psum[u][:, 0:512],
                        AF.Copy),
                        reads=[pb[u]], writes=[dstb], tag="evkv")
            feat_map(qT[:, :, :], [qTb], fmscr[:, :, :], fmscrb, tag="fmq")
            feat_map(kT[:, :, :], [kTb], fmscr[:, :, :], fmscrb, tag="fmk")
            feat_map(kk.rearrange("p a c -> p (a c)"), [kkb],
                     fmscr.rearrange("p a c -> p (a c)"), fmscrb, tag="fmkk")
            for u in range(NCH):
                ci = s * NCH + u
                for p in range(8):
                    kpair = kk[:, u, p * 128:(p + 1) * 128]
                    vpair = vv[:, u, p * 128:(p + 1) * 128]
                    kvbank = KVE if p < 4 else KVO
                    cc0 = (p % 4) * 128
                    mm(psum[kvbank][:, cc0:cc0 + 128], kpair, vpair,
                       start=(p % 4 == 0), stop=(p % 4 == 3),
                       reads=[kkb, vvb], bank=kvbank, tag="kvup")
                    mm(psum[KS][:, p:p + 1], kpair, C["ones_col"],
                       start=(p == 0), stop=(p == 7),
                       reads=[kkb, cbuf], bank=KS, tag="ksup")
                for h in range(H):
                    bank = h % 2
                    hp = (h % 2) * 64
                    hc = h // 2
                    lk = kT[hp:hp + 64, hc, u * 128:(u + 1) * 128]
                    lq = qT[hp:hp + 64, hc, u * 128:(u + 1) * 128]
                    mm(psum[bank][:, 0:128], lk, lq, start=True, stop=True,
                       reads=[kTb, qTb], bank=bank, tag="AT")
                    b.do("vector", lambda e, h=h, bank=bank: e.tensor_tensor(
                        amt[:, h, :], psum[bank][:, 0:128], C["triu"],
                        ALU.mult),
                        reads=[pb[bank], cbuf], writes=[amtb], tag="mask")
                for h in range(H):
                    hp = (h % 2) * 64
                    hc = h // 2
                    nbank = 2 + (h % 2)
                    ncol = (h // 2) * 64
                    lq = qT[hp:hp + 64, hc, u * 128:(u + 1) * 128]
                    mm(psum[nbank][:, ncol:ncol + 64], amt[:, h, :],
                       vv[:, u, h * 64:(h + 1) * 64], start=(h < 2),
                       stop=False, reads=[amtb, vvb], bank=nbank, tag="numi")
                    mm(psum[nbank][:, ncol:ncol + 64], lq,
                       kvsb[hp:hp + 64, (h // 2) * 64:(h // 2) * 64 + 64],
                       start=False, stop=(h >= H - 2),
                       reads=[qTb, kvsbb], bank=nbank, tag="nume")
                    mm(psum[4][:, h:h + 1], amt[:, h, :], C["ones_col"],
                       start=(h == 0), stop=False,
                       reads=[amtb, cbuf], bank=4, tag="dena")
                    mm(psum[4][:, h:h + 1], lq,
                       kvsb[hp:hp + 64, 512 + h // 2:513 + h // 2],
                       start=False, stop=(h == H - 1),
                       reads=[qTb, kvsbb], bank=4, tag="denb")
                b.do("vector", lambda e: e.tensor_scalar(
                    rden[:, 0:16], psum[4][:, 0:16], 1e-6, None, ALU.add),
                    reads=[pb[4]], writes=[rdenb], tag="deps")
                b.do("vector", lambda e: e.reciprocal(rden[:, 0:16],
                                                      rden[:, 0:16]),
                     reads=[rdenb], writes=[rdenb], tag="drcp")
                for h in range(H):
                    nbank = 2 + (h % 2)
                    ncol = (h // 2) * 64
                    b.do("vector", lambda e, h=h, u=u, nbank=nbank, ncol=ncol:
                         e.tensor_scalar(yy[:, u, h * 64:(h + 1) * 64],
                                         psum[nbank][:, ncol:ncol + 64],
                                         rden[:, h:h + 1], None, ALU.mult),
                         reads=[pb[nbank], rdenb], writes=[yyb], tag="yev")
                for bank, p0 in ((KVE, 0), (KVO, 4)):
                    dvE = psum[bank][0:64, 0:512].rearrange(
                        "p (a c) -> p a c", a=4, c=128)[:, :, 0:64]
                    dvO = psum[bank][64:128, 0:512].rearrange(
                        "p (a c) -> p a c", a=4, c=128)[:, :, 64:128]
                    tgtE = kvsb[0:64, p0 * 64:(p0 + 4) * 64].rearrange(
                        "p (a c) -> p a c", a=4, c=64)
                    tgtO = kvsb[64:128, p0 * 64:(p0 + 4) * 64].rearrange(
                        "p (a c) -> p a c", a=4, c=64)
                    b.do("vector", lambda e, t=tgtE, dd=dvE: e.tensor_tensor(
                        t, t, dd, ALU.add),
                        reads=[pb[bank]], writes=[kvsbb], tag="addKVe")
                    b.do("vector", lambda e, t=tgtO, dd=dvO: e.tensor_tensor(
                        t, t, dd, ALU.add),
                        reads=[pb[bank]], writes=[kvsbb], tag="addKVo")
                b.do("vector", lambda e: e.tensor_tensor(
                    kvsb[:, 512:520], kvsb[:, 512:520], psum[KS][:, 0:8],
                    ALU.add),
                    reads=[pb[KS]], writes=[kvsbb], tag="addKS")
            for c in range(8):
                for u in range(NCH):
                    bank = (c * NCH + u) % 2
                    tr(psum[bank][:, 0:128], yy[:, u, c * 128:(c + 1) * 128],
                       [yyb], bank, tag="try")
                    b.do("vector", lambda e, c=c, u=u, bank=bank: e.tensor_copy(
                        yT[:, c, u * 128:(u + 1) * 128], psum[bank][:, 0:128]),
                        reads=[pb[bank]], writes=[yTb], tag="evy")
            for g in range(2):
                for c in range(8):
                    slot = c % 2
                    wap, wb_ = wpr[slot]
                    b.do("sync", lambda e, c=c, g=g, w=wap: e.dma_start(
                        out=w[:, :],
                        in_=W_proj[c * 128:(c + 1) * 128,
                                   g * 512:(g + 1) * 512]),
                        writes=[wb_], dma=f"c_wpr{slot}", tag="ldwpr")
                    for u in range(NCH):
                        mm(psum[g * 2 + u][:, 0:512],
                           yT[:, c, u * 128:(u + 1) * 128], wap[:, :],
                           start=(c == 0), stop=False,
                           reads=[wb_, yTb], bank=g * 2 + u, tag="proj")
                for u in range(NCH):
                    mm(psum[g * 2 + u][:, 0:512], C["ones_1x128"],
                       bprjr[0:1, g * 512:(g + 1) * 512], start=False,
                       stop=True, reads=[cbuf, bprjrb], bank=g * 2 + u,
                       tag="projb")
            for u in range(NCH):
                i = tiles[u]
                for g in range(2):
                    b.do("vector", lambda e, i=i, g=g, u=u: e.tensor_tensor(
                        res[:, g * 512:(g + 1) * 512],
                        psum[g * 2 + u][:, 0:512],
                        Xt[i][:, g * 512:(g + 1) * 512], ALU.add),
                        reads=[pb[g * 2 + u], Xb[i]], writes=[resb],
                        tag="resadd")
                ln_apply(res[:, :], [resb], Xt[i], [Xb[i]], tmp[:, :], tmpb,
                         stc, stcb, eps=1e-5, tag=f"lnA{i}")
        A.pop()

    def stage_b():
        A.push()
        fenceB = A.fence()

        def BufB(name):
            bb = Buf(name)
            bb.readers = list(fenceB)
            return bb
        xn_sl = A.take(128, 4, D)
        xn_slb = BufB("xn_sl")
        xnT = A.take(128, 8, 512)
        xnTb = BufB("xnT")
        rms_rep = A.take(128, D)
        rms_repb = BufB("rms_rep")
        g0T = A.take(128, 8, E)
        g0Tb = BufB("g0T")
        gate = A.take(128, 2, E)
        gateb = BufB("gate")
        gcol = A.take(128, 8)
        gcolb = BufB("gcol")
        score = A.take(128, NTL)
        scoreb = BufB("score")
        slotf = A.take(128, NTL)
        slotfb = BufB("slotf")
        postile = A.take(128, E)
        postileb = BufB("postile")
        carry = A.take(128, E)
        carryb = BufB("carry")
        zt = A.take(128, D)
        ztb = BufB("zt")
        xg_sl = A.take(128, 3, D)
        xg_slb = BufB("xg_sl")
        xgT = A.take(128, 8, CAP0)
        xgTb = BufB("xgT")
        y_sl = A.take(128, 3, D)
        y_slb = BufB("y_sl")
        w1s = [(A.take(128, 256), BufB(f"w1s{i}")) for i in range(2)]
        w2s = [(A.take(128, D), BufB(f"w2s{i}")) for i in range(2)]
        h_m = A.take(128, 2, CAP0)
        h_mb = BufB("h_m")
        b1c = A.take(128, 32)
        b1cb = BufB("b1c")
        b2r = A.take(128, D)
        b2rb = BufB("b2r")
        y1 = A.take(128, D)
        y1b = BufB("y1")
        tmp = A.take(128, D)
        tmpb = BufB("tmpB")
        res = A.take(128, D)
        resb = BufB("resB")
        stc = A.take(128, 8)
        stcb = BufB("stcB")

        idxb = BufB("idx")

        g0r = gate0_w.rearrange("e (c p) -> p c e", p=128)
        for c in range(8):
            b.do("sync", lambda e, c=c: e.dma_start(
                out=g0T[:, c, :], in_=g0r[:, c, :],
                allow_slow_non_contiguous=True),
                writes=[g0Tb], dma="c_pre2", tag="ld_g0T")
        rmsrow = A.take(128, D)
        rmsrowb = BufB("rmsrow")
        b.do("sync", lambda e: e.dma_start(out=rmsrow[0:1, :], in_=rms_w[0:1, :]),
             writes=[rmsrowb], dma="c_pre2", tag="ld_rms")
        for g in range(2):
            mm(psum[6][:, 0:512], C["ones_1x128"], rmsrow[0:1, g * 512:(g + 1) * 512],
               start=True, stop=True, reads=[cbuf, rmsrowb], bank=6, tag="rmsbc")
            b.do("vector", lambda e, g=g: e.tensor_copy(
                rms_rep[:, g * 512:(g + 1) * 512], psum[6][:, 0:512]),
                reads=[pb[6]], writes=[rms_repb], tag="rmsbc_ev")
        b.do("vector", lambda e: e.memset(zt[:, :], 0.0), writes=[ztb])
        NZR = (NSLOT0 + 128) // 128
        for i in range(NZR):
            b.do("gpsimd", lambda e, i=i: e.dma_start(
                out=xg0_hbm[i * 128:(i + 1) * 128, :], in_=zt[:, :]),
                reads=[ztb], dma="c_zero0", tag="zero_xg0")
        b.do("vector", lambda e: e.memset(carry[0:1, :], 0.0), writes=[carryb])

        for s in range(4):
            for u in range(4):
                i = s * 4 + u
                b.do("vector", lambda e, i=i: e.scalar_tensor_tensor(
                    tmp, Xt[i], 1.0, Xt[i], ALU.mult, ALU.mult,
                    accum_out=stc[:, 0:1]),
                    reads=[Xb[i]], writes=[tmpb, stcb], tag="rms_sq")
                b.do("vector", lambda e: e.tensor_scalar(
                    stc[:, 1:2], stc[:, 0:1], 1.0 / D, 1e-6, ALU.mult, ALU.add),
                    reads=[stcb], writes=[stcb], tag="rms_ms")
                b.do("vector", lambda e: e.reciprocal(stc[:, 1:2], stc[:, 1:2]),
                     reads=[stcb], writes=[stcb], tag="rms_rc")
                b.do("scalar", lambda e: e.activation(
                    stc[:, 1:2], stc[:, 1:2], AF.Sqrt),
                    reads=[stcb], writes=[stcb], tag="rms_sq2")
                b.do("scalar", lambda e, i=i, u=u: e.activation(
                    xn_sl[:, u, :], Xt[i], AF.Copy, scale=stc[:, 1:2]),
                    reads=[Xb[i], stcb], writes=[xn_slb], tag="rms_ap")
                b.do("vector", lambda e, u=u: e.tensor_tensor(
                    xn_sl[:, u, :], xn_sl[:, u, :], rms_rep[:, :], ALU.mult),
                    reads=[xn_slb, rms_repb], writes=[xn_slb], tag="rms_w")
            for c in range(8):
                for u in range(4):
                    bank = (c * 4 + u) % 2
                    tr(psum[bank][:, 0:128], xn_sl[:, u, c * 128:(c + 1) * 128],
                       [xn_slb], bank, tag="trxn")
                    b.do("vector", lambda e, c=c, u=u, bank=bank: e.tensor_copy(
                        xnT[:, c, u * 128:(u + 1) * 128], psum[bank][:, 0:128]),
                        reads=[pb[bank]], writes=[xnTb], tag="evxn")
            for u in range(4):
                i = s * 4 + u
                for c in range(8):
                    mm(psum[2][:, 0:E], xnT[:, c, u * 128:(u + 1) * 128],
                       g0T[:, c, :], start=(c == 0), stop=(c == 7),
                       reads=[xnTb, g0Tb], bank=2, tag="gate")
                b.do("vector", lambda e: e.reduce_max(
                    gcol[:, 0:1], psum[2][:, 0:E], axis=AX.X),
                    reads=[pb[2]], writes=[gcolb], tag="g_mx")
                b.do("vector", lambda e: e.tensor_scalar(
                    gcol[:, 1:2], gcol[:, 0:1], -1.0, None, ALU.mult),
                    reads=[gcolb], writes=[gcolb], tag="g_nmx")
                b.do("scalar", lambda e: e.activation(
                    gate[:, 0, :], psum[2][:, 0:E], AF.Exp,
                    bias=gcol[:, 1:2], accum_out=gcol[:, 2:3]),
                    reads=[pb[2], gcolb], writes=[gateb, gcolb], tag="g_exp")
                b.do("vector", lambda e: e.reciprocal(
                    gcol[:, 3:4], gcol[:, 2:3]),
                    reads=[gcolb], writes=[gcolb], tag="g_rs")
                b.do("vector", lambda e: e.tensor_scalar(
                    gate[:, 0, :], gate[:, 0, :], gcol[:, 3:4], None, ALU.mult),
                    reads=[gateb, gcolb], writes=[gateb], tag="g_pr")
                for ee in range(E):
                    b.do("vector", lambda e, ee=ee: e.tensor_scalar(
                        tmp[:, 0:E], gate[:, 0, :], gate[:, 0, ee:ee + 1],
                        None, ALU.is_gt),
                        reads=[gateb], writes=[tmpb], tag="r_gt")
                    b.do("vector", lambda e: e.reduce_sum(
                        gcol[:, 4:5], tmp[:, 0:E], axis=AX.X),
                        reads=[tmpb], writes=[gcolb], tag="r_gts")
                    if ee > 0:
                        b.do("vector", lambda e, ee=ee: e.tensor_scalar(
                            tmp[:, 0:ee], gate[:, 0, 0:ee],
                            gate[:, 0, ee:ee + 1], None, ALU.is_equal),
                            reads=[gateb], writes=[tmpb], tag="r_eq")
                        b.do("vector", lambda e, ee=ee: e.reduce_sum(
                            gcol[:, 5:6], tmp[:, 0:ee], axis=AX.X),
                            reads=[tmpb], writes=[gcolb], tag="r_eqs")
                        b.do("vector", lambda e: e.tensor_tensor(
                            gcol[:, 4:5], gcol[:, 4:5], gcol[:, 5:6], ALU.add),
                            reads=[gcolb], writes=[gcolb], tag="r_add")
                    b.do("vector", lambda e, ee=ee: e.tensor_scalar(
                        gate[:, 1, ee:ee + 1], gcol[:, 4:5], 0.0, None,
                        ALU.is_equal),
                        reads=[gcolb], writes=[gateb], tag="r_sel")
                b.do("vector", lambda e: e.tensor_tensor(
                    tmp[:, 0:E], gate[:, 1, :], C["iota8"], ALU.mult),
                    reads=[gateb, cbuf], writes=[tmpb], tag="q_em")
                b.do("vector", lambda e: e.reduce_sum(
                    gcol[:, 6:7], tmp[:, 0:E], axis=AX.X),
                    reads=[tmpb], writes=[gcolb], tag="q_e")
                b.do("vector", lambda e: e.tensor_tensor(
                    tmp[:, 0:E], gate[:, 1, :], gate[:, 0, :], ALU.mult),
                    reads=[gateb], writes=[tmpb], tag="q_sm")
                b.do("vector", lambda e, i=i: e.reduce_sum(
                    score[:, i:i + 1], tmp[:, 0:E], axis=AX.X),
                    reads=[tmpb], writes=[scoreb], tag="q_s")
                mm(psum[3][:, 0:E], C["triuS"], gate[:, 1, :],
                   start=True, stop=False, reads=[cbuf, gateb], bank=3,
                   tag="scanp")
                mm(psum[3][:, 0:E], C["ones_1x128"], carry[0:1, :],
                   start=False, stop=True, reads=[cbuf, carryb], bank=3,
                   tag="scanc")
                b.do("vector", lambda e: e.tensor_copy(
                    postile[:, :], psum[3][:, 0:E]),
                    reads=[pb[3]], writes=[postileb], tag="scanev")
                mm(psum[4][0:1, 0:E], C["ones_col"], gate[:, 1, :],
                   start=True, stop=True, reads=[cbuf, gateb], bank=4,
                   tag="colsum")
                b.do("vector", lambda e: e.tensor_tensor(
                    carry[0:1, :], carry[0:1, :], psum[4][0:1, 0:E], ALU.add),
                    reads=[pb[4]], writes=[carryb], tag="carryup")
                b.do("vector", lambda e: e.tensor_tensor(
                    tmp[:, 0:E], postile[:, :], gate[:, 1, :], ALU.mult),
                    reads=[postileb, gateb], writes=[tmpb], tag="q_pm")
                b.do("vector", lambda e: e.reduce_sum(
                    gcol[:, 7:8], tmp[:, 0:E], axis=AX.X),
                    reads=[tmpb], writes=[gcolb], tag="q_p")
                b.do("vector", lambda e, i=i: e.tensor_scalar(
                    slotf[:, i:i + 1], gcol[:, 6:7], float(CAP0),
                    None, ALU.mult),
                    reads=[gcolb], writes=[slotfb], tag="q_sl1")
                b.do("vector", lambda e, i=i: e.tensor_tensor(
                    slotf[:, i:i + 1], slotf[:, i:i + 1], gcol[:, 7:8],
                    ALU.add),
                    reads=[slotfb, gcolb], writes=[slotfb], tag="q_sl2")
                b.do("vector", lambda e, i=i: e.tensor_copy(
                    idxs[:, i:i + 1], slotf[:, i:i + 1]),
                    reads=[slotfb], writes=[idxb], tag="q_cast")
                b.do("gpsimd", lambda e, i=i, u=u: e.indirect_dma_start(
                    out=xg0_hbm[0:NSLOT0, :],
                    out_offset=bass.IndirectOffsetOnAxis(
                        ap=idxs[:, i:i + 1], axis=0),
                    in_=xn_sl[:, u, :], in_offset=None),
                    reads=[idxb, xn_slb], deps=[b.q["gpsimd"][-1]]
                    if b.q["gpsimd"] else [],
                    dma="c_scat0", tag="scat0")

        for ee in range(E):
            b.do("sync", lambda e, ee=ee: e.dma_start(
                out=xg_sl[:, :, :],
                in_=xg0_hbm[ee * CAP0:(ee + 1) * CAP0, :].rearrange(
                    "(a p) d -> p a d", p=128)),
                reads=[], writes=[xg_slb],
                deps=[o for o in b.q["gpsimd"] if o.tag == "scat0"],
                dma="c_xg0", tag="ldxg")
            b.do("sync", lambda e, ee=ee: e.dma_start(
                out=b1c[:, :],
                in_=b1_0[ee * 32:(ee + 1) * 32, :].rearrange("a p -> p a"),
                allow_slow_non_contiguous=True),
                writes=[b1cb], dma="c_b1", tag="ldb1")
            b.do("sync", lambda e, ee=ee: e.dma_start(
                out=b2r[0:1, :], in_=b2_0[ee:ee + 1, :]),
                writes=[b2rb], dma="c_b2", tag="ldb2")
            for c in range(8):
                for uu in range(3):
                    bank = (c * 3 + uu) % 2
                    tr(psum[bank][:, 0:128], xg_sl[:, uu, c * 128:(c + 1) * 128],
                       [xg_slb], bank, tag="trxg")
                    b.do("vector", lambda e, c=c, uu=uu, bank=bank:
                         e.tensor_copy(xgT[:, c, uu * 128:(uu + 1) * 128],
                                       psum[bank][:, 0:128]),
                         reads=[pb[bank]], writes=[xgTb], tag="evxg")
            for fg in range(16):
                for c in range(8):
                    slot = c % 2
                    wap, wb_ = w1s[slot]
                    b.do("sync", lambda e, ee=ee, fg=fg, c=c, w=wap:
                         e.dma_start(out=w[:, :],
                                     in_=W1_0[ee * 1024 + c * 128:
                                              ee * 1024 + (c + 1) * 128,
                                              fg * 256:(fg + 1) * 256]),
                         writes=[wb_], dma=f"c_w1{slot}", tag="ldw1")
                    for j in range(2):
                        mm(psum[j][:, 0:CAP0], wap[:, j * 128:(j + 1) * 128],
                           xgT[:, c, :], start=(c == 0), stop=(c == 7),
                           reads=[wb_, xgTb], bank=j, tag="hmm")
                for j in range(2):
                    fc = fg * 2 + j
                    b.do("scalar", lambda e, j=j, fc=fc: e.activation(
                        h_m[:, j, :], psum[j][:, 0:CAP0], AF.Silu,
                        bias=b1c[:, fc:fc + 1]),
                        reads=[pb[j], b1cb], writes=[h_mb], tag="silu")
                    wap2, wb2_ = w2s[j]
                    b.do("sync", lambda e, ee=ee, fc=fc, w=wap2: e.dma_start(
                        out=w[:, :],
                        in_=W2_0[ee * 4096 + fc * 128:
                                 ee * 4096 + (fc + 1) * 128, :]),
                        writes=[wb2_], dma=f"c_w2{j}", tag="ldw2")
                    for uu in range(3):
                        for g in range(2):
                            mm(psum[2 + uu * 2 + g][:, 0:512],
                               h_m[:, j, uu * 128:(uu + 1) * 128],
                               wap2[:, g * 512:(g + 1) * 512],
                               start=(fc == 0), stop=False,
                               reads=[h_mb, wb2_], bank=2 + uu * 2 + g,
                               tag="yemm")
            for uu in range(3):
                for g in range(2):
                    mm(psum[2 + uu * 2 + g][:, 0:512], C["ones_1x128"],
                       b2r[0:1, g * 512:(g + 1) * 512], start=False, stop=True,
                       reads=[cbuf, b2rb], bank=2 + uu * 2 + g, tag="yeb")
                    b.do("vector", lambda e, uu=uu, g=g: e.tensor_copy(
                        y_sl[:, uu, g * 512:(g + 1) * 512],
                        psum[2 + uu * 2 + g][:, 0:512]),
                        reads=[pb[2 + uu * 2 + g]], writes=[y_slb], tag="evye")
            b.do("gpsimd", lambda e, ee=ee: e.dma_start(
                out=yg0_hbm[ee * CAP0:(ee + 1) * CAP0, :].rearrange(
                    "(a p) d -> p a d", p=128), in_=y_sl[:, :, :]),
                reads=[y_slb], dma="c_yg0", tag="styg")

        yg_dep = [o for o in b.q["gpsimd"] if o.tag == "styg"]
        for i in range(NTL):
            b.do("gpsimd", lambda e, i=i: e.indirect_dma_start(
                out=y1[:, :], out_offset=None,
                in_=yg0_hbm[0:NSLOT0, :],
                in_offset=bass.IndirectOffsetOnAxis(ap=idxs[:, i:i + 1],
                                                    axis=0)),
                reads=[idxb], writes=[y1b], deps=yg_dep,
                dma="c_ga0", tag="ga0")
            b.do("vector", lambda e, i=i: e.scalar_tensor_tensor(
                res, y1, score[:, i:i + 1], Xt[i], ALU.mult, ALU.add),
                reads=[y1b, scoreb, Xb[i]], writes=[resb], tag="comb")
            ln_apply(res, [resb], Xt[i], [Xb[i]], tmp, tmpb, stc, stcb,
                     eps=1e-5, tag=f"lnB{i}")
        A.pop()

    def stage_c():
        A.push()
        fenceC = A.fence()

        def BufC(name):
            bb = Buf(name)
            bb.readers = list(fenceC)
            return bb
        xT = A.take(128, 8, 512)
        xTb = BufC("xTc")
        wq3 = A.take(128, 3, HD)
        wq3b = BufC("wq3")
        wraw = A.take(128, 3, HD)
        wrawbs = [BufC(f"wraw{j}") for j in range(3)]
        qkv = A.take(128, 3, HD)
        qkvb = BufC("qkv")
        ds_sb = A.take(128, NTL, H)
        ds_sbb = BufC("ds_sb")
        dsall = A.take(128, 8, NTL * H)
        dsallb = BufC("dsall")
        mx = A.take(128, NTL * H)
        mxb = BufC("mx")
        sexp = A.take(128, NTL * H)
        sexpb = BufC("sexp")
        t2 = A.take(128, NTL * H)
        t2b = BufC("t2")
        vsr = A.take(128, D)
        vsrb = BufC("vsr")
        vrep = A.take(128, D)
        vrepb = BufC("vrep")
        att = A.take(128, D)
        attb = BufC("att")
        tmp = A.take(128, D)
        tmpb = BufC("tmpC")
        res = A.take(128, D)
        resb = BufC("resC")
        stc = A.take(128, 8)
        stcb = BufC("stcC")

        for j, w in enumerate((Wq1, Wk1, Wv1)):
            b.do("sync", lambda e, j=j, w=w: e.dma_start(
                out=wraw[0:64, j, :], in_=w[:, :]),
                writes=[wrawbs[j]], dma="c_pre3", tag="ldw1a")
        for j in range(3):
            tr(psum[5][0:64, 0:64], wraw[0:64, j, :], [wrawbs[j]], 5,
               tag="trw1a")
            b.do("vector", lambda e, j=j: e.tensor_copy(
                wq3[0:64, j, :], psum[5][0:64, 0:64]),
                reads=[pb[5]], writes=[wq3b], tag="evw1a")
        for j in range(3):
            b.do("gpsimd", lambda e, j=j: e.dma_start(
                out=wq3[64:128, j, :], in_=wq3[0:64, j, :]),
                reads=[wq3b], dma="c_pre3b", tag="mirw1a")

        VS0, VS1 = 6, 7
        first_vs = [True, True]
        for s in range(4):
            for c in range(8):
                for u in range(4):
                    bank = (c * 4 + u) % 2
                    i = s * 4 + u
                    tr(psum[bank][:, 0:128], Xt[i][:, c * 128:(c + 1) * 128],
                       [Xb[i]], bank, tag="trxc")
                    b.do("vector", lambda e, c=c, u=u, bank=bank: e.tensor_copy(
                        xT[:, c, u * 128:(u + 1) * 128], psum[bank][:, 0:128]),
                        reads=[pb[bank]], writes=[xTb], tag="evxc")
            for u in range(4):
                i = s * 4 + u
                for h in range(H):
                    hp = (h % 2) * 64
                    hc = h // 2
                    lx = xT[hp:hp + 64, hc, u * 128:(u + 1) * 128]
                    for j in range(3):
                        mm(psum[2 + j][:, 0:HD], lx, wq3[hp:hp + 64, j, :],
                           start=True, stop=True, reads=[xTb, wq3b],
                           bank=2 + j, tag="qkvC")
                        b.do("vector", lambda e, j=j: e.tensor_copy(
                            qkv[:, j, :], psum[2 + j][:, 0:HD]),
                            reads=[pb[2 + j]], writes=[qkvb], tag="evqkvC")
                    b.do("vector", lambda e: e.tensor_tensor(
                        tmp[:, 0:HD], qkv[:, 0, :], qkv[:, 1, :], ALU.mult),
                        reads=[qkvb], writes=[tmpb], tag="dsm")
                    b.do("vector", lambda e, i=i, h=h: e.reduce_sum(
                        ds_sb[:, i, h:h + 1], tmp[:, 0:HD], axis=AX.X),
                        reads=[tmpb], writes=[ds_sbb], tag="dsr")
                    vsb = VS0 + (h % 2)
                    fidx = h % 2
                    mm(psum[vsb][0:1, (h // 2) * 64:(h // 2) * 64 + 64],
                       C["ones_col"], qkv[:, 2, :],
                       start=(first_vs[fidx]), stop=(i == NTL - 1 and h >= H - 2),
                       reads=[cbuf, qkvb], bank=vsb, tag="vsum")
                    first_vs[fidx] = False
        b.do("vector", lambda e: e.tensor_scalar(
            ds_sb[:, :, :], ds_sb[:, :, :], 1.0 / 32.0, None, ALU.mult),
            reads=[ds_sbb], writes=[ds_sbb], tag="dssc")
        cc_in_r = cc_in.rearrange("(p x) -> p x", p=128)
        b.do("gpsimd", lambda e: e.dma_start(
            out=cc_in_r,
            in_=ds_sb.rearrange("p a h -> p (a h)")),
            reads=[ds_sbb], dma="c_cc1", tag="stds")
        coll = b.do("gpsimd", lambda e: e.collective_compute(
            "AllGather", ALU.bypass, replica_groups=[list(range(8))],
            ins=[cc_in[:]], outs=[cc_out[:]]),
            deps=[b.q["gpsimd"][-1]], tag="allgather")
        b.do("sync", lambda e: e.dma_start(
            out=dsall[:, :, :],
            in_=cc_out.rearrange("(r p x) -> p r x", r=8, p=128)),
            writes=[dsallb], deps=[coll], dma="c_cc2", tag="lddsall")
        b.do("vector", lambda e: e.tensor_copy(mx[:, :], dsall[:, 0, :]),
             reads=[dsallb], writes=[mxb], tag="smx0")
        for r in range(1, 8):
            b.do("vector", lambda e, r=r: e.tensor_tensor(
                mx[:, :], mx[:, :], dsall[:, r, :], ALU.max),
                reads=[mxb, dsallb], writes=[mxb], tag="smx")
        for r in range(8):
            b.do("vector", lambda e, r=r: e.tensor_sub(
                t2[:, :], dsall[:, r, :], mx[:, :]),
                reads=[dsallb, mxb], writes=[t2b], tag="sse1")
            b.do("scalar", lambda e: e.activation(t2[:, :], t2[:, :], AF.Exp),
                 reads=[t2b], writes=[t2b], tag="sse2")
            if r == 0:
                b.do("vector", lambda e: e.tensor_copy(sexp[:, :], t2[:, :]),
                     reads=[t2b], writes=[sexpb], tag="sse3")
            else:
                b.do("vector", lambda e: e.tensor_add(
                    sexp[:, :], sexp[:, :], t2[:, :]),
                    reads=[sexpb, t2b], writes=[sexpb], tag="sse3")
        b.do("vector", lambda e: e.reciprocal(sexp[:, :], sexp[:, :]),
             reads=[sexpb], writes=[sexpb], tag="srcp")
        b.do("vector", lambda e: e.tensor_sub(
            t2[:, :], ds_sb.rearrange("p a h -> p (a h)"), mx[:, :]),
            reads=[ds_sbb, mxb], writes=[t2b], tag="spr1")
        b.do("scalar", lambda e: e.activation(t2[:, :], t2[:, :], AF.Exp),
             reads=[t2b], writes=[t2b], tag="spr2")
        b.do("vector", lambda e: e.tensor_tensor(
            t2[:, :], t2[:, :], sexp[:, :], ALU.mult),
            reads=[t2b, sexpb], writes=[t2b], tag="spr3")
        for j in range(2):
            b.do("vector", lambda e, j=j: e.tensor_copy(
                vsr[0:1, j * 512:(j + 1) * 512].rearrange(
                    "p (a c) -> p a c", a=4, c=128),
                psum[VS0 + j][0:1, 0:512].rearrange(
                    "p (a c) -> p a c", a=4, c=128)),
                reads=[pb[VS0 + j]], writes=[vsrb], tag="evvs")
        for h in range(H):
            src_off = (h % 2) * 512 + (h // 2) * 64
            mm(psum[0][:, h * 64:(h + 1) * 64] if h < 8 else
               psum[1][:, (h - 8) * 64:(h - 7) * 64],
               C["ones_1x128"], vsr[0:1, src_off:src_off + 64],
               start=(h % 8 == 0), stop=(h % 8 == 7),
               reads=[cbuf, vsrb], bank=(0 if h < 8 else 1), tag="vrepmm")
        for j in range(2):
            b.do("vector", lambda e, j=j: e.tensor_copy(
                vrep[:, j * 512:(j + 1) * 512], psum[j][:, 0:512]),
                reads=[pb[j]], writes=[vrepb], tag="evvrep")
        t2v = t2.rearrange("p (a h) -> p a h", a=NTL, h=H)
        for i in range(NTL):
            for h in range(H):
                b.do("vector", lambda e, i=i, h=h: e.tensor_scalar(
                    att[:, h * 64:(h + 1) * 64], vrep[:, h * 64:(h + 1) * 64],
                    t2v[:, i, h:h + 1], None, ALU.mult),
                    reads=[vrepb, t2b], writes=[attb], tag="attm")
            b.do("vector", lambda e, i=i: e.tensor_add(
                res, att, Xt[i]), reads=[attb, Xb[i]], writes=[resb],
                tag="resC")
            ln_apply(res, [resb], Xt[i], [Xb[i]], tmp, tmpb, stc, stcb,
                     eps=1e-5, tag=f"lnC{i}")
        A.pop()

    def stage_d():
        x3r = x3_hbm.rearrange("(a p) d -> p a d", p=128)
        for i in range(0, NTL, 4):
            b.do("gpsimd", lambda e, i=i: e.dma_start(
                out=x3r[:, i:i + 4, :], in_=xh[:, i:i + 4, :]),
                reads=[Xb[j] for j in range(i, i + 4)], dma="c_x3st",
                tag="stx3")
        x3_dep = [o for o in b.q["gpsimd"] if o.tag == "stx3"]
        A.push()
        fenceD = A.fence()

        def BufD(name):
            bb = Buf(name)
            bb.readers = list(fenceD)
            return bb
        bcols = A.take(128, 16)
        bcolsb = BufD("bcols")
        brows = A.take(128, 4, D)
        browsb = BufD("brows")
        ew = A.take(128, 6, NTL)
        ewb = BufD("ew")
        rowsD = A.take(128, 8, 40)
        rowsDb = BufD("rowsD")
        erep = A.take(128, 4, 40)
        erepb = BufD("erep")
        tb8 = A.take(128, E)
        tb8b = BufD("tb8")
        idxb1 = Buf("idxb1")

        A.push()
        fence0 = A.fence()

        def nb(name):
            bb = Buf(name)
            bb.readers = list(fence0)
            return bb

        xT = A.take(128, 8, 512)
        xTb = nb("xTd")
        wsl = [(A.take(128, 512), nb(f"wsld{i}")) for i in range(2)]
        stg = [(A.take(128, 512), nb(f"stgd{i}")) for i in range(2)]

        b.do("sync", lambda e: e.dma_start(
            out=bcols[:, 0:8], in_=bq2.rearrange("o (a p) -> p (o a)", p=128),
            allow_slow_non_contiguous=True),
            writes=[bcolsb], dma="c_pre4", tag="ldbq2")
        b.do("sync", lambda e: e.dma_start(
            out=bcols[:, 8:16], in_=bk2.rearrange("o (a p) -> p (o a)", p=128),
            allow_slow_non_contiguous=True),
            writes=[bcolsb], dma="c_pre4", tag="ldbk2")
        b.do("sync", lambda e: e.dma_start(
            out=brows[0:1, 0, :], in_=bv2[0:1, :]),
            writes=[browsb], dma="c_pre4", tag="ldbv2")
        b.do("sync", lambda e: e.dma_start(
            out=brows[0:1, 1, :], in_=bo2[0:1, :]),
            writes=[browsb], dma="c_pre4", tag="ldbo2")

        for s in range(4):
            for c in range(8):
                for u in range(4):
                    bank = (c * 4 + u) % 2
                    i = s * 4 + u
                    tr(psum[bank][:, 0:128], Xt[i][:, c * 128:(c + 1) * 128],
                       [Xb[i]], bank, tag="trxd")
                    b.do("vector", lambda e, c=c, u=u, bank=bank: e.tensor_copy(
                        xT[:, c, u * 128:(u + 1) * 128], psum[bank][:, 0:128]),
                        reads=[pb[bank]], writes=[xTb], tag="evxd")
            for wi, W in enumerate((Wq2, Wk2)):
                dsth = qT_hbm if wi == 0 else kT_hbm
                for mg in range(2):
                    for c in range(8):
                        slot = c % 2
                        wap, wb_ = wsl[slot]
                        b.do("sync", lambda e, W=W, c=c, mg=mg, w=wap:
                             e.dma_start(out=w[:, :],
                                         in_=W[c * 128:(c + 1) * 128,
                                              mg * 512:(mg + 1) * 512]),
                             writes=[wb_], dma=f"c_wd{slot}", tag="ldwqk2")
                        for j in range(4):
                            mm(psum[2 + j][:, 0:512],
                               wap[:, j * 128:(j + 1) * 128], xT[:, c, :],
                               start=(c == 0), stop=(c == 7),
                               reads=[wb_, xTb], bank=2 + j, tag="qkT2")
                    for j in range(4):
                        m = mg * 4 + j
                        slot2 = j % 2
                        sap, sb_ = stg[slot2]
                        if wi == 0:
                            b.do("vector", lambda e, j=j, m=m, sap=sap:
                                 e.tensor_scalar(sap[:, :], psum[2 + j][:, 0:512],
                                                 bcols[:, m:m + 1], 0.125,
                                                 ALU.add, ALU.mult),
                                 reads=[pb[2 + j], bcolsb], writes=[sb_],
                                 tag="evq2")
                        else:
                            b.do("vector", lambda e, j=j, m=m, sap=sap:
                                 e.tensor_scalar(sap[:, :], psum[2 + j][:, 0:512],
                                                 bcols[:, 8 + m:9 + m], None,
                                                 ALU.add),
                                 reads=[pb[2 + j], bcolsb], writes=[sb_],
                                 tag="evk2")
                        b.do("gpsimd", lambda e, dsth=dsth, m=m, s=s, sap=sap:
                             e.dma_start(out=dsth[m * 128:(m + 1) * 128,
                                                  s * 512:(s + 1) * 512],
                                         in_=sap[:, :]),
                             reads=[sb_], dma=f"c_qkst{slot2}", tag="stqkT")
            for g in range(2):
                for c in range(8):
                    slot = c % 2
                    wap, wb_ = wsl[slot]
                    b.do("sync", lambda e, c=c, g=g, w=wap: e.dma_start(
                        out=w[:, :], in_=Wv2[c * 128:(c + 1) * 128,
                                             g * 512:(g + 1) * 512]),
                        writes=[wb_], dma=f"c_wd{slot}", tag="ldwv2")
                    for u in range(4):
                        mm(psum[2 + u][:, 0:512],
                           xT[:, c, u * 128:(u + 1) * 128], wap[:, :],
                           start=(c == 0), stop=False,
                           reads=[wb_, xTb], bank=2 + u, tag="vmm2")
                for u in range(4):
                    mm(psum[2 + u][:, 0:512], C["ones_1x128"],
                       brows[0:1, 0, g * 512:(g + 1) * 512], start=False,
                       stop=True, reads=[cbuf, browsb], bank=2 + u, tag="vb2")
                    slot2 = u % 2
                    sap, sb_ = stg[slot2]
                    b.do("vector", lambda e, u=u, sap=sap: e.tensor_copy(
                        sap[:, :], psum[2 + u][:, 0:512]),
                        reads=[pb[2 + u]], writes=[sb_], tag="evv2")
                    b.do("gpsimd", lambda e, s=s, u=u, g=g, sap=sap:
                         e.dma_start(out=v_hbm[s * 512 + u * 128:
                                               s * 512 + (u + 1) * 128,
                                               g * 512:(g + 1) * 512],
                                     in_=sap[:, :]),
                         reads=[sb_], dma=f"c_qkst{slot2}", tag="stv2")
        A.pop()

        A.push()
        fence1 = A.fence()

        def nb1(name):
            bb = Buf(name)
            bb.readers = list(fence1)
            return bb

        vres = A.take(128, 16, 1040)
        vresb = nb1("vres")
        qh = A.take(128, 2, 512)
        qhb = [nb1("qh0"), nb1("qh1")]
        khs = A.take(128, 2, 512)
        khsb = [nb1("khs0"), nb1("khs1")]
        PTr = A.take(128, 6, 512)
        PTrbs = [nb1(f"PTr{i}") for i in range(6)]
        OT = A.take(128, 8, 512)
        OTb = nb1("OT")
        wos = [(A.take(128, 512), nb1(f"wos{i}")) for i in range(2)]
        drow = A.take(128, 2, 512)
        drowb = nb1("drow")
        rdB = A.take(128, 2, 512)
        rdBb = nb1("rdB")
        lnr = A.take(128, 2, D)
        lnrb = nb1("lnr")
        zzq = OT.rearrange("p a c -> p (a c)").rearrange(
            "p (a c) -> p a c", a=4, c=1024)
        zzb = OTb
        tmp = rdB.rearrange("p a c -> p (a c)")
        tmpb = rdBb
        stc4 = A.take(128, 4, 8)
        stc4b = [nb1(f"stcD1_{i}") for i in range(4)]
        for j, wsrc in enumerate((ln1_g, ln1_b)):
            b.do("sync", lambda e, j=j, w=wsrc: e.dma_start(
                out=brows[0:1, 2 + j, :], in_=w[0:1, :]),
                writes=[browsb], deps=fence1, dma="c_pre5", tag="ldln1")
        ld_ln1 = [o for o in b.q["sync"] if o.tag == "ldln1"]
        for j in range(2):
            for g in range(2):
                mm(psum[6][:, 0:512], C["ones_1x128"],
                   brows[0:1, 2 + j, g * 512:(g + 1) * 512], start=True,
                   stop=True, reads=[cbuf], bank=6, tag="ln1bc")
                b.q["tensor"][-1].deps.extend(ld_ln1)
                b.do("vector", lambda e, j=j, g=g: e.tensor_copy(
                    lnr[:, j, g * 512:(g + 1) * 512], psum[6][:, 0:512]),
                    reads=[pb[6]], writes=[lnrb], tag="ln1bc_ev")
        b.do("sync", lambda e: e.dma_start(
            out=cs[64:65, 321:385], in_=ic["ones_1x128"][0:1, 0:64]),
            writes=[cbuf], dma="c_pre5b", tag="ldones64")
        ones_p64 = cs[64:65, 321:385]

        qk_dep = [o for o in b.q["gpsimd"] if o.tag in ("stqkT", "stv2")]
        vr = v_hbm.rearrange("(a p) d -> p a d", p=128)
        vre = vres.rearrange("p a (h c) -> p a h c", h=16, c=65)
        for i in range(NTL):
            b.do("sync", lambda e, i=i: e.dma_start(
                out=vre[:, i, :, 0:64], in_=vr[:, i, :].rearrange(
                    "p (h c) -> p h c", h=16, c=64),
                allow_slow_non_contiguous=True),
                writes=[vresb], deps=qk_dep, dma="c_vres", tag="ldvres")
        b.do("vector", lambda e: e.memset(vre[:, :, :, 64:65], 1.0),
             writes=[vresb], tag="vones")

        pend_tail = None
        pend_lns = []
        for s in range(4):
            for hpair in range(8):
                qs_ = hpair % 2
                b.do("sync", lambda e, s=s, hpair=hpair, qs_=qs_:
                     e.dma_start(
                         out=qh[:, qs_, :],
                         in_=qT_hbm[hpair * 128:(hpair + 1) * 128,
                                    s * 512:(s + 1) * 512]),
                     writes=[qhb[qs_]], deps=qk_dep,
                     dma=f"c_qh{qs_}", tag="ldqh")
                def ldk(c, hpair=hpair):
                    b.do("sync", lambda e, hpair=hpair, c=c:
                         e.dma_start(
                             out=khs[:, c % 2, :],
                             in_=kT_hbm[hpair * 128:(hpair + 1) * 128,
                                        c * 512:(c + 1) * 512]),
                         writes=[khsb[c % 2]], deps=qk_dep,
                         dma=f"c_kh{c % 2}", tag="ldkh")
                ldk(0)
                ldk(1)
                lq = [qh[0:64, qs_, :], qh[64:128, qs_, :]]
                for tkc in range(18):
                    if tkc >= 2:
                        t2 = tkc - 2
                        for par in range(2):
                            slot = par * 3 + t2 % 3
                            avb = 4 if par == 0 else 6
                            h2 = 2 * hpair + par
                            mm(psum[avb][0:65, 0:512],
                               vres[:, t2, h2 * 65:h2 * 65 + 65],
                               PTr[:, slot, :],
                               start=(t2 == 0), stop=(t2 == 15),
                               reads=[vresb, PTrbs[slot]], bank=avb,
                               tag="AVmm")
                    if tkc < 16:
                        if tkc in (4, 8):
                            ldk(2 + (tkc == 8))
                        ck = (tkc // 4) % 2
                        for par in range(2):
                            bank = par * 2 + tkc % 2
                            slot = par * 3 + tkc % 3
                            mm(psum[bank][:, 0:512],
                               khs[par * 64:par * 64 + 64, ck,
                                   (tkc % 4) * 128:(tkc % 4) * 128 + 128],
                               lq[par], start=True, stop=True,
                               reads=[khsb[ck], qhb[qs_]], bank=bank,
                               tag="STmm")
                            b.do("scalar", lambda e, bank=bank, slot=slot:
                                 e.activation(PTr[:, slot, :],
                                              psum[bank][:, 0:512], AF.Exp),
                                 reads=[pb[bank]], writes=[PTrbs[slot]],
                                 tag="PTev")
                    if tkc == 0 and pend_tail is not None:
                        pend_tail()
                        pend_tail = None
                    if tkc >= 2 and tkc % 2 == 0 and pend_lns:
                        pend_lns.pop(0)()

                def tail(hpair=hpair):
                    for par in range(2):
                        avb = 4 if par == 0 else 6
                        b.do("vector", lambda e, avb=avb, par=par:
                             e.reciprocal(drow[64:65, par, :],
                                          psum[avb][64:65, 0:512]),
                             reads=[pb[avb]], writes=[drowb], tag="Srd")
                        mm(psum[5][0:64, 0:512], ones_p64,
                           drow[64:65, par, :], start=True, stop=True,
                           reads=[cbuf, drowb], bank=5, tag="rdbc")
                        b.do("vector", lambda e, par=par: e.tensor_copy(
                            rdB[0:64, par, :], psum[5][0:64, 0:512]),
                            reads=[pb[5]], writes=[rdBb], tag="evrd")
                        b.do("vector",
                             lambda e, hpair=hpair, par=par, avb=avb:
                             e.tensor_tensor(
                                 OT[par * 64:par * 64 + 64, hpair, :],
                                 psum[avb][0:64, 0:512], rdB[0:64, par, :],
                                 ALU.mult),
                             reads=[pb[avb], rdBb], writes=[OTb], tag="Oev")
                pend_tail = tail
            pend_tail()
            pend_tail = None
            for g in range(2):
                for c in range(8):
                    wap, wb_ = wos[c % 2]
                    b.do("sync", lambda e, c=c, g=g, w=wap: e.dma_start(
                        out=w[:, :], in_=Wo2[c * 128:(c + 1) * 128,
                                             g * 512:(g + 1) * 512]),
                        writes=[wb_], dma=f"c_wo{c % 2}", tag="ldwo2")
                    for tqs in range(4):
                        mm(psum[tqs * 2 + g][:, 0:512],
                           OT[:, c, tqs * 128:(tqs + 1) * 128], wap[:, :],
                           start=(c == 0), stop=False,
                           reads=[wb_, OTb], bank=tqs * 2 + g, tag="prj2")
                for tqs in range(4):
                    mm(psum[tqs * 2 + g][:, 0:512], C["ones_1x128"],
                       brows[0:1, 1, g * 512:(g + 1) * 512], start=False,
                       stop=True, reads=[cbuf, browsb], bank=tqs * 2 + g,
                       tag="prj2b")
            for tqs in range(4):
                for g in range(2):
                    b.do("vector", lambda e, g=g, tqs=tqs: e.tensor_copy(
                        zzq[:, tqs, g * 512:(g + 1) * 512],
                        psum[tqs * 2 + g][:, 0:512]),
                        reads=[pb[tqs * 2 + g]], writes=[zzb], tag="evz")
            def mk_ln(s=s):
                pieces = {}
                for tqs in range(4):
                    i = s * 4 + tqs
                    zz = zzq[:, tqs, :]
                    sc, scb = stc4[:, tqs, :], stc4b[tqs]
                    ssum, ssq = sc[:, 0:1], sc[:, 1:2]
                    mean, var = sc[:, 2:3], sc[:, 3:4]
                    rstd, negmr = sc[:, 4:5], sc[:, 5:6]

                    def p1(e=None, zz=zz, ssum=ssum, ssq=ssq, scb=scb):
                        b.do("vector", lambda e: e.reduce_sum(
                            ssum, zz, axis=AX.X),
                            reads=[zzb], writes=[scb], tag="lnp1a")
                        b.do("vector", lambda e: e.scalar_tensor_tensor(
                            tmp, zz, 1.0, zz, ALU.mult, ALU.mult,
                            accum_out=ssq),
                            reads=[zzb], writes=[tmpb, scb], tag="lnp1b")

                    def p2(e=None, mean=mean, ssum=ssum, var=var, ssq=ssq,
                           rstd=rstd, scb=scb):
                        b.do("vector", lambda e: e.tensor_scalar(
                            mean, ssum, 1.0 / D, None, ALU.mult),
                            reads=[scb], writes=[scb], tag="lnp2a")
                        b.do("vector", lambda e: e.scalar_tensor_tensor(
                            var, mean, -1.0, mean, ALU.mult, ALU.mult),
                            reads=[scb], writes=[scb], tag="lnp2b")
                        b.do("vector", lambda e: e.scalar_tensor_tensor(
                            var, ssq, 1.0 / D, var, ALU.mult, ALU.add),
                            reads=[scb], writes=[scb], tag="lnp2c")
                        b.do("vector", lambda e: e.tensor_scalar(
                            var, var, 1e-5, None, ALU.add),
                            reads=[scb], writes=[scb], tag="lnp2d")
                        b.do("vector", lambda e: e.reciprocal(rstd, var),
                             reads=[scb], writes=[scb], tag="lnp2e")
                        b.do("scalar", lambda e: e.activation(
                            rstd, rstd, AF.Sqrt),
                            reads=[scb], writes=[scb], tag="lnp2f")

                    def p3(e=None, negmr=negmr, mean=mean, rstd=rstd,
                           zz=zz, scb=scb):
                        b.do("vector", lambda e: e.tensor_scalar(
                            negmr, mean, rstd, -1.0, ALU.mult, ALU.mult),
                            reads=[scb], writes=[scb], tag="lnp3a")
                        b.do("vector", lambda e: e.tensor_scalar(
                            zz, zz, rstd, negmr, ALU.mult, ALU.add),
                            reads=[zzb, scb], writes=[zzb], tag="lnp3b")
                        b.do("vector", lambda e: e.tensor_tensor(
                            zz, zz, lnr[:, 0, :], ALU.mult),
                            reads=[zzb, lnrb], writes=[zzb], tag="lnp3c")

                    def p4(e=None, zz=zz, i=i):
                        b.do("vector", lambda e: e.tensor_tensor(
                            zz, zz, lnr[:, 1, :], ALU.add),
                            reads=[zzb, lnrb], writes=[zzb], tag="lnp4a")
                        b.do("vector", lambda e: e.tensor_tensor(
                            Xt[i], Xt[i], zz, ALU.add),
                            reads=[Xb[i], zzb], writes=[Xb[i]], tag="lnp4b")
                    pieces[tqs] = [p1, p2, p3, p4]
                return [pieces[t][k] for t in range(4) for k in range(4)]
            pend_lns = mk_ln()
        for f in pend_lns:
            f()
        pend_lns = []
        A.pop()

        if d_sub in ("d1", "datt"):
            A.pop()
            return
        A.push()
        fence2 = A.fence()

        def nb2(name):
            bb = Buf(name)
            bb.readers = list(fence2)
            return bb

        xT2 = A.take(128, 8, 512)
        xT2b = nb2("xT2")
        g1s = A.take(128, 8, E)
        g1sb = nb2("g1s")
        g1br = A.take(128, E)
        g1brb = nb2("g1br")
        gate = A.take(128, 3, E)
        gateb = nb2("gateD")
        gcol = A.take(128, 8)
        gcolb = nb2("gcolD")
        postile = A.take(128, E)
        postileb = nb2("postileD")
        carry = A.take(128, E)
        carryb = nb2("carryD")
        zt2 = A.take(128, D)
        zt2b = nb2("zt2")
        tmp2 = A.take(128, D)
        tmp2b = nb2("tmp2")

        for c in range(8):
            b.do("sync", lambda e, c=c: e.dma_start(
                out=g1s[:, c, :], in_=gate1_w[c * 128:(c + 1) * 128, :]),
                writes=[g1sb], dma="c_pre6", tag="ldg1")
        b.do("sync", lambda e: e.dma_start(out=g1br[0:1, :], in_=gate1_b[0:1, :]),
             writes=[g1brb], dma="c_pre6", tag="ldg1b")
        b.do("vector", lambda e: e.memset(carry[0:1, :], 0.0), writes=[carryb])
        b.do("vector", lambda e: e.memset(zt2[:, :], 0.0), writes=[zt2b])
        NZR1 = (NSLOT1 + 128) // 128
        for i in range(NZR1):
            b.do("gpsimd", lambda e, i=i: e.dma_start(
                out=xg1_hbm[i * 128:(i + 1) * 128, :], in_=zt2[:, :]),
                reads=[zt2b], dma="c_zero1", tag="zero_xg1")

        for s in range(4):
            for c in range(8):
                for u in range(4):
                    bank = (c * 4 + u) % 2
                    i = s * 4 + u
                    tr(psum[bank][:, 0:128], Xt[i][:, c * 128:(c + 1) * 128],
                       [Xb[i]], bank, tag="trx2")
                    b.do("vector", lambda e, c=c, u=u, bank=bank: e.tensor_copy(
                        xT2[:, c, u * 128:(u + 1) * 128], psum[bank][:, 0:128]),
                        reads=[pb[bank]], writes=[xT2b], tag="evx2")
            for u in range(4):
                i = s * 4 + u
                for c in range(8):
                    mm(psum[2][:, 0:E], xT2[:, c, u * 128:(u + 1) * 128],
                       g1s[:, c, :], start=(c == 0), stop=False,
                       reads=[xT2b, g1sb], bank=2, tag="gate1")
                mm(psum[2][:, 0:E], C["ones_1x128"], g1br[0:1, :],
                   start=False, stop=True, reads=[cbuf, g1brb], bank=2,
                   tag="gate1b")
                b.do("vector", lambda e: e.reduce_max(
                    gcol[:, 0:1], psum[2][:, 0:E], axis=AX.X),
                    reads=[pb[2]], writes=[gcolb], tag="g1mx")
                b.do("vector", lambda e: e.tensor_scalar(
                    gcol[:, 1:2], gcol[:, 0:1], -1.0, None, ALU.mult),
                    reads=[gcolb], writes=[gcolb], tag="g1nm")
                b.do("scalar", lambda e: e.activation(
                    gate[:, 0, :], psum[2][:, 0:E], AF.Exp,
                    bias=gcol[:, 1:2], accum_out=gcol[:, 2:3]),
                    reads=[pb[2], gcolb], writes=[gateb, gcolb], tag="g1ex")
                b.do("vector", lambda e: e.reciprocal(
                    gcol[:, 3:4], gcol[:, 2:3]),
                    reads=[gcolb], writes=[gcolb], tag="g1rc")
                b.do("vector", lambda e: e.tensor_scalar(
                    gate[:, 0, :], gate[:, 0, :], gcol[:, 3:4], None, ALU.mult),
                    reads=[gateb, gcolb], writes=[gateb], tag="g1pr")
                for ee in range(E):
                    b.do("vector", lambda e, ee=ee: e.tensor_scalar(
                        tmp2[:, 0:E], gate[:, 0, :], gate[:, 0, ee:ee + 1],
                        None, ALU.is_gt),
                        reads=[gateb], writes=[tmp2b], tag="r1gt")
                    b.do("vector", lambda e: e.reduce_sum(
                        gcol[:, 4:5], tmp2[:, 0:E], axis=AX.X),
                        reads=[tmp2b], writes=[gcolb], tag="r1gs")
                    if ee > 0:
                        b.do("vector", lambda e, ee=ee: e.tensor_scalar(
                            tmp2[:, 0:ee], gate[:, 0, 0:ee],
                            gate[:, 0, ee:ee + 1], None, ALU.is_equal),
                            reads=[gateb], writes=[tmp2b], tag="r1eq")
                        b.do("vector", lambda e, ee=ee: e.reduce_sum(
                            gcol[:, 5:6], tmp2[:, 0:ee], axis=AX.X),
                            reads=[tmp2b], writes=[gcolb], tag="r1es")
                        b.do("vector", lambda e: e.tensor_tensor(
                            gcol[:, 4:5], gcol[:, 4:5], gcol[:, 5:6], ALU.add),
                            reads=[gcolb], writes=[gcolb], tag="r1ad")
                    b.do("vector", lambda e, ee=ee: e.tensor_scalar(
                        gate[:, 1, ee:ee + 1], gcol[:, 4:5], 0.0, None,
                        ALU.is_equal),
                        reads=[gcolb], writes=[gateb], tag="r1s0")
                    b.do("vector", lambda e, ee=ee: e.tensor_scalar(
                        gate[:, 2, ee:ee + 1], gcol[:, 4:5], 1.0, None,
                        ALU.is_equal),
                        reads=[gcolb], writes=[gateb], tag="r1s1")
                for r_, (selr, dst) in enumerate(
                        (((1), 0), ((2), 1))):
                    b.do("vector", lambda e, selr=selr: e.tensor_tensor(
                        tmp2[:, 0:E], gate[:, selr, :], C["iota8"], ALU.mult),
                        reads=[gateb, cbuf], writes=[tmp2b], tag="q1em")
                    b.do("vector", lambda e, i=i, dst=dst: e.reduce_sum(
                        ew[:, dst, i:i + 1], tmp2[:, 0:E], axis=AX.X),
                        reads=[tmp2b], writes=[ewb], tag="q1e")
                    b.do("vector", lambda e, selr=selr: e.tensor_tensor(
                        tmp2[:, 0:E], gate[:, selr, :], gate[:, 0, :],
                        ALU.mult),
                        reads=[gateb], writes=[tmp2b], tag="q1wm")
                    b.do("vector", lambda e, i=i, dst=dst: e.reduce_sum(
                        ew[:, 2 + dst, i:i + 1], tmp2[:, 0:E], axis=AX.X),
                        reads=[tmp2b], writes=[ewb], tag="q1w")
                b.do("vector", lambda e: e.tensor_tensor(
                    tmp2[:, 0:E], gate[:, 1, :], gate[:, 2, :], ALU.add),
                    reads=[gateb], writes=[tmp2b], tag="clm")
                mm(psum[3][:, 0:E], C["triuS"], tmp2[:, 0:E],
                   start=True, stop=False, reads=[cbuf, tmp2b], bank=3,
                   tag="scan1p")
                mm(psum[3][:, 0:E], C["ones_1x128"], carry[0:1, :],
                   start=False, stop=True, reads=[cbuf, carryb], bank=3,
                   tag="scan1c")
                b.do("vector", lambda e: e.tensor_copy(
                    postile[:, :], psum[3][:, 0:E]),
                    reads=[pb[3]], writes=[postileb], tag="scan1e")
                mm(psum[4][0:1, 0:E], C["ones_col"], tmp2[:, 0:E],
                   start=True, stop=True, reads=[cbuf, tmp2b], bank=4,
                   tag="cols1")
                b.do("vector", lambda e: e.tensor_tensor(
                    carry[0:1, :], carry[0:1, :], psum[4][0:1, 0:E], ALU.add),
                    reads=[pb[4]], writes=[carryb], tag="cup1")
                for r_, selr in enumerate((1, 2)):
                    b.do("vector", lambda e, selr=selr: e.tensor_tensor(
                        tmp2[:, 0:E], postile[:, :], gate[:, selr, :],
                        ALU.mult),
                        reads=[postileb, gateb], writes=[tmp2b], tag="q1pm")
                    b.do("vector", lambda e, i=i, r_=r_: e.reduce_sum(
                        ew[:, 4 + r_, i:i + 1], tmp2[:, 0:E], axis=AX.X),
                        reads=[tmp2b], writes=[ewb], tag="q1p")

        ntscr = rowsD.rearrange("p a c -> p (a c)")[0:1, 184:320].rearrange(
            "p (a c) -> p a c", a=E, c=17)
        b.do("vector", lambda e: e.tensor_tensor(
            ntscr,
            carry[0:1, :].unsqueeze(2).broadcast_to([1, E, 17]),
            C["thr17"].unsqueeze(1).broadcast_to([1, E, 17]), ALU.is_gt),
            reads=[carryb, cbuf], writes=[rowsDb], tag="ntc")
        b.do("vector", lambda e: e.reduce_sum(
            rowsD[0:1, 1, 0:E], ntscr, axis=AX.X),
            reads=[rowsDb], writes=[rowsDb], tag="ntr")
        b.do("vector", lambda e: e.memset(rowsD[0:1, 2, 0:1], 0.0),
             writes=[rowsDb])
        for ee in range(1, E):
            b.do("vector", lambda e, ee=ee: e.tensor_tensor(
                rowsD[0:1, 2, ee:ee + 1], rowsD[0:1, 2, ee - 1:ee],
                rowsD[0:1, 1, ee - 1:ee], ALU.add),
                reads=[rowsDb], writes=[rowsDb], tag="cum")
        b.do("vector", lambda e: e.tensor_tensor(
            rowsD[0:1, 3, 0:E], rowsD[0:1, 2, 0:E], rowsD[0:1, 1, 0:E],
            ALU.add),
            reads=[rowsDb], writes=[rowsDb], tag="incl")
        b.do("vector", lambda e: e.memset(rowsD[0:1, 4, :], 0.0),
             writes=[rowsDb])
        for ee in range(E):
            b.do("vector", lambda e, ee=ee: e.tensor_scalar(
                rowsD[0:1, 5, :], C["iota40"], rowsD[0:1, 3, ee:ee + 1],
                None, ALU.is_ge),
                reads=[rowsDb, cbuf], writes=[rowsDb], tag="eof1")
            b.do("vector", lambda e: e.tensor_tensor(
                rowsD[0:1, 4, :], rowsD[0:1, 4, :], rowsD[0:1, 5, :], ALU.add),
                reads=[rowsDb], writes=[rowsDb], tag="eof2")
        b.do("vector", lambda e: e.tensor_scalar(
            rowsD[0:1, 4, :], rowsD[0:1, 4, :], 7.0, None, ALU.min),
            reads=[rowsDb], writes=[rowsDb], tag="eof3")
        for r_, mul in ((5, 1024.0), (6, 4096.0), (7, 32.0)):
            b.do("vector", lambda e, r_=r_, mul=mul: e.tensor_scalar(
                rowsD[0:1, r_, :], rowsD[0:1, 4, :], mul, None, ALU.mult),
                reads=[rowsDb], writes=[rowsDb], tag="eder")
        for r_, dst in ((5, 0), (6, 1), (7, 2), (4, 3)):
            mm(psum[5][:, 0:40], C["ones_1x128"], rowsD[0:1, r_, :],
               start=True, stop=True, reads=[cbuf, rowsDb], bank=5,
               tag="erepmm")
            b.do("vector", lambda e, dst=dst: e.tensor_copy(
                erep[:, dst, :], psum[5][:, 0:40]),
                reads=[pb[5]], writes=[erepb], tag="erepev")
        b.do("vector", lambda e: e.tensor_scalar(
            rowsD[0:1, 2, 0:E], rowsD[0:1, 2, 0:E], 128.0, None, ALU.mult),
            reads=[rowsDb], writes=[rowsDb], tag="tb1")
        mm(psum[5][:, 0:E], C["ones_1x128"], rowsD[0:1, 2, 0:E],
           start=True, stop=True, reads=[cbuf, rowsDb], bank=5, tag="tbmm")
        b.do("vector", lambda e: e.tensor_copy(tb8[:, :], psum[5][:, 0:E]),
             reads=[pb[5]], writes=[tb8b], tag="tbev")
        zero_dep1 = [o for o in b.q["gpsimd"] if o.tag == "zero_xg1"]
        for i in range(NTL):
            for r_ in range(2):
                b.do("vector", lambda e, i=i, r_=r_: e.tensor_scalar(
                    gate[:, 0, :], C["iota8"], ew[:, r_, i:i + 1], None,
                    ALU.is_equal),
                    reads=[ewb, cbuf], writes=[gateb], tag="sl_eq")
                b.do("vector", lambda e: e.tensor_tensor(
                    gate[:, 1, :], gate[:, 0, :], tb8[:, :], ALU.mult),
                    reads=[gateb, tb8b], writes=[gateb], tag="sl_tb")
                b.do("vector", lambda e: e.reduce_sum(
                    gcol[:, 0:1], gate[:, 1, :], axis=AX.X),
                    reads=[gateb], writes=[gcolb], tag="sl_rs")
                b.do("vector", lambda e, i=i, r_=r_: e.tensor_tensor(
                    gcol[:, 1:2], gcol[:, 0:1], ew[:, 4 + r_, i:i + 1],
                    ALU.add),
                    reads=[gcolb, ewb], writes=[gcolb], tag="sl_add")
                b.do("vector", lambda e, i=i, r_=r_: e.tensor_copy(
                    idxs[:, 16 + r_ * 16 + i:17 + r_ * 16 + i], gcol[:, 1:2]),
                    reads=[gcolb], writes=[idxb1], tag="sl_cast")
        A.pop()

        sc_prev = [None]
        for i in range(NTL):
            for r_ in range(2):
                op = b.do("gpsimd", lambda e, i=i, r_=r_: e.indirect_dma_start(
                    out=xg1_hbm[0:NSLOT1, :],
                    out_offset=bass.IndirectOffsetOnAxis(
                        ap=idxs[:, 16 + r_ * 16 + i:17 + r_ * 16 + i], axis=0),
                    in_=Xt[i], in_offset=None),
                    reads=[idxb1, Xb[i]],
                    deps=([sc_prev[0]] if sc_prev[0] is not None else
                          zero_dep1),
                    dma="c_scat1", tag="scat1")
                sc_prev[0] = op

        A.push()
        fence3 = A.fence()

        def nb3(name):
            bb = Buf(name)
            bb.readers = list(fence3)
            return bb

        xgt = A.take(128, D)
        xgtb = nb3("xgt")
        xgT2 = A.take(128, 8, 128)
        xgT2b = nb3("xgT2")
        w1f = [(A.take(128, 4096), nb3(f"w1f{i}")) for i in range(2)]
        w2f = [(A.take(128, D), nb3(f"w2f{i}")) for i in range(2)]
        h_m2 = A.take(128, 32, 128)
        h_m2b = nb3("h_m2")
        b1f = A.take(128, 128)
        b1fb = nb3("b1f")
        b1T = A.take(128, 32)
        b1Tb = nb3("b1T")
        b2f = A.take(128, D)
        b2fb = nb3("b2f")
        ygt = A.take(128, D)
        ygtb = nb3("ygt")

        idxw = [Buf(f"idxw{i}") for i in range(4)]
        scat1_dep = [o for o in b.q["gpsimd"] if o.tag == "scat1"]
        W1v = W1_1
        W2v = W2_1
        b1v = b1_1

        for j in range(NTILE1):
            b.do("sync", lambda e, j=j: e.dma_start(
                out=xgt[:, :], in_=xg1_hbm[j * 128:(j + 1) * 128, :]),
                writes=[xgtb], deps=scat1_dep, dma="c_xg1", tag="ldxg1")
            for c in range(8):
                bank = c % 2
                tr(psum[bank][:, 0:128], xgt[:, c * 128:(c + 1) * 128],
                   [xgtb], bank, tag="trxg1")
                b.do("vector", lambda e, c=c, bank=bank: e.tensor_copy(
                    xgT2[:, c, :], psum[bank][:, 0:128]),
                    reads=[pb[bank]], writes=[xgT2b], tag="evxg1")
            b.do("vector", lambda e, j=j: e.tensor_scalar(
                idxs[0:32, 64:65].bitcast(F32), C["iota_col"][0:32, :],
                erep[0:32, 2, j:j + 1], None, ALU.add),
                reads=[erepb, cbuf], writes=[idxw[2]], tag="i_b1f")
            b.do("vector", lambda e: e.tensor_copy(
                idxs[0:32, 65:66], idxs[0:32, 64:65].bitcast(F32)),
                reads=[idxw[2]], writes=[idxw[2]], tag="i_b1c")
            b.do("gpsimd", lambda e: e.indirect_dma_start(
                out=b1f[0:32, 0:128], out_offset=None,
                in_=b1v[:, :],
                in_offset=bass.IndirectOffsetOnAxis(ap=idxs[0:32, 65:66],
                                                    axis=0)),
                reads=[idxw[2]], writes=[b1fb], dma="c_b1f", tag="f_b1")
            tr(psum[2][:, 0:32], b1f[0:32, 0:128], [b1fb], 2, tag="trb1")
            b.do("vector", lambda e: e.tensor_copy(b1T[:, :], psum[2][:, 0:32]),
                 reads=[pb[2]], writes=[b1Tb], tag="evb1")
            b.do("vector", lambda e, j=j: e.tensor_copy(
                idxs[0:2, 66:67].bitcast(F32), erep[0:2, 3, j:j + 1]),
                reads=[erepb], writes=[idxw[3]], tag="i_b2f")
            b.do("vector", lambda e: e.tensor_copy(
                idxs[0:2, 67:68], idxs[0:2, 66:67].bitcast(F32)),
                reads=[idxw[3]], writes=[idxw[3]], tag="i_b2c")
            b.do("gpsimd", lambda e: e.indirect_dma_start(
                out=b2f[0:2, :], out_offset=None,
                in_=b2_1[:, :],
                in_offset=bass.IndirectOffsetOnAxis(ap=idxs[0:2, 67:68],
                                                    axis=0)),
                reads=[idxw[3]], writes=[b2fb], dma="c_b2f", tag="f_b2")
            for c in range(8):
                slot = c % 2
                wap, wb_ = w1f[slot]
                b.do("vector", lambda e, j=j, c=c, slot=slot: e.tensor_scalar(
                    idxs[:, 68 + slot:69 + slot].bitcast(F32), C["iota_col"],
                    erep[:, 0, j:j + 1], float(c * 128), ALU.add, ALU.add),
                    reads=[erepb, cbuf], writes=[idxw[slot]], tag="i_w1")
                b.do("vector", lambda e, slot=slot: e.tensor_copy(
                    idxs[:, 70 + slot:71 + slot],
                    idxs[:, 68 + slot:69 + slot].bitcast(F32)),
                    reads=[idxw[slot]], writes=[idxw[slot]], tag="i_w1c")
                b.do("gpsimd", lambda e, slot=slot, w=wap: e.indirect_dma_start(
                    out=w[:, :], out_offset=None,
                    in_=W1v[:, :],
                    in_offset=bass.IndirectOffsetOnAxis(
                        ap=idxs[:, 70 + slot:71 + slot], axis=0)),
                    reads=[idxw[slot]], writes=[wb_], dma=f"c_w1f{slot}",
                    tag="f_w1")
                for fc in range(32):
                    mm(psum[fc // 4][:, (fc % 4) * 128:(fc % 4) * 128 + 128],
                       wap[:, fc * 128:(fc + 1) * 128], xgT2[:, c, :],
                       start=(c == 0 and fc % 4 == 0),
                       stop=(c == 7 and fc % 4 == 3),
                       reads=[wb_, xgT2b], bank=fc // 4, tag="hmm1")
            for fc in range(32):
                b.do("scalar", lambda e, fc=fc: e.activation(
                    h_m2[:, fc, :],
                    psum[fc // 4][:, (fc % 4) * 128:(fc % 4) * 128 + 128],
                    AF.Gelu, bias=b1T[:, fc:fc + 1]),
                    reads=[pb[fc // 4], b1Tb], writes=[h_m2b], tag="gelu1")
            for fc in range(32):
                slot = fc % 2
                wap2, wb2_ = w2f[slot]
                b.do("vector", lambda e, j=j, fc=fc, slot=slot: e.tensor_scalar(
                    idxs[:, 72 + slot:73 + slot].bitcast(F32), C["iota_col"],
                    erep[:, 1, j:j + 1], float(fc * 128), ALU.add, ALU.add),
                    reads=[erepb, cbuf], writes=[idxw[slot]], tag="i_w2")
                b.do("vector", lambda e, slot=slot: e.tensor_copy(
                    idxs[:, 74 + slot:75 + slot],
                    idxs[:, 72 + slot:73 + slot].bitcast(F32)),
                    reads=[idxw[slot]], writes=[idxw[slot]], tag="i_w2c")
                b.do("gpsimd", lambda e, slot=slot, w=wap2:
                     e.indirect_dma_start(
                         out=w[:, :], out_offset=None, in_=W2v[:, :],
                         in_offset=bass.IndirectOffsetOnAxis(
                             ap=idxs[:, 74 + slot:75 + slot], axis=0)),
                     reads=[idxw[slot]], writes=[wb2_], dma=f"c_w2f{slot}",
                     tag="f_w2")
                for g in range(2):
                    mm(psum[6 + g][:, 0:512], h_m2[:, fc, :],
                       wap2[:, g * 512:(g + 1) * 512],
                       start=(fc == 0), stop=False,
                       reads=[h_m2b, wb2_], bank=6 + g, tag="yemm1")
            for g in range(2):
                mm(psum[6 + g][:, 0:512], C["ones_1x128"],
                   b2f[0:1, g * 512:(g + 1) * 512], start=False, stop=True,
                   reads=[cbuf, b2fb], bank=6 + g, tag="yeb1")
                b.do("vector", lambda e, g=g: e.tensor_copy(
                    ygt[:, g * 512:(g + 1) * 512], psum[6 + g][:, 0:512]),
                    reads=[pb[6 + g]], writes=[ygtb], tag="evye1")
            b.do("gpsimd", lambda e, j=j: e.dma_start(
                out=yg1_hbm[j * 128:(j + 1) * 128, :], in_=ygt[:, :]),
                reads=[ygtb], dma="c_yg1", tag="styg1")
        A.pop()

        A.push()
        fence4 = A.fence()

        def nb4(name):
            bb = Buf(name)
            bb.readers = list(fence4)
            return bb

        x3t = A.take(128, D)
        x3tb = nb4("x3t")
        y1 = A.take(128, D)
        y1b = nb4("y1D")
        y2 = A.take(128, D)
        y2b = nb4("y2D")
        vv2 = A.take(128, D)
        vv2b = nb4("vvD")
        tt2 = A.take(128, D)
        tt2b = nb4("ttD")
        lnr2 = A.take(128, 2, D)
        lnr2b = nb4("lnr2")
        lnrow2 = A.take(128, 2, D)
        lnrow2b = nb4("lnrow2")
        res = A.take(128, D)
        resb = nb4("resD")
        tmp3 = A.take(128, D)
        tmp3b = nb4("tmp3")
        stc3 = A.take(128, 8)
        stc3b = nb4("stc3")

        for j, wsrc in enumerate((ln2_g, ln2_b)):
            b.do("sync", lambda e, j=j, w=wsrc: e.dma_start(
                out=lnrow2[0:1, j, :], in_=w[0:1, :]),
                writes=[], deps=fence4, dma="c_pre7", tag="ldln2")
        ld_ln2 = [o for o in b.q["sync"] if o.tag == "ldln2"]
        for j in range(2):
            for g in range(2):
                mm(psum[6][:, 0:512], C["ones_1x128"],
                   lnrow2[0:1, j, g * 512:(g + 1) * 512], start=True,
                   stop=True, reads=[cbuf], bank=6, tag="ln2bc")
                b.q["tensor"][-1].deps.extend(ld_ln2)
                b.do("vector", lambda e, j=j, g=g: e.tensor_copy(
                    lnr2[:, j, g * 512:(g + 1) * 512], psum[6][:, 0:512]),
                    reads=[pb[6]], writes=[lnr2b], tag="ln2bc_ev")

        yg1_dep = [o for o in b.q["gpsimd"] if o.tag == "styg1"]
        for i in range(NTL):
            b.do("sync", lambda e, i=i: e.dma_start(
                out=x3t[:, :], in_=x3r[:, i, :]),
                writes=[x3tb], deps=x3_dep, dma="c_x3ld", tag="ldx3")
            b.do("gpsimd", lambda e, i=i: e.indirect_dma_start(
                out=y1[:, :], out_offset=None, in_=yg1_hbm[0:NSLOT1, :],
                in_offset=bass.IndirectOffsetOnAxis(
                    ap=idxs[:, 16 + i:17 + i], axis=0)),
                reads=[idxb1], writes=[y1b], deps=yg1_dep, dma="c_ga1",
                tag="ga1")
            b.do("gpsimd", lambda e, i=i: e.indirect_dma_start(
                out=y2[:, :], out_offset=None, in_=yg1_hbm[0:NSLOT1, :],
                in_offset=bass.IndirectOffsetOnAxis(
                    ap=idxs[:, 32 + i:33 + i], axis=0)),
                reads=[idxb1], writes=[y2b], deps=yg1_dep, dma="c_ga2",
                tag="ga2")
            b.do("vector", lambda e, i=i: e.tensor_scalar(
                vv2, y1, ew[:, 2, i:i + 1], None, ALU.mult),
                reads=[y1b, ewb], writes=[vv2b], tag="cmb1")
            b.do("vector", lambda e, i=i: e.scalar_tensor_tensor(
                vv2, y2, ew[:, 3, i:i + 1], vv2, ALU.mult, ALU.add),
                reads=[y2b, ewb, vv2b], writes=[vv2b], tag="cmb2")
            ln_apply(vv2, [vv2b], tt2, [tt2b], tmp3, tmp3b, stc3, stc3b,
                     eps=1e-5, tag=f"lnD2_{i}")
            b.do("vector", lambda e: e.tensor_tensor(
                tt2, tt2, lnr2[:, 0, :], ALU.mult),
                reads=[tt2b, lnr2b], writes=[tt2b], tag="ln2g")
            b.do("vector", lambda e: e.tensor_tensor(
                tt2, tt2, lnr2[:, 1, :], ALU.add),
                reads=[tt2b, lnr2b], writes=[tt2b], tag="ln2b")
            b.do("vector", lambda e, i=i: e.tensor_tensor(
                res, tt2, Xt[i], ALU.add),
                reads=[tt2b, Xb[i]], writes=[resb], tag="resD2")
            b.do("vector", lambda e: e.tensor_tensor(
                res, res, x3t, ALU.add),
                reads=[resb, x3tb], writes=[resb], tag="resD2b")
            ln_apply(res, [resb], Xt[i], [Xb[i]], tmp3, tmp3b, stc3, stc3b,
                     eps=1e-5, tag=f"lnD3_{i}")
        A.pop()
        A.pop()

    if stage_lo <= 0 <= stage_hi:
        stage_a()
    if stage_lo <= 1 <= stage_hi:
        stage_b()
    if stage_lo <= 2 <= stage_hi:
        stage_c()
    if stage_lo <= 3 <= stage_hi:
        stage_d()

    finals = []
    for i in range(0, NTL, 4):
        o = b.do("gpsimd", lambda e, i=i: e.dma_start(
            out=out_hbm.rearrange("(a p) d -> p a d", p=128)[:, i:i + 4, :],
            in_=xh[:, i:i + 4, :]),
            reads=[Xb[j] for j in range(i, i + 4)], dma="c_st", tag=f"stX{i}")
        finals.append(o)
    counts = b.finalize(finals)
    sems = {ch: st.enter_context(nc.semaphore(f"s_{ch}"))
            for ch in sorted(set(counts) | set(ENGS))}

    with nc.Block() as block:
        @block.sync
        def _(eng):
            b.emit("sync", eng, sems)

        @block.tensor
        def _(eng):
            b.emit("tensor", eng, sems)

        @block.vector
        def _(eng):
            b.emit("vector", eng, sems)

        @block.scalar
        def _(eng):
            b.emit("scalar", eng, sems)

        @block.gpsimd
        def _(eng):
            b.emit("gpsimd", eng, sems)
            eng.wait_ge(sems["c_st"], counts["c_st"])

    st.close()
    return nc


def shard_inputs(inputs, core):
    d = inputs
    r1 = lambda a: np.ascontiguousarray(a).reshape(1, -1)
    return {
        "x": np.ascontiguousarray(d["L0_attn_x"][core]),
        "W_attn": d["L0_attn_W_attn"],
        "b_attn": r1(d["L0_attn_b_attn"]),
        "W_proj": d["L0_attn_W_proj"],
        "b_proj": r1(d["L0_attn_b_proj"]),
        "rms_w": r1(d["L0_ffn_rms_w"]),
        "gate0_w": d["L0_ffn_gate_w"],
        "W1_0": np.ascontiguousarray(d["L0_ffn_W1"]).reshape(E * D, FF),
        "b1_0": np.ascontiguousarray(d["L0_ffn_b1"]).reshape(E * 32, 128),
        "W2_0": np.ascontiguousarray(d["L0_ffn_W2"]).reshape(E * FF, D),
        "b2_0": np.ascontiguousarray(d["L0_ffn_b2"]),
        "Wq1": d["L1_attn_Wq"], "Wk1": d["L1_attn_Wk"], "Wv1": d["L1_attn_Wv"],
        "Wq2": d["L1_ffn_Wq"], "Wk2": d["L1_ffn_Wk"], "Wv2": d["L1_ffn_Wv"],
        "Wo2": d["L1_ffn_Wo"],
        "bq2": r1(d["L1_ffn_bq"]), "bk2": r1(d["L1_ffn_bk"]),
        "bv2": r1(d["L1_ffn_bv"]), "bo2": r1(d["L1_ffn_bo"]),
        "ln1_g": r1(d["L1_ffn_ln1_g"]), "ln1_b": r1(d["L1_ffn_ln1_b"]),
        "ln2_g": r1(d["L1_ffn_ln2_g"]), "ln2_b": r1(d["L1_ffn_ln2_b"]),
        "gate1_w": d["L1_ffn_gate_w"], "gate1_b": r1(d["L1_ffn_gate_b"]),
        "W1_1": np.ascontiguousarray(d["L1_ffn_W1"]).reshape(E * D, FF),
        "b1_1": np.ascontiguousarray(d["L1_ffn_b1"]).reshape(E * 32, 128),
        "W2_1": np.ascontiguousarray(d["L1_ffn_W2"]).reshape(E * FF, D),
        "b2_1": np.ascontiguousarray(d["L1_ffn_b2"]),
    }


def kernel(**inputs):
    nc = build_nc()
    in_maps = [shard_inputs(inputs, c) for c in range(8)]
    res = run_bass_kernel_spmd(nc, in_maps, core_ids=list(range(8)))
    return np.stack([res.results[c]["out"] for c in range(8)])


# version 03
# speedup vs baseline: 1.5192x; 1.1423x over previous
from contextlib import ExitStack

import numpy as np
import concourse.bass as bass
import concourse.mybir as mybir
from concourse.bass_utils import run_bass_kernel_spmd

F32 = mybir.dt.float32
U32 = mybir.dt.uint32
AF = mybir.ActivationFunctionType
ALU = mybir.AluOpType
AX = mybir.AxisListType

T = 2048
D = 1024
NTL = 16
H, HD = 16, 64
FF = 4096
E = 8
CAP0 = 384
NSLOT0 = E * CAP0
NTILE1 = 40
NSLOT1 = NTILE1 * 128

ENGS = ("sync", "tensor", "vector", "scalar", "gpsimd")


class Op:
    __slots__ = ("eng", "fn", "deps", "dma", "ch", "sem_val", "tag", "seq")

    def __init__(self, eng, fn, deps, dma, ch, tag):
        self.eng, self.fn, self.deps, self.dma, self.ch, self.tag = (
            eng, fn, deps, dma, ch, tag)
        self.sem_val = None


class Buf:
    __slots__ = ("w", "readers", "name")

    def __init__(self, name=""):
        self.w = None
        self.readers = []
        self.name = name


class B:

    def __init__(self, nc):
        self.nc = nc
        self.q = {e: [] for e in ENGS}
        self.used = set()
        self.oneshot = set()
        self.seq = 0

    serial = False

    def do(self, eng, fn, reads=(), writes=(), deps=(), dma=None, tag=""):
        if dma == "c_dbg" and getattr(self, "_last_dbg", None) is not None:
            deps = list(deps) + [self._last_dbg]
        if self.serial and getattr(self, "_prev", None) is not None:
            deps = list(deps) + [self._prev]
        dd = []
        for bb in reads:
            if bb.w is not None:
                dd.append(bb.w)
        for bb in writes:
            if bb.w is not None:
                dd.append(bb.w)
            dd.extend(bb.readers)
        dd.extend(d for d in deps if d is not None)
        ch = dma if dma else eng
        op = Op(eng, fn, dd, dma is not None, ch, tag)
        op.seq = self.seq
        self.seq += 1
        self.q[eng].append(op)
        for bb in reads:
            bb.readers.append(op)
        for bb in writes:
            bb.w = op
            bb.readers = []
        if dma == "c_dbg":
            self._last_dbg = op
        self._prev = op
        return op

    def finalize(self, finals):
        for op in finals:
            self.used.add(op)
        for e in ENGS:
            for op in self.q[e]:
                kept = []
                for d in op.deps:
                    if d.eng == op.eng and not d.dma and op.eng == "tensor":
                        continue
                    if (op.dma and d.dma and d.ch == op.ch
                            and d.ch in self.oneshot):
                        continue
                    kept.append(d)
                    self.used.add(d)
                op.deps = kept
        counts = {}
        for e in ENGS:
            for op in self.q[e]:
                if op.dma or op in self.used:
                    counts[op.ch] = counts.get(op.ch, 0) + (16 if op.dma else 1)
                    op.sem_val = counts[op.ch]
        last_dma = {}
        first_use = {}
        for e in ENGS:
            for op in self.q[e]:
                if op.dma and op.ch in self.oneshot:
                    last_dma[op.ch] = max(last_dma.get(op.ch, -1), op.seq)
                for d in op.deps:
                    if d.ch in self.oneshot and d.dma:
                        first_use.setdefault(d.ch, op.seq)
                        first_use[d.ch] = min(first_use[d.ch], op.seq)
        for ch, lu in last_dma.items():
            fu = first_use.get(ch)
            assert fu is None or fu > lu, (
                f"oneshot channel {ch}: consumer at seq {fu} before last DMA "
                f"at seq {lu} -> deadlock risk")
        self.final_counts = counts
        return counts

    def emit(self, eng_name, eng, sems):
        watermark = {}
        for op in self.q[eng_name]:
            need = {}
            for d in op.deps:
                if d.sem_val is None:
                    raise RuntimeError(f"dep of {op.tag} has no sem: {d.tag}")
                v = (self.final_counts[d.ch]
                     if d.ch in self.oneshot else d.sem_val)
                if v > need.get(d.ch, 0):
                    need[d.ch] = v
            for se, v in sorted(need.items()):
                if watermark.get(se, 0) < v:
                    eng.wait_ge(sems[se], v)
                    watermark[se] = v
            inst = op.fn(eng)
            if op.sem_val is not None:
                inst.then_inc(sems[op.ch], 16 if op.dma else 1)



def build_nc(stage_lo=0, stage_hi=3, d_sub="all", serial=False):
    nc = bass.Bass("TRN2", target_bir_lowering=False, debug=False)
    st = ExitStack()

    def din(name, shape):
        return nc.dram_tensor(name, shape, F32, kind="ExternalInput").ap()

    xin = din("x", [T, D])
    W_attn = din("W_attn", [D, 3 * D])
    b_attn = din("b_attn", [1, 3 * D])
    W_proj = din("W_proj", [D, D])
    b_proj = din("b_proj", [1, D])
    rms_w = din("rms_w", [1, D])
    gate0_w = din("gate0_w", [E, D])
    W1_0 = din("W1_0", [E * D, FF])
    b1_0 = din("b1_0", [E * 32, 128])
    W2_0 = din("W2_0", [E * FF, D])
    b2_0 = din("b2_0", [E, D])
    Wq1 = din("Wq1", [HD, HD])
    Wk1 = din("Wk1", [HD, HD])
    Wv1 = din("Wv1", [HD, HD])
    Wq2 = din("Wq2", [D, D])
    Wk2 = din("Wk2", [D, D])
    Wv2 = din("Wv2", [D, D])
    Wo2 = din("Wo2", [D, D])
    bq2 = din("bq2", [1, D])
    bk2 = din("bk2", [1, D])
    bv2 = din("bv2", [1, D])
    bo2 = din("bo2", [1, D])
    ln1_g = din("ln1_g", [1, D])
    ln1_b = din("ln1_b", [1, D])
    ln2_g = din("ln2_g", [1, D])
    ln2_b = din("ln2_b", [1, D])
    gate1_w = din("gate1_w", [D, E])
    gate1_b = din("gate1_b", [1, E])
    W1_1 = din("W1_1", [E * D, FF])
    b1_1 = din("b1_1", [E * 32, 128])
    W2_1 = din("W2_1", [E * FF, D])
    b2_1 = din("b2_1", [E, D])
    out_hbm = nc.dram_tensor("out", [T, D], F32, kind="ExternalOutput").ap()

    def scratch(name, shape, **kw):
        return nc.dram_tensor(name, shape, F32, kind="Internal", **kw).ap()

    xg0_hbm = scratch("xg0", [NSLOT0 + 128, D])
    yg0_hbm = scratch("yg0", [NSLOT0 + 128, D])
    qT_hbm = scratch("qTh", [D, T])
    kT_hbm = scratch("kTh", [D, T])
    v_hbm = scratch("vh", [T, D])
    xg1_hbm = scratch("xg1", [NSLOT1 + 128, D])
    x3_hbm = scratch("x3h", [T, D])
    yg1_hbm = scratch("yg1", [NSLOT1 + 128, D])
    dbg_hbm = scratch("dbg", [2048, D])
    if stage_lo <= 2 <= stage_hi:
        cc_in = scratch("ccin", [T * H])
        cc_out = nc.dram_tensor("ccout", [8 * T * H], F32, kind="Internal",
                                addr_space="Shared").ap()

    ic = {}

    def const(name, arr):
        ic[name] = nc.inline_tensor(np.ascontiguousarray(arr, np.float32),
                                    name="c_" + name).ap()

    const("ident", np.eye(128))
    const("triu", np.triu(np.ones((128, 128))))
    const("triuS", np.triu(np.ones((128, 128)), 1))
    const("ones_col", np.ones((128, 1)))
    const("ones_1x128", np.ones((1, 128)))
    const("iota_col", np.arange(128)[:, None])
    const("iota8", np.tile(np.arange(8), (128, 1)))
    const("iota40", np.arange(40)[None, :])
    const("thr17", (np.arange(17) * 128.0)[None, :])
    const("cidx", np.arange(128)[:, None] + 128.0 * np.arange(32)[None, :])

    ARENA = 36000
    xh = st.enter_context(nc.sbuf_tensor("Xs", [128, NTL, D], F32))
    ar = st.enter_context(nc.sbuf_tensor("AR", [128, ARENA], F32))
    cs = st.enter_context(nc.sbuf_tensor("CS", [128, 580], F32))
    idxs = st.enter_context(nc.sbuf_tensor("IDX", [128, 208], U32))
    psum = [st.enter_context(nc.psum_tensor(f"PS{i}", [128, 512], F32))
            for i in range(8)]

    b = B(nc)
    b.serial = serial
    b.oneshot = {"c_pre", "c_pre2", "c_pre3", "c_pre3b", "c_pre4", "c_pre5",
                 "c_pre6", "c_pre7", "c_st", "c_zero0", "c_zero1"}
    cbuf = Buf("consts")

    C = {}
    coff = 0
    for name, width in (("ident", 128), ("triu", 128), ("triuS", 128),
                        ("ones_col", 1), ("ones_1x128", 128),
                        ("iota_col", 1), ("iota8", 8), ("iota40", 40),
                        ("thr17", 17)):
        src = ic[name]
        rows = src.shape[0]
        C[name] = cs[0:rows, coff:coff + width]
        b.do("sync", lambda e, d=C[name], s=src:
             e.dma_start(out=d, in_=s), writes=[cbuf], dma="c_pre",
             tag=f"ldc_{name}")
        coff += width
    b.do("sync", lambda e: e.dma_start(
        out=idxs[:, 136:168].bitcast(F32), in_=ic["cidx"]),
        writes=[cbuf], dma="c_pre", tag="ldc_cidx")
    cidxF = idxs[:, 136:168].bitcast(F32)

    Xb = [Buf(f"X{i}") for i in range(NTL)]
    Xt = [xh[:, i, :] for i in range(NTL)]

    x_r = xin.rearrange("(a p) d -> p a d", p=128)
    for i in range(0, NTL, 4):
        b.do("sync", lambda e, i=i: e.dma_start(
            out=xh[:, i:i + 4, :], in_=x_r[:, i:i + 4, :]),
            writes=[Xb[j] for j in range(i, i + 4)], dma="c_pre", tag=f"ldX{i}")

    class Arena:
        def __init__(self):
            self.off = 0
            self.marks = []

        def push(self):
            self.marks.append(self.off)

        def pop(self):
            self.off = self.marks.pop()

        def fence(self):
            return [b.q[e][-1] for e in ENGS if b.q[e]]

        def take(self, rows, *shape):
            n = 1
            for s in shape:
                n *= s
            assert self.off + n <= ARENA, f"arena overflow {self.off}+{n}"
            ap = ar[0:rows, self.off:self.off + n]
            self.off += n
            if len(shape) > 1:
                ap = ap.rearrange("p (a c) -> p a c", a=shape[0], c=shape[1])
            return ap

    A = Arena()
    pb = [Buf(f"ps{i}") for i in range(8)]

    def mm(ps_ap, lhsT, rhs, start, stop, reads, bank, tag="", tp=None):
        wl = [pb[bank]] if start else []
        dl = [] if start else [pb[bank].w]
        op = b.do("tensor",
                  lambda e, o=ps_ap, l=lhsT, r=rhs, s1=start, s2=stop, t=tp:
                  e.matmul(o, l, r, start=s1, stop=s2, tile_position=t),
                  reads=list(reads), writes=wl, deps=dl, tag=tag)
        if not start:
            pb[bank].w = op
        return op

    def tr(ps_ap, src_ap, reads, bank, tag="tr"):
        kk_ = src_ap.shape[0]
        return b.do("tensor",
                    lambda e, o=ps_ap, s=src_ap, k=kk_:
                    e.transpose(o, s, C["ident"][0:k, 0:k]),
                    reads=list(reads) + [cbuf], writes=[pb[bank]], tag=tag)

    def ln_apply(res_ap, res_bufs, out_ap, out_bufs, tmp, tmpb, stc, stcb,
                 eps=1e-5, tag="ln"):
        ssum, ssq = stc[:, 0:1], stc[:, 1:2]
        mean, var = stc[:, 2:3], stc[:, 3:4]
        rstd, negmr = stc[:, 4:5], stc[:, 5:6]
        b.do("vector", lambda e: e.reduce_sum(ssum, res_ap, axis=AX.X),
             reads=list(res_bufs), writes=[stcb], tag=tag + "s")
        b.do("vector", lambda e: e.scalar_tensor_tensor(
            tmp, res_ap, 1.0, res_ap, ALU.mult, ALU.mult, accum_out=ssq),
            reads=list(res_bufs), writes=[tmpb, stcb], tag=tag + "q")
        b.do("vector", lambda e: e.tensor_scalar(
            mean, ssum, 1.0 / D, None, ALU.mult),
            reads=[stcb], writes=[stcb], tag=tag + "m")
        b.do("vector", lambda e: e.scalar_tensor_tensor(
            var, mean, -1.0, mean, ALU.mult, ALU.mult),
            reads=[stcb], writes=[stcb], tag=tag + "mm")
        b.do("vector", lambda e: e.scalar_tensor_tensor(
            var, ssq, 1.0 / D, var, ALU.mult, ALU.add),
            reads=[stcb], writes=[stcb], tag=tag + "v")
        b.do("vector", lambda e: e.tensor_scalar(
            var, var, eps, None, ALU.add),
            reads=[stcb], writes=[stcb], tag=tag + "e")
        b.do("vector", lambda e: e.reciprocal(rstd, var),
             reads=[stcb], writes=[stcb], tag=tag + "r")
        b.do("scalar", lambda e: e.activation(rstd, rstd, AF.Sqrt),
             reads=[stcb], writes=[stcb], tag=tag + "sq")
        b.do("vector", lambda e: e.tensor_scalar(
            negmr, mean, rstd, -1.0, ALU.mult, ALU.mult),
            reads=[stcb], writes=[stcb], tag=tag + "n")
        return b.do("vector", lambda e: e.tensor_scalar(
            out_ap, res_ap, rstd, negmr, ALU.mult, ALU.add),
            reads=list(res_bufs) + [stcb], writes=list(out_bufs), tag=tag + "a")

    def feat_map(ap_, bufs, scr, scrb, tag="fm"):
        b.do("vector", lambda e: e.tensor_scalar(
            scr, ap_, 0.0, None, ALU.min),
            reads=list(bufs), writes=[scrb], tag=tag + "1")
        b.do("scalar", lambda e: e.activation(scr, scr, AF.Exp),
             reads=[scrb], writes=[scrb], tag=tag + "2")
        b.do("vector", lambda e: e.scalar_tensor_tensor(
            ap_, ap_, 0.0, scr, ALU.max, ALU.add),
            reads=list(bufs) + [scrb], writes=list(bufs), tag=tag + "3")

    def stage_a():
        A.push()
        fenceA = A.fence()

        def Buf_(name):
            bb = Buf(name)
            bb.readers = list(fenceA)
            return bb
        SL, NCH = 256, 2
        NSL = T // SL
        xT = A.take(128, 8, SL)
        xTb = Buf_("xT")
        qT = A.take(128, 8, SL)
        qTb = Buf_("qT")
        kT = A.take(128, 8, SL)
        kTb = Buf_("kT")
        kk = A.take(128, NCH, D)
        kkb = Buf_("kk")
        vv = A.take(128, NCH, D)
        vvb = Buf_("vv")
        yy = A.take(128, NCH, D)
        yyb = Buf_("yy")
        yT = A.take(128, 8, SL)
        yTb = Buf_("yT")
        amt = A.take(128, 16, 128)
        amtb = Buf_("amt")
        fmscr = A.take(128, 8, SL)
        fmscrb = Buf_("fmscr")
        wqk = [(A.take(128, 512), Buf(f"wqk{i}")) for i in range(4)]
        wkv = [(A.take(128, 512), Buf(f"wkv{i}")) for i in range(2)]
        wpr = [(A.take(128, 512), Buf(f"wpr{i}")) for i in range(2)]
        kvsb = A.take(128, 520)
        kvsbb = Buf_("kvsb")
        rden = A.take(128, 16)
        rdenb = Buf_("rden")
        battc = A.take(128, 24)
        battcb = Buf_("battc")
        battr = A.take(128, 3072)
        battrb = Buf_("battr")
        bprjr = A.take(128, D)
        bprjrb = Buf_("bprjr")
        tmp = A.take(128, D)
        tmpb = Buf_("tmpA")
        res = A.take(128, D)
        resb = Buf_("resA")
        stc = A.take(128, 8)
        stcb = Buf_("stcA")

        b.do("sync", lambda e: e.dma_start(
            out=battc[:, :], in_=b_attn.rearrange("o (a p) -> p (o a)", p=128),
            allow_slow_non_contiguous=True),
            writes=[battcb], dma="c_pre", tag="ld_battc")
        b.do("sync", lambda e: e.dma_start(
            out=battr[0:1, :], in_=b_attn[0:1, :]),
            writes=[battrb], dma="c_pre", tag="ld_battr")
        b.do("sync", lambda e: e.dma_start(
            out=bprjr[0:1, :], in_=b_proj[0:1, :]),
            writes=[bprjrb], dma="c_pre", tag="ld_bprj")

        b.do("vector", lambda e: e.memset(kvsb[:, :], 0.0), writes=[kvsbb])

        KVE, KVO, KS = 5, 6, 7

        for s in range(NSL):
            t0 = s * SL
            tiles = [t0 // 128, t0 // 128 + 1]
            for c in range(8):
                for u in range(NCH):
                    bank = (c * NCH + u) % 2
                    i = tiles[u]
                    tr(psum[bank][:, 0:128], Xt[i][:, c * 128:(c + 1) * 128],
                       [Xb[i]], bank, tag=f"trx{s}")
                    b.do("vector", lambda e, c=c, u=u, bank=bank: e.tensor_copy(
                        xT[:, c, u * 128:(u + 1) * 128], psum[bank][:, 0:128]),
                        reads=[pb[bank]], writes=[xTb], tag="evx")
            for mg in range(4):
                for c in range(8):
                    slot = c % 4
                    wap, wb_ = wqk[slot]
                    b.do("sync", lambda e, c=c, mg=mg, w=wap: e.dma_start(
                        out=w[:, :],
                        in_=W_attn[c * 128:(c + 1) * 128,
                                   mg * 512:(mg + 1) * 512]),
                        writes=[wb_], dma=f"c_wqk{slot}", tag="ldwqk")
                    for j in range(4):
                        mm(psum[j][:, 0:SL], wap[:, j * 128:(j + 1) * 128],
                           xT[:, c, :], start=(c == 0), stop=(c == 7),
                           reads=[wb_, xTb], bank=j, tag=f"qkT{mg}{j}")
                for j in range(4):
                    m = mg * 4 + j
                    dst = qT if m < 8 else kT
                    dstb = qTb if m < 8 else kTb
                    ch = m % 8
                    b.do("vector", lambda e, d=dst, ch=ch, j=j, m=m:
                         e.tensor_scalar(d[:, ch, :], psum[j][:, 0:SL],
                                         battc[:, m:m + 1], None, ALU.add),
                         reads=[pb[j], battcb], writes=[dstb], tag="evqkT")
            for g in range(4):
                col0 = 1024 + 512 * g
                dst, dstb = (kk, kkb) if g < 2 else (vv, vvb)
                go = g % 2
                for c in range(8):
                    slot = c % 2
                    wap, wb_ = wkv[slot]
                    b.do("sync", lambda e, c=c, col0=col0, w=wap: e.dma_start(
                        out=w[:, :],
                        in_=W_attn[c * 128:(c + 1) * 128, col0:col0 + 512]),
                        writes=[wb_], dma=f"c_wkv{slot}", tag="ldwkv")
                    for u in range(NCH):
                        mm(psum[u][:, 0:512],
                           xT[:, c, u * 128:(u + 1) * 128], wap[:, :],
                           start=(c == 0), stop=False,
                           reads=[wb_, xTb], bank=u, tag=f"kv{g}")
                for u in range(NCH):
                    mm(psum[u][:, 0:512], C["ones_1x128"],
                       battr[0:1, col0:col0 + 512], start=False, stop=True,
                       reads=[cbuf, battrb], bank=u, tag="kvb")
                    b.do("scalar", lambda e, d=dst, u=u, go=go: e.activation(
                        d[:, u, go * 512:(go + 1) * 512], psum[u][:, 0:512],
                        AF.Copy),
                        reads=[pb[u]], writes=[dstb], tag="evkv")
            feat_map(qT[:, :, :], [qTb], fmscr[:, :, :], fmscrb, tag="fmq")
            feat_map(kT[:, :, :], [kTb], fmscr[:, :, :], fmscrb, tag="fmk")
            feat_map(kk.rearrange("p a c -> p (a c)"), [kkb],
                     fmscr.rearrange("p a c -> p (a c)"), fmscrb, tag="fmkk")
            for u in range(NCH):
                ci = s * NCH + u
                for p in range(8):
                    kpair = kk[:, u, p * 128:(p + 1) * 128]
                    vpair = vv[:, u, p * 128:(p + 1) * 128]
                    kvbank = KVE if p < 4 else KVO
                    cc0 = (p % 4) * 128
                    mm(psum[kvbank][:, cc0:cc0 + 128], kpair, vpair,
                       start=(p % 4 == 0), stop=(p % 4 == 3),
                       reads=[kkb, vvb], bank=kvbank, tag="kvup")
                    mm(psum[KS][:, p:p + 1], kpair, C["ones_col"],
                       start=(p == 0), stop=(p == 7),
                       reads=[kkb, cbuf], bank=KS, tag="ksup")
                for h in range(H):
                    bank = h % 2
                    hp = (h % 2) * 64
                    hc = h // 2
                    lk = kT[hp:hp + 64, hc, u * 128:(u + 1) * 128]
                    lq = qT[hp:hp + 64, hc, u * 128:(u + 1) * 128]
                    mm(psum[bank][:, 0:128], lk, lq, start=True, stop=True,
                       reads=[kTb, qTb], bank=bank, tag="AT")
                    b.do("vector", lambda e, h=h, bank=bank: e.tensor_tensor(
                        amt[:, h, :], psum[bank][:, 0:128], C["triu"],
                        ALU.mult),
                        reads=[pb[bank], cbuf], writes=[amtb], tag="mask")
                for h in range(H):
                    hp = (h % 2) * 64
                    hc = h // 2
                    nbank = 2 + (h % 2)
                    ncol = (h // 2) * 64
                    lq = qT[hp:hp + 64, hc, u * 128:(u + 1) * 128]
                    mm(psum[nbank][:, ncol:ncol + 64], amt[:, h, :],
                       vv[:, u, h * 64:(h + 1) * 64], start=(h < 2),
                       stop=False, reads=[amtb, vvb], bank=nbank, tag="numi")
                    mm(psum[nbank][:, ncol:ncol + 64], lq,
                       kvsb[hp:hp + 64, (h // 2) * 64:(h // 2) * 64 + 64],
                       start=False, stop=(h >= H - 2),
                       reads=[qTb, kvsbb], bank=nbank, tag="nume")
                    mm(psum[4][:, h:h + 1], amt[:, h, :], C["ones_col"],
                       start=(h == 0), stop=False,
                       reads=[amtb, cbuf], bank=4, tag="dena")
                    mm(psum[4][:, h:h + 1], lq,
                       kvsb[hp:hp + 64, 512 + h // 2:513 + h // 2],
                       start=False, stop=(h == H - 1),
                       reads=[qTb, kvsbb], bank=4, tag="denb")
                b.do("vector", lambda e: e.tensor_scalar(
                    rden[:, 0:16], psum[4][:, 0:16], 1e-6, None, ALU.add),
                    reads=[pb[4]], writes=[rdenb], tag="deps")
                b.do("vector", lambda e: e.reciprocal(rden[:, 0:16],
                                                      rden[:, 0:16]),
                     reads=[rdenb], writes=[rdenb], tag="drcp")
                for h in range(H):
                    nbank = 2 + (h % 2)
                    ncol = (h // 2) * 64
                    b.do("vector", lambda e, h=h, u=u, nbank=nbank, ncol=ncol:
                         e.tensor_scalar(yy[:, u, h * 64:(h + 1) * 64],
                                         psum[nbank][:, ncol:ncol + 64],
                                         rden[:, h:h + 1], None, ALU.mult),
                         reads=[pb[nbank], rdenb], writes=[yyb], tag="yev")
                for bank, p0 in ((KVE, 0), (KVO, 4)):
                    dvE = psum[bank][0:64, 0:512].rearrange(
                        "p (a c) -> p a c", a=4, c=128)[:, :, 0:64]
                    dvO = psum[bank][64:128, 0:512].rearrange(
                        "p (a c) -> p a c", a=4, c=128)[:, :, 64:128]
                    tgtE = kvsb[0:64, p0 * 64:(p0 + 4) * 64].rearrange(
                        "p (a c) -> p a c", a=4, c=64)
                    tgtO = kvsb[64:128, p0 * 64:(p0 + 4) * 64].rearrange(
                        "p (a c) -> p a c", a=4, c=64)
                    b.do("vector", lambda e, t=tgtE, dd=dvE: e.tensor_tensor(
                        t, t, dd, ALU.add),
                        reads=[pb[bank]], writes=[kvsbb], tag="addKVe")
                    b.do("vector", lambda e, t=tgtO, dd=dvO: e.tensor_tensor(
                        t, t, dd, ALU.add),
                        reads=[pb[bank]], writes=[kvsbb], tag="addKVo")
                b.do("vector", lambda e: e.tensor_tensor(
                    kvsb[:, 512:520], kvsb[:, 512:520], psum[KS][:, 0:8],
                    ALU.add),
                    reads=[pb[KS]], writes=[kvsbb], tag="addKS")
            for c in range(8):
                for u in range(NCH):
                    bank = (c * NCH + u) % 2
                    tr(psum[bank][:, 0:128], yy[:, u, c * 128:(c + 1) * 128],
                       [yyb], bank, tag="try")
                    b.do("vector", lambda e, c=c, u=u, bank=bank: e.tensor_copy(
                        yT[:, c, u * 128:(u + 1) * 128], psum[bank][:, 0:128]),
                        reads=[pb[bank]], writes=[yTb], tag="evy")
            for g in range(2):
                for c in range(8):
                    slot = c % 2
                    wap, wb_ = wpr[slot]
                    b.do("sync", lambda e, c=c, g=g, w=wap: e.dma_start(
                        out=w[:, :],
                        in_=W_proj[c * 128:(c + 1) * 128,
                                   g * 512:(g + 1) * 512]),
                        writes=[wb_], dma=f"c_wpr{slot}", tag="ldwpr")
                    for u in range(NCH):
                        mm(psum[g * 2 + u][:, 0:512],
                           yT[:, c, u * 128:(u + 1) * 128], wap[:, :],
                           start=(c == 0), stop=False,
                           reads=[wb_, yTb], bank=g * 2 + u, tag="proj")
                for u in range(NCH):
                    mm(psum[g * 2 + u][:, 0:512], C["ones_1x128"],
                       bprjr[0:1, g * 512:(g + 1) * 512], start=False,
                       stop=True, reads=[cbuf, bprjrb], bank=g * 2 + u,
                       tag="projb")
            for u in range(NCH):
                i = tiles[u]
                for g in range(2):
                    b.do("vector", lambda e, i=i, g=g, u=u: e.tensor_tensor(
                        res[:, g * 512:(g + 1) * 512],
                        psum[g * 2 + u][:, 0:512],
                        Xt[i][:, g * 512:(g + 1) * 512], ALU.add),
                        reads=[pb[g * 2 + u], Xb[i]], writes=[resb],
                        tag="resadd")
                ln_apply(res[:, :], [resb], Xt[i], [Xb[i]], tmp[:, :], tmpb,
                         stc, stcb, eps=1e-5, tag=f"lnA{i}")
        A.pop()

    def stage_b():
        A.push()
        fenceB = A.fence()

        def BufB(name):
            bb = Buf(name)
            bb.readers = list(fenceB)
            return bb
        xn_sl = A.take(128, 4, D)
        xn_slb = BufB("xn_sl")
        xnT = A.take(128, 8, 512)
        xnTb = BufB("xnT")
        rms_rep = A.take(128, D)
        rms_repb = BufB("rms_rep")
        g0T = A.take(128, 8, E)
        g0Tb = BufB("g0T")
        gate = A.take(128, 2, E)
        gateb = BufB("gate")
        gcol = A.take(128, 8)
        gcolb = BufB("gcol")
        score = A.take(128, NTL)
        scoreb = BufB("score")
        slotf = A.take(128, NTL)
        slotfb = BufB("slotf")
        postile = A.take(128, E)
        postileb = BufB("postile")
        carry = A.take(128, E)
        carryb = BufB("carry")
        zt = A.take(128, D)
        ztb = BufB("zt")
        xg_sl = A.take(128, 3, D)
        xg_slb = BufB("xg_sl")
        xgT = A.take(128, 8, CAP0)
        xgTb = BufB("xgT")
        y_sl = A.take(128, 3, D)
        y_slb = BufB("y_sl")
        w1s = [(A.take(128, 256), BufB(f"w1s{i}")) for i in range(2)]
        w2s = [(A.take(128, D), BufB(f"w2s{i}")) for i in range(2)]
        h_m = A.take(128, 2, CAP0)
        h_mb = BufB("h_m")
        b1c = A.take(128, 32)
        b1cb = BufB("b1c")
        b2r = A.take(128, D)
        b2rb = BufB("b2r")
        y1 = A.take(128, D)
        y1b = BufB("y1")
        tmp = A.take(128, D)
        tmpb = BufB("tmpB")
        res = A.take(128, D)
        resb = BufB("resB")
        stc = A.take(128, 8)
        stcb = BufB("stcB")

        idxb = BufB("idx")

        g0r = gate0_w.rearrange("e (c p) -> p c e", p=128)
        for c in range(8):
            b.do("sync", lambda e, c=c: e.dma_start(
                out=g0T[:, c, :], in_=g0r[:, c, :],
                allow_slow_non_contiguous=True),
                writes=[g0Tb], dma="c_pre2", tag="ld_g0T")
        rmsrow = A.take(128, D)
        rmsrowb = BufB("rmsrow")
        b.do("sync", lambda e: e.dma_start(out=rmsrow[0:1, :], in_=rms_w[0:1, :]),
             writes=[rmsrowb], dma="c_pre2", tag="ld_rms")
        for g in range(2):
            mm(psum[6][:, 0:512], C["ones_1x128"], rmsrow[0:1, g * 512:(g + 1) * 512],
               start=True, stop=True, reads=[cbuf, rmsrowb], bank=6, tag="rmsbc")
            b.do("vector", lambda e, g=g: e.tensor_copy(
                rms_rep[:, g * 512:(g + 1) * 512], psum[6][:, 0:512]),
                reads=[pb[6]], writes=[rms_repb], tag="rmsbc_ev")
        b.do("vector", lambda e: e.memset(zt[:, :], 0.0), writes=[ztb])
        NZR = (NSLOT0 + 128) // 128
        for i in range(NZR):
            b.do("gpsimd", lambda e, i=i: e.dma_start(
                out=xg0_hbm[i * 128:(i + 1) * 128, :], in_=zt[:, :]),
                reads=[ztb], dma="c_zero0", tag="zero_xg0")
        b.do("vector", lambda e: e.memset(carry[0:1, :], 0.0), writes=[carryb])

        for s in range(4):
            for u in range(4):
                i = s * 4 + u
                b.do("vector", lambda e, i=i: e.scalar_tensor_tensor(
                    tmp, Xt[i], 1.0, Xt[i], ALU.mult, ALU.mult,
                    accum_out=stc[:, 0:1]),
                    reads=[Xb[i]], writes=[tmpb, stcb], tag="rms_sq")
                b.do("vector", lambda e: e.tensor_scalar(
                    stc[:, 1:2], stc[:, 0:1], 1.0 / D, 1e-6, ALU.mult, ALU.add),
                    reads=[stcb], writes=[stcb], tag="rms_ms")
                b.do("vector", lambda e: e.reciprocal(stc[:, 1:2], stc[:, 1:2]),
                     reads=[stcb], writes=[stcb], tag="rms_rc")
                b.do("scalar", lambda e: e.activation(
                    stc[:, 1:2], stc[:, 1:2], AF.Sqrt),
                    reads=[stcb], writes=[stcb], tag="rms_sq2")
                b.do("scalar", lambda e, i=i, u=u: e.activation(
                    xn_sl[:, u, :], Xt[i], AF.Copy, scale=stc[:, 1:2]),
                    reads=[Xb[i], stcb], writes=[xn_slb], tag="rms_ap")
                b.do("vector", lambda e, u=u: e.tensor_tensor(
                    xn_sl[:, u, :], xn_sl[:, u, :], rms_rep[:, :], ALU.mult),
                    reads=[xn_slb, rms_repb], writes=[xn_slb], tag="rms_w")
            for c in range(8):
                for u in range(4):
                    bank = (c * 4 + u) % 2
                    tr(psum[bank][:, 0:128], xn_sl[:, u, c * 128:(c + 1) * 128],
                       [xn_slb], bank, tag="trxn")
                    b.do("vector", lambda e, c=c, u=u, bank=bank: e.tensor_copy(
                        xnT[:, c, u * 128:(u + 1) * 128], psum[bank][:, 0:128]),
                        reads=[pb[bank]], writes=[xnTb], tag="evxn")
            for u in range(4):
                i = s * 4 + u
                for c in range(8):
                    mm(psum[2][:, 0:E], xnT[:, c, u * 128:(u + 1) * 128],
                       g0T[:, c, :], start=(c == 0), stop=(c == 7),
                       reads=[xnTb, g0Tb], bank=2, tag="gate")
                b.do("vector", lambda e: e.reduce_max(
                    gcol[:, 0:1], psum[2][:, 0:E], axis=AX.X),
                    reads=[pb[2]], writes=[gcolb], tag="g_mx")
                b.do("vector", lambda e: e.tensor_scalar(
                    gcol[:, 1:2], gcol[:, 0:1], -1.0, None, ALU.mult),
                    reads=[gcolb], writes=[gcolb], tag="g_nmx")
                b.do("scalar", lambda e: e.activation(
                    gate[:, 0, :], psum[2][:, 0:E], AF.Exp,
                    bias=gcol[:, 1:2], accum_out=gcol[:, 2:3]),
                    reads=[pb[2], gcolb], writes=[gateb, gcolb], tag="g_exp")
                b.do("vector", lambda e: e.reciprocal(
                    gcol[:, 3:4], gcol[:, 2:3]),
                    reads=[gcolb], writes=[gcolb], tag="g_rs")
                b.do("vector", lambda e: e.tensor_scalar(
                    gate[:, 0, :], gate[:, 0, :], gcol[:, 3:4], None, ALU.mult),
                    reads=[gateb, gcolb], writes=[gateb], tag="g_pr")
                for ee in range(E):
                    b.do("vector", lambda e, ee=ee: e.tensor_scalar(
                        tmp[:, 0:E], gate[:, 0, :], gate[:, 0, ee:ee + 1],
                        None, ALU.is_gt),
                        reads=[gateb], writes=[tmpb], tag="r_gt")
                    b.do("vector", lambda e: e.reduce_sum(
                        gcol[:, 4:5], tmp[:, 0:E], axis=AX.X),
                        reads=[tmpb], writes=[gcolb], tag="r_gts")
                    if ee > 0:
                        b.do("vector", lambda e, ee=ee: e.tensor_scalar(
                            tmp[:, 0:ee], gate[:, 0, 0:ee],
                            gate[:, 0, ee:ee + 1], None, ALU.is_equal),
                            reads=[gateb], writes=[tmpb], tag="r_eq")
                        b.do("vector", lambda e, ee=ee: e.reduce_sum(
                            gcol[:, 5:6], tmp[:, 0:ee], axis=AX.X),
                            reads=[tmpb], writes=[gcolb], tag="r_eqs")
                        b.do("vector", lambda e: e.tensor_tensor(
                            gcol[:, 4:5], gcol[:, 4:5], gcol[:, 5:6], ALU.add),
                            reads=[gcolb], writes=[gcolb], tag="r_add")
                    b.do("vector", lambda e, ee=ee: e.tensor_scalar(
                        gate[:, 1, ee:ee + 1], gcol[:, 4:5], 0.0, None,
                        ALU.is_equal),
                        reads=[gcolb], writes=[gateb], tag="r_sel")
                b.do("vector", lambda e: e.tensor_tensor(
                    tmp[:, 0:E], gate[:, 1, :], C["iota8"], ALU.mult),
                    reads=[gateb, cbuf], writes=[tmpb], tag="q_em")
                b.do("vector", lambda e: e.reduce_sum(
                    gcol[:, 6:7], tmp[:, 0:E], axis=AX.X),
                    reads=[tmpb], writes=[gcolb], tag="q_e")
                b.do("vector", lambda e: e.tensor_tensor(
                    tmp[:, 0:E], gate[:, 1, :], gate[:, 0, :], ALU.mult),
                    reads=[gateb], writes=[tmpb], tag="q_sm")
                b.do("vector", lambda e, i=i: e.reduce_sum(
                    score[:, i:i + 1], tmp[:, 0:E], axis=AX.X),
                    reads=[tmpb], writes=[scoreb], tag="q_s")
                mm(psum[3][:, 0:E], C["triuS"], gate[:, 1, :],
                   start=True, stop=False, reads=[cbuf, gateb], bank=3,
                   tag="scanp")
                mm(psum[3][:, 0:E], C["ones_1x128"], carry[0:1, :],
                   start=False, stop=True, reads=[cbuf, carryb], bank=3,
                   tag="scanc")
                b.do("vector", lambda e: e.tensor_copy(
                    postile[:, :], psum[3][:, 0:E]),
                    reads=[pb[3]], writes=[postileb], tag="scanev")
                mm(psum[4][0:1, 0:E], C["ones_col"], gate[:, 1, :],
                   start=True, stop=True, reads=[cbuf, gateb], bank=4,
                   tag="colsum")
                b.do("vector", lambda e: e.tensor_tensor(
                    carry[0:1, :], carry[0:1, :], psum[4][0:1, 0:E], ALU.add),
                    reads=[pb[4]], writes=[carryb], tag="carryup")
                b.do("vector", lambda e: e.tensor_tensor(
                    tmp[:, 0:E], postile[:, :], gate[:, 1, :], ALU.mult),
                    reads=[postileb, gateb], writes=[tmpb], tag="q_pm")
                b.do("vector", lambda e: e.reduce_sum(
                    gcol[:, 7:8], tmp[:, 0:E], axis=AX.X),
                    reads=[tmpb], writes=[gcolb], tag="q_p")
                b.do("vector", lambda e, i=i: e.tensor_scalar(
                    slotf[:, i:i + 1], gcol[:, 6:7], float(CAP0),
                    None, ALU.mult),
                    reads=[gcolb], writes=[slotfb], tag="q_sl1")
                b.do("vector", lambda e, i=i: e.tensor_tensor(
                    slotf[:, i:i + 1], slotf[:, i:i + 1], gcol[:, 7:8],
                    ALU.add),
                    reads=[slotfb, gcolb], writes=[slotfb], tag="q_sl2")
                b.do("vector", lambda e, i=i: e.tensor_copy(
                    idxs[:, i:i + 1], slotf[:, i:i + 1]),
                    reads=[slotfb], writes=[idxb], tag="q_cast")
                b.do("gpsimd", lambda e, i=i, u=u: e.indirect_dma_start(
                    out=xg0_hbm[0:NSLOT0, :],
                    out_offset=bass.IndirectOffsetOnAxis(
                        ap=idxs[:, i:i + 1], axis=0),
                    in_=xn_sl[:, u, :], in_offset=None),
                    reads=[idxb, xn_slb], deps=[b.q["gpsimd"][-1]]
                    if b.q["gpsimd"] else [],
                    dma="c_scat0", tag="scat0")

        for ee in range(E):
            b.do("sync", lambda e, ee=ee: e.dma_start(
                out=xg_sl[:, :, :],
                in_=xg0_hbm[ee * CAP0:(ee + 1) * CAP0, :].rearrange(
                    "(a p) d -> p a d", p=128)),
                reads=[], writes=[xg_slb],
                deps=[o for o in b.q["gpsimd"] if o.tag == "scat0"],
                dma="c_xg0", tag="ldxg")
            b.do("sync", lambda e, ee=ee: e.dma_start(
                out=b1c[:, :],
                in_=b1_0[ee * 32:(ee + 1) * 32, :].rearrange("a p -> p a"),
                allow_slow_non_contiguous=True),
                writes=[b1cb], dma="c_b1", tag="ldb1")
            b.do("sync", lambda e, ee=ee: e.dma_start(
                out=b2r[0:1, :], in_=b2_0[ee:ee + 1, :]),
                writes=[b2rb], dma="c_b2", tag="ldb2")
            for c in range(8):
                for uu in range(3):
                    bank = (c * 3 + uu) % 2
                    tr(psum[bank][:, 0:128], xg_sl[:, uu, c * 128:(c + 1) * 128],
                       [xg_slb], bank, tag="trxg")
                    b.do("vector", lambda e, c=c, uu=uu, bank=bank:
                         e.tensor_copy(xgT[:, c, uu * 128:(uu + 1) * 128],
                                       psum[bank][:, 0:128]),
                         reads=[pb[bank]], writes=[xgTb], tag="evxg")
            for fg in range(16):
                for c in range(8):
                    slot = c % 2
                    wap, wb_ = w1s[slot]
                    b.do("sync", lambda e, ee=ee, fg=fg, c=c, w=wap:
                         e.dma_start(out=w[:, :],
                                     in_=W1_0[ee * 1024 + c * 128:
                                              ee * 1024 + (c + 1) * 128,
                                              fg * 256:(fg + 1) * 256]),
                         writes=[wb_], dma=f"c_w1{slot}", tag="ldw1")
                    for j in range(2):
                        mm(psum[j][:, 0:CAP0], wap[:, j * 128:(j + 1) * 128],
                           xgT[:, c, :], start=(c == 0), stop=(c == 7),
                           reads=[wb_, xgTb], bank=j, tag="hmm")
                for j in range(2):
                    fc = fg * 2 + j
                    b.do("scalar", lambda e, j=j, fc=fc: e.activation(
                        h_m[:, j, :], psum[j][:, 0:CAP0], AF.Silu,
                        bias=b1c[:, fc:fc + 1]),
                        reads=[pb[j], b1cb], writes=[h_mb], tag="silu")
                    wap2, wb2_ = w2s[j]
                    b.do("sync", lambda e, ee=ee, fc=fc, w=wap2: e.dma_start(
                        out=w[:, :],
                        in_=W2_0[ee * 4096 + fc * 128:
                                 ee * 4096 + (fc + 1) * 128, :]),
                        writes=[wb2_], dma=f"c_w2{j}", tag="ldw2")
                    for uu in range(3):
                        for g in range(2):
                            mm(psum[2 + uu * 2 + g][:, 0:512],
                               h_m[:, j, uu * 128:(uu + 1) * 128],
                               wap2[:, g * 512:(g + 1) * 512],
                               start=(fc == 0), stop=False,
                               reads=[h_mb, wb2_], bank=2 + uu * 2 + g,
                               tag="yemm")
            for uu in range(3):
                for g in range(2):
                    mm(psum[2 + uu * 2 + g][:, 0:512], C["ones_1x128"],
                       b2r[0:1, g * 512:(g + 1) * 512], start=False, stop=True,
                       reads=[cbuf, b2rb], bank=2 + uu * 2 + g, tag="yeb")
                    b.do("vector", lambda e, uu=uu, g=g: e.tensor_copy(
                        y_sl[:, uu, g * 512:(g + 1) * 512],
                        psum[2 + uu * 2 + g][:, 0:512]),
                        reads=[pb[2 + uu * 2 + g]], writes=[y_slb], tag="evye")
            b.do("gpsimd", lambda e, ee=ee: e.dma_start(
                out=yg0_hbm[ee * CAP0:(ee + 1) * CAP0, :].rearrange(
                    "(a p) d -> p a d", p=128), in_=y_sl[:, :, :]),
                reads=[y_slb], dma="c_yg0", tag="styg")

        yg_dep = [o for o in b.q["gpsimd"] if o.tag == "styg"]
        for i in range(NTL):
            b.do("gpsimd", lambda e, i=i: e.indirect_dma_start(
                out=y1[:, :], out_offset=None,
                in_=yg0_hbm[0:NSLOT0, :],
                in_offset=bass.IndirectOffsetOnAxis(ap=idxs[:, i:i + 1],
                                                    axis=0)),
                reads=[idxb], writes=[y1b], deps=yg_dep,
                dma="c_ga0", tag="ga0")
            b.do("vector", lambda e, i=i: e.scalar_tensor_tensor(
                res, y1, score[:, i:i + 1], Xt[i], ALU.mult, ALU.add),
                reads=[y1b, scoreb, Xb[i]], writes=[resb], tag="comb")
            ln_apply(res, [resb], Xt[i], [Xb[i]], tmp, tmpb, stc, stcb,
                     eps=1e-5, tag=f"lnB{i}")
        A.pop()

    def stage_c():
        A.push()
        fenceC = A.fence()

        def BufC(name):
            bb = Buf(name)
            bb.readers = list(fenceC)
            return bb
        xT = A.take(128, 8, 512)
        xTb = BufC("xTc")
        wq3 = A.take(128, 3, HD)
        wq3b = BufC("wq3")
        wraw = A.take(128, 3, HD)
        wrawbs = [BufC(f"wraw{j}") for j in range(3)]
        qkv = A.take(128, 3, HD)
        qkvb = BufC("qkv")
        ds_sb = A.take(128, NTL, H)
        ds_sbb = BufC("ds_sb")
        dsall = A.take(128, 8, NTL * H)
        dsallb = BufC("dsall")
        mx = A.take(128, NTL * H)
        mxb = BufC("mx")
        sexp = A.take(128, NTL * H)
        sexpb = BufC("sexp")
        t2 = A.take(128, NTL * H)
        t2b = BufC("t2")
        vsr = A.take(128, D)
        vsrb = BufC("vsr")
        vrep = A.take(128, D)
        vrepb = BufC("vrep")
        att = A.take(128, D)
        attb = BufC("att")
        tmp = A.take(128, D)
        tmpb = BufC("tmpC")
        res = A.take(128, D)
        resb = BufC("resC")
        stc = A.take(128, 8)
        stcb = BufC("stcC")

        for j, w in enumerate((Wq1, Wk1, Wv1)):
            b.do("sync", lambda e, j=j, w=w: e.dma_start(
                out=wraw[0:64, j, :], in_=w[:, :]),
                writes=[wrawbs[j]], dma="c_pre3", tag="ldw1a")
        for j in range(3):
            tr(psum[5][0:64, 0:64], wraw[0:64, j, :], [wrawbs[j]], 5,
               tag="trw1a")
            b.do("vector", lambda e, j=j: e.tensor_copy(
                wq3[0:64, j, :], psum[5][0:64, 0:64]),
                reads=[pb[5]], writes=[wq3b], tag="evw1a")
        for j in range(3):
            b.do("gpsimd", lambda e, j=j: e.dma_start(
                out=wq3[64:128, j, :], in_=wq3[0:64, j, :]),
                reads=[wq3b], dma="c_pre3b", tag="mirw1a")

        VS0, VS1 = 6, 7
        first_vs = [True, True]
        for s in range(4):
            for c in range(8):
                for u in range(4):
                    bank = (c * 4 + u) % 2
                    i = s * 4 + u
                    tr(psum[bank][:, 0:128], Xt[i][:, c * 128:(c + 1) * 128],
                       [Xb[i]], bank, tag="trxc")
                    b.do("vector", lambda e, c=c, u=u, bank=bank: e.tensor_copy(
                        xT[:, c, u * 128:(u + 1) * 128], psum[bank][:, 0:128]),
                        reads=[pb[bank]], writes=[xTb], tag="evxc")
            for u in range(4):
                i = s * 4 + u
                for h in range(H):
                    hp = (h % 2) * 64
                    hc = h // 2
                    lx = xT[hp:hp + 64, hc, u * 128:(u + 1) * 128]
                    for j in range(3):
                        mm(psum[2 + j][:, 0:HD], lx, wq3[hp:hp + 64, j, :],
                           start=True, stop=True, reads=[xTb, wq3b],
                           bank=2 + j, tag="qkvC")
                        b.do("vector", lambda e, j=j: e.tensor_copy(
                            qkv[:, j, :], psum[2 + j][:, 0:HD]),
                            reads=[pb[2 + j]], writes=[qkvb], tag="evqkvC")
                    b.do("vector", lambda e: e.tensor_tensor(
                        tmp[:, 0:HD], qkv[:, 0, :], qkv[:, 1, :], ALU.mult),
                        reads=[qkvb], writes=[tmpb], tag="dsm")
                    b.do("vector", lambda e, i=i, h=h: e.reduce_sum(
                        ds_sb[:, i, h:h + 1], tmp[:, 0:HD], axis=AX.X),
                        reads=[tmpb], writes=[ds_sbb], tag="dsr")
                    vsb = VS0 + (h % 2)
                    fidx = h % 2
                    mm(psum[vsb][0:1, (h // 2) * 64:(h // 2) * 64 + 64],
                       C["ones_col"], qkv[:, 2, :],
                       start=(first_vs[fidx]), stop=(i == NTL - 1 and h >= H - 2),
                       reads=[cbuf, qkvb], bank=vsb, tag="vsum")
                    first_vs[fidx] = False
        b.do("vector", lambda e: e.tensor_scalar(
            ds_sb[:, :, :], ds_sb[:, :, :], 1.0 / 32.0, None, ALU.mult),
            reads=[ds_sbb], writes=[ds_sbb], tag="dssc")
        cc_in_r = cc_in.rearrange("(p x) -> p x", p=128)
        b.do("gpsimd", lambda e: e.dma_start(
            out=cc_in_r,
            in_=ds_sb.rearrange("p a h -> p (a h)")),
            reads=[ds_sbb], dma="c_cc1", tag="stds")
        coll = b.do("gpsimd", lambda e: e.collective_compute(
            "AllGather", ALU.bypass, replica_groups=[list(range(8))],
            ins=[cc_in[:]], outs=[cc_out[:]]),
            deps=[b.q["gpsimd"][-1]], tag="allgather")
        b.do("sync", lambda e: e.dma_start(
            out=dsall[:, :, :],
            in_=cc_out.rearrange("(r p x) -> p r x", r=8, p=128)),
            writes=[dsallb], deps=[coll], dma="c_cc2", tag="lddsall")
        b.do("vector", lambda e: e.tensor_copy(mx[:, :], dsall[:, 0, :]),
             reads=[dsallb], writes=[mxb], tag="smx0")
        for r in range(1, 8):
            b.do("vector", lambda e, r=r: e.tensor_tensor(
                mx[:, :], mx[:, :], dsall[:, r, :], ALU.max),
                reads=[mxb, dsallb], writes=[mxb], tag="smx")
        for r in range(8):
            b.do("vector", lambda e, r=r: e.tensor_sub(
                t2[:, :], dsall[:, r, :], mx[:, :]),
                reads=[dsallb, mxb], writes=[t2b], tag="sse1")
            b.do("scalar", lambda e: e.activation(t2[:, :], t2[:, :], AF.Exp),
                 reads=[t2b], writes=[t2b], tag="sse2")
            if r == 0:
                b.do("vector", lambda e: e.tensor_copy(sexp[:, :], t2[:, :]),
                     reads=[t2b], writes=[sexpb], tag="sse3")
            else:
                b.do("vector", lambda e: e.tensor_add(
                    sexp[:, :], sexp[:, :], t2[:, :]),
                    reads=[sexpb, t2b], writes=[sexpb], tag="sse3")
        b.do("vector", lambda e: e.reciprocal(sexp[:, :], sexp[:, :]),
             reads=[sexpb], writes=[sexpb], tag="srcp")
        b.do("vector", lambda e: e.tensor_sub(
            t2[:, :], ds_sb.rearrange("p a h -> p (a h)"), mx[:, :]),
            reads=[ds_sbb, mxb], writes=[t2b], tag="spr1")
        b.do("scalar", lambda e: e.activation(t2[:, :], t2[:, :], AF.Exp),
             reads=[t2b], writes=[t2b], tag="spr2")
        b.do("vector", lambda e: e.tensor_tensor(
            t2[:, :], t2[:, :], sexp[:, :], ALU.mult),
            reads=[t2b, sexpb], writes=[t2b], tag="spr3")
        for j in range(2):
            b.do("vector", lambda e, j=j: e.tensor_copy(
                vsr[0:1, j * 512:(j + 1) * 512].rearrange(
                    "p (a c) -> p a c", a=4, c=128),
                psum[VS0 + j][0:1, 0:512].rearrange(
                    "p (a c) -> p a c", a=4, c=128)),
                reads=[pb[VS0 + j]], writes=[vsrb], tag="evvs")
        for h in range(H):
            src_off = (h % 2) * 512 + (h // 2) * 64
            mm(psum[0][:, h * 64:(h + 1) * 64] if h < 8 else
               psum[1][:, (h - 8) * 64:(h - 7) * 64],
               C["ones_1x128"], vsr[0:1, src_off:src_off + 64],
               start=(h % 8 == 0), stop=(h % 8 == 7),
               reads=[cbuf, vsrb], bank=(0 if h < 8 else 1), tag="vrepmm")
        for j in range(2):
            b.do("vector", lambda e, j=j: e.tensor_copy(
                vrep[:, j * 512:(j + 1) * 512], psum[j][:, 0:512]),
                reads=[pb[j]], writes=[vrepb], tag="evvrep")
        t2v = t2.rearrange("p (a h) -> p a h", a=NTL, h=H)
        for i in range(NTL):
            for h in range(H):
                b.do("vector", lambda e, i=i, h=h: e.tensor_scalar(
                    att[:, h * 64:(h + 1) * 64], vrep[:, h * 64:(h + 1) * 64],
                    t2v[:, i, h:h + 1], None, ALU.mult),
                    reads=[vrepb, t2b], writes=[attb], tag="attm")
            b.do("vector", lambda e, i=i: e.tensor_add(
                res, att, Xt[i]), reads=[attb, Xb[i]], writes=[resb],
                tag="resC")
            ln_apply(res, [resb], Xt[i], [Xb[i]], tmp, tmpb, stc, stcb,
                     eps=1e-5, tag=f"lnC{i}")
        A.pop()

    def stage_d():
        x3r = x3_hbm.rearrange("(a p) d -> p a d", p=128)
        for i in range(0, NTL, 4):
            b.do("gpsimd", lambda e, i=i: e.dma_start(
                out=x3r[:, i:i + 4, :], in_=xh[:, i:i + 4, :]),
                reads=[Xb[j] for j in range(i, i + 4)], dma="c_x3st",
                tag="stx3")
        x3_dep = [o for o in b.q["gpsimd"] if o.tag == "stx3"]
        A.push()
        fenceD = A.fence()

        def BufD(name):
            bb = Buf(name)
            bb.readers = list(fenceD)
            return bb
        bcols = A.take(128, 16)
        bcolsb = BufD("bcols")
        brows = A.take(128, 4, D)
        browsb = BufD("brows")
        ew = A.take(128, 6, NTL)
        ewb = BufD("ew")
        rowsD = A.take(128, 8, 40)
        rowsDb = BufD("rowsD")
        erep = A.take(128, 4, 40)
        erepb = BufD("erep")
        tb8 = A.take(128, E)
        tb8b = BufD("tb8")
        idxb1 = Buf("idxb1")

        A.push()
        fence0 = A.fence()

        def nb(name):
            bb = Buf(name)
            bb.readers = list(fence0)
            return bb

        xT = A.take(128, 8, 512)
        xTb = nb("xTd")
        wsl = [(A.take(128, 512), nb(f"wsld{i}")) for i in range(2)]
        stg = [(A.take(128, 512), nb(f"stgd{i}")) for i in range(2)]

        b.do("sync", lambda e: e.dma_start(
            out=bcols[:, 0:8], in_=bq2.rearrange("o (a p) -> p (o a)", p=128),
            allow_slow_non_contiguous=True),
            writes=[bcolsb], dma="c_pre4", tag="ldbq2")
        b.do("sync", lambda e: e.dma_start(
            out=bcols[:, 8:16], in_=bk2.rearrange("o (a p) -> p (o a)", p=128),
            allow_slow_non_contiguous=True),
            writes=[bcolsb], dma="c_pre4", tag="ldbk2")
        b.do("sync", lambda e: e.dma_start(
            out=brows[0:1, 0, :], in_=bv2[0:1, :]),
            writes=[browsb], dma="c_pre4", tag="ldbv2")
        b.do("sync", lambda e: e.dma_start(
            out=brows[0:1, 1, :], in_=bo2[0:1, :]),
            writes=[browsb], dma="c_pre4", tag="ldbo2")

        for s in range(4):
            for c in range(8):
                for u in range(4):
                    bank = (c * 4 + u) % 2
                    i = s * 4 + u
                    tr(psum[bank][:, 0:128], Xt[i][:, c * 128:(c + 1) * 128],
                       [Xb[i]], bank, tag="trxd")
                    b.do("vector", lambda e, c=c, u=u, bank=bank: e.tensor_copy(
                        xT[:, c, u * 128:(u + 1) * 128], psum[bank][:, 0:128]),
                        reads=[pb[bank]], writes=[xTb], tag="evxd")
            for wi, W in enumerate((Wq2, Wk2)):
                dsth = qT_hbm if wi == 0 else kT_hbm
                for mg in range(2):
                    for c in range(8):
                        slot = c % 2
                        wap, wb_ = wsl[slot]
                        b.do("sync", lambda e, W=W, c=c, mg=mg, w=wap:
                             e.dma_start(out=w[:, :],
                                         in_=W[c * 128:(c + 1) * 128,
                                              mg * 512:(mg + 1) * 512]),
                             writes=[wb_], dma=f"c_wd{slot}", tag="ldwqk2")
                        for j in range(4):
                            mm(psum[2 + j][:, 0:512],
                               wap[:, j * 128:(j + 1) * 128], xT[:, c, :],
                               start=(c == 0), stop=(c == 7),
                               reads=[wb_, xTb], bank=2 + j, tag="qkT2")
                    for j in range(4):
                        m = mg * 4 + j
                        slot2 = j % 2
                        sap, sb_ = stg[slot2]
                        if wi == 0:
                            b.do("vector", lambda e, j=j, m=m, sap=sap:
                                 e.tensor_scalar(sap[:, :], psum[2 + j][:, 0:512],
                                                 bcols[:, m:m + 1], 0.125,
                                                 ALU.add, ALU.mult),
                                 reads=[pb[2 + j], bcolsb], writes=[sb_],
                                 tag="evq2")
                        else:
                            b.do("vector", lambda e, j=j, m=m, sap=sap:
                                 e.tensor_scalar(sap[:, :], psum[2 + j][:, 0:512],
                                                 bcols[:, 8 + m:9 + m], None,
                                                 ALU.add),
                                 reads=[pb[2 + j], bcolsb], writes=[sb_],
                                 tag="evk2")
                        b.do("gpsimd", lambda e, dsth=dsth, m=m, s=s, sap=sap:
                             e.dma_start(out=dsth[m * 128:(m + 1) * 128,
                                                  s * 512:(s + 1) * 512],
                                         in_=sap[:, :]),
                             reads=[sb_], dma=f"c_qkst{slot2}", tag="stqkT")
            for g in range(2):
                for c in range(8):
                    slot = c % 2
                    wap, wb_ = wsl[slot]
                    b.do("sync", lambda e, c=c, g=g, w=wap: e.dma_start(
                        out=w[:, :], in_=Wv2[c * 128:(c + 1) * 128,
                                             g * 512:(g + 1) * 512]),
                        writes=[wb_], dma=f"c_wd{slot}", tag="ldwv2")
                    for u in range(4):
                        mm(psum[2 + u][:, 0:512],
                           xT[:, c, u * 128:(u + 1) * 128], wap[:, :],
                           start=(c == 0), stop=False,
                           reads=[wb_, xTb], bank=2 + u, tag="vmm2")
                for u in range(4):
                    mm(psum[2 + u][:, 0:512], C["ones_1x128"],
                       brows[0:1, 0, g * 512:(g + 1) * 512], start=False,
                       stop=True, reads=[cbuf, browsb], bank=2 + u, tag="vb2")
                    slot2 = u % 2
                    sap, sb_ = stg[slot2]
                    b.do("vector", lambda e, u=u, sap=sap: e.tensor_copy(
                        sap[:, :], psum[2 + u][:, 0:512]),
                        reads=[pb[2 + u]], writes=[sb_], tag="evv2")
                    b.do("gpsimd", lambda e, s=s, u=u, g=g, sap=sap:
                         e.dma_start(out=v_hbm[s * 512 + u * 128:
                                               s * 512 + (u + 1) * 128,
                                               g * 512:(g + 1) * 512],
                                     in_=sap[:, :]),
                         reads=[sb_], dma=f"c_qkst{slot2}", tag="stv2")
        A.pop()

        A.push()
        fence1 = A.fence()

        def nb1(name):
            bb = Buf(name)
            bb.readers = list(fence1)
            return bb

        vres = A.take(128, 16, 1040)
        vresb = nb1("vres")
        qh = A.take(128, 2, 512)
        qhb = [nb1("qh0"), nb1("qh1")]
        khs = A.take(128, 2, 512)
        khsb = [nb1("khs0"), nb1("khs1")]
        PTr = A.take(128, 6, 512)
        PTrbs = [nb1(f"PTr{i}") for i in range(6)]
        OT = A.take(128, 8, 512)
        OTb = nb1("OT")
        wos = [(A.take(128, 512), nb1(f"wos{i}")) for i in range(2)]
        drow = A.take(128, 2, 512)
        drowb = nb1("drow")
        rdB = A.take(128, 2, 512)
        rdBb = nb1("rdB")
        lnr = A.take(128, 2, D)
        lnrb = nb1("lnr")
        zzq = OT.rearrange("p a c -> p (a c)").rearrange(
            "p (a c) -> p a c", a=4, c=1024)
        zzb = OTb
        tmp = rdB.rearrange("p a c -> p (a c)")
        tmpb = rdBb
        stc4 = A.take(128, 4, 8)
        stc4b = [nb1(f"stcD1_{i}") for i in range(4)]
        for j, wsrc in enumerate((ln1_g, ln1_b)):
            b.do("sync", lambda e, j=j, w=wsrc: e.dma_start(
                out=brows[0:1, 2 + j, :], in_=w[0:1, :]),
                writes=[browsb], deps=fence1, dma="c_pre5", tag="ldln1")
        ld_ln1 = [o for o in b.q["sync"] if o.tag == "ldln1"]
        for j in range(2):
            for g in range(2):
                mm(psum[6][:, 0:512], C["ones_1x128"],
                   brows[0:1, 2 + j, g * 512:(g + 1) * 512], start=True,
                   stop=True, reads=[cbuf], bank=6, tag="ln1bc")
                b.q["tensor"][-1].deps.extend(ld_ln1)
                b.do("vector", lambda e, j=j, g=g: e.tensor_copy(
                    lnr[:, j, g * 512:(g + 1) * 512], psum[6][:, 0:512]),
                    reads=[pb[6]], writes=[lnrb], tag="ln1bc_ev")
        b.do("sync", lambda e: e.dma_start(
            out=cs[64:65, 321:385], in_=ic["ones_1x128"][0:1, 0:64]),
            writes=[cbuf], dma="c_pre5b", tag="ldones64")
        ones_p64 = cs[64:65, 321:385]

        qk_dep = [o for o in b.q["gpsimd"] if o.tag in ("stqkT", "stv2")]
        vr = v_hbm.rearrange("(a p) d -> p a d", p=128)
        vre = vres.rearrange("p a (h c) -> p a h c", h=16, c=65)
        for i in range(NTL):
            b.do("sync", lambda e, i=i: e.dma_start(
                out=vre[:, i, :, 0:64], in_=vr[:, i, :].rearrange(
                    "p (h c) -> p h c", h=16, c=64),
                allow_slow_non_contiguous=True),
                writes=[vresb], deps=qk_dep, dma="c_vres", tag="ldvres")
        b.do("vector", lambda e: e.memset(vre[:, :, :, 64:65], 1.0),
             writes=[vresb], tag="vones")

        pend_tail = None
        pend_lns = []
        for s in range(4):
            for hpair in range(8):
                qs_ = hpair % 2
                b.do("sync", lambda e, s=s, hpair=hpair, qs_=qs_:
                     e.dma_start(
                         out=qh[:, qs_, :],
                         in_=qT_hbm[hpair * 128:(hpair + 1) * 128,
                                    s * 512:(s + 1) * 512]),
                     writes=[qhb[qs_]], deps=qk_dep,
                     dma=f"c_qh{qs_}", tag="ldqh")
                def ldk(c, hpair=hpair):
                    b.do("sync", lambda e, hpair=hpair, c=c:
                         e.dma_start(
                             out=khs[:, c % 2, :],
                             in_=kT_hbm[hpair * 128:(hpair + 1) * 128,
                                        c * 512:(c + 1) * 512]),
                         writes=[khsb[c % 2]], deps=qk_dep,
                         dma=f"c_kh{c % 2}", tag="ldkh")
                ldk(0)
                ldk(1)
                lq = [qh[0:64, qs_, :], qh[64:128, qs_, :]]
                for tkc in range(18):
                    if tkc >= 2:
                        t2 = tkc - 2
                        for par in range(2):
                            slot = par * 3 + t2 % 3
                            avb = 4 if par == 0 else 6
                            h2 = 2 * hpair + par
                            mm(psum[avb][0:65, 0:512],
                               vres[:, t2, h2 * 65:h2 * 65 + 65],
                               PTr[:, slot, :],
                               start=(t2 == 0), stop=(t2 == 15),
                               reads=[vresb, PTrbs[slot]], bank=avb,
                               tag="AVmm")
                    if tkc < 16:
                        if tkc in (4, 8):
                            ldk(2 + (tkc == 8))
                        ck = (tkc // 4) % 2
                        for par in range(2):
                            bank = par * 2 + tkc % 2
                            slot = par * 3 + tkc % 3
                            mm(psum[bank][:, 0:512],
                               khs[par * 64:par * 64 + 64, ck,
                                   (tkc % 4) * 128:(tkc % 4) * 128 + 128],
                               lq[par], start=True, stop=True,
                               reads=[khsb[ck], qhb[qs_]], bank=bank,
                               tag="STmm")
                            b.do("scalar", lambda e, bank=bank, slot=slot:
                                 e.activation(PTr[:, slot, :],
                                              psum[bank][:, 0:512], AF.Exp),
                                 reads=[pb[bank]], writes=[PTrbs[slot]],
                                 tag="PTev")
                    if tkc == 0 and pend_tail is not None:
                        pend_tail()
                        pend_tail = None
                    if tkc >= 2 and tkc % 2 == 0 and pend_lns:
                        pend_lns.pop(0)()

                def tail(hpair=hpair):
                    for par in range(2):
                        avb = 4 if par == 0 else 6
                        b.do("vector", lambda e, avb=avb, par=par:
                             e.reciprocal(drow[64:65, par, :],
                                          psum[avb][64:65, 0:512]),
                             reads=[pb[avb]], writes=[drowb], tag="Srd")
                        mm(psum[5][0:64, 0:512], ones_p64,
                           drow[64:65, par, :], start=True, stop=True,
                           reads=[cbuf, drowb], bank=5, tag="rdbc")
                        b.do("vector", lambda e, par=par: e.tensor_copy(
                            rdB[0:64, par, :], psum[5][0:64, 0:512]),
                            reads=[pb[5]], writes=[rdBb], tag="evrd")
                        b.do("vector",
                             lambda e, hpair=hpair, par=par, avb=avb:
                             e.tensor_tensor(
                                 OT[par * 64:par * 64 + 64, hpair, :],
                                 psum[avb][0:64, 0:512], rdB[0:64, par, :],
                                 ALU.mult),
                             reads=[pb[avb], rdBb], writes=[OTb], tag="Oev")
                pend_tail = tail
            pend_tail()
            pend_tail = None
            for g in range(2):
                for c in range(8):
                    wap, wb_ = wos[c % 2]
                    b.do("sync", lambda e, c=c, g=g, w=wap: e.dma_start(
                        out=w[:, :], in_=Wo2[c * 128:(c + 1) * 128,
                                             g * 512:(g + 1) * 512]),
                        writes=[wb_], dma=f"c_wo{c % 2}", tag="ldwo2")
                    for tqs in range(4):
                        mm(psum[tqs * 2 + g][:, 0:512],
                           OT[:, c, tqs * 128:(tqs + 1) * 128], wap[:, :],
                           start=(c == 0), stop=False,
                           reads=[wb_, OTb], bank=tqs * 2 + g, tag="prj2")
                for tqs in range(4):
                    mm(psum[tqs * 2 + g][:, 0:512], C["ones_1x128"],
                       brows[0:1, 1, g * 512:(g + 1) * 512], start=False,
                       stop=True, reads=[cbuf, browsb], bank=tqs * 2 + g,
                       tag="prj2b")
            for tqs in range(4):
                for g in range(2):
                    b.do("vector", lambda e, g=g, tqs=tqs: e.tensor_copy(
                        zzq[:, tqs, g * 512:(g + 1) * 512],
                        psum[tqs * 2 + g][:, 0:512]),
                        reads=[pb[tqs * 2 + g]], writes=[zzb], tag="evz")
            def mk_ln(s=s):
                pieces = {}
                for tqs in range(4):
                    i = s * 4 + tqs
                    zz = zzq[:, tqs, :]
                    sc, scb = stc4[:, tqs, :], stc4b[tqs]
                    ssum, ssq = sc[:, 0:1], sc[:, 1:2]
                    mean, var = sc[:, 2:3], sc[:, 3:4]
                    rstd, negmr = sc[:, 4:5], sc[:, 5:6]

                    def p1(e=None, zz=zz, ssum=ssum, ssq=ssq, scb=scb):
                        b.do("vector", lambda e: e.reduce_sum(
                            ssum, zz, axis=AX.X),
                            reads=[zzb], writes=[scb], tag="lnp1a")
                        b.do("vector", lambda e: e.scalar_tensor_tensor(
                            tmp, zz, 1.0, zz, ALU.mult, ALU.mult,
                            accum_out=ssq),
                            reads=[zzb], writes=[tmpb, scb], tag="lnp1b")

                    def p2(e=None, mean=mean, ssum=ssum, var=var, ssq=ssq,
                           rstd=rstd, scb=scb):
                        b.do("vector", lambda e: e.tensor_scalar(
                            mean, ssum, 1.0 / D, None, ALU.mult),
                            reads=[scb], writes=[scb], tag="lnp2a")
                        b.do("vector", lambda e: e.scalar_tensor_tensor(
                            var, mean, -1.0, mean, ALU.mult, ALU.mult),
                            reads=[scb], writes=[scb], tag="lnp2b")
                        b.do("vector", lambda e: e.scalar_tensor_tensor(
                            var, ssq, 1.0 / D, var, ALU.mult, ALU.add),
                            reads=[scb], writes=[scb], tag="lnp2c")
                        b.do("vector", lambda e: e.tensor_scalar(
                            var, var, 1e-5, None, ALU.add),
                            reads=[scb], writes=[scb], tag="lnp2d")
                        b.do("vector", lambda e: e.reciprocal(rstd, var),
                             reads=[scb], writes=[scb], tag="lnp2e")
                        b.do("scalar", lambda e: e.activation(
                            rstd, rstd, AF.Sqrt),
                            reads=[scb], writes=[scb], tag="lnp2f")

                    def p3(e=None, negmr=negmr, mean=mean, rstd=rstd,
                           zz=zz, scb=scb):
                        b.do("vector", lambda e: e.tensor_scalar(
                            negmr, mean, rstd, -1.0, ALU.mult, ALU.mult),
                            reads=[scb], writes=[scb], tag="lnp3a")
                        b.do("vector", lambda e: e.tensor_scalar(
                            zz, zz, rstd, negmr, ALU.mult, ALU.add),
                            reads=[zzb, scb], writes=[zzb], tag="lnp3b")
                        b.do("vector", lambda e: e.tensor_tensor(
                            zz, zz, lnr[:, 0, :], ALU.mult),
                            reads=[zzb, lnrb], writes=[zzb], tag="lnp3c")

                    def p4(e=None, zz=zz, i=i):
                        b.do("vector", lambda e: e.tensor_tensor(
                            zz, zz, lnr[:, 1, :], ALU.add),
                            reads=[zzb, lnrb], writes=[zzb], tag="lnp4a")
                        b.do("vector", lambda e: e.tensor_tensor(
                            Xt[i], Xt[i], zz, ALU.add),
                            reads=[Xb[i], zzb], writes=[Xb[i]], tag="lnp4b")
                    pieces[tqs] = [p1, p2, p3, p4]
                return [pieces[t][k] for t in range(4) for k in range(4)]
            pend_lns = mk_ln()
        for f in pend_lns:
            f()
        pend_lns = []
        A.pop()

        if d_sub in ("d1", "datt"):
            A.pop()
            return
        A.push()
        fence2 = A.fence()

        def nb2(name):
            bb = Buf(name)
            bb.readers = list(fence2)
            return bb

        xT2 = A.take(128, 8, 512)
        xT2b = nb2("xT2")
        g1s = A.take(128, 8, E)
        g1sb = nb2("g1s")
        g1br = A.take(128, E)
        g1brb = nb2("g1br")
        gate = A.take(128, 3, E)
        gateb = nb2("gateD")
        gcol = A.take(128, 8)
        gcolb = nb2("gcolD")
        postile = A.take(128, E)
        postileb = nb2("postileD")
        carry = A.take(128, E)
        carryb = nb2("carryD")
        zt2 = A.take(128, D)
        zt2b = nb2("zt2")
        tmp2 = A.take(128, D)
        tmp2b = nb2("tmp2")

        for c in range(8):
            b.do("sync", lambda e, c=c: e.dma_start(
                out=g1s[:, c, :], in_=gate1_w[c * 128:(c + 1) * 128, :]),
                writes=[g1sb], dma="c_pre6", tag="ldg1")
        b.do("sync", lambda e: e.dma_start(out=g1br[0:1, :], in_=gate1_b[0:1, :]),
             writes=[g1brb], dma="c_pre6", tag="ldg1b")
        b.do("vector", lambda e: e.memset(carry[0:1, :], 0.0), writes=[carryb])
        b.do("vector", lambda e: e.memset(zt2[:, :], 0.0), writes=[zt2b])
        NZR1 = (NSLOT1 + 128) // 128
        for i in range(NZR1):
            b.do("gpsimd", lambda e, i=i: e.dma_start(
                out=xg1_hbm[i * 128:(i + 1) * 128, :], in_=zt2[:, :]),
                reads=[zt2b], dma="c_zero1", tag="zero_xg1")

        for s in range(4):
            for c in range(8):
                for u in range(4):
                    bank = (c * 4 + u) % 2
                    i = s * 4 + u
                    tr(psum[bank][:, 0:128], Xt[i][:, c * 128:(c + 1) * 128],
                       [Xb[i]], bank, tag="trx2")
                    b.do("vector", lambda e, c=c, u=u, bank=bank: e.tensor_copy(
                        xT2[:, c, u * 128:(u + 1) * 128], psum[bank][:, 0:128]),
                        reads=[pb[bank]], writes=[xT2b], tag="evx2")
            for u in range(4):
                i = s * 4 + u
                for c in range(8):
                    mm(psum[2][:, 0:E], xT2[:, c, u * 128:(u + 1) * 128],
                       g1s[:, c, :], start=(c == 0), stop=False,
                       reads=[xT2b, g1sb], bank=2, tag="gate1")
                mm(psum[2][:, 0:E], C["ones_1x128"], g1br[0:1, :],
                   start=False, stop=True, reads=[cbuf, g1brb], bank=2,
                   tag="gate1b")
                b.do("vector", lambda e: e.reduce_max(
                    gcol[:, 0:1], psum[2][:, 0:E], axis=AX.X),
                    reads=[pb[2]], writes=[gcolb], tag="g1mx")
                b.do("vector", lambda e: e.tensor_scalar(
                    gcol[:, 1:2], gcol[:, 0:1], -1.0, None, ALU.mult),
                    reads=[gcolb], writes=[gcolb], tag="g1nm")
                b.do("scalar", lambda e: e.activation(
                    gate[:, 0, :], psum[2][:, 0:E], AF.Exp,
                    bias=gcol[:, 1:2], accum_out=gcol[:, 2:3]),
                    reads=[pb[2], gcolb], writes=[gateb, gcolb], tag="g1ex")
                b.do("vector", lambda e: e.reciprocal(
                    gcol[:, 3:4], gcol[:, 2:3]),
                    reads=[gcolb], writes=[gcolb], tag="g1rc")
                b.do("vector", lambda e: e.tensor_scalar(
                    gate[:, 0, :], gate[:, 0, :], gcol[:, 3:4], None, ALU.mult),
                    reads=[gateb, gcolb], writes=[gateb], tag="g1pr")
                for ee in range(E):
                    b.do("vector", lambda e, ee=ee: e.tensor_scalar(
                        tmp2[:, 0:E], gate[:, 0, :], gate[:, 0, ee:ee + 1],
                        None, ALU.is_gt),
                        reads=[gateb], writes=[tmp2b], tag="r1gt")
                    b.do("vector", lambda e: e.reduce_sum(
                        gcol[:, 4:5], tmp2[:, 0:E], axis=AX.X),
                        reads=[tmp2b], writes=[gcolb], tag="r1gs")
                    if ee > 0:
                        b.do("vector", lambda e, ee=ee: e.tensor_scalar(
                            tmp2[:, 0:ee], gate[:, 0, 0:ee],
                            gate[:, 0, ee:ee + 1], None, ALU.is_equal),
                            reads=[gateb], writes=[tmp2b], tag="r1eq")
                        b.do("vector", lambda e, ee=ee: e.reduce_sum(
                            gcol[:, 5:6], tmp2[:, 0:ee], axis=AX.X),
                            reads=[tmp2b], writes=[gcolb], tag="r1es")
                        b.do("vector", lambda e: e.tensor_tensor(
                            gcol[:, 4:5], gcol[:, 4:5], gcol[:, 5:6], ALU.add),
                            reads=[gcolb], writes=[gcolb], tag="r1ad")
                    b.do("vector", lambda e, ee=ee: e.tensor_scalar(
                        gate[:, 1, ee:ee + 1], gcol[:, 4:5], 0.0, None,
                        ALU.is_equal),
                        reads=[gcolb], writes=[gateb], tag="r1s0")
                    b.do("vector", lambda e, ee=ee: e.tensor_scalar(
                        gate[:, 2, ee:ee + 1], gcol[:, 4:5], 1.0, None,
                        ALU.is_equal),
                        reads=[gcolb], writes=[gateb], tag="r1s1")
                for r_, (selr, dst) in enumerate(
                        (((1), 0), ((2), 1))):
                    b.do("vector", lambda e, selr=selr: e.tensor_tensor(
                        tmp2[:, 0:E], gate[:, selr, :], C["iota8"], ALU.mult),
                        reads=[gateb, cbuf], writes=[tmp2b], tag="q1em")
                    b.do("vector", lambda e, i=i, dst=dst: e.reduce_sum(
                        ew[:, dst, i:i + 1], tmp2[:, 0:E], axis=AX.X),
                        reads=[tmp2b], writes=[ewb], tag="q1e")
                    b.do("vector", lambda e, selr=selr: e.tensor_tensor(
                        tmp2[:, 0:E], gate[:, selr, :], gate[:, 0, :],
                        ALU.mult),
                        reads=[gateb], writes=[tmp2b], tag="q1wm")
                    b.do("vector", lambda e, i=i, dst=dst: e.reduce_sum(
                        ew[:, 2 + dst, i:i + 1], tmp2[:, 0:E], axis=AX.X),
                        reads=[tmp2b], writes=[ewb], tag="q1w")
                b.do("vector", lambda e: e.tensor_tensor(
                    tmp2[:, 0:E], gate[:, 1, :], gate[:, 2, :], ALU.add),
                    reads=[gateb], writes=[tmp2b], tag="clm")
                mm(psum[3][:, 0:E], C["triuS"], tmp2[:, 0:E],
                   start=True, stop=False, reads=[cbuf, tmp2b], bank=3,
                   tag="scan1p")
                mm(psum[3][:, 0:E], C["ones_1x128"], carry[0:1, :],
                   start=False, stop=True, reads=[cbuf, carryb], bank=3,
                   tag="scan1c")
                b.do("vector", lambda e: e.tensor_copy(
                    postile[:, :], psum[3][:, 0:E]),
                    reads=[pb[3]], writes=[postileb], tag="scan1e")
                mm(psum[4][0:1, 0:E], C["ones_col"], tmp2[:, 0:E],
                   start=True, stop=True, reads=[cbuf, tmp2b], bank=4,
                   tag="cols1")
                b.do("vector", lambda e: e.tensor_tensor(
                    carry[0:1, :], carry[0:1, :], psum[4][0:1, 0:E], ALU.add),
                    reads=[pb[4]], writes=[carryb], tag="cup1")
                for r_, selr in enumerate((1, 2)):
                    b.do("vector", lambda e, selr=selr: e.tensor_tensor(
                        tmp2[:, 0:E], postile[:, :], gate[:, selr, :],
                        ALU.mult),
                        reads=[postileb, gateb], writes=[tmp2b], tag="q1pm")
                    b.do("vector", lambda e, i=i, r_=r_: e.reduce_sum(
                        ew[:, 4 + r_, i:i + 1], tmp2[:, 0:E], axis=AX.X),
                        reads=[tmp2b], writes=[ewb], tag="q1p")

        ntscr = rowsD.rearrange("p a c -> p (a c)")[0:1, 184:320].rearrange(
            "p (a c) -> p a c", a=E, c=17)
        b.do("vector", lambda e: e.tensor_tensor(
            ntscr,
            carry[0:1, :].unsqueeze(2).broadcast_to([1, E, 17]),
            C["thr17"].unsqueeze(1).broadcast_to([1, E, 17]), ALU.is_gt),
            reads=[carryb, cbuf], writes=[rowsDb], tag="ntc")
        b.do("vector", lambda e: e.reduce_sum(
            rowsD[0:1, 1, 0:E], ntscr, axis=AX.X),
            reads=[rowsDb], writes=[rowsDb], tag="ntr")
        b.do("vector", lambda e: e.memset(rowsD[0:1, 2, 0:1], 0.0),
             writes=[rowsDb])
        for ee in range(1, E):
            b.do("vector", lambda e, ee=ee: e.tensor_tensor(
                rowsD[0:1, 2, ee:ee + 1], rowsD[0:1, 2, ee - 1:ee],
                rowsD[0:1, 1, ee - 1:ee], ALU.add),
                reads=[rowsDb], writes=[rowsDb], tag="cum")
        b.do("vector", lambda e: e.tensor_tensor(
            rowsD[0:1, 3, 0:E], rowsD[0:1, 2, 0:E], rowsD[0:1, 1, 0:E],
            ALU.add),
            reads=[rowsDb], writes=[rowsDb], tag="incl")
        b.do("vector", lambda e: e.memset(rowsD[0:1, 4, :], 0.0),
             writes=[rowsDb])
        for ee in range(E):
            b.do("vector", lambda e, ee=ee: e.tensor_scalar(
                rowsD[0:1, 5, :], C["iota40"], rowsD[0:1, 3, ee:ee + 1],
                None, ALU.is_ge),
                reads=[rowsDb, cbuf], writes=[rowsDb], tag="eof1")
            b.do("vector", lambda e: e.tensor_tensor(
                rowsD[0:1, 4, :], rowsD[0:1, 4, :], rowsD[0:1, 5, :], ALU.add),
                reads=[rowsDb], writes=[rowsDb], tag="eof2")
        b.do("vector", lambda e: e.tensor_scalar(
            rowsD[0:1, 4, :], rowsD[0:1, 4, :], 7.0, None, ALU.min),
            reads=[rowsDb], writes=[rowsDb], tag="eof3")
        for r_, mul in ((5, 1024.0), (6, 4096.0), (7, 32.0)):
            b.do("vector", lambda e, r_=r_, mul=mul: e.tensor_scalar(
                rowsD[0:1, r_, :], rowsD[0:1, 4, :], mul, None, ALU.mult),
                reads=[rowsDb], writes=[rowsDb], tag="eder")
        for r_, dst in ((5, 0), (6, 1), (7, 2), (4, 3)):
            mm(psum[5][:, 0:40], C["ones_1x128"], rowsD[0:1, r_, :],
               start=True, stop=True, reads=[cbuf, rowsDb], bank=5,
               tag="erepmm")
            b.do("vector", lambda e, dst=dst: e.tensor_copy(
                erep[:, dst, :], psum[5][:, 0:40]),
                reads=[pb[5]], writes=[erepb], tag="erepev")
        b.do("vector", lambda e: e.tensor_scalar(
            rowsD[0:1, 2, 0:E], rowsD[0:1, 2, 0:E], 128.0, None, ALU.mult),
            reads=[rowsDb], writes=[rowsDb], tag="tb1")
        mm(psum[5][:, 0:E], C["ones_1x128"], rowsD[0:1, 2, 0:E],
           start=True, stop=True, reads=[cbuf, rowsDb], bank=5, tag="tbmm")
        b.do("vector", lambda e: e.tensor_copy(tb8[:, :], psum[5][:, 0:E]),
             reads=[pb[5]], writes=[tb8b], tag="tbev")
        zero_dep1 = [o for o in b.q["gpsimd"] if o.tag == "zero_xg1"]
        for i in range(NTL):
            for r_ in range(2):
                b.do("vector", lambda e, i=i, r_=r_: e.tensor_scalar(
                    gate[:, 0, :], C["iota8"], ew[:, r_, i:i + 1], None,
                    ALU.is_equal),
                    reads=[ewb, cbuf], writes=[gateb], tag="sl_eq")
                b.do("vector", lambda e: e.tensor_tensor(
                    gate[:, 1, :], gate[:, 0, :], tb8[:, :], ALU.mult),
                    reads=[gateb, tb8b], writes=[gateb], tag="sl_tb")
                b.do("vector", lambda e: e.reduce_sum(
                    gcol[:, 0:1], gate[:, 1, :], axis=AX.X),
                    reads=[gateb], writes=[gcolb], tag="sl_rs")
                b.do("vector", lambda e, i=i, r_=r_: e.tensor_tensor(
                    gcol[:, 1:2], gcol[:, 0:1], ew[:, 4 + r_, i:i + 1],
                    ALU.add),
                    reads=[gcolb, ewb], writes=[gcolb], tag="sl_add")
                b.do("vector", lambda e, i=i, r_=r_: e.tensor_copy(
                    idxs[:, 16 + r_ * 16 + i:17 + r_ * 16 + i], gcol[:, 1:2]),
                    reads=[gcolb], writes=[idxb1], tag="sl_cast")
        A.pop()

        sc_prev = [None]
        for i in range(NTL):
            for r_ in range(2):
                op = b.do("gpsimd", lambda e, i=i, r_=r_: e.indirect_dma_start(
                    out=xg1_hbm[0:NSLOT1, :],
                    out_offset=bass.IndirectOffsetOnAxis(
                        ap=idxs[:, 16 + r_ * 16 + i:17 + r_ * 16 + i], axis=0),
                    in_=Xt[i], in_offset=None),
                    reads=[idxb1, Xb[i]],
                    deps=([sc_prev[0]] if sc_prev[0] is not None else
                          zero_dep1),
                    dma="c_scat1", tag="scat1")
                sc_prev[0] = op

        A.push()
        fence3 = A.fence()

        def nb3(name):
            bb = Buf(name)
            bb.readers = list(fence3)
            return bb

        xgt = A.take(128, D)
        xgtb = nb3("xgt")
        xgT2 = A.take(128, 8, 128)
        xgT2b = nb3("xgT2")
        w1f = [(A.take(128, 4096), nb3(f"w1f{i}")) for i in range(3)]
        w2f = [(A.take(128, D), nb3(f"w2f{i}")) for i in range(4)]
        h_m2 = A.take(128, 32, 128)
        h_m2b = nb3("h_m2")
        b1f = A.take(128, 128)
        b1fb = nb3("b1f")
        b1T = A.take(128, 32)
        b1Tb = nb3("b1T")
        b2f = A.take(128, D)
        b2fb = nb3("b2f")
        ygt = A.take(128, D)
        ygtb = nb3("ygt")

        idxw = [Buf(f"idxw{i}") for i in range(4)]
        scat1_dep = [o for o in b.q["gpsimd"] if o.tag == "scat1"]
        W1v = W1_1
        W2v = W2_1
        b1v = b1_1

        for j in range(NTILE1):
            b.do("sync", lambda e, j=j: e.dma_start(
                out=xgt[:, :], in_=xg1_hbm[j * 128:(j + 1) * 128, :]),
                writes=[xgtb], deps=scat1_dep, dma="c_xg1", tag="ldxg1")
            for c in range(8):
                bank = c % 2
                tr(psum[bank][:, 0:128], xgt[:, c * 128:(c + 1) * 128],
                   [xgtb], bank, tag="trxg1")
                b.do("vector", lambda e, c=c, bank=bank: e.tensor_copy(
                    xgT2[:, c, :], psum[bank][:, 0:128]),
                    reads=[pb[bank]], writes=[xgT2b], tag="evxg1")
            base = 48 + (j % 2) * 44
            jp = j % 2
            b.do("vector", lambda e, j=j, base=base: e.tensor_scalar(
                idxs[:, base:base + 8].bitcast(F32), cidxF[:, 0:8],
                erep[:, 0, j:j + 1], None, ALU.add),
                reads=[erepb, cbuf], writes=[idxw[jp]], tag="i_w1")
            b.do("vector", lambda e, base=base: e.tensor_copy(
                idxs[:, base:base + 8],
                idxs[:, base:base + 8].bitcast(F32)),
                reads=[idxw[jp]], writes=[idxw[jp]], tag="i_w1c")
            b.do("vector", lambda e, j=j, base=base: e.tensor_scalar(
                idxs[:, base + 8:base + 40].bitcast(F32), cidxF[:, 0:32],
                erep[:, 1, j:j + 1], None, ALU.add),
                reads=[erepb, cbuf], writes=[idxw[jp]], tag="i_w2")
            b.do("vector", lambda e, base=base: e.tensor_copy(
                idxs[:, base + 8:base + 40],
                idxs[:, base + 8:base + 40].bitcast(F32)),
                reads=[idxw[jp]], writes=[idxw[jp]], tag="i_w2c")
            b.do("vector", lambda e, j=j: e.tensor_scalar(
                idxs[0:32, 168:169].bitcast(F32), cidxF[0:32, 0:1],
                erep[0:32, 2, j:j + 1], None, ALU.add),
                reads=[erepb, cbuf], writes=[idxw[2]], tag="i_b1f")
            b.do("vector", lambda e: e.tensor_copy(
                idxs[0:32, 168:169], idxs[0:32, 168:169].bitcast(F32)),
                reads=[idxw[2]], writes=[idxw[2]], tag="i_b1c")
            b.do("gpsimd", lambda e: e.indirect_dma_start(
                out=b1f[0:32, 0:128], out_offset=None,
                in_=b1v[:, :],
                in_offset=bass.IndirectOffsetOnAxis(ap=idxs[0:32, 168:169],
                                                    axis=0)),
                reads=[idxw[2]], writes=[b1fb], dma="c_b1f", tag="f_b1")
            tr(psum[2][:, 0:32], b1f[0:32, 0:128], [b1fb], 2, tag="trb1")
            b.do("vector", lambda e: e.tensor_copy(b1T[:, :], psum[2][:, 0:32]),
                 reads=[pb[2]], writes=[b1Tb], tag="evb1")
            b.do("vector", lambda e, j=j: e.tensor_copy(
                idxs[0:2, 169:170].bitcast(F32), erep[0:2, 3, j:j + 1]),
                reads=[erepb], writes=[idxw[3]], tag="i_b2f")
            b.do("vector", lambda e: e.tensor_copy(
                idxs[0:2, 169:170], idxs[0:2, 169:170].bitcast(F32)),
                reads=[idxw[3]], writes=[idxw[3]], tag="i_b2c")
            b.do("gpsimd", lambda e: e.indirect_dma_start(
                out=b2f[0:2, :], out_offset=None,
                in_=b2_1[:, :],
                in_offset=bass.IndirectOffsetOnAxis(ap=idxs[0:2, 169:170],
                                                    axis=0)),
                reads=[idxw[3]], writes=[b2fb], dma="c_b2f", tag="f_b2")
            for c in range(8):
                slot = c % 3
                wap, wb_ = w1f[slot]
                b.do("gpsimd", lambda e, c=c, base=base, w=wap:
                     e.indirect_dma_start(
                         out=w[:, :], out_offset=None,
                         in_=W1v[:, :],
                         in_offset=bass.IndirectOffsetOnAxis(
                             ap=idxs[:, base + c:base + c + 1], axis=0)),
                     reads=[idxw[jp]], writes=[wb_], dma=f"c_w1f{slot}",
                     tag="f_w1")
                for fc in range(32):
                    mm(psum[fc // 4][:, (fc % 4) * 128:(fc % 4) * 128 + 128],
                       wap[:, fc * 128:(fc + 1) * 128], xgT2[:, c, :],
                       start=(c == 0 and fc % 4 == 0),
                       stop=(c == 7 and fc % 4 == 3),
                       reads=[wb_, xgT2b], bank=fc // 4, tag="hmm1")
            for fc in range(32):
                b.do("scalar", lambda e, fc=fc: e.activation(
                    h_m2[:, fc, :],
                    psum[fc // 4][:, (fc % 4) * 128:(fc % 4) * 128 + 128],
                    AF.Gelu, bias=b1T[:, fc:fc + 1]),
                    reads=[pb[fc // 4], b1Tb], writes=[h_m2b], tag="gelu1")
            for fc in range(32):
                slot = fc % 4
                wap2, wb2_ = w2f[slot]
                b.do("gpsimd", lambda e, fc=fc, base=base, w=wap2:
                     e.indirect_dma_start(
                         out=w[:, :], out_offset=None, in_=W2v[:, :],
                         in_offset=bass.IndirectOffsetOnAxis(
                             ap=idxs[:, base + 8 + fc:base + 9 + fc],
                             axis=0)),
                     reads=[idxw[jp]], writes=[wb2_], dma=f"c_w2f{slot}",
                     tag="f_w2")
                for g in range(2):
                    mm(psum[6 + g][:, 0:512], h_m2[:, fc, :],
                       wap2[:, g * 512:(g + 1) * 512],
                       start=(fc == 0), stop=False,
                       reads=[h_m2b, wb2_], bank=6 + g, tag="yemm1")
            for g in range(2):
                mm(psum[6 + g][:, 0:512], C["ones_1x128"],
                   b2f[0:1, g * 512:(g + 1) * 512], start=False, stop=True,
                   reads=[cbuf, b2fb], bank=6 + g, tag="yeb1")
                b.do("vector", lambda e, g=g: e.tensor_copy(
                    ygt[:, g * 512:(g + 1) * 512], psum[6 + g][:, 0:512]),
                    reads=[pb[6 + g]], writes=[ygtb], tag="evye1")
            b.do("gpsimd", lambda e, j=j: e.dma_start(
                out=yg1_hbm[j * 128:(j + 1) * 128, :], in_=ygt[:, :]),
                reads=[ygtb], dma="c_yg1", tag="styg1")
        A.pop()

        A.push()
        fence4 = A.fence()

        def nb4(name):
            bb = Buf(name)
            bb.readers = list(fence4)
            return bb

        x3t = A.take(128, D)
        x3tb = nb4("x3t")
        y1 = A.take(128, D)
        y1b = nb4("y1D")
        y2 = A.take(128, D)
        y2b = nb4("y2D")
        vv2 = A.take(128, D)
        vv2b = nb4("vvD")
        tt2 = A.take(128, D)
        tt2b = nb4("ttD")
        lnr2 = A.take(128, 2, D)
        lnr2b = nb4("lnr2")
        lnrow2 = A.take(128, 2, D)
        lnrow2b = nb4("lnrow2")
        res = A.take(128, D)
        resb = nb4("resD")
        tmp3 = A.take(128, D)
        tmp3b = nb4("tmp3")
        stc3 = A.take(128, 8)
        stc3b = nb4("stc3")

        for j, wsrc in enumerate((ln2_g, ln2_b)):
            b.do("sync", lambda e, j=j, w=wsrc: e.dma_start(
                out=lnrow2[0:1, j, :], in_=w[0:1, :]),
                writes=[], deps=fence4, dma="c_pre7", tag="ldln2")
        ld_ln2 = [o for o in b.q["sync"] if o.tag == "ldln2"]
        for j in range(2):
            for g in range(2):
                mm(psum[6][:, 0:512], C["ones_1x128"],
                   lnrow2[0:1, j, g * 512:(g + 1) * 512], start=True,
                   stop=True, reads=[cbuf], bank=6, tag="ln2bc")
                b.q["tensor"][-1].deps.extend(ld_ln2)
                b.do("vector", lambda e, j=j, g=g: e.tensor_copy(
                    lnr2[:, j, g * 512:(g + 1) * 512], psum[6][:, 0:512]),
                    reads=[pb[6]], writes=[lnr2b], tag="ln2bc_ev")

        yg1_dep = [o for o in b.q["gpsimd"] if o.tag == "styg1"]
        for i in range(NTL):
            b.do("sync", lambda e, i=i: e.dma_start(
                out=x3t[:, :], in_=x3r[:, i, :]),
                writes=[x3tb], deps=x3_dep, dma="c_x3ld", tag="ldx3")
            b.do("gpsimd", lambda e, i=i: e.indirect_dma_start(
                out=y1[:, :], out_offset=None, in_=yg1_hbm[0:NSLOT1, :],
                in_offset=bass.IndirectOffsetOnAxis(
                    ap=idxs[:, 16 + i:17 + i], axis=0)),
                reads=[idxb1], writes=[y1b], deps=yg1_dep, dma="c_ga1",
                tag="ga1")
            b.do("gpsimd", lambda e, i=i: e.indirect_dma_start(
                out=y2[:, :], out_offset=None, in_=yg1_hbm[0:NSLOT1, :],
                in_offset=bass.IndirectOffsetOnAxis(
                    ap=idxs[:, 32 + i:33 + i], axis=0)),
                reads=[idxb1], writes=[y2b], deps=yg1_dep, dma="c_ga2",
                tag="ga2")
            b.do("vector", lambda e, i=i: e.tensor_scalar(
                vv2, y1, ew[:, 2, i:i + 1], None, ALU.mult),
                reads=[y1b, ewb], writes=[vv2b], tag="cmb1")
            b.do("vector", lambda e, i=i: e.scalar_tensor_tensor(
                vv2, y2, ew[:, 3, i:i + 1], vv2, ALU.mult, ALU.add),
                reads=[y2b, ewb, vv2b], writes=[vv2b], tag="cmb2")
            ln_apply(vv2, [vv2b], tt2, [tt2b], tmp3, tmp3b, stc3, stc3b,
                     eps=1e-5, tag=f"lnD2_{i}")
            b.do("vector", lambda e: e.tensor_tensor(
                tt2, tt2, lnr2[:, 0, :], ALU.mult),
                reads=[tt2b, lnr2b], writes=[tt2b], tag="ln2g")
            b.do("vector", lambda e: e.tensor_tensor(
                tt2, tt2, lnr2[:, 1, :], ALU.add),
                reads=[tt2b, lnr2b], writes=[tt2b], tag="ln2b")
            b.do("vector", lambda e, i=i: e.tensor_tensor(
                res, tt2, Xt[i], ALU.add),
                reads=[tt2b, Xb[i]], writes=[resb], tag="resD2")
            b.do("vector", lambda e: e.tensor_tensor(
                res, res, x3t, ALU.add),
                reads=[resb, x3tb], writes=[resb], tag="resD2b")
            ln_apply(res, [resb], Xt[i], [Xb[i]], tmp3, tmp3b, stc3, stc3b,
                     eps=1e-5, tag=f"lnD3_{i}")
        A.pop()
        A.pop()

    if stage_lo <= 0 <= stage_hi:
        stage_a()
    if stage_lo <= 1 <= stage_hi:
        stage_b()
    if stage_lo <= 2 <= stage_hi:
        stage_c()
    if stage_lo <= 3 <= stage_hi:
        stage_d()

    finals = []
    for i in range(0, NTL, 4):
        o = b.do("gpsimd", lambda e, i=i: e.dma_start(
            out=out_hbm.rearrange("(a p) d -> p a d", p=128)[:, i:i + 4, :],
            in_=xh[:, i:i + 4, :]),
            reads=[Xb[j] for j in range(i, i + 4)], dma="c_st", tag=f"stX{i}")
        finals.append(o)
    counts = b.finalize(finals)
    sems = {ch: st.enter_context(nc.semaphore(f"s_{ch}"))
            for ch in sorted(set(counts) | set(ENGS))}

    with nc.Block() as block:
        @block.sync
        def _(eng):
            b.emit("sync", eng, sems)

        @block.tensor
        def _(eng):
            b.emit("tensor", eng, sems)

        @block.vector
        def _(eng):
            b.emit("vector", eng, sems)

        @block.scalar
        def _(eng):
            b.emit("scalar", eng, sems)

        @block.gpsimd
        def _(eng):
            b.emit("gpsimd", eng, sems)
            eng.wait_ge(sems["c_st"], counts["c_st"])

    st.close()
    return nc


def shard_inputs(inputs, core):
    d = inputs
    r1 = lambda a: np.ascontiguousarray(a).reshape(1, -1)
    return {
        "x": np.ascontiguousarray(d["L0_attn_x"][core]),
        "W_attn": d["L0_attn_W_attn"],
        "b_attn": r1(d["L0_attn_b_attn"]),
        "W_proj": d["L0_attn_W_proj"],
        "b_proj": r1(d["L0_attn_b_proj"]),
        "rms_w": r1(d["L0_ffn_rms_w"]),
        "gate0_w": d["L0_ffn_gate_w"],
        "W1_0": np.ascontiguousarray(d["L0_ffn_W1"]).reshape(E * D, FF),
        "b1_0": np.ascontiguousarray(d["L0_ffn_b1"]).reshape(E * 32, 128),
        "W2_0": np.ascontiguousarray(d["L0_ffn_W2"]).reshape(E * FF, D),
        "b2_0": np.ascontiguousarray(d["L0_ffn_b2"]),
        "Wq1": d["L1_attn_Wq"], "Wk1": d["L1_attn_Wk"], "Wv1": d["L1_attn_Wv"],
        "Wq2": d["L1_ffn_Wq"], "Wk2": d["L1_ffn_Wk"], "Wv2": d["L1_ffn_Wv"],
        "Wo2": d["L1_ffn_Wo"],
        "bq2": r1(d["L1_ffn_bq"]), "bk2": r1(d["L1_ffn_bk"]),
        "bv2": r1(d["L1_ffn_bv"]), "bo2": r1(d["L1_ffn_bo"]),
        "ln1_g": r1(d["L1_ffn_ln1_g"]), "ln1_b": r1(d["L1_ffn_ln1_b"]),
        "ln2_g": r1(d["L1_ffn_ln2_g"]), "ln2_b": r1(d["L1_ffn_ln2_b"]),
        "gate1_w": d["L1_ffn_gate_w"], "gate1_b": r1(d["L1_ffn_gate_b"]),
        "W1_1": np.ascontiguousarray(d["L1_ffn_W1"]).reshape(E * D, FF),
        "b1_1": np.ascontiguousarray(d["L1_ffn_b1"]).reshape(E * 32, 128),
        "W2_1": np.ascontiguousarray(d["L1_ffn_W2"]).reshape(E * FF, D),
        "b2_1": np.ascontiguousarray(d["L1_ffn_b2"]),
    }


def kernel(**inputs):
    nc = build_nc()
    in_maps = [shard_inputs(inputs, c) for c in range(8)]
    res = run_bass_kernel_spmd(nc, in_maps, core_ids=list(range(8)))
    return np.stack([res.results[c]["out"] for c in range(8)])


# version 04
# speedup vs baseline: 1.5673x; 1.0317x over previous
from contextlib import ExitStack

import numpy as np
import concourse.bass as bass
import concourse.mybir as mybir
from concourse.bass_utils import run_bass_kernel_spmd

F32 = mybir.dt.float32
U32 = mybir.dt.uint32
AF = mybir.ActivationFunctionType
ALU = mybir.AluOpType
AX = mybir.AxisListType

T = 2048
D = 1024
NTL = 16
H, HD = 16, 64
FF = 4096
E = 8
CAP0 = 384
NSLOT0 = E * CAP0
NTILE1 = 40
NSLOT1 = NTILE1 * 128

ENGS = ("sync", "tensor", "vector", "scalar", "gpsimd")


class Op:
    __slots__ = ("eng", "fn", "deps", "dma", "ch", "sem_val", "tag", "seq")

    def __init__(self, eng, fn, deps, dma, ch, tag):
        self.eng, self.fn, self.deps, self.dma, self.ch, self.tag = (
            eng, fn, deps, dma, ch, tag)
        self.sem_val = None


class Buf:
    __slots__ = ("w", "readers", "name")

    def __init__(self, name=""):
        self.w = None
        self.readers = []
        self.name = name


class B:

    def __init__(self, nc):
        self.nc = nc
        self.q = {e: [] for e in ENGS}
        self.used = set()
        self.oneshot = set()
        self.seq = 0

    serial = False

    def do(self, eng, fn, reads=(), writes=(), deps=(), dma=None, tag=""):
        if dma == "c_dbg" and getattr(self, "_last_dbg", None) is not None:
            deps = list(deps) + [self._last_dbg]
        if self.serial and getattr(self, "_prev", None) is not None:
            deps = list(deps) + [self._prev]
        dd = []
        for bb in reads:
            if bb.w is not None:
                dd.append(bb.w)
        for bb in writes:
            if bb.w is not None:
                dd.append(bb.w)
            dd.extend(bb.readers)
        dd.extend(d for d in deps if d is not None)
        ch = dma if dma else eng
        op = Op(eng, fn, dd, dma is not None, ch, tag)
        op.seq = self.seq
        self.seq += 1
        self.q[eng].append(op)
        for bb in reads:
            bb.readers.append(op)
        for bb in writes:
            bb.w = op
            bb.readers = []
        if dma == "c_dbg":
            self._last_dbg = op
        self._prev = op
        return op

    def finalize(self, finals):
        for op in finals:
            self.used.add(op)
        for e in ENGS:
            for op in self.q[e]:
                kept = []
                for d in op.deps:
                    if d.eng == op.eng and not d.dma and op.eng == "tensor":
                        continue
                    if (op.dma and d.dma and d.ch == op.ch
                            and d.ch in self.oneshot):
                        continue
                    kept.append(d)
                    self.used.add(d)
                op.deps = kept
        counts = {}
        for e in ENGS:
            for op in self.q[e]:
                if op.dma or op in self.used:
                    counts[op.ch] = counts.get(op.ch, 0) + (16 if op.dma else 1)
                    op.sem_val = counts[op.ch]
        last_dma = {}
        first_use = {}
        for e in ENGS:
            for op in self.q[e]:
                if op.dma and op.ch in self.oneshot:
                    last_dma[op.ch] = max(last_dma.get(op.ch, -1), op.seq)
                for d in op.deps:
                    if d.ch in self.oneshot and d.dma:
                        first_use.setdefault(d.ch, op.seq)
                        first_use[d.ch] = min(first_use[d.ch], op.seq)
        for ch, lu in last_dma.items():
            fu = first_use.get(ch)
            assert fu is None or fu > lu, (
                f"oneshot channel {ch}: consumer at seq {fu} before last DMA "
                f"at seq {lu} -> deadlock risk")
        self.final_counts = counts
        return counts

    def emit(self, eng_name, eng, sems):
        watermark = {}
        for op in self.q[eng_name]:
            need = {}
            for d in op.deps:
                if d.sem_val is None:
                    raise RuntimeError(f"dep of {op.tag} has no sem: {d.tag}")
                v = (self.final_counts[d.ch]
                     if d.ch in self.oneshot else d.sem_val)
                if v > need.get(d.ch, 0):
                    need[d.ch] = v
            for se, v in sorted(need.items()):
                if watermark.get(se, 0) < v:
                    eng.wait_ge(sems[se], v)
                    watermark[se] = v
            inst = op.fn(eng)
            if op.sem_val is not None:
                inst.then_inc(sems[op.ch], 16 if op.dma else 1)



def build_nc(stage_lo=0, stage_hi=3, d_sub="all", serial=False):
    nc = bass.Bass("TRN2", target_bir_lowering=False, debug=False)
    st = ExitStack()

    def din(name, shape):
        return nc.dram_tensor(name, shape, F32, kind="ExternalInput").ap()

    xin = din("x", [T, D])
    W_attn = din("W_attn", [D, 3 * D])
    b_attn = din("b_attn", [1, 3 * D])
    W_proj = din("W_proj", [D, D])
    b_proj = din("b_proj", [1, D])
    rms_w = din("rms_w", [1, D])
    gate0_w = din("gate0_w", [E, D])
    W1_0 = din("W1_0", [E * D, FF])
    b1_0 = din("b1_0", [E * 32, 128])
    W2_0 = din("W2_0", [E * FF, D])
    b2_0 = din("b2_0", [E, D])
    Wq1 = din("Wq1", [HD, HD])
    Wk1 = din("Wk1", [HD, HD])
    Wv1 = din("Wv1", [HD, HD])
    Wq2 = din("Wq2", [D, D])
    Wk2 = din("Wk2", [D, D])
    Wv2 = din("Wv2", [D, D])
    Wo2 = din("Wo2", [D, D])
    bq2 = din("bq2", [1, D])
    bk2 = din("bk2", [1, D])
    bv2 = din("bv2", [1, D])
    bo2 = din("bo2", [1, D])
    ln1_g = din("ln1_g", [1, D])
    ln1_b = din("ln1_b", [1, D])
    ln2_g = din("ln2_g", [1, D])
    ln2_b = din("ln2_b", [1, D])
    gate1_w = din("gate1_w", [D, E])
    gate1_b = din("gate1_b", [1, E])
    W1_1 = din("W1_1", [E * D, FF])
    b1_1 = din("b1_1", [E * 32, 128])
    W2_1 = din("W2_1", [E * FF, D])
    b2_1 = din("b2_1", [E, D])
    out_hbm = nc.dram_tensor("out", [T, D], F32, kind="ExternalOutput").ap()

    def scratch(name, shape, **kw):
        return nc.dram_tensor(name, shape, F32, kind="Internal", **kw).ap()

    xg0_hbm = scratch("xg0", [NSLOT0 + 128, D])
    yg0_hbm = scratch("yg0", [NSLOT0 + 128, D])
    qT_hbm = scratch("qTh", [D, T])
    kT_hbm = scratch("kTh", [D, T])
    v_hbm = scratch("vh", [T, D])
    xg1_hbm = scratch("xg1", [NSLOT1 + 128, D])
    x3_hbm = scratch("x3h", [T, D])
    yg1_hbm = scratch("yg1", [NSLOT1 + 128, D])
    dbg_hbm = scratch("dbg", [2048, D])
    if stage_lo <= 2 <= stage_hi:
        cc_in = scratch("ccin", [T * H])
        cc_out = nc.dram_tensor("ccout", [8 * T * H], F32, kind="Internal",
                                addr_space="Shared").ap()

    ic = {}

    def const(name, arr):
        ic[name] = nc.inline_tensor(np.ascontiguousarray(arr, np.float32),
                                    name="c_" + name).ap()

    const("ident", np.eye(128))
    const("triu", np.triu(np.ones((128, 128))))
    const("triuS", np.triu(np.ones((128, 128)), 1))
    const("ones_col", np.ones((128, 1)))
    const("ones_1x128", np.ones((1, 128)))
    const("iota_col", np.arange(128)[:, None])
    const("iota8", np.tile(np.arange(8), (128, 1)))
    const("iota40", np.arange(40)[None, :])
    const("thr17", (np.arange(17) * 128.0)[None, :])
    const("cidx", np.arange(128)[:, None] + 128.0 * np.arange(32)[None, :])

    ARENA = 36000
    xh = st.enter_context(nc.sbuf_tensor("Xs", [128, NTL, D], F32))
    ar = st.enter_context(nc.sbuf_tensor("AR", [128, ARENA], F32))
    cs = st.enter_context(nc.sbuf_tensor("CS", [128, 580], F32))
    idxs = st.enter_context(nc.sbuf_tensor("IDX", [128, 208], U32))
    psum = [st.enter_context(nc.psum_tensor(f"PS{i}", [128, 512], F32))
            for i in range(8)]

    b = B(nc)
    b.serial = serial
    b.oneshot = {"c_pre", "c_pre2", "c_pre3", "c_pre3b", "c_pre4", "c_pre5",
                 "c_pre6", "c_pre7", "c_st", "c_zero0", "c_zero1"}
    cbuf = Buf("consts")

    C = {}
    coff = 0
    for name, width in (("ident", 128), ("triu", 128), ("triuS", 128),
                        ("ones_col", 1), ("ones_1x128", 128),
                        ("iota_col", 1), ("iota8", 8), ("iota40", 40),
                        ("thr17", 17)):
        src = ic[name]
        rows = src.shape[0]
        C[name] = cs[0:rows, coff:coff + width]
        b.do("sync", lambda e, d=C[name], s=src:
             e.dma_start(out=d, in_=s), writes=[cbuf], dma="c_pre",
             tag=f"ldc_{name}")
        coff += width
    b.do("sync", lambda e: e.dma_start(
        out=idxs[:, 136:168].bitcast(F32), in_=ic["cidx"]),
        writes=[cbuf], dma="c_pre", tag="ldc_cidx")
    cidxF = idxs[:, 136:168].bitcast(F32)

    Xb = [Buf(f"X{i}") for i in range(NTL)]
    Xt = [xh[:, i, :] for i in range(NTL)]

    x_r = xin.rearrange("(a p) d -> p a d", p=128)
    for i in range(0, NTL, 4):
        b.do("sync", lambda e, i=i: e.dma_start(
            out=xh[:, i:i + 4, :], in_=x_r[:, i:i + 4, :]),
            writes=[Xb[j] for j in range(i, i + 4)], dma="c_pre", tag=f"ldX{i}")

    class Arena:
        def __init__(self):
            self.off = 0
            self.marks = []

        def push(self):
            self.marks.append(self.off)

        def pop(self):
            self.off = self.marks.pop()

        def fence(self):
            return [b.q[e][-1] for e in ENGS if b.q[e]]

        def take(self, rows, *shape):
            n = 1
            for s in shape:
                n *= s
            assert self.off + n <= ARENA, f"arena overflow {self.off}+{n}"
            ap = ar[0:rows, self.off:self.off + n]
            self.off += n
            if len(shape) > 1:
                ap = ap.rearrange("p (a c) -> p a c", a=shape[0], c=shape[1])
            return ap

    A = Arena()
    pb = [Buf(f"ps{i}") for i in range(8)]

    def mm(ps_ap, lhsT, rhs, start, stop, reads, bank, tag="", tp=None):
        wl = [pb[bank]] if start else []
        dl = [] if start else [pb[bank].w]
        op = b.do("tensor",
                  lambda e, o=ps_ap, l=lhsT, r=rhs, s1=start, s2=stop, t=tp:
                  e.matmul(o, l, r, start=s1, stop=s2, tile_position=t),
                  reads=list(reads), writes=wl, deps=dl, tag=tag)
        if not start:
            pb[bank].w = op
        return op

    def tr(ps_ap, src_ap, reads, bank, tag="tr"):
        kk_ = src_ap.shape[0]
        return b.do("tensor",
                    lambda e, o=ps_ap, s=src_ap, k=kk_:
                    e.transpose(o, s, C["ident"][0:k, 0:k]),
                    reads=list(reads) + [cbuf], writes=[pb[bank]], tag=tag)

    def ln_apply(res_ap, res_bufs, out_ap, out_bufs, tmp, tmpb, stc, stcb,
                 eps=1e-5, tag="ln"):
        ssum, ssq = stc[:, 0:1], stc[:, 1:2]
        mean, var = stc[:, 2:3], stc[:, 3:4]
        rstd, negmr = stc[:, 4:5], stc[:, 5:6]
        b.do("vector", lambda e: e.reduce_sum(ssum, res_ap, axis=AX.X),
             reads=list(res_bufs), writes=[stcb], tag=tag + "s")
        b.do("vector", lambda e: e.scalar_tensor_tensor(
            tmp, res_ap, 1.0, res_ap, ALU.mult, ALU.mult, accum_out=ssq),
            reads=list(res_bufs), writes=[tmpb, stcb], tag=tag + "q")
        b.do("vector", lambda e: e.tensor_scalar(
            mean, ssum, 1.0 / D, None, ALU.mult),
            reads=[stcb], writes=[stcb], tag=tag + "m")
        b.do("vector", lambda e: e.scalar_tensor_tensor(
            var, mean, -1.0, mean, ALU.mult, ALU.mult),
            reads=[stcb], writes=[stcb], tag=tag + "mm")
        b.do("vector", lambda e: e.scalar_tensor_tensor(
            var, ssq, 1.0 / D, var, ALU.mult, ALU.add),
            reads=[stcb], writes=[stcb], tag=tag + "v")
        b.do("vector", lambda e: e.tensor_scalar(
            var, var, eps, None, ALU.add),
            reads=[stcb], writes=[stcb], tag=tag + "e")
        b.do("vector", lambda e: e.reciprocal(rstd, var),
             reads=[stcb], writes=[stcb], tag=tag + "r")
        b.do("scalar", lambda e: e.activation(rstd, rstd, AF.Sqrt),
             reads=[stcb], writes=[stcb], tag=tag + "sq")
        b.do("vector", lambda e: e.tensor_scalar(
            negmr, mean, rstd, -1.0, ALU.mult, ALU.mult),
            reads=[stcb], writes=[stcb], tag=tag + "n")
        return b.do("vector", lambda e: e.tensor_scalar(
            out_ap, res_ap, rstd, negmr, ALU.mult, ALU.add),
            reads=list(res_bufs) + [stcb], writes=list(out_bufs), tag=tag + "a")

    def feat_map(ap_, bufs, scr, scrb, tag="fm"):
        b.do("vector", lambda e: e.tensor_scalar(
            scr, ap_, 0.0, None, ALU.min),
            reads=list(bufs), writes=[scrb], tag=tag + "1")
        b.do("scalar", lambda e: e.activation(scr, scr, AF.Exp),
             reads=[scrb], writes=[scrb], tag=tag + "2")
        b.do("vector", lambda e: e.scalar_tensor_tensor(
            ap_, ap_, 0.0, scr, ALU.max, ALU.add),
            reads=list(bufs) + [scrb], writes=list(bufs), tag=tag + "3")

    def stage_a():
        A.push()
        fenceA = A.fence()

        def Buf_(name):
            bb = Buf(name)
            bb.readers = list(fenceA)
            return bb
        SL, NCH = 256, 2
        NSL = T // SL
        xT = A.take(128, 8, SL)
        xTb = Buf_("xT")
        qT = A.take(128, 8, SL)
        qTb = Buf_("qT")
        kT = A.take(128, 8, SL)
        kTb = Buf_("kT")
        kk = A.take(128, NCH, D)
        kkb = Buf_("kk")
        vv = A.take(128, NCH, D)
        vvb = Buf_("vv")
        yy = A.take(128, NCH, D)
        yyb = Buf_("yy")
        yT = A.take(128, 8, SL)
        yTb = Buf_("yT")
        amt = A.take(128, 16, 128)
        amtb = Buf_("amt")
        fmscr = A.take(128, 8, SL)
        fmscrb = Buf_("fmscr")
        wqk = [(A.take(128, 512), Buf(f"wqk{i}")) for i in range(4)]
        wkv = [(A.take(128, 512), Buf(f"wkv{i}")) for i in range(2)]
        wpr = [(A.take(128, 512), Buf(f"wpr{i}")) for i in range(2)]
        kvsb = A.take(128, 520)
        kvsbb = Buf_("kvsb")
        rden = A.take(128, 16)
        rdenb = Buf_("rden")
        battc = A.take(128, 24)
        battcb = Buf_("battc")
        battr = A.take(128, 3072)
        battrb = Buf_("battr")
        bprjr = A.take(128, D)
        bprjrb = Buf_("bprjr")
        tmp = A.take(128, D)
        tmpb = Buf_("tmpA")
        res = A.take(128, D)
        resb = Buf_("resA")
        stc = A.take(128, 8)
        stcb = Buf_("stcA")

        b.do("sync", lambda e: e.dma_start(
            out=battc[:, :], in_=b_attn.rearrange("o (a p) -> p (o a)", p=128),
            allow_slow_non_contiguous=True),
            writes=[battcb], dma="c_pre", tag="ld_battc")
        b.do("sync", lambda e: e.dma_start(
            out=battr[0:1, :], in_=b_attn[0:1, :]),
            writes=[battrb], dma="c_pre", tag="ld_battr")
        b.do("sync", lambda e: e.dma_start(
            out=bprjr[0:1, :], in_=b_proj[0:1, :]),
            writes=[bprjrb], dma="c_pre", tag="ld_bprj")

        b.do("vector", lambda e: e.memset(kvsb[:, :], 0.0), writes=[kvsbb])

        KVE, KVO, KS = 5, 6, 7

        for s in range(NSL):
            t0 = s * SL
            tiles = [t0 // 128, t0 // 128 + 1]
            for c in range(8):
                for u in range(NCH):
                    bank = (c * NCH + u) % 2
                    i = tiles[u]
                    tr(psum[bank][:, 0:128], Xt[i][:, c * 128:(c + 1) * 128],
                       [Xb[i]], bank, tag=f"trx{s}")
                    b.do("vector", lambda e, c=c, u=u, bank=bank: e.tensor_copy(
                        xT[:, c, u * 128:(u + 1) * 128], psum[bank][:, 0:128]),
                        reads=[pb[bank]], writes=[xTb], tag="evx")
            for mg in range(4):
                for c in range(8):
                    slot = c % 4
                    wap, wb_ = wqk[slot]
                    b.do("sync", lambda e, c=c, mg=mg, w=wap: e.dma_start(
                        out=w[:, :],
                        in_=W_attn[c * 128:(c + 1) * 128,
                                   mg * 512:(mg + 1) * 512]),
                        writes=[wb_], dma=f"c_wqk{slot}", tag="ldwqk")
                    for j in range(4):
                        mm(psum[j][:, 0:SL], wap[:, j * 128:(j + 1) * 128],
                           xT[:, c, :], start=(c == 0), stop=(c == 7),
                           reads=[wb_, xTb], bank=j, tag=f"qkT{mg}{j}")
                for j in range(4):
                    m = mg * 4 + j
                    dst = qT if m < 8 else kT
                    dstb = qTb if m < 8 else kTb
                    ch = m % 8
                    b.do("vector", lambda e, d=dst, ch=ch, j=j, m=m:
                         e.tensor_scalar(d[:, ch, :], psum[j][:, 0:SL],
                                         battc[:, m:m + 1], None, ALU.add),
                         reads=[pb[j], battcb], writes=[dstb], tag="evqkT")
            for g in range(4):
                col0 = 1024 + 512 * g
                dst, dstb = (kk, kkb) if g < 2 else (vv, vvb)
                go = g % 2
                for c in range(8):
                    slot = c % 2
                    wap, wb_ = wkv[slot]
                    b.do("sync", lambda e, c=c, col0=col0, w=wap: e.dma_start(
                        out=w[:, :],
                        in_=W_attn[c * 128:(c + 1) * 128, col0:col0 + 512]),
                        writes=[wb_], dma=f"c_wkv{slot}", tag="ldwkv")
                    for u in range(NCH):
                        mm(psum[u][:, 0:512],
                           xT[:, c, u * 128:(u + 1) * 128], wap[:, :],
                           start=(c == 0), stop=False,
                           reads=[wb_, xTb], bank=u, tag=f"kv{g}")
                for u in range(NCH):
                    mm(psum[u][:, 0:512], C["ones_1x128"],
                       battr[0:1, col0:col0 + 512], start=False, stop=True,
                       reads=[cbuf, battrb], bank=u, tag="kvb")
                    b.do("scalar", lambda e, d=dst, u=u, go=go: e.activation(
                        d[:, u, go * 512:(go + 1) * 512], psum[u][:, 0:512],
                        AF.Copy),
                        reads=[pb[u]], writes=[dstb], tag="evkv")
            feat_map(qT[:, :, :], [qTb], fmscr[:, :, :], fmscrb, tag="fmq")
            feat_map(kT[:, :, :], [kTb], fmscr[:, :, :], fmscrb, tag="fmk")
            feat_map(kk.rearrange("p a c -> p (a c)"), [kkb],
                     fmscr.rearrange("p a c -> p (a c)"), fmscrb, tag="fmkk")
            for u in range(NCH):
                ci = s * NCH + u
                for p in range(8):
                    kpair = kk[:, u, p * 128:(p + 1) * 128]
                    vpair = vv[:, u, p * 128:(p + 1) * 128]
                    kvbank = KVE if p < 4 else KVO
                    cc0 = (p % 4) * 128
                    mm(psum[kvbank][:, cc0:cc0 + 128], kpair, vpair,
                       start=(p % 4 == 0), stop=(p % 4 == 3),
                       reads=[kkb, vvb], bank=kvbank, tag="kvup")
                    mm(psum[KS][:, p:p + 1], kpair, C["ones_col"],
                       start=(p == 0), stop=(p == 7),
                       reads=[kkb, cbuf], bank=KS, tag="ksup")
                for h in range(H):
                    bank = h % 2
                    hp = (h % 2) * 64
                    hc = h // 2
                    lk = kT[hp:hp + 64, hc, u * 128:(u + 1) * 128]
                    lq = qT[hp:hp + 64, hc, u * 128:(u + 1) * 128]
                    mm(psum[bank][:, 0:128], lk, lq, start=True, stop=True,
                       reads=[kTb, qTb], bank=bank, tag="AT")
                    b.do("vector", lambda e, h=h, bank=bank: e.tensor_tensor(
                        amt[:, h, :], psum[bank][:, 0:128], C["triu"],
                        ALU.mult),
                        reads=[pb[bank], cbuf], writes=[amtb], tag="mask")
                for h in range(H):
                    hp = (h % 2) * 64
                    hc = h // 2
                    nbank = 2 + (h % 2)
                    ncol = (h // 2) * 64
                    lq = qT[hp:hp + 64, hc, u * 128:(u + 1) * 128]
                    mm(psum[nbank][:, ncol:ncol + 64], amt[:, h, :],
                       vv[:, u, h * 64:(h + 1) * 64], start=(h < 2),
                       stop=False, reads=[amtb, vvb], bank=nbank, tag="numi")
                    mm(psum[nbank][:, ncol:ncol + 64], lq,
                       kvsb[hp:hp + 64, (h // 2) * 64:(h // 2) * 64 + 64],
                       start=False, stop=(h >= H - 2),
                       reads=[qTb, kvsbb], bank=nbank, tag="nume")
                    mm(psum[4][:, h:h + 1], amt[:, h, :], C["ones_col"],
                       start=(h == 0), stop=False,
                       reads=[amtb, cbuf], bank=4, tag="dena")
                    mm(psum[4][:, h:h + 1], lq,
                       kvsb[hp:hp + 64, 512 + h // 2:513 + h // 2],
                       start=False, stop=(h == H - 1),
                       reads=[qTb, kvsbb], bank=4, tag="denb")
                b.do("vector", lambda e: e.tensor_scalar(
                    rden[:, 0:16], psum[4][:, 0:16], 1e-6, None, ALU.add),
                    reads=[pb[4]], writes=[rdenb], tag="deps")
                b.do("vector", lambda e: e.reciprocal(rden[:, 0:16],
                                                      rden[:, 0:16]),
                     reads=[rdenb], writes=[rdenb], tag="drcp")
                for h in range(H):
                    nbank = 2 + (h % 2)
                    ncol = (h // 2) * 64
                    b.do("vector", lambda e, h=h, u=u, nbank=nbank, ncol=ncol:
                         e.tensor_scalar(yy[:, u, h * 64:(h + 1) * 64],
                                         psum[nbank][:, ncol:ncol + 64],
                                         rden[:, h:h + 1], None, ALU.mult),
                         reads=[pb[nbank], rdenb], writes=[yyb], tag="yev")
                for bank, p0 in ((KVE, 0), (KVO, 4)):
                    dvE = psum[bank][0:64, 0:512].rearrange(
                        "p (a c) -> p a c", a=4, c=128)[:, :, 0:64]
                    dvO = psum[bank][64:128, 0:512].rearrange(
                        "p (a c) -> p a c", a=4, c=128)[:, :, 64:128]
                    tgtE = kvsb[0:64, p0 * 64:(p0 + 4) * 64].rearrange(
                        "p (a c) -> p a c", a=4, c=64)
                    tgtO = kvsb[64:128, p0 * 64:(p0 + 4) * 64].rearrange(
                        "p (a c) -> p a c", a=4, c=64)
                    b.do("vector", lambda e, t=tgtE, dd=dvE: e.tensor_tensor(
                        t, t, dd, ALU.add),
                        reads=[pb[bank]], writes=[kvsbb], tag="addKVe")
                    b.do("vector", lambda e, t=tgtO, dd=dvO: e.tensor_tensor(
                        t, t, dd, ALU.add),
                        reads=[pb[bank]], writes=[kvsbb], tag="addKVo")
                b.do("vector", lambda e: e.tensor_tensor(
                    kvsb[:, 512:520], kvsb[:, 512:520], psum[KS][:, 0:8],
                    ALU.add),
                    reads=[pb[KS]], writes=[kvsbb], tag="addKS")
            for c in range(8):
                for u in range(NCH):
                    bank = (c * NCH + u) % 2
                    tr(psum[bank][:, 0:128], yy[:, u, c * 128:(c + 1) * 128],
                       [yyb], bank, tag="try")
                    b.do("vector", lambda e, c=c, u=u, bank=bank: e.tensor_copy(
                        yT[:, c, u * 128:(u + 1) * 128], psum[bank][:, 0:128]),
                        reads=[pb[bank]], writes=[yTb], tag="evy")
            for g in range(2):
                for c in range(8):
                    slot = c % 2
                    wap, wb_ = wpr[slot]
                    b.do("sync", lambda e, c=c, g=g, w=wap: e.dma_start(
                        out=w[:, :],
                        in_=W_proj[c * 128:(c + 1) * 128,
                                   g * 512:(g + 1) * 512]),
                        writes=[wb_], dma=f"c_wpr{slot}", tag="ldwpr")
                    for u in range(NCH):
                        mm(psum[g * 2 + u][:, 0:512],
                           yT[:, c, u * 128:(u + 1) * 128], wap[:, :],
                           start=(c == 0), stop=False,
                           reads=[wb_, yTb], bank=g * 2 + u, tag="proj")
                for u in range(NCH):
                    mm(psum[g * 2 + u][:, 0:512], C["ones_1x128"],
                       bprjr[0:1, g * 512:(g + 1) * 512], start=False,
                       stop=True, reads=[cbuf, bprjrb], bank=g * 2 + u,
                       tag="projb")
            for u in range(NCH):
                i = tiles[u]
                for g in range(2):
                    b.do("vector", lambda e, i=i, g=g, u=u: e.tensor_tensor(
                        res[:, g * 512:(g + 1) * 512],
                        psum[g * 2 + u][:, 0:512],
                        Xt[i][:, g * 512:(g + 1) * 512], ALU.add),
                        reads=[pb[g * 2 + u], Xb[i]], writes=[resb],
                        tag="resadd")
                ln_apply(res[:, :], [resb], Xt[i], [Xb[i]], tmp[:, :], tmpb,
                         stc, stcb, eps=1e-5, tag=f"lnA{i}")
        A.pop()

    def stage_b():
        A.push()
        fenceB = A.fence()

        def BufB(name):
            bb = Buf(name)
            bb.readers = list(fenceB)
            return bb
        xn_sl = A.take(128, 4, D)
        xn_slb = BufB("xn_sl")
        xnT = A.take(128, 8, 512)
        xnTb = BufB("xnT")
        rms_rep = A.take(128, D)
        rms_repb = BufB("rms_rep")
        g0T = A.take(128, 8, E)
        g0Tb = BufB("g0T")
        gate = A.take(128, 2, E)
        gateb = BufB("gate")
        gcol = A.take(128, 8)
        gcolb = BufB("gcol")
        score = A.take(128, NTL)
        scoreb = BufB("score")
        slotf = A.take(128, NTL)
        slotfb = BufB("slotf")
        postile = A.take(128, E)
        postileb = BufB("postile")
        carry = A.take(128, E)
        carryb = BufB("carry")
        zt = A.take(128, D)
        ztb = BufB("zt")
        xg_sl = A.take(128, 3, D)
        xg_slb = BufB("xg_sl")
        xgT = A.take(128, 8, CAP0)
        xgTb = BufB("xgT")
        y_sl = A.take(128, 3, D)
        y_slb = BufB("y_sl")
        w1s = [(A.take(128, 512), BufB(f"w1s{i}")) for i in range(8)]
        w2s = [(A.take(128, D), BufB(f"w2s{i}")) for i in range(4)]
        h_m = A.take(128, 2, CAP0)
        h_mb = BufB("h_m")
        b1c = A.take(128, 32)
        b1cb = BufB("b1c")
        b2r = A.take(128, D)
        b2rb = BufB("b2r")
        y1 = A.take(128, D)
        y1b = BufB("y1")
        tmp = A.take(128, D)
        tmpb = BufB("tmpB")
        res = A.take(128, D)
        resb = BufB("resB")
        stc = A.take(128, 8)
        stcb = BufB("stcB")

        idxb = BufB("idx")

        g0r = gate0_w.rearrange("e (c p) -> p c e", p=128)
        for c in range(8):
            b.do("sync", lambda e, c=c: e.dma_start(
                out=g0T[:, c, :], in_=g0r[:, c, :],
                allow_slow_non_contiguous=True),
                writes=[g0Tb], dma="c_pre2", tag="ld_g0T")
        rmsrow = A.take(128, D)
        rmsrowb = BufB("rmsrow")
        b.do("sync", lambda e: e.dma_start(out=rmsrow[0:1, :], in_=rms_w[0:1, :]),
             writes=[rmsrowb], dma="c_pre2", tag="ld_rms")
        for g in range(2):
            mm(psum[6][:, 0:512], C["ones_1x128"], rmsrow[0:1, g * 512:(g + 1) * 512],
               start=True, stop=True, reads=[cbuf, rmsrowb], bank=6, tag="rmsbc")
            b.do("vector", lambda e, g=g: e.tensor_copy(
                rms_rep[:, g * 512:(g + 1) * 512], psum[6][:, 0:512]),
                reads=[pb[6]], writes=[rms_repb], tag="rmsbc_ev")
        b.do("vector", lambda e: e.memset(zt[:, :], 0.0), writes=[ztb])
        NZR = (NSLOT0 + 128) // 128
        for i in range(NZR):
            b.do("gpsimd", lambda e, i=i: e.dma_start(
                out=xg0_hbm[i * 128:(i + 1) * 128, :], in_=zt[:, :]),
                reads=[ztb], dma="c_zero0", tag="zero_xg0")
        b.do("vector", lambda e: e.memset(carry[0:1, :], 0.0), writes=[carryb])

        for s in range(4):
            for u in range(4):
                i = s * 4 + u
                b.do("vector", lambda e, i=i: e.scalar_tensor_tensor(
                    tmp, Xt[i], 1.0, Xt[i], ALU.mult, ALU.mult,
                    accum_out=stc[:, 0:1]),
                    reads=[Xb[i]], writes=[tmpb, stcb], tag="rms_sq")
                b.do("vector", lambda e: e.tensor_scalar(
                    stc[:, 1:2], stc[:, 0:1], 1.0 / D, 1e-6, ALU.mult, ALU.add),
                    reads=[stcb], writes=[stcb], tag="rms_ms")
                b.do("vector", lambda e: e.reciprocal(stc[:, 1:2], stc[:, 1:2]),
                     reads=[stcb], writes=[stcb], tag="rms_rc")
                b.do("scalar", lambda e: e.activation(
                    stc[:, 1:2], stc[:, 1:2], AF.Sqrt),
                    reads=[stcb], writes=[stcb], tag="rms_sq2")
                b.do("scalar", lambda e, i=i, u=u: e.activation(
                    xn_sl[:, u, :], Xt[i], AF.Copy, scale=stc[:, 1:2]),
                    reads=[Xb[i], stcb], writes=[xn_slb], tag="rms_ap")
                b.do("vector", lambda e, u=u: e.tensor_tensor(
                    xn_sl[:, u, :], xn_sl[:, u, :], rms_rep[:, :], ALU.mult),
                    reads=[xn_slb, rms_repb], writes=[xn_slb], tag="rms_w")
            for c in range(8):
                for u in range(4):
                    bank = (c * 4 + u) % 2
                    tr(psum[bank][:, 0:128], xn_sl[:, u, c * 128:(c + 1) * 128],
                       [xn_slb], bank, tag="trxn")
                    b.do("vector", lambda e, c=c, u=u, bank=bank: e.tensor_copy(
                        xnT[:, c, u * 128:(u + 1) * 128], psum[bank][:, 0:128]),
                        reads=[pb[bank]], writes=[xnTb], tag="evxn")
            for u in range(4):
                i = s * 4 + u
                for c in range(8):
                    mm(psum[2][:, 0:E], xnT[:, c, u * 128:(u + 1) * 128],
                       g0T[:, c, :], start=(c == 0), stop=(c == 7),
                       reads=[xnTb, g0Tb], bank=2, tag="gate")
                b.do("vector", lambda e: e.reduce_max(
                    gcol[:, 0:1], psum[2][:, 0:E], axis=AX.X),
                    reads=[pb[2]], writes=[gcolb], tag="g_mx")
                b.do("vector", lambda e: e.tensor_scalar(
                    gcol[:, 1:2], gcol[:, 0:1], -1.0, None, ALU.mult),
                    reads=[gcolb], writes=[gcolb], tag="g_nmx")
                b.do("scalar", lambda e: e.activation(
                    gate[:, 0, :], psum[2][:, 0:E], AF.Exp,
                    bias=gcol[:, 1:2], accum_out=gcol[:, 2:3]),
                    reads=[pb[2], gcolb], writes=[gateb, gcolb], tag="g_exp")
                b.do("vector", lambda e: e.reciprocal(
                    gcol[:, 3:4], gcol[:, 2:3]),
                    reads=[gcolb], writes=[gcolb], tag="g_rs")
                b.do("vector", lambda e: e.tensor_scalar(
                    gate[:, 0, :], gate[:, 0, :], gcol[:, 3:4], None, ALU.mult),
                    reads=[gateb, gcolb], writes=[gateb], tag="g_pr")
                for ee in range(E):
                    b.do("vector", lambda e, ee=ee: e.tensor_scalar(
                        tmp[:, 0:E], gate[:, 0, :], gate[:, 0, ee:ee + 1],
                        None, ALU.is_gt),
                        reads=[gateb], writes=[tmpb], tag="r_gt")
                    b.do("vector", lambda e: e.reduce_sum(
                        gcol[:, 4:5], tmp[:, 0:E], axis=AX.X),
                        reads=[tmpb], writes=[gcolb], tag="r_gts")
                    if ee > 0:
                        b.do("vector", lambda e, ee=ee: e.tensor_scalar(
                            tmp[:, 0:ee], gate[:, 0, 0:ee],
                            gate[:, 0, ee:ee + 1], None, ALU.is_equal),
                            reads=[gateb], writes=[tmpb], tag="r_eq")
                        b.do("vector", lambda e, ee=ee: e.reduce_sum(
                            gcol[:, 5:6], tmp[:, 0:ee], axis=AX.X),
                            reads=[tmpb], writes=[gcolb], tag="r_eqs")
                        b.do("vector", lambda e: e.tensor_tensor(
                            gcol[:, 4:5], gcol[:, 4:5], gcol[:, 5:6], ALU.add),
                            reads=[gcolb], writes=[gcolb], tag="r_add")
                    b.do("vector", lambda e, ee=ee: e.tensor_scalar(
                        gate[:, 1, ee:ee + 1], gcol[:, 4:5], 0.0, None,
                        ALU.is_equal),
                        reads=[gcolb], writes=[gateb], tag="r_sel")
                b.do("vector", lambda e: e.tensor_tensor(
                    tmp[:, 0:E], gate[:, 1, :], C["iota8"], ALU.mult),
                    reads=[gateb, cbuf], writes=[tmpb], tag="q_em")
                b.do("vector", lambda e: e.reduce_sum(
                    gcol[:, 6:7], tmp[:, 0:E], axis=AX.X),
                    reads=[tmpb], writes=[gcolb], tag="q_e")
                b.do("vector", lambda e: e.tensor_tensor(
                    tmp[:, 0:E], gate[:, 1, :], gate[:, 0, :], ALU.mult),
                    reads=[gateb], writes=[tmpb], tag="q_sm")
                b.do("vector", lambda e, i=i: e.reduce_sum(
                    score[:, i:i + 1], tmp[:, 0:E], axis=AX.X),
                    reads=[tmpb], writes=[scoreb], tag="q_s")
                mm(psum[3][:, 0:E], C["triuS"], gate[:, 1, :],
                   start=True, stop=False, reads=[cbuf, gateb], bank=3,
                   tag="scanp")
                mm(psum[3][:, 0:E], C["ones_1x128"], carry[0:1, :],
                   start=False, stop=True, reads=[cbuf, carryb], bank=3,
                   tag="scanc")
                b.do("vector", lambda e: e.tensor_copy(
                    postile[:, :], psum[3][:, 0:E]),
                    reads=[pb[3]], writes=[postileb], tag="scanev")
                mm(psum[4][0:1, 0:E], C["ones_col"], gate[:, 1, :],
                   start=True, stop=True, reads=[cbuf, gateb], bank=4,
                   tag="colsum")
                b.do("vector", lambda e: e.tensor_tensor(
                    carry[0:1, :], carry[0:1, :], psum[4][0:1, 0:E], ALU.add),
                    reads=[pb[4]], writes=[carryb], tag="carryup")
                b.do("vector", lambda e: e.tensor_tensor(
                    tmp[:, 0:E], postile[:, :], gate[:, 1, :], ALU.mult),
                    reads=[postileb, gateb], writes=[tmpb], tag="q_pm")
                b.do("vector", lambda e: e.reduce_sum(
                    gcol[:, 7:8], tmp[:, 0:E], axis=AX.X),
                    reads=[tmpb], writes=[gcolb], tag="q_p")
                b.do("vector", lambda e, i=i: e.tensor_scalar(
                    slotf[:, i:i + 1], gcol[:, 6:7], float(CAP0),
                    None, ALU.mult),
                    reads=[gcolb], writes=[slotfb], tag="q_sl1")
                b.do("vector", lambda e, i=i: e.tensor_tensor(
                    slotf[:, i:i + 1], slotf[:, i:i + 1], gcol[:, 7:8],
                    ALU.add),
                    reads=[slotfb, gcolb], writes=[slotfb], tag="q_sl2")
                b.do("vector", lambda e, i=i: e.tensor_copy(
                    idxs[:, i:i + 1], slotf[:, i:i + 1]),
                    reads=[slotfb], writes=[idxb], tag="q_cast")
                b.do("gpsimd", lambda e, i=i, u=u: e.indirect_dma_start(
                    out=xg0_hbm[0:NSLOT0, :],
                    out_offset=bass.IndirectOffsetOnAxis(
                        ap=idxs[:, i:i + 1], axis=0),
                    in_=xn_sl[:, u, :], in_offset=None),
                    reads=[idxb, xn_slb], deps=[b.q["gpsimd"][-1]]
                    if b.q["gpsimd"] else [],
                    dma="c_scat0", tag="scat0")

        for ee in range(E):
            b.do("sync", lambda e, ee=ee: e.dma_start(
                out=xg_sl[:, :, :],
                in_=xg0_hbm[ee * CAP0:(ee + 1) * CAP0, :].rearrange(
                    "(a p) d -> p a d", p=128)),
                reads=[], writes=[xg_slb],
                deps=[o for o in b.q["gpsimd"] if o.tag == "scat0"],
                dma="c_xg0", tag="ldxg")
            b.do("sync", lambda e, ee=ee: e.dma_start(
                out=b1c[:, :],
                in_=b1_0[ee * 32:(ee + 1) * 32, :].rearrange("a p -> p a"),
                allow_slow_non_contiguous=True),
                writes=[b1cb], dma="c_b1", tag="ldb1")
            b.do("sync", lambda e, ee=ee: e.dma_start(
                out=b2r[0:1, :], in_=b2_0[ee:ee + 1, :]),
                writes=[b2rb], dma="c_b2", tag="ldb2")
            for c in range(8):
                for uu in range(3):
                    bank = (c * 3 + uu) % 2
                    tr(psum[bank][:, 0:128], xg_sl[:, uu, c * 128:(c + 1) * 128],
                       [xg_slb], bank, tag="trxg")
                    b.do("vector", lambda e, c=c, uu=uu, bank=bank:
                         e.tensor_copy(xgT[:, c, uu * 128:(uu + 1) * 128],
                                       psum[bank][:, 0:128]),
                         reads=[pb[bank]], writes=[xgTb], tag="evxg")
            for fgp in range(8):
                for sub in range(2):
                    fg = fgp * 2 + sub
                    for c in range(8):
                        wap, wb_ = w1s[c]
                        if sub == 0:
                            b.do("sync", lambda e, ee=ee, fgp=fgp, c=c, w=wap:
                                 e.dma_start(
                                     out=w[:, :],
                                     in_=W1_0[ee * 1024 + c * 128:
                                              ee * 1024 + (c + 1) * 128,
                                              fgp * 512:(fgp + 1) * 512]),
                                 writes=[wb_], dma=f"c_w1{c % 4}",
                                 tag="ldw1")
                        for j in range(2):
                            mm(psum[j][:, 0:CAP0],
                               wap[:, sub * 256 + j * 128:
                                   sub * 256 + (j + 1) * 128],
                               xgT[:, c, :], start=(c == 0), stop=(c == 7),
                               reads=[wb_, xgTb], bank=j, tag="hmm")
                    for j in range(2):
                        fc = fg * 2 + j
                        b.do("scalar", lambda e, j=j, fc=fc: e.activation(
                            h_m[:, j, :], psum[j][:, 0:CAP0], AF.Silu,
                            bias=b1c[:, fc:fc + 1]),
                            reads=[pb[j], b1cb], writes=[h_mb], tag="silu")
                        wap2, wb2_ = w2s[fc % 4]
                        b.do("sync", lambda e, ee=ee, fc=fc, w=wap2:
                             e.dma_start(
                                 out=w[:, :],
                                 in_=W2_0[ee * 4096 + fc * 128:
                                          ee * 4096 + (fc + 1) * 128, :]),
                             writes=[wb2_], dma=f"c_w2{fc % 4}", tag="ldw2")
                        for uu in range(3):
                            for g in range(2):
                                mm(psum[2 + uu * 2 + g][:, 0:512],
                                   h_m[:, j, uu * 128:(uu + 1) * 128],
                                   wap2[:, g * 512:(g + 1) * 512],
                                   start=(fc == 0), stop=False,
                                   reads=[h_mb, wb2_], bank=2 + uu * 2 + g,
                                   tag="yemm")
            for uu in range(3):
                for g in range(2):
                    mm(psum[2 + uu * 2 + g][:, 0:512], C["ones_1x128"],
                       b2r[0:1, g * 512:(g + 1) * 512], start=False, stop=True,
                       reads=[cbuf, b2rb], bank=2 + uu * 2 + g, tag="yeb")
                    b.do("vector", lambda e, uu=uu, g=g: e.tensor_copy(
                        y_sl[:, uu, g * 512:(g + 1) * 512],
                        psum[2 + uu * 2 + g][:, 0:512]),
                        reads=[pb[2 + uu * 2 + g]], writes=[y_slb], tag="evye")
            b.do("gpsimd", lambda e, ee=ee: e.dma_start(
                out=yg0_hbm[ee * CAP0:(ee + 1) * CAP0, :].rearrange(
                    "(a p) d -> p a d", p=128), in_=y_sl[:, :, :]),
                reads=[y_slb], dma="c_yg0", tag="styg")

        yg_dep = [o for o in b.q["gpsimd"] if o.tag == "styg"]
        for i in range(NTL):
            b.do("gpsimd", lambda e, i=i: e.indirect_dma_start(
                out=y1[:, :], out_offset=None,
                in_=yg0_hbm[0:NSLOT0, :],
                in_offset=bass.IndirectOffsetOnAxis(ap=idxs[:, i:i + 1],
                                                    axis=0)),
                reads=[idxb], writes=[y1b], deps=yg_dep,
                dma="c_ga0", tag="ga0")
            b.do("vector", lambda e, i=i: e.scalar_tensor_tensor(
                res, y1, score[:, i:i + 1], Xt[i], ALU.mult, ALU.add),
                reads=[y1b, scoreb, Xb[i]], writes=[resb], tag="comb")
            ln_apply(res, [resb], Xt[i], [Xb[i]], tmp, tmpb, stc, stcb,
                     eps=1e-5, tag=f"lnB{i}")
        A.pop()

    def stage_c():
        A.push()
        fenceC = A.fence()

        def BufC(name):
            bb = Buf(name)
            bb.readers = list(fenceC)
            return bb
        xT = A.take(128, 8, 512)
        xTb = BufC("xTc")
        wq3 = A.take(128, 3, HD)
        wq3b = BufC("wq3")
        wraw = A.take(128, 3, HD)
        wrawbs = [BufC(f"wraw{j}") for j in range(3)]
        qkv = A.take(128, 3, HD)
        qkvb = BufC("qkv")
        ds_sb = A.take(128, NTL, H)
        ds_sbb = BufC("ds_sb")
        dsall = A.take(128, 8, NTL * H)
        dsallb = BufC("dsall")
        mx = A.take(128, NTL * H)
        mxb = BufC("mx")
        sexp = A.take(128, NTL * H)
        sexpb = BufC("sexp")
        t2 = A.take(128, NTL * H)
        t2b = BufC("t2")
        vsr = A.take(128, D)
        vsrb = BufC("vsr")
        vrep = A.take(128, D)
        vrepb = BufC("vrep")
        att = A.take(128, D)
        attb = BufC("att")
        tmp = A.take(128, D)
        tmpb = BufC("tmpC")
        res = A.take(128, D)
        resb = BufC("resC")
        stc = A.take(128, 8)
        stcb = BufC("stcC")

        for j, w in enumerate((Wq1, Wk1, Wv1)):
            b.do("sync", lambda e, j=j, w=w: e.dma_start(
                out=wraw[0:64, j, :], in_=w[:, :]),
                writes=[wrawbs[j]], dma="c_pre3", tag="ldw1a")
        for j in range(3):
            tr(psum[5][0:64, 0:64], wraw[0:64, j, :], [wrawbs[j]], 5,
               tag="trw1a")
            b.do("vector", lambda e, j=j: e.tensor_copy(
                wq3[0:64, j, :], psum[5][0:64, 0:64]),
                reads=[pb[5]], writes=[wq3b], tag="evw1a")
        for j in range(3):
            b.do("gpsimd", lambda e, j=j: e.dma_start(
                out=wq3[64:128, j, :], in_=wq3[0:64, j, :]),
                reads=[wq3b], dma="c_pre3b", tag="mirw1a")

        VS0, VS1 = 6, 7
        first_vs = [True, True]
        for s in range(4):
            for c in range(8):
                for u in range(4):
                    bank = (c * 4 + u) % 2
                    i = s * 4 + u
                    tr(psum[bank][:, 0:128], Xt[i][:, c * 128:(c + 1) * 128],
                       [Xb[i]], bank, tag="trxc")
                    b.do("vector", lambda e, c=c, u=u, bank=bank: e.tensor_copy(
                        xT[:, c, u * 128:(u + 1) * 128], psum[bank][:, 0:128]),
                        reads=[pb[bank]], writes=[xTb], tag="evxc")
            for u in range(4):
                i = s * 4 + u
                for h in range(H):
                    hp = (h % 2) * 64
                    hc = h // 2
                    lx = xT[hp:hp + 64, hc, u * 128:(u + 1) * 128]
                    for j in range(3):
                        mm(psum[2 + j][:, 0:HD], lx, wq3[hp:hp + 64, j, :],
                           start=True, stop=True, reads=[xTb, wq3b],
                           bank=2 + j, tag="qkvC")
                        b.do("vector", lambda e, j=j: e.tensor_copy(
                            qkv[:, j, :], psum[2 + j][:, 0:HD]),
                            reads=[pb[2 + j]], writes=[qkvb], tag="evqkvC")
                    b.do("vector", lambda e: e.tensor_tensor(
                        tmp[:, 0:HD], qkv[:, 0, :], qkv[:, 1, :], ALU.mult),
                        reads=[qkvb], writes=[tmpb], tag="dsm")
                    b.do("vector", lambda e, i=i, h=h: e.reduce_sum(
                        ds_sb[:, i, h:h + 1], tmp[:, 0:HD], axis=AX.X),
                        reads=[tmpb], writes=[ds_sbb], tag="dsr")
                    vsb = VS0 + (h % 2)
                    fidx = h % 2
                    mm(psum[vsb][0:1, (h // 2) * 64:(h // 2) * 64 + 64],
                       C["ones_col"], qkv[:, 2, :],
                       start=(first_vs[fidx]), stop=(i == NTL - 1 and h >= H - 2),
                       reads=[cbuf, qkvb], bank=vsb, tag="vsum")
                    first_vs[fidx] = False
        b.do("vector", lambda e: e.tensor_scalar(
            ds_sb[:, :, :], ds_sb[:, :, :], 1.0 / 32.0, None, ALU.mult),
            reads=[ds_sbb], writes=[ds_sbb], tag="dssc")
        cc_in_r = cc_in.rearrange("(p x) -> p x", p=128)
        b.do("gpsimd", lambda e: e.dma_start(
            out=cc_in_r,
            in_=ds_sb.rearrange("p a h -> p (a h)")),
            reads=[ds_sbb], dma="c_cc1", tag="stds")
        coll = b.do("gpsimd", lambda e: e.collective_compute(
            "AllGather", ALU.bypass, replica_groups=[list(range(8))],
            ins=[cc_in[:]], outs=[cc_out[:]]),
            deps=[b.q["gpsimd"][-1]], tag="allgather")
        b.do("sync", lambda e: e.dma_start(
            out=dsall[:, :, :],
            in_=cc_out.rearrange("(r p x) -> p r x", r=8, p=128)),
            writes=[dsallb], deps=[coll], dma="c_cc2", tag="lddsall")
        b.do("vector", lambda e: e.tensor_copy(mx[:, :], dsall[:, 0, :]),
             reads=[dsallb], writes=[mxb], tag="smx0")
        for r in range(1, 8):
            b.do("vector", lambda e, r=r: e.tensor_tensor(
                mx[:, :], mx[:, :], dsall[:, r, :], ALU.max),
                reads=[mxb, dsallb], writes=[mxb], tag="smx")
        for r in range(8):
            b.do("vector", lambda e, r=r: e.tensor_sub(
                t2[:, :], dsall[:, r, :], mx[:, :]),
                reads=[dsallb, mxb], writes=[t2b], tag="sse1")
            b.do("scalar", lambda e: e.activation(t2[:, :], t2[:, :], AF.Exp),
                 reads=[t2b], writes=[t2b], tag="sse2")
            if r == 0:
                b.do("vector", lambda e: e.tensor_copy(sexp[:, :], t2[:, :]),
                     reads=[t2b], writes=[sexpb], tag="sse3")
            else:
                b.do("vector", lambda e: e.tensor_add(
                    sexp[:, :], sexp[:, :], t2[:, :]),
                    reads=[sexpb, t2b], writes=[sexpb], tag="sse3")
        b.do("vector", lambda e: e.reciprocal(sexp[:, :], sexp[:, :]),
             reads=[sexpb], writes=[sexpb], tag="srcp")
        b.do("vector", lambda e: e.tensor_sub(
            t2[:, :], ds_sb.rearrange("p a h -> p (a h)"), mx[:, :]),
            reads=[ds_sbb, mxb], writes=[t2b], tag="spr1")
        b.do("scalar", lambda e: e.activation(t2[:, :], t2[:, :], AF.Exp),
             reads=[t2b], writes=[t2b], tag="spr2")
        b.do("vector", lambda e: e.tensor_tensor(
            t2[:, :], t2[:, :], sexp[:, :], ALU.mult),
            reads=[t2b, sexpb], writes=[t2b], tag="spr3")
        for j in range(2):
            b.do("vector", lambda e, j=j: e.tensor_copy(
                vsr[0:1, j * 512:(j + 1) * 512].rearrange(
                    "p (a c) -> p a c", a=4, c=128),
                psum[VS0 + j][0:1, 0:512].rearrange(
                    "p (a c) -> p a c", a=4, c=128)),
                reads=[pb[VS0 + j]], writes=[vsrb], tag="evvs")
        for h in range(H):
            src_off = (h % 2) * 512 + (h // 2) * 64
            mm(psum[0][:, h * 64:(h + 1) * 64] if h < 8 else
               psum[1][:, (h - 8) * 64:(h - 7) * 64],
               C["ones_1x128"], vsr[0:1, src_off:src_off + 64],
               start=(h % 8 == 0), stop=(h % 8 == 7),
               reads=[cbuf, vsrb], bank=(0 if h < 8 else 1), tag="vrepmm")
        for j in range(2):
            b.do("vector", lambda e, j=j: e.tensor_copy(
                vrep[:, j * 512:(j + 1) * 512], psum[j][:, 0:512]),
                reads=[pb[j]], writes=[vrepb], tag="evvrep")
        t2v = t2.rearrange("p (a h) -> p a h", a=NTL, h=H)
        for i in range(NTL):
            for h in range(H):
                b.do("vector", lambda e, i=i, h=h: e.tensor_scalar(
                    att[:, h * 64:(h + 1) * 64], vrep[:, h * 64:(h + 1) * 64],
                    t2v[:, i, h:h + 1], None, ALU.mult),
                    reads=[vrepb, t2b], writes=[attb], tag="attm")
            b.do("vector", lambda e, i=i: e.tensor_add(
                res, att, Xt[i]), reads=[attb, Xb[i]], writes=[resb],
                tag="resC")
            ln_apply(res, [resb], Xt[i], [Xb[i]], tmp, tmpb, stc, stcb,
                     eps=1e-5, tag=f"lnC{i}")
        A.pop()

    def stage_d():
        x3r = x3_hbm.rearrange("(a p) d -> p a d", p=128)
        for i in range(0, NTL, 4):
            b.do("gpsimd", lambda e, i=i: e.dma_start(
                out=x3r[:, i:i + 4, :], in_=xh[:, i:i + 4, :]),
                reads=[Xb[j] for j in range(i, i + 4)], dma="c_x3st",
                tag="stx3")
        x3_dep = [o for o in b.q["gpsimd"] if o.tag == "stx3"]
        A.push()
        fenceD = A.fence()

        def BufD(name):
            bb = Buf(name)
            bb.readers = list(fenceD)
            return bb
        bcols = A.take(128, 16)
        bcolsb = BufD("bcols")
        brows = A.take(128, 4, D)
        browsb = BufD("brows")
        ew = A.take(128, 6, NTL)
        ewb = BufD("ew")
        rowsD = A.take(128, 8, 40)
        rowsDb = BufD("rowsD")
        erep = A.take(128, 4, 40)
        erepb = BufD("erep")
        tb8 = A.take(128, E)
        tb8b = BufD("tb8")
        idxb1 = Buf("idxb1")

        A.push()
        fence0 = A.fence()

        def nb(name):
            bb = Buf(name)
            bb.readers = list(fence0)
            return bb

        xT = A.take(128, 8, 512)
        xTb = nb("xTd")
        wsl = [(A.take(128, 512), nb(f"wsld{i}")) for i in range(2)]
        stg = [(A.take(128, 512), nb(f"stgd{i}")) for i in range(2)]

        b.do("sync", lambda e: e.dma_start(
            out=bcols[:, 0:8], in_=bq2.rearrange("o (a p) -> p (o a)", p=128),
            allow_slow_non_contiguous=True),
            writes=[bcolsb], dma="c_pre4", tag="ldbq2")
        b.do("sync", lambda e: e.dma_start(
            out=bcols[:, 8:16], in_=bk2.rearrange("o (a p) -> p (o a)", p=128),
            allow_slow_non_contiguous=True),
            writes=[bcolsb], dma="c_pre4", tag="ldbk2")
        b.do("sync", lambda e: e.dma_start(
            out=brows[0:1, 0, :], in_=bv2[0:1, :]),
            writes=[browsb], dma="c_pre4", tag="ldbv2")
        b.do("sync", lambda e: e.dma_start(
            out=brows[0:1, 1, :], in_=bo2[0:1, :]),
            writes=[browsb], dma="c_pre4", tag="ldbo2")

        for s in range(4):
            for c in range(8):
                for u in range(4):
                    bank = (c * 4 + u) % 2
                    i = s * 4 + u
                    tr(psum[bank][:, 0:128], Xt[i][:, c * 128:(c + 1) * 128],
                       [Xb[i]], bank, tag="trxd")
                    b.do("vector", lambda e, c=c, u=u, bank=bank: e.tensor_copy(
                        xT[:, c, u * 128:(u + 1) * 128], psum[bank][:, 0:128]),
                        reads=[pb[bank]], writes=[xTb], tag="evxd")
            for wi, W in enumerate((Wq2, Wk2)):
                dsth = qT_hbm if wi == 0 else kT_hbm
                for mg in range(2):
                    for c in range(8):
                        slot = c % 2
                        wap, wb_ = wsl[slot]
                        b.do("sync", lambda e, W=W, c=c, mg=mg, w=wap:
                             e.dma_start(out=w[:, :],
                                         in_=W[c * 128:(c + 1) * 128,
                                              mg * 512:(mg + 1) * 512]),
                             writes=[wb_], dma=f"c_wd{slot}", tag="ldwqk2")
                        for j in range(4):
                            mm(psum[2 + j][:, 0:512],
                               wap[:, j * 128:(j + 1) * 128], xT[:, c, :],
                               start=(c == 0), stop=(c == 7),
                               reads=[wb_, xTb], bank=2 + j, tag="qkT2")
                    for j in range(4):
                        m = mg * 4 + j
                        slot2 = j % 2
                        sap, sb_ = stg[slot2]
                        if wi == 0:
                            b.do("vector", lambda e, j=j, m=m, sap=sap:
                                 e.tensor_scalar(sap[:, :], psum[2 + j][:, 0:512],
                                                 bcols[:, m:m + 1], 0.125,
                                                 ALU.add, ALU.mult),
                                 reads=[pb[2 + j], bcolsb], writes=[sb_],
                                 tag="evq2")
                        else:
                            b.do("vector", lambda e, j=j, m=m, sap=sap:
                                 e.tensor_scalar(sap[:, :], psum[2 + j][:, 0:512],
                                                 bcols[:, 8 + m:9 + m], None,
                                                 ALU.add),
                                 reads=[pb[2 + j], bcolsb], writes=[sb_],
                                 tag="evk2")
                        b.do("gpsimd", lambda e, dsth=dsth, m=m, s=s, sap=sap:
                             e.dma_start(out=dsth[m * 128:(m + 1) * 128,
                                                  s * 512:(s + 1) * 512],
                                         in_=sap[:, :]),
                             reads=[sb_], dma=f"c_qkst{slot2}", tag="stqkT")
            for g in range(2):
                for c in range(8):
                    slot = c % 2
                    wap, wb_ = wsl[slot]
                    b.do("sync", lambda e, c=c, g=g, w=wap: e.dma_start(
                        out=w[:, :], in_=Wv2[c * 128:(c + 1) * 128,
                                             g * 512:(g + 1) * 512]),
                        writes=[wb_], dma=f"c_wd{slot}", tag="ldwv2")
                    for u in range(4):
                        mm(psum[2 + u][:, 0:512],
                           xT[:, c, u * 128:(u + 1) * 128], wap[:, :],
                           start=(c == 0), stop=False,
                           reads=[wb_, xTb], bank=2 + u, tag="vmm2")
                for u in range(4):
                    mm(psum[2 + u][:, 0:512], C["ones_1x128"],
                       brows[0:1, 0, g * 512:(g + 1) * 512], start=False,
                       stop=True, reads=[cbuf, browsb], bank=2 + u, tag="vb2")
                    slot2 = u % 2
                    sap, sb_ = stg[slot2]
                    b.do("vector", lambda e, u=u, sap=sap: e.tensor_copy(
                        sap[:, :], psum[2 + u][:, 0:512]),
                        reads=[pb[2 + u]], writes=[sb_], tag="evv2")
                    b.do("gpsimd", lambda e, s=s, u=u, g=g, sap=sap:
                         e.dma_start(out=v_hbm[s * 512 + u * 128:
                                               s * 512 + (u + 1) * 128,
                                               g * 512:(g + 1) * 512],
                                     in_=sap[:, :]),
                         reads=[sb_], dma=f"c_qkst{slot2}", tag="stv2")
        A.pop()

        A.push()
        fence1 = A.fence()

        def nb1(name):
            bb = Buf(name)
            bb.readers = list(fence1)
            return bb

        vres = A.take(128, 16, 1040)
        vresb = nb1("vres")
        qh = A.take(128, 2, 512)
        qhb = [nb1("qh0"), nb1("qh1")]
        khs = A.take(128, 2, 512)
        khsb = [nb1("khs0"), nb1("khs1")]
        PTr = A.take(128, 6, 512)
        PTrbs = [nb1(f"PTr{i}") for i in range(6)]
        OT = A.take(128, 8, 512)
        OTb = nb1("OT")
        wos = [(A.take(128, 512), nb1(f"wos{i}")) for i in range(2)]
        drow = A.take(128, 2, 512)
        drowb = nb1("drow")
        rdB = A.take(128, 2, 512)
        rdBb = nb1("rdB")
        lnr = A.take(128, 2, D)
        lnrb = nb1("lnr")
        zzq = OT.rearrange("p a c -> p (a c)").rearrange(
            "p (a c) -> p a c", a=4, c=1024)
        zzb = OTb
        tmp = rdB.rearrange("p a c -> p (a c)")
        tmpb = rdBb
        stc4 = A.take(128, 4, 8)
        stc4b = [nb1(f"stcD1_{i}") for i in range(4)]
        for j, wsrc in enumerate((ln1_g, ln1_b)):
            b.do("sync", lambda e, j=j, w=wsrc: e.dma_start(
                out=brows[0:1, 2 + j, :], in_=w[0:1, :]),
                writes=[browsb], deps=fence1, dma="c_pre5", tag="ldln1")
        ld_ln1 = [o for o in b.q["sync"] if o.tag == "ldln1"]
        for j in range(2):
            for g in range(2):
                mm(psum[6][:, 0:512], C["ones_1x128"],
                   brows[0:1, 2 + j, g * 512:(g + 1) * 512], start=True,
                   stop=True, reads=[cbuf], bank=6, tag="ln1bc")
                b.q["tensor"][-1].deps.extend(ld_ln1)
                b.do("vector", lambda e, j=j, g=g: e.tensor_copy(
                    lnr[:, j, g * 512:(g + 1) * 512], psum[6][:, 0:512]),
                    reads=[pb[6]], writes=[lnrb], tag="ln1bc_ev")
        b.do("sync", lambda e: e.dma_start(
            out=cs[64:65, 321:385], in_=ic["ones_1x128"][0:1, 0:64]),
            writes=[cbuf], dma="c_pre5b", tag="ldones64")
        ones_p64 = cs[64:65, 321:385]

        qk_dep = [o for o in b.q["gpsimd"] if o.tag in ("stqkT", "stv2")]
        vr = v_hbm.rearrange("(a p) d -> p a d", p=128)
        vre = vres.rearrange("p a (h c) -> p a h c", h=16, c=65)
        for i in range(NTL):
            b.do("sync", lambda e, i=i: e.dma_start(
                out=vre[:, i, :, 0:64], in_=vr[:, i, :].rearrange(
                    "p (h c) -> p h c", h=16, c=64),
                allow_slow_non_contiguous=True),
                writes=[vresb], deps=qk_dep, dma="c_vres", tag="ldvres")
        b.do("vector", lambda e: e.memset(vre[:, :, :, 64:65], 1.0),
             writes=[vresb], tag="vones")

        pend_tail = None
        pend_lns = []
        for s in range(4):
            for hpair in range(8):
                qs_ = hpair % 2
                b.do("sync", lambda e, s=s, hpair=hpair, qs_=qs_:
                     e.dma_start(
                         out=qh[:, qs_, :],
                         in_=qT_hbm[hpair * 128:(hpair + 1) * 128,
                                    s * 512:(s + 1) * 512]),
                     writes=[qhb[qs_]], deps=qk_dep,
                     dma=f"c_qh{qs_}", tag="ldqh")
                def ldk(c, hpair=hpair):
                    b.do("sync", lambda e, hpair=hpair, c=c:
                         e.dma_start(
                             out=khs[:, c % 2, :],
                             in_=kT_hbm[hpair * 128:(hpair + 1) * 128,
                                        c * 512:(c + 1) * 512]),
                         writes=[khsb[c % 2]], deps=qk_dep,
                         dma=f"c_kh{c % 2}", tag="ldkh")
                ldk(0)
                ldk(1)
                lq = [qh[0:64, qs_, :], qh[64:128, qs_, :]]
                for tkc in range(18):
                    if tkc >= 2:
                        t2 = tkc - 2
                        for par in range(2):
                            slot = par * 3 + t2 % 3
                            avb = 4 if par == 0 else 6
                            h2 = 2 * hpair + par
                            mm(psum[avb][0:65, 0:512],
                               vres[:, t2, h2 * 65:h2 * 65 + 65],
                               PTr[:, slot, :],
                               start=(t2 == 0), stop=(t2 == 15),
                               reads=[vresb, PTrbs[slot]], bank=avb,
                               tag="AVmm")
                    if tkc < 16:
                        if tkc in (4, 8):
                            ldk(2 + (tkc == 8))
                        ck = (tkc // 4) % 2
                        for par in range(2):
                            bank = par * 2 + tkc % 2
                            slot = par * 3 + tkc % 3
                            mm(psum[bank][:, 0:512],
                               khs[par * 64:par * 64 + 64, ck,
                                   (tkc % 4) * 128:(tkc % 4) * 128 + 128],
                               lq[par], start=True, stop=True,
                               reads=[khsb[ck], qhb[qs_]], bank=bank,
                               tag="STmm")
                            b.do("scalar", lambda e, bank=bank, slot=slot:
                                 e.activation(PTr[:, slot, :],
                                              psum[bank][:, 0:512], AF.Exp),
                                 reads=[pb[bank]], writes=[PTrbs[slot]],
                                 tag="PTev")
                    if tkc == 0 and pend_tail is not None:
                        pend_tail()
                        pend_tail = None
                    if tkc >= 2 and tkc % 2 == 0 and pend_lns:
                        pend_lns.pop(0)()

                def tail(hpair=hpair):
                    for par in range(2):
                        avb = 4 if par == 0 else 6
                        b.do("vector", lambda e, avb=avb, par=par:
                             e.reciprocal(drow[64:65, par, :],
                                          psum[avb][64:65, 0:512]),
                             reads=[pb[avb]], writes=[drowb], tag="Srd")
                        mm(psum[5][0:64, 0:512], ones_p64,
                           drow[64:65, par, :], start=True, stop=True,
                           reads=[cbuf, drowb], bank=5, tag="rdbc")
                        b.do("vector", lambda e, par=par: e.tensor_copy(
                            rdB[0:64, par, :], psum[5][0:64, 0:512]),
                            reads=[pb[5]], writes=[rdBb], tag="evrd")
                        b.do("vector",
                             lambda e, hpair=hpair, par=par, avb=avb:
                             e.tensor_tensor(
                                 OT[par * 64:par * 64 + 64, hpair, :],
                                 psum[avb][0:64, 0:512], rdB[0:64, par, :],
                                 ALU.mult),
                             reads=[pb[avb], rdBb], writes=[OTb], tag="Oev")
                pend_tail = tail
            pend_tail()
            pend_tail = None
            for g in range(2):
                for c in range(8):
                    wap, wb_ = wos[c % 2]
                    b.do("sync", lambda e, c=c, g=g, w=wap: e.dma_start(
                        out=w[:, :], in_=Wo2[c * 128:(c + 1) * 128,
                                             g * 512:(g + 1) * 512]),
                        writes=[wb_], dma=f"c_wo{c % 2}", tag="ldwo2")
                    for tqs in range(4):
                        mm(psum[tqs * 2 + g][:, 0:512],
                           OT[:, c, tqs * 128:(tqs + 1) * 128], wap[:, :],
                           start=(c == 0), stop=False,
                           reads=[wb_, OTb], bank=tqs * 2 + g, tag="prj2")
                for tqs in range(4):
                    mm(psum[tqs * 2 + g][:, 0:512], C["ones_1x128"],
                       brows[0:1, 1, g * 512:(g + 1) * 512], start=False,
                       stop=True, reads=[cbuf, browsb], bank=tqs * 2 + g,
                       tag="prj2b")
            for tqs in range(4):
                for g in range(2):
                    b.do("vector", lambda e, g=g, tqs=tqs: e.tensor_copy(
                        zzq[:, tqs, g * 512:(g + 1) * 512],
                        psum[tqs * 2 + g][:, 0:512]),
                        reads=[pb[tqs * 2 + g]], writes=[zzb], tag="evz")
            def mk_ln(s=s):
                pieces = {}
                for tqs in range(4):
                    i = s * 4 + tqs
                    zz = zzq[:, tqs, :]
                    sc, scb = stc4[:, tqs, :], stc4b[tqs]
                    ssum, ssq = sc[:, 0:1], sc[:, 1:2]
                    mean, var = sc[:, 2:3], sc[:, 3:4]
                    rstd, negmr = sc[:, 4:5], sc[:, 5:6]

                    def p1(e=None, zz=zz, ssum=ssum, ssq=ssq, scb=scb):
                        b.do("vector", lambda e: e.reduce_sum(
                            ssum, zz, axis=AX.X),
                            reads=[zzb], writes=[scb], tag="lnp1a")
                        b.do("vector", lambda e: e.scalar_tensor_tensor(
                            tmp, zz, 1.0, zz, ALU.mult, ALU.mult,
                            accum_out=ssq),
                            reads=[zzb], writes=[tmpb, scb], tag="lnp1b")

                    def p2(e=None, mean=mean, ssum=ssum, var=var, ssq=ssq,
                           rstd=rstd, scb=scb):
                        b.do("vector", lambda e: e.tensor_scalar(
                            mean, ssum, 1.0 / D, None, ALU.mult),
                            reads=[scb], writes=[scb], tag="lnp2a")
                        b.do("vector", lambda e: e.scalar_tensor_tensor(
                            var, mean, -1.0, mean, ALU.mult, ALU.mult),
                            reads=[scb], writes=[scb], tag="lnp2b")
                        b.do("vector", lambda e: e.scalar_tensor_tensor(
                            var, ssq, 1.0 / D, var, ALU.mult, ALU.add),
                            reads=[scb], writes=[scb], tag="lnp2c")
                        b.do("vector", lambda e: e.tensor_scalar(
                            var, var, 1e-5, None, ALU.add),
                            reads=[scb], writes=[scb], tag="lnp2d")
                        b.do("vector", lambda e: e.reciprocal(rstd, var),
                             reads=[scb], writes=[scb], tag="lnp2e")
                        b.do("scalar", lambda e: e.activation(
                            rstd, rstd, AF.Sqrt),
                            reads=[scb], writes=[scb], tag="lnp2f")

                    def p3(e=None, negmr=negmr, mean=mean, rstd=rstd,
                           zz=zz, scb=scb):
                        b.do("vector", lambda e: e.tensor_scalar(
                            negmr, mean, rstd, -1.0, ALU.mult, ALU.mult),
                            reads=[scb], writes=[scb], tag="lnp3a")
                        b.do("vector", lambda e: e.tensor_scalar(
                            zz, zz, rstd, negmr, ALU.mult, ALU.add),
                            reads=[zzb, scb], writes=[zzb], tag="lnp3b")
                        b.do("vector", lambda e: e.tensor_tensor(
                            zz, zz, lnr[:, 0, :], ALU.mult),
                            reads=[zzb, lnrb], writes=[zzb], tag="lnp3c")

                    def p4(e=None, zz=zz, i=i):
                        b.do("vector", lambda e: e.tensor_tensor(
                            zz, zz, lnr[:, 1, :], ALU.add),
                            reads=[zzb, lnrb], writes=[zzb], tag="lnp4a")
                        b.do("vector", lambda e: e.tensor_tensor(
                            Xt[i], Xt[i], zz, ALU.add),
                            reads=[Xb[i], zzb], writes=[Xb[i]], tag="lnp4b")
                    pieces[tqs] = [p1, p2, p3, p4]
                return [pieces[t][k] for t in range(4) for k in range(4)]
            pend_lns = mk_ln()
        for f in pend_lns:
            f()
        pend_lns = []
        A.pop()

        if d_sub in ("d1", "datt"):
            A.pop()
            return
        A.push()
        fence2 = A.fence()

        def nb2(name):
            bb = Buf(name)
            bb.readers = list(fence2)
            return bb

        xT2 = A.take(128, 8, 512)
        xT2b = nb2("xT2")
        g1s = A.take(128, 8, E)
        g1sb = nb2("g1s")
        g1br = A.take(128, E)
        g1brb = nb2("g1br")
        gate = A.take(128, 3, E)
        gateb = nb2("gateD")
        gcol = A.take(128, 8)
        gcolb = nb2("gcolD")
        postile = A.take(128, E)
        postileb = nb2("postileD")
        carry = A.take(128, E)
        carryb = nb2("carryD")
        zt2 = A.take(128, D)
        zt2b = nb2("zt2")
        tmp2 = A.take(128, D)
        tmp2b = nb2("tmp2")

        for c in range(8):
            b.do("sync", lambda e, c=c: e.dma_start(
                out=g1s[:, c, :], in_=gate1_w[c * 128:(c + 1) * 128, :]),
                writes=[g1sb], dma="c_pre6", tag="ldg1")
        b.do("sync", lambda e: e.dma_start(out=g1br[0:1, :], in_=gate1_b[0:1, :]),
             writes=[g1brb], dma="c_pre6", tag="ldg1b")
        b.do("vector", lambda e: e.memset(carry[0:1, :], 0.0), writes=[carryb])
        b.do("vector", lambda e: e.memset(zt2[:, :], 0.0), writes=[zt2b])
        NZR1 = (NSLOT1 + 128) // 128
        for i in range(NZR1):
            b.do("gpsimd", lambda e, i=i: e.dma_start(
                out=xg1_hbm[i * 128:(i + 1) * 128, :], in_=zt2[:, :]),
                reads=[zt2b], dma="c_zero1", tag="zero_xg1")

        for s in range(4):
            for c in range(8):
                for u in range(4):
                    bank = (c * 4 + u) % 2
                    i = s * 4 + u
                    tr(psum[bank][:, 0:128], Xt[i][:, c * 128:(c + 1) * 128],
                       [Xb[i]], bank, tag="trx2")
                    b.do("vector", lambda e, c=c, u=u, bank=bank: e.tensor_copy(
                        xT2[:, c, u * 128:(u + 1) * 128], psum[bank][:, 0:128]),
                        reads=[pb[bank]], writes=[xT2b], tag="evx2")
            for u in range(4):
                i = s * 4 + u
                for c in range(8):
                    mm(psum[2][:, 0:E], xT2[:, c, u * 128:(u + 1) * 128],
                       g1s[:, c, :], start=(c == 0), stop=False,
                       reads=[xT2b, g1sb], bank=2, tag="gate1")
                mm(psum[2][:, 0:E], C["ones_1x128"], g1br[0:1, :],
                   start=False, stop=True, reads=[cbuf, g1brb], bank=2,
                   tag="gate1b")
                b.do("vector", lambda e: e.reduce_max(
                    gcol[:, 0:1], psum[2][:, 0:E], axis=AX.X),
                    reads=[pb[2]], writes=[gcolb], tag="g1mx")
                b.do("vector", lambda e: e.tensor_scalar(
                    gcol[:, 1:2], gcol[:, 0:1], -1.0, None, ALU.mult),
                    reads=[gcolb], writes=[gcolb], tag="g1nm")
                b.do("scalar", lambda e: e.activation(
                    gate[:, 0, :], psum[2][:, 0:E], AF.Exp,
                    bias=gcol[:, 1:2], accum_out=gcol[:, 2:3]),
                    reads=[pb[2], gcolb], writes=[gateb, gcolb], tag="g1ex")
                b.do("vector", lambda e: e.reciprocal(
                    gcol[:, 3:4], gcol[:, 2:3]),
                    reads=[gcolb], writes=[gcolb], tag="g1rc")
                b.do("vector", lambda e: e.tensor_scalar(
                    gate[:, 0, :], gate[:, 0, :], gcol[:, 3:4], None, ALU.mult),
                    reads=[gateb, gcolb], writes=[gateb], tag="g1pr")
                for ee in range(E):
                    b.do("vector", lambda e, ee=ee: e.tensor_scalar(
                        tmp2[:, 0:E], gate[:, 0, :], gate[:, 0, ee:ee + 1],
                        None, ALU.is_gt),
                        reads=[gateb], writes=[tmp2b], tag="r1gt")
                    b.do("vector", lambda e: e.reduce_sum(
                        gcol[:, 4:5], tmp2[:, 0:E], axis=AX.X),
                        reads=[tmp2b], writes=[gcolb], tag="r1gs")
                    if ee > 0:
                        b.do("vector", lambda e, ee=ee: e.tensor_scalar(
                            tmp2[:, 0:ee], gate[:, 0, 0:ee],
                            gate[:, 0, ee:ee + 1], None, ALU.is_equal),
                            reads=[gateb], writes=[tmp2b], tag="r1eq")
                        b.do("vector", lambda e, ee=ee: e.reduce_sum(
                            gcol[:, 5:6], tmp2[:, 0:ee], axis=AX.X),
                            reads=[tmp2b], writes=[gcolb], tag="r1es")
                        b.do("vector", lambda e: e.tensor_tensor(
                            gcol[:, 4:5], gcol[:, 4:5], gcol[:, 5:6], ALU.add),
                            reads=[gcolb], writes=[gcolb], tag="r1ad")
                    b.do("vector", lambda e, ee=ee: e.tensor_scalar(
                        gate[:, 1, ee:ee + 1], gcol[:, 4:5], 0.0, None,
                        ALU.is_equal),
                        reads=[gcolb], writes=[gateb], tag="r1s0")
                    b.do("vector", lambda e, ee=ee: e.tensor_scalar(
                        gate[:, 2, ee:ee + 1], gcol[:, 4:5], 1.0, None,
                        ALU.is_equal),
                        reads=[gcolb], writes=[gateb], tag="r1s1")
                for r_, (selr, dst) in enumerate(
                        (((1), 0), ((2), 1))):
                    b.do("vector", lambda e, selr=selr: e.tensor_tensor(
                        tmp2[:, 0:E], gate[:, selr, :], C["iota8"], ALU.mult),
                        reads=[gateb, cbuf], writes=[tmp2b], tag="q1em")
                    b.do("vector", lambda e, i=i, dst=dst: e.reduce_sum(
                        ew[:, dst, i:i + 1], tmp2[:, 0:E], axis=AX.X),
                        reads=[tmp2b], writes=[ewb], tag="q1e")
                    b.do("vector", lambda e, selr=selr: e.tensor_tensor(
                        tmp2[:, 0:E], gate[:, selr, :], gate[:, 0, :],
                        ALU.mult),
                        reads=[gateb], writes=[tmp2b], tag="q1wm")
                    b.do("vector", lambda e, i=i, dst=dst: e.reduce_sum(
                        ew[:, 2 + dst, i:i + 1], tmp2[:, 0:E], axis=AX.X),
                        reads=[tmp2b], writes=[ewb], tag="q1w")
                b.do("vector", lambda e: e.tensor_tensor(
                    tmp2[:, 0:E], gate[:, 1, :], gate[:, 2, :], ALU.add),
                    reads=[gateb], writes=[tmp2b], tag="clm")
                mm(psum[3][:, 0:E], C["triuS"], tmp2[:, 0:E],
                   start=True, stop=False, reads=[cbuf, tmp2b], bank=3,
                   tag="scan1p")
                mm(psum[3][:, 0:E], C["ones_1x128"], carry[0:1, :],
                   start=False, stop=True, reads=[cbuf, carryb], bank=3,
                   tag="scan1c")
                b.do("vector", lambda e: e.tensor_copy(
                    postile[:, :], psum[3][:, 0:E]),
                    reads=[pb[3]], writes=[postileb], tag="scan1e")
                mm(psum[4][0:1, 0:E], C["ones_col"], tmp2[:, 0:E],
                   start=True, stop=True, reads=[cbuf, tmp2b], bank=4,
                   tag="cols1")
                b.do("vector", lambda e: e.tensor_tensor(
                    carry[0:1, :], carry[0:1, :], psum[4][0:1, 0:E], ALU.add),
                    reads=[pb[4]], writes=[carryb], tag="cup1")
                for r_, selr in enumerate((1, 2)):
                    b.do("vector", lambda e, selr=selr: e.tensor_tensor(
                        tmp2[:, 0:E], postile[:, :], gate[:, selr, :],
                        ALU.mult),
                        reads=[postileb, gateb], writes=[tmp2b], tag="q1pm")
                    b.do("vector", lambda e, i=i, r_=r_: e.reduce_sum(
                        ew[:, 4 + r_, i:i + 1], tmp2[:, 0:E], axis=AX.X),
                        reads=[tmp2b], writes=[ewb], tag="q1p")

        ntscr = rowsD.rearrange("p a c -> p (a c)")[0:1, 184:320].rearrange(
            "p (a c) -> p a c", a=E, c=17)
        b.do("vector", lambda e: e.tensor_tensor(
            ntscr,
            carry[0:1, :].unsqueeze(2).broadcast_to([1, E, 17]),
            C["thr17"].unsqueeze(1).broadcast_to([1, E, 17]), ALU.is_gt),
            reads=[carryb, cbuf], writes=[rowsDb], tag="ntc")
        b.do("vector", lambda e: e.reduce_sum(
            rowsD[0:1, 1, 0:E], ntscr, axis=AX.X),
            reads=[rowsDb], writes=[rowsDb], tag="ntr")
        b.do("vector", lambda e: e.memset(rowsD[0:1, 2, 0:1], 0.0),
             writes=[rowsDb])
        for ee in range(1, E):
            b.do("vector", lambda e, ee=ee: e.tensor_tensor(
                rowsD[0:1, 2, ee:ee + 1], rowsD[0:1, 2, ee - 1:ee],
                rowsD[0:1, 1, ee - 1:ee], ALU.add),
                reads=[rowsDb], writes=[rowsDb], tag="cum")
        b.do("vector", lambda e: e.tensor_tensor(
            rowsD[0:1, 3, 0:E], rowsD[0:1, 2, 0:E], rowsD[0:1, 1, 0:E],
            ALU.add),
            reads=[rowsDb], writes=[rowsDb], tag="incl")
        b.do("vector", lambda e: e.memset(rowsD[0:1, 4, :], 0.0),
             writes=[rowsDb])
        for ee in range(E):
            b.do("vector", lambda e, ee=ee: e.tensor_scalar(
                rowsD[0:1, 5, :], C["iota40"], rowsD[0:1, 3, ee:ee + 1],
                None, ALU.is_ge),
                reads=[rowsDb, cbuf], writes=[rowsDb], tag="eof1")
            b.do("vector", lambda e: e.tensor_tensor(
                rowsD[0:1, 4, :], rowsD[0:1, 4, :], rowsD[0:1, 5, :], ALU.add),
                reads=[rowsDb], writes=[rowsDb], tag="eof2")
        b.do("vector", lambda e: e.tensor_scalar(
            rowsD[0:1, 4, :], rowsD[0:1, 4, :], 7.0, None, ALU.min),
            reads=[rowsDb], writes=[rowsDb], tag="eof3")
        for r_, mul in ((5, 1024.0), (6, 4096.0), (7, 32.0)):
            b.do("vector", lambda e, r_=r_, mul=mul: e.tensor_scalar(
                rowsD[0:1, r_, :], rowsD[0:1, 4, :], mul, None, ALU.mult),
                reads=[rowsDb], writes=[rowsDb], tag="eder")
        for r_, dst in ((5, 0), (6, 1), (7, 2), (4, 3)):
            mm(psum[5][:, 0:40], C["ones_1x128"], rowsD[0:1, r_, :],
               start=True, stop=True, reads=[cbuf, rowsDb], bank=5,
               tag="erepmm")
            b.do("vector", lambda e, dst=dst: e.tensor_copy(
                erep[:, dst, :], psum[5][:, 0:40]),
                reads=[pb[5]], writes=[erepb], tag="erepev")
        b.do("vector", lambda e: e.tensor_scalar(
            rowsD[0:1, 2, 0:E], rowsD[0:1, 2, 0:E], 128.0, None, ALU.mult),
            reads=[rowsDb], writes=[rowsDb], tag="tb1")
        mm(psum[5][:, 0:E], C["ones_1x128"], rowsD[0:1, 2, 0:E],
           start=True, stop=True, reads=[cbuf, rowsDb], bank=5, tag="tbmm")
        b.do("vector", lambda e: e.tensor_copy(tb8[:, :], psum[5][:, 0:E]),
             reads=[pb[5]], writes=[tb8b], tag="tbev")
        zero_dep1 = [o for o in b.q["gpsimd"] if o.tag == "zero_xg1"]
        for i in range(NTL):
            for r_ in range(2):
                b.do("vector", lambda e, i=i, r_=r_: e.tensor_scalar(
                    gate[:, 0, :], C["iota8"], ew[:, r_, i:i + 1], None,
                    ALU.is_equal),
                    reads=[ewb, cbuf], writes=[gateb], tag="sl_eq")
                b.do("vector", lambda e: e.tensor_tensor(
                    gate[:, 1, :], gate[:, 0, :], tb8[:, :], ALU.mult),
                    reads=[gateb, tb8b], writes=[gateb], tag="sl_tb")
                b.do("vector", lambda e: e.reduce_sum(
                    gcol[:, 0:1], gate[:, 1, :], axis=AX.X),
                    reads=[gateb], writes=[gcolb], tag="sl_rs")
                b.do("vector", lambda e, i=i, r_=r_: e.tensor_tensor(
                    gcol[:, 1:2], gcol[:, 0:1], ew[:, 4 + r_, i:i + 1],
                    ALU.add),
                    reads=[gcolb, ewb], writes=[gcolb], tag="sl_add")
                b.do("vector", lambda e, i=i, r_=r_: e.tensor_copy(
                    idxs[:, 16 + r_ * 16 + i:17 + r_ * 16 + i], gcol[:, 1:2]),
                    reads=[gcolb], writes=[idxb1], tag="sl_cast")
        A.pop()

        sc_prev = [None]
        for i in range(NTL):
            for r_ in range(2):
                op = b.do("gpsimd", lambda e, i=i, r_=r_: e.indirect_dma_start(
                    out=xg1_hbm[0:NSLOT1, :],
                    out_offset=bass.IndirectOffsetOnAxis(
                        ap=idxs[:, 16 + r_ * 16 + i:17 + r_ * 16 + i], axis=0),
                    in_=Xt[i], in_offset=None),
                    reads=[idxb1, Xb[i]],
                    deps=([sc_prev[0]] if sc_prev[0] is not None else
                          zero_dep1),
                    dma="c_scat1", tag="scat1")
                sc_prev[0] = op

        A.push()
        fence3 = A.fence()

        def nb3(name):
            bb = Buf(name)
            bb.readers = list(fence3)
            return bb

        xgt = A.take(128, D)
        xgtb = nb3("xgt")
        xgT2 = A.take(128, 8, 128)
        xgT2b = nb3("xgT2")
        w1f = [(A.take(128, 4096), nb3(f"w1f{i}")) for i in range(3)]
        w2f = [(A.take(128, D), nb3(f"w2f{i}")) for i in range(4)]
        h_m2 = A.take(128, 32, 128)
        h_m2b = nb3("h_m2")
        b1f = A.take(128, 128)
        b1fb = nb3("b1f")
        b1T = A.take(128, 32)
        b1Tb = nb3("b1T")
        b2f = A.take(128, D)
        b2fb = nb3("b2f")
        ygt = A.take(128, D)
        ygtb = nb3("ygt")

        idxw = [Buf(f"idxw{i}") for i in range(4)]
        scat1_dep = [o for o in b.q["gpsimd"] if o.tag == "scat1"]
        W1v = W1_1
        W2v = W2_1
        b1v = b1_1

        for j in range(NTILE1):
            b.do("sync", lambda e, j=j: e.dma_start(
                out=xgt[:, :], in_=xg1_hbm[j * 128:(j + 1) * 128, :]),
                writes=[xgtb], deps=scat1_dep, dma="c_xg1", tag="ldxg1")
            for c in range(8):
                bank = c % 2
                tr(psum[bank][:, 0:128], xgt[:, c * 128:(c + 1) * 128],
                   [xgtb], bank, tag="trxg1")
                b.do("vector", lambda e, c=c, bank=bank: e.tensor_copy(
                    xgT2[:, c, :], psum[bank][:, 0:128]),
                    reads=[pb[bank]], writes=[xgT2b], tag="evxg1")
            base = 48 + (j % 2) * 44
            jp = j % 2
            b.do("vector", lambda e, j=j, base=base: e.tensor_scalar(
                idxs[:, base:base + 8].bitcast(F32), cidxF[:, 0:8],
                erep[:, 0, j:j + 1], None, ALU.add),
                reads=[erepb, cbuf], writes=[idxw[jp]], tag="i_w1")
            b.do("vector", lambda e, base=base: e.tensor_copy(
                idxs[:, base:base + 8],
                idxs[:, base:base + 8].bitcast(F32)),
                reads=[idxw[jp]], writes=[idxw[jp]], tag="i_w1c")
            b.do("vector", lambda e, j=j, base=base: e.tensor_scalar(
                idxs[:, base + 8:base + 40].bitcast(F32), cidxF[:, 0:32],
                erep[:, 1, j:j + 1], None, ALU.add),
                reads=[erepb, cbuf], writes=[idxw[jp]], tag="i_w2")
            b.do("vector", lambda e, base=base: e.tensor_copy(
                idxs[:, base + 8:base + 40],
                idxs[:, base + 8:base + 40].bitcast(F32)),
                reads=[idxw[jp]], writes=[idxw[jp]], tag="i_w2c")
            b.do("vector", lambda e, j=j: e.tensor_scalar(
                idxs[0:32, 168:169].bitcast(F32), cidxF[0:32, 0:1],
                erep[0:32, 2, j:j + 1], None, ALU.add),
                reads=[erepb, cbuf], writes=[idxw[2]], tag="i_b1f")
            b.do("vector", lambda e: e.tensor_copy(
                idxs[0:32, 168:169], idxs[0:32, 168:169].bitcast(F32)),
                reads=[idxw[2]], writes=[idxw[2]], tag="i_b1c")
            b.do("gpsimd", lambda e: e.indirect_dma_start(
                out=b1f[0:32, 0:128], out_offset=None,
                in_=b1v[:, :],
                in_offset=bass.IndirectOffsetOnAxis(ap=idxs[0:32, 168:169],
                                                    axis=0)),
                reads=[idxw[2]], writes=[b1fb], dma="c_b1f", tag="f_b1")
            tr(psum[2][:, 0:32], b1f[0:32, 0:128], [b1fb], 2, tag="trb1")
            b.do("vector", lambda e: e.tensor_copy(b1T[:, :], psum[2][:, 0:32]),
                 reads=[pb[2]], writes=[b1Tb], tag="evb1")
            b.do("vector", lambda e, j=j: e.tensor_copy(
                idxs[0:2, 169:170].bitcast(F32), erep[0:2, 3, j:j + 1]),
                reads=[erepb], writes=[idxw[3]], tag="i_b2f")
            b.do("vector", lambda e: e.tensor_copy(
                idxs[0:2, 169:170], idxs[0:2, 169:170].bitcast(F32)),
                reads=[idxw[3]], writes=[idxw[3]], tag="i_b2c")
            b.do("gpsimd", lambda e: e.indirect_dma_start(
                out=b2f[0:2, :], out_offset=None,
                in_=b2_1[:, :],
                in_offset=bass.IndirectOffsetOnAxis(ap=idxs[0:2, 169:170],
                                                    axis=0)),
                reads=[idxw[3]], writes=[b2fb], dma="c_b2f", tag="f_b2")
            for c in range(8):
                slot = c % 3
                wap, wb_ = w1f[slot]
                b.do("gpsimd", lambda e, c=c, base=base, w=wap:
                     e.indirect_dma_start(
                         out=w[:, :], out_offset=None,
                         in_=W1v[:, :],
                         in_offset=bass.IndirectOffsetOnAxis(
                             ap=idxs[:, base + c:base + c + 1], axis=0)),
                     reads=[idxw[jp]], writes=[wb_], dma=f"c_w1f{slot}",
                     tag="f_w1")
                for fc in range(32):
                    mm(psum[fc // 4][:, (fc % 4) * 128:(fc % 4) * 128 + 128],
                       wap[:, fc * 128:(fc + 1) * 128], xgT2[:, c, :],
                       start=(c == 0 and fc % 4 == 0),
                       stop=(c == 7 and fc % 4 == 3),
                       reads=[wb_, xgT2b], bank=fc // 4, tag="hmm1")
            for fc in range(32):
                b.do("scalar", lambda e, fc=fc: e.activation(
                    h_m2[:, fc, :],
                    psum[fc // 4][:, (fc % 4) * 128:(fc % 4) * 128 + 128],
                    AF.Gelu, bias=b1T[:, fc:fc + 1]),
                    reads=[pb[fc // 4], b1Tb], writes=[h_m2b], tag="gelu1")
            for fc in range(32):
                slot = fc % 4
                wap2, wb2_ = w2f[slot]
                b.do("gpsimd", lambda e, fc=fc, base=base, w=wap2:
                     e.indirect_dma_start(
                         out=w[:, :], out_offset=None, in_=W2v[:, :],
                         in_offset=bass.IndirectOffsetOnAxis(
                             ap=idxs[:, base + 8 + fc:base + 9 + fc],
                             axis=0)),
                     reads=[idxw[jp]], writes=[wb2_], dma=f"c_w2f{slot}",
                     tag="f_w2")
                for g in range(2):
                    mm(psum[6 + g][:, 0:512], h_m2[:, fc, :],
                       wap2[:, g * 512:(g + 1) * 512],
                       start=(fc == 0), stop=False,
                       reads=[h_m2b, wb2_], bank=6 + g, tag="yemm1")
            for g in range(2):
                mm(psum[6 + g][:, 0:512], C["ones_1x128"],
                   b2f[0:1, g * 512:(g + 1) * 512], start=False, stop=True,
                   reads=[cbuf, b2fb], bank=6 + g, tag="yeb1")
                b.do("vector", lambda e, g=g: e.tensor_copy(
                    ygt[:, g * 512:(g + 1) * 512], psum[6 + g][:, 0:512]),
                    reads=[pb[6 + g]], writes=[ygtb], tag="evye1")
            b.do("gpsimd", lambda e, j=j: e.dma_start(
                out=yg1_hbm[j * 128:(j + 1) * 128, :], in_=ygt[:, :]),
                reads=[ygtb], dma="c_yg1", tag="styg1")
        A.pop()

        A.push()
        fence4 = A.fence()

        def nb4(name):
            bb = Buf(name)
            bb.readers = list(fence4)
            return bb

        x3t = A.take(128, D)
        x3tb = nb4("x3t")
        y1 = A.take(128, D)
        y1b = nb4("y1D")
        y2 = A.take(128, D)
        y2b = nb4("y2D")
        vv2 = A.take(128, D)
        vv2b = nb4("vvD")
        tt2 = A.take(128, D)
        tt2b = nb4("ttD")
        lnr2 = A.take(128, 2, D)
        lnr2b = nb4("lnr2")
        lnrow2 = A.take(128, 2, D)
        lnrow2b = nb4("lnrow2")
        res = A.take(128, D)
        resb = nb4("resD")
        tmp3 = A.take(128, D)
        tmp3b = nb4("tmp3")
        stc3 = A.take(128, 8)
        stc3b = nb4("stc3")

        for j, wsrc in enumerate((ln2_g, ln2_b)):
            b.do("sync", lambda e, j=j, w=wsrc: e.dma_start(
                out=lnrow2[0:1, j, :], in_=w[0:1, :]),
                writes=[], deps=fence4, dma="c_pre7", tag="ldln2")
        ld_ln2 = [o for o in b.q["sync"] if o.tag == "ldln2"]
        for j in range(2):
            for g in range(2):
                mm(psum[6][:, 0:512], C["ones_1x128"],
                   lnrow2[0:1, j, g * 512:(g + 1) * 512], start=True,
                   stop=True, reads=[cbuf], bank=6, tag="ln2bc")
                b.q["tensor"][-1].deps.extend(ld_ln2)
                b.do("vector", lambda e, j=j, g=g: e.tensor_copy(
                    lnr2[:, j, g * 512:(g + 1) * 512], psum[6][:, 0:512]),
                    reads=[pb[6]], writes=[lnr2b], tag="ln2bc_ev")

        yg1_dep = [o for o in b.q["gpsimd"] if o.tag == "styg1"]
        for i in range(NTL):
            b.do("sync", lambda e, i=i: e.dma_start(
                out=x3t[:, :], in_=x3r[:, i, :]),
                writes=[x3tb], deps=x3_dep, dma="c_x3ld", tag="ldx3")
            b.do("gpsimd", lambda e, i=i: e.indirect_dma_start(
                out=y1[:, :], out_offset=None, in_=yg1_hbm[0:NSLOT1, :],
                in_offset=bass.IndirectOffsetOnAxis(
                    ap=idxs[:, 16 + i:17 + i], axis=0)),
                reads=[idxb1], writes=[y1b], deps=yg1_dep, dma="c_ga1",
                tag="ga1")
            b.do("gpsimd", lambda e, i=i: e.indirect_dma_start(
                out=y2[:, :], out_offset=None, in_=yg1_hbm[0:NSLOT1, :],
                in_offset=bass.IndirectOffsetOnAxis(
                    ap=idxs[:, 32 + i:33 + i], axis=0)),
                reads=[idxb1], writes=[y2b], deps=yg1_dep, dma="c_ga2",
                tag="ga2")
            b.do("vector", lambda e, i=i: e.tensor_scalar(
                vv2, y1, ew[:, 2, i:i + 1], None, ALU.mult),
                reads=[y1b, ewb], writes=[vv2b], tag="cmb1")
            b.do("vector", lambda e, i=i: e.scalar_tensor_tensor(
                vv2, y2, ew[:, 3, i:i + 1], vv2, ALU.mult, ALU.add),
                reads=[y2b, ewb, vv2b], writes=[vv2b], tag="cmb2")
            ln_apply(vv2, [vv2b], tt2, [tt2b], tmp3, tmp3b, stc3, stc3b,
                     eps=1e-5, tag=f"lnD2_{i}")
            b.do("vector", lambda e: e.tensor_tensor(
                tt2, tt2, lnr2[:, 0, :], ALU.mult),
                reads=[tt2b, lnr2b], writes=[tt2b], tag="ln2g")
            b.do("vector", lambda e: e.tensor_tensor(
                tt2, tt2, lnr2[:, 1, :], ALU.add),
                reads=[tt2b, lnr2b], writes=[tt2b], tag="ln2b")
            b.do("vector", lambda e, i=i: e.tensor_tensor(
                res, tt2, Xt[i], ALU.add),
                reads=[tt2b, Xb[i]], writes=[resb], tag="resD2")
            b.do("vector", lambda e: e.tensor_tensor(
                res, res, x3t, ALU.add),
                reads=[resb, x3tb], writes=[resb], tag="resD2b")
            ln_apply(res, [resb], Xt[i], [Xb[i]], tmp3, tmp3b, stc3, stc3b,
                     eps=1e-5, tag=f"lnD3_{i}")
        A.pop()
        A.pop()

    if stage_lo <= 0 <= stage_hi:
        stage_a()
    if stage_lo <= 1 <= stage_hi:
        stage_b()
    if stage_lo <= 2 <= stage_hi:
        stage_c()
    if stage_lo <= 3 <= stage_hi:
        stage_d()

    finals = []
    for i in range(0, NTL, 4):
        o = b.do("gpsimd", lambda e, i=i: e.dma_start(
            out=out_hbm.rearrange("(a p) d -> p a d", p=128)[:, i:i + 4, :],
            in_=xh[:, i:i + 4, :]),
            reads=[Xb[j] for j in range(i, i + 4)], dma="c_st", tag=f"stX{i}")
        finals.append(o)
    counts = b.finalize(finals)
    sems = {ch: st.enter_context(nc.semaphore(f"s_{ch}"))
            for ch in sorted(set(counts) | set(ENGS))}

    with nc.Block() as block:
        @block.sync
        def _(eng):
            b.emit("sync", eng, sems)

        @block.tensor
        def _(eng):
            b.emit("tensor", eng, sems)

        @block.vector
        def _(eng):
            b.emit("vector", eng, sems)

        @block.scalar
        def _(eng):
            b.emit("scalar", eng, sems)

        @block.gpsimd
        def _(eng):
            b.emit("gpsimd", eng, sems)
            eng.wait_ge(sems["c_st"], counts["c_st"])

    st.close()
    return nc


def shard_inputs(inputs, core):
    d = inputs
    r1 = lambda a: np.ascontiguousarray(a).reshape(1, -1)
    return {
        "x": np.ascontiguousarray(d["L0_attn_x"][core]),
        "W_attn": d["L0_attn_W_attn"],
        "b_attn": r1(d["L0_attn_b_attn"]),
        "W_proj": d["L0_attn_W_proj"],
        "b_proj": r1(d["L0_attn_b_proj"]),
        "rms_w": r1(d["L0_ffn_rms_w"]),
        "gate0_w": d["L0_ffn_gate_w"],
        "W1_0": np.ascontiguousarray(d["L0_ffn_W1"]).reshape(E * D, FF),
        "b1_0": np.ascontiguousarray(d["L0_ffn_b1"]).reshape(E * 32, 128),
        "W2_0": np.ascontiguousarray(d["L0_ffn_W2"]).reshape(E * FF, D),
        "b2_0": np.ascontiguousarray(d["L0_ffn_b2"]),
        "Wq1": d["L1_attn_Wq"], "Wk1": d["L1_attn_Wk"], "Wv1": d["L1_attn_Wv"],
        "Wq2": d["L1_ffn_Wq"], "Wk2": d["L1_ffn_Wk"], "Wv2": d["L1_ffn_Wv"],
        "Wo2": d["L1_ffn_Wo"],
        "bq2": r1(d["L1_ffn_bq"]), "bk2": r1(d["L1_ffn_bk"]),
        "bv2": r1(d["L1_ffn_bv"]), "bo2": r1(d["L1_ffn_bo"]),
        "ln1_g": r1(d["L1_ffn_ln1_g"]), "ln1_b": r1(d["L1_ffn_ln1_b"]),
        "ln2_g": r1(d["L1_ffn_ln2_g"]), "ln2_b": r1(d["L1_ffn_ln2_b"]),
        "gate1_w": d["L1_ffn_gate_w"], "gate1_b": r1(d["L1_ffn_gate_b"]),
        "W1_1": np.ascontiguousarray(d["L1_ffn_W1"]).reshape(E * D, FF),
        "b1_1": np.ascontiguousarray(d["L1_ffn_b1"]).reshape(E * 32, 128),
        "W2_1": np.ascontiguousarray(d["L1_ffn_W2"]).reshape(E * FF, D),
        "b2_1": np.ascontiguousarray(d["L1_ffn_b2"]),
    }


def kernel(**inputs):
    nc = build_nc()
    in_maps = [shard_inputs(inputs, c) for c in range(8)]
    res = run_bass_kernel_spmd(nc, in_maps, core_ids=list(range(8)))
    return np.stack([res.results[c]["out"] for c in range(8)])
